# version 04
# baseline (speedup 1.0000x reference)
import bisect
import numpy as np
import ml_dtypes
from contextlib import ExitStack

import concourse.bass as bass
import concourse.mybir as mybir
from concourse.bass_utils import run_bass_kernel_spmd
from concourse.alu_op_type import AluOpType as OP
from concourse.bass import IndirectOffsetOnAxis

F32 = mybir.dt.float32
I32 = mybir.dt.int32
BF16 = mybir.dt.bfloat16
AF = mybir.ActivationFunctionType
AX = mybir.AxisListType

NCORES = 8
D = 1024
DT = 8
NSEQ = 2
TSEQ = 2048
NTOK = 4096
NEG = -1.0e30
RS2 = 0.70710678118654752
MOE_SPARSE = True
MOE_NSB = 24
MOE_BIG = 1.0e6
DBG = []


class Kit:
    ENGINES = ("tensor", "vector", "scalar", "sync", "gpsimd")
    DMA_Q = ("sync", "gpsimd", "scalar")
    NSLOT = 24

    def __init__(self, nc, ctx):
        self.nc = nc
        self.esem = {e: ctx.enter_context(nc.semaphore(f"es_{e}")) for e in self.ENGINES}
        self.dsem = {(q, i): ctx.enter_context(nc.semaphore(f"ds_{q}{i}"))
                     for q in self.DMA_Q for i in range(self.NSLOT)}
        self.rec = {e: [] for e in self.ENGINES}
        self.instrs = {e: [] for e in self.ENGINES}
        self.incs = {e: [] for e in self.ENGINES}
        self.dcount = {k: 0 for k in self.dsem}
        self.slot_of = {}
        self.slot_rr = {q: 0 for q in self.DMA_Q}
        self.slot_key = {}
        self.slot_mixed = set()
        self.writer = {}
        self.readers = {}
        self.waited = {e: {} for e in self.ENGINES}

    def _slot(self, q, stream):
        key = (q, stream)
        s = self.slot_of.get(key)
        if s is None:
            if self.slot_rr[q] >= self.NSLOT:
                raise RuntimeError(f"dma stream overflow on {q}: {stream}")
            s = self.slot_rr[q]
            self.slot_rr[q] += 1
            self.slot_of[key] = s
        return s

    def _wait(self, e, tok):
        if tok[0] == "e":
            _, pe, pos = tok
            if pe == e == "tensor":
                return
            lst = self.incs[pe]
            i = bisect.bisect_left(lst, (pos, -1))
            if i < len(lst):
                val = lst[i][1]
            else:
                val = (lst[-1][1] + 1) if lst else 1
                self.instrs[pe][pos][4].append((self.esem[pe], 1))
                lst.append((pos, val))
            key, sem = ("es", pe), self.esem[pe]
        else:
            _, qe, slot, val = tok
            if (qe, slot) in self.slot_mixed:
                val = self.dcount[(qe, slot)]
            key, sem = ("ds", qe, slot), self.dsem[(qe, slot)]
        if self.waited[e].get(key, 0) >= val:
            return
        self.rec[e].append(("wait", sem, val))
        self.waited[e][key] = val

    def emit(self, e, op, *args, rd=(), wr=(), dma=False, stream=None, **kw):
        for r in rd:
            t = self.writer.get(r)
            if t is not None:
                self._wait(e, t)
        for r in wr:
            t = self.writer.get(r)
            if t is not None:
                self._wait(e, t)
            for rt in self.readers.get(r, {}).values():
                self._wait(e, rt)
        entry = ["op", op, args, kw, []]
        self.rec[e].append(entry)
        if dma:
            wkey = wr[0] if wr else None
            if stream is None:
                stream = wkey if wkey is not None else (rd[0] if rd else "main")
                if isinstance(stream, tuple):
                    stream = stream[0]
            slot = self._slot(e, stream)
            sk = (e, slot)
            if self.slot_key.get(sk, wkey) != wkey or                     "dram" in str(getattr(args[0], "space", "")).lower():
                self.slot_mixed.add(sk)
            self.slot_key[sk] = wkey
            self.dcount[sk] += 16
            entry[4].append((self.dsem[sk], 16))
            tok = ("d", e, slot, self.dcount[sk])
            src = ("d", e, slot)
        else:
            self.instrs[e].append(entry)
            tok = ("e", e, len(self.instrs[e]) - 1)
            src = ("e", e)
        for r in rd:
            self.readers.setdefault(r, {})[src] = tok
        for r in wr:
            self.writer[r] = tok
            self.readers[r] = {}
        return entry

    def barrier(self):
        toks = []
        for pe in self.ENGINES:
            n = len(self.instrs[pe])
            if n:
                toks.append(("e", pe, n - 1))
        for (qe, slot), cnt in self.dcount.items():
            if cnt:
                toks.append(("d", qe, slot, cnt))
                self.slot_mixed.add((qe, slot))
        for e in self.ENGINES:
            for t in toks:
                self._wait(e, t)
        self.writer.clear()
        self.readers.clear()
        self.slot_of.clear()
        self.slot_key.clear()
        self.slot_mixed.clear()
        self.slot_rr = {q: 0 for q in self.DMA_Q}

    def finish(self):
        for (qe, slot), cnt in self.dcount.items():
            if cnt:
                self.rec[qe].append(("wait", self.dsem[(qe, slot)], cnt))

    def replay_into(self, blk):
        def mk(en):
            def body(h):
                for ent in self.rec[en]:
                    if ent[0] == "wait":
                        h.wait_ge(ent[1], ent[2])
                    else:
                        try:
                            inst = getattr(h, ent[1])(*ent[2], **ent[3])
                        except Exception:
                            import sys as _s
                            print(f"REPLAY FAIL {en} op={ent[1]} args={[str(a)[:150] for a in ent[2]]} kw={ent[3]}", file=_s.stderr)
                            raise
                        for sem, n in ent[4]:
                            inst.then_inc(sem, n)
            return body
        blk.tensor(mk("tensor"))
        blk.vector(mk("vector"))
        blk.scalar(mk("scalar"))
        blk.gpsimd(mk("gpsimd"))
        blk.sync(mk("sync"))


def sort_network(n):
    pairs = []

    def merge(lo, hi, r):
        step = r * 2
        if step < hi - lo:
            merge(lo, hi, step)
            merge(lo + r, hi, step)
            for i in range(lo + r, hi - r, step):
                pairs.append((i, i + r))
        else:
            pairs.append((lo, lo + r))

    def srt(lo, hi):
        if (hi - lo) >= 1:
            mid = lo + ((hi - lo) // 2)
            srt(lo, mid)
            srt(mid + 1, hi)
            merge(lo, hi, 1)

    srt(0, n - 1)
    return pairs


class Ctx:
    pass


def alloc_common(nc, ctx, C):
    sb = lambda name, shape, dt: ctx.enter_context(nc.sbuf_tensor(name, shape, dt))
    C.ident = sb("ident", [128, 128], BF16)
    C.identf = sb("identf", [128, 128], F32)
    C.mtri = sb("mtri", [128, 1024], F32)
    C.kidx = sb("kidx", [128, 16], F32)
    C.invk = sb("invk", [128, 16], F32)
    C.kval = sb("kval", [128, 16], F32)
    C.onesb = sb("onesb", [128, 1], BF16)
    C.onesbr = sb("onesbr", [1, 128], BF16)
    C.onesf = sb("onesf", [128, 512], F32)
    C.consts = sb("consts", [128, 8], F32)
    C.PA = ctx.enter_context(nc.psum_tensor("PA", [128, 2048], F32))
    C.PB = ctx.enter_context(nc.psum_tensor("PB", [128, 1024], F32))
    C.PS1 = ctx.enter_context(nc.psum_tensor("PS1", [128, 512], F32))
    C.PT = ctx.enter_context(nc.psum_tensor("PT", [128, 512], BF16))
    C.pa = lambda m: C.PA[:, m * 512:(m + 1) * 512]
    C.pb = lambda m: C.PB[:, m * 512:(m + 1) * 512]


def load_consts(K, C, dram):
    K.emit("sync", "dma_start", C.ident[:, :], dram["c_ident"], wr=[("ident",)], dma=True)
    K.emit("sync", "dma_start", C.identf[:, :], dram["c_identf"], wr=[("identf",)], dma=True)
    K.emit("sync", "dma_start", C.mtri[:, :], dram["c_mtri"], wr=[("mtri",)], dma=True)
    K.emit("sync", "dma_start", C.kidx[:, :], dram["c_kidx"], wr=[("kidx",)], dma=True)
    K.emit("sync", "dma_start", C.invk[:, :], dram["c_invk"], wr=[("invk",)], dma=True)
    K.emit("sync", "dma_start", C.kval[:, :], dram["c_kval"], wr=[("kval",)], dma=True)
    K.emit("gpsimd", "memset", C.onesb[:, :], 1.0, wr=[("onesb",)])
    K.emit("gpsimd", "memset", C.onesbr[:, :], 1.0, wr=[("onesbr",)])
    K.emit("gpsimd", "memset", C.onesf[:, :], 1.0, wr=[("onesf",)])
    K.emit("gpsimd", "memset", C.consts[:, 0:1], 1.0e-5, wr=[("consts",)])
    K.emit("gpsimd", "memset", C.consts[:, 1:2], 1.0, wr=[("consts",)])
    K.emit("gpsimd", "memset", C.consts[:, 2:3], 1.0e-4, wr=[("consts",)])
    K.emit("gpsimd", "memset", C.consts[:, 3:4], 1.0e-38, wr=[("consts",)])
    K.emit("gpsimd", "memset", C.consts[:, 4:5], -float(np.float32(np.sqrt(0.5))) /
           float(np.float32(np.sqrt(1.0 / (4.0 * np.pi)) * np.sqrt(2.0))), wr=[("consts",)])


def dbg_tap(K, dram, name, ap, key, col0=0):
    if name in DBG:
        n = 1
        for d in ap.shape[1:]:
            n *= d
        q = "scalar" if ap.dtype == F32 else "gpsimd"
        K.emit(q, "dma_start", dram["dbg"][0:ap.shape[0], col0:col0 + n],
               ap, rd=[key], dma=True)


def rsqrt_ip(K, C, ap, key, eps_col, extra_scale=1.0):
    K.emit("scalar", "activation", ap, ap, AF.Ln, bias=eps_col,
           rd=[key, ("consts",)], wr=[key])
    K.emit("scalar", "activation", ap, ap, AF.Exp, scale=-0.5, rd=[key], wr=[key])
    if extra_scale != 1.0:
        K.emit("vector", "tensor_scalar_mul", ap, ap, float(extra_scale), rd=[key], wr=[key])


class NormBufs:
    def __init__(self, nc, ctx, tag, rms_only=False):
        sb = lambda name, shape, dt: ctx.enter_context(nc.sbuf_tensor(name, shape, dt))
        self.xg = sb(f"{tag}_xg", [128, 8, 512], F32)
        self.xgb = sb(f"{tag}_xgb", [128, 8, 512], BF16)
        self.sq = sb(f"{tag}_sq", [128, 2, 512], BF16)
        if not rms_only:
            self.mb = sb(f"{tag}_mb", [128, 512], F32)
            self.vb = sb(f"{tag}_vb", [128, 512], F32)
        self.invb = sb(f"{tag}_invb", [128, 512], F32)
        self.srow = sb(f"{tag}_srow", [1, 1024], F32)
        self.r1 = sb(f"{tag}_r1", [1, 512], F32)
        self.tag = tag


def norm_group(K, C, NB, src3, g, out_xn, kind, srckeys, xnkey, out_f32=None,
               f32key=None, row_cache=None, gslice=None):
    tg = NB.tag
    kxg, kxgb = (f"{tg}xg",), (f"{tg}xgb",)
    gs = gslice if gslice is not None else slice(g * 512, (g + 1) * 512)
    src_slice = src3[:, :, gs]
    K.emit("sync", "dma_start", NB.xg[:, :, :], src_slice, rd=srckeys, wr=[kxg], dma=True)
    K.emit("gpsimd", "dma_start", NB.xgb[:, :, :], src_slice, rd=srckeys, wr=[kxgb], dma=True)
    if kind == "ln":
        for t in range(DT):
            K.emit("tensor", "matmul", C.PS1[0:1, :], C.onesb[:, 0:1], NB.xgb[:, t, :],
                   start=(t == 0), stop=(t == DT - 1), rd=[("onesb",), kxgb], wr=[("PS1",)])
        K.emit("vector", "tensor_copy", NB.srow[:, 0:512], C.PS1[0:1, :],
               rd=[("PS1",)], wr=[(f"{tg}srow",)])
    for t in range(DT):
        K.emit("scalar", "activation", NB.sq[:, t % 2, :], NB.xgb[:, t, :], AF.Square,
               rd=[kxgb], wr=[(f"{tg}sq", t % 2)])
        K.emit("tensor", "matmul", C.PS1[0:1, :], C.onesb[:, 0:1], NB.sq[:, t % 2, :],
               start=(t == 0), stop=(t == DT - 1), rd=[("onesb",), (f"{tg}sq", t % 2)],
               wr=[("PS1",)])
    K.emit("vector", "tensor_copy", NB.srow[:, 512:1024], C.PS1[0:1, :],
           rd=[("PS1",)], wr=[(f"{tg}srow",)])
    if kind == "ln":
        K.emit("tensor", "matmul", C.pb(0), C.onesf[0:1, 0:128], NB.srow[0:1, 0:512],
               start=True, stop=True, rd=[("onesf",), (f"{tg}srow",)], wr=[("PB", 0)])
    K.emit("tensor", "matmul", C.pb(1), C.onesf[0:1, 0:128], NB.srow[0:1, 512:1024],
           start=True, stop=True, rd=[("onesf",), (f"{tg}srow",)], wr=[("PB", 1)])
    if kind == "ln":
        K.emit("vector", "tensor_scalar_mul", NB.mb[:, :], C.pb(0), 1.0 / D,
               rd=[("PB", 0)], wr=[(f"{tg}mb",)])
        K.emit("scalar", "activation", NB.vb[:, :], NB.mb[:, :], AF.Square,
               rd=[(f"{tg}mb",)], wr=[(f"{tg}vb",)])
        K.emit("vector", "scalar_tensor_tensor", NB.invb[:, :], C.pb(1), 1.0 / D,
               NB.vb[:, :], OP.mult, OP.subtract,
               rd=[("PB", 1), (f"{tg}vb",)], wr=[(f"{tg}invb",)])
    else:
        K.emit("vector", "tensor_scalar_mul", NB.invb[:, :], C.pb(1), 1.0 / D,
               rd=[("PB", 1)], wr=[(f"{tg}invb",)])
    rsqrt_ip(K, C, NB.invb[:, :], (f"{tg}invb",), C.consts[:, 0:1])
    if row_cache is not None:
        rws = row_cache
        K.emit("vector", "tensor_scalar_mul", NB.r1[0:1, :],
               NB.srow[0:1, 0:512], 1.0 / D, rd=[(f"{tg}srow",)], wr=[(f"{tg}r1",)])
        K.emit("sync", "dma_start", rws[0:1, g * 512:(g + 1) * 512], NB.r1[0:1, :],
               rd=[(f"{tg}r1",)], wr=[("RWS", "m", g)], stream="rws", dma=True)
        K.emit("sync", "dma_start", rws[1:2, g * 512:(g + 1) * 512], NB.invb[0:1, :],
               rd=[(f"{tg}invb",)], wr=[("RWS", "i", g)], stream="rws", dma=True)
    if kind == "ln":
        K.emit("vector", "tensor_tensor", NB.xg[:, :, :], NB.xg[:, :, :],
               NB.mb[:, :].unsqueeze(1).broadcast_to([128, 8, 512]), OP.subtract,
               rd=[kxg, (f"{tg}mb",)], wr=[kxg])
    if out_xn is not None:
        K.emit("vector", "tensor_tensor", out_xn, NB.xg[:, :, :],
               NB.invb[:, :].unsqueeze(1).broadcast_to([128, 8, 512]), OP.mult,
               rd=[kxg, (f"{tg}invb",)], wr=[xnkey])
    if out_f32 is not None:
        K.emit("vector", "tensor_tensor", out_f32, NB.xg[:, :, :],
               NB.invb[:, :].unsqueeze(1).broadcast_to([128, 8, 512]), OP.mult,
               rd=[kxg, (f"{tg}invb",)], wr=[f32key])
    return kxg



def build_ffn0(K, C, W, scr, src3, dst3, srckey, dstkey, callid):
    nc = K.nc
    dram = W
    xns = scr["XNS"]
    with ExitStack() as ctx:
        sb = lambda name, shape, dt: ctx.enter_context(nc.sbuf_tensor(f"f{callid}_{name}", shape, dt))
        t = lambda n: (f"f{callid}_{n}",)
        NB = NormBufs(nc, ctx, f"f{callid}")
        XN = sb("xn", [128, 8, 512], BF16)
        AMAJ = sb("amaj", [128, 32, 16], F32)
        ZS = sb("zs", [128, 32, 16], F32)
        ZU = sb("zu", [128, 32, 16], F32)
        CS1 = sb("cs1", [128, 32, 16], F32)
        CS2 = sb("cs2", [128, 32, 16], F32)
        TMPS = sb("tmps", [128, 32, 16], F32)
        SM1 = sb("sm1", [128, 32], F32)
        SM2 = sb("sm2", [128, 32], F32)
        TMX = sb("tmx", [128, 32], F32)
        AT = sb("at", [16, 4096], BF16)
        APRE = sb("apre", [16, 512], F32)
        wproj = sb("wproj", [128, 8, 16], BF16)
        wc = sb("wc", [16, 1024], BF16)
        wwin1 = sb("wwin1", [128, 8, 512], BF16)
        bwin1 = sb("bwin1", [128, 4], F32)
        wwout1 = sb("wwout1", [128, 4, 4096], BF16)
        wwin2 = sb("wwin2", [128, 32, 512], BF16)
        bwin2 = sb("bwin2", [128, 4], F32)
        wwout2 = sb("wwout2", [128, 4, 1024], BF16)
        G1 = sb("g1", [128, 4, 512], BF16)
        G2 = sb("g2", [128, 4, 512], BF16)
        Z1 = sb("z1", [128, 4, 512], BF16)
        HB = sb("hb", [128, 16, 512], BF16)
        XG2 = sb("xg2", [128, 8, 512], F32)

        K.emit("gpsimd", "dma_start", wproj[:, :, :],
               W["L0_ffn_proj"].rearrange("(t p) e -> p t e", p=128), wr=[t("wproj")], dma=True)
        K.emit("gpsimd", "dma_start", wc[:, 0:512], W["L0_ffn_C1"], wr=[t("wc")], dma=True)
        K.emit("gpsimd", "dma_start", wc[:, 512:1024], W["L0_ffn_C2"], wr=[t("wc")], dma=True)
        K.emit("gpsimd", "dma_start", wwin1[:, :, :],
               W["L0_ffn_Win1"][0:1024, :].rearrange("(t p) r -> p t r", p=128),
               wr=[t("wwin1")], dma=True)
        K.emit("sync", "dma_start", bwin1[:, :], W["h_win1bias"], wr=[t("bwin1")], dma=True)
        K.emit("gpsimd", "dma_start", wwout1[:, :, :],
               W["h_wout1T"].rearrange("(t p) o -> p t o", p=128), wr=[t("wwout1")], dma=True)
        K.emit("gpsimd", "dma_start", wwin2[:, :, :],
               W["L0_ffn_Win2"][0:4096, :].rearrange("(t p) r -> p t r", p=128),
               wr=[t("wwin2")], dma=True)
        K.emit("sync", "dma_start", bwin2[:, :], W["h_win2bias"], wr=[t("bwin2")], dma=True)
        K.emit("gpsimd", "dma_start", wwout2[:, :, :],
               W["h_wout2T"].rearrange("(t p) o -> p t o", p=128), wr=[t("wwout2")], dma=True)

        for g in range(8):
            gs = slice(g * 512, (g + 1) * 512)
            srk = [srckey] if isinstance(srckey, tuple) else [(srckey, g)]
            norm_group(K, C, NB, src3, g, XN[:, :, :], "ln", srk, t("xn"))
            K.emit("sync", "dma_start", xns[:, :, gs], XN[:, :, :],
                   rd=[t("xn")], wr=[("XNS", g)], dma=True)
            if g == 0:
                dbg_tap(K, W, "ffn_xn", XN[:, :, :], t("xn"))
            for k in range(DT):
                K.emit("tensor", "matmul", C.PS1[0:16, :], wproj[:, k, :], XN[:, k, :],
                       start=(k == 0), stop=(k == DT - 1),
                       rd=[t("wproj"), t("xn")], wr=[("PS1",)])
            K.emit("vector", "tensor_copy", APRE[:, :], C.PS1[0:16, :], rd=[("PS1",)], wr=[t("apre")])
            for j in range(4):
                K.emit("tensor", "transpose", C.pb(0)[0:128, 0:16],
                       APRE[:, j * 128:(j + 1) * 128], C.identf[0:16, 0:16],
                       rd=[t("apre"), ("identf",)], wr=[("PB", 0)])
                K.emit("vector", "tensor_copy", AMAJ[:, g * 4 + j, :], C.pb(0)[0:128, 0:16],
                       rd=[("PB", 0)], wr=[t("amaj")])

        dbg_tap(K, W, "ffn_amaj", AMAJ[:, :, :], t("amaj"), col0=4096)
        am = AMAJ[:, :, :]
        b3 = lambda ap2: ap2.unsqueeze(2).broadcast_to([128, 32, 16])
        bi = lambda ap2: ap2.unsqueeze(1).broadcast_to([128, 32, 16])
        K.emit("vector", "reduce_sum", SM1[:, :], am, axis=AX.X, rd=[t("amaj")], wr=[t("sm1")])
        K.emit("scalar", "activation", TMPS[:, :, :], am, AF.Square, rd=[t("amaj")], wr=[t("tmps")])
        K.emit("vector", "reduce_sum", SM2[:, :], TMPS[:, :, :], axis=AX.X, rd=[t("tmps")], wr=[t("sm2")])
        K.emit("vector", "tensor_scalar_mul", SM1[:, :], SM1[:, :], 1.0 / 16, rd=[t("sm1")], wr=[t("sm1")])
        K.emit("vector", "tensor_tensor", TMX[:, :], SM1[:, :], SM1[:, :], OP.mult,
               rd=[t("sm1")], wr=[t("tmx")])
        K.emit("vector", "scalar_tensor_tensor", SM2[:, :], SM2[:, :], 1.0 / 16, TMX[:, :],
               OP.mult, OP.subtract, rd=[t("sm2"), t("tmx")], wr=[t("sm2")])
        rsqrt_ip(K, C, SM2[:, :], t("sm2"), C.consts[:, 0:1], extra_scale=0.5)
        K.emit("vector", "tensor_tensor", ZU[:, :, :], am, b3(SM1[:, :]), OP.subtract,
               rd=[t("amaj"), t("sm1")], wr=[t("zu")])
        K.emit("vector", "tensor_tensor", ZU[:, :, :], ZU[:, :, :], b3(SM2[:, :]), OP.mult,
               rd=[t("zu"), t("sm2")], wr=[t("zu")])
        K.emit("vector", "reduce_max", SM1[:, :], ZU[:, :, :], axis=AX.X, rd=[t("zu")], wr=[t("sm1")])
        K.emit("vector", "tensor_tensor", ZU[:, :, :], ZU[:, :, :], b3(SM1[:, :]), OP.subtract,
               rd=[t("zu"), t("sm1")], wr=[t("zu")])
        K.emit("vector", "tensor_copy", ZS[:, :, :], ZU[:, :, :], rd=[t("zu")], wr=[t("zs")])
        for (i, j) in sort_network(16):
            K.emit("vector", "tensor_tensor", TMX[:, :], ZS[:, :, i], ZS[:, :, j], OP.max,
                   rd=[t("zs")], wr=[t("tmx")])
            K.emit("vector", "tensor_tensor", ZS[:, :, j], ZS[:, :, i], ZS[:, :, j], OP.min,
                   rd=[t("zs")], wr=[t("zs")])
            K.emit("vector", "tensor_copy", ZS[:, :, i], TMX[:, :], rd=[t("tmx")], wr=[t("zs")])
        K.emit("vector", "tensor_copy", CS1[:, :, :], ZS[:, :, :], rd=[t("zs")], wr=[t("cs1")])
        K.emit("scalar", "activation", CS2[:, :, :], ZS[:, :, :], AF.Square, rd=[t("zs")], wr=[t("cs2")])
        for CS, kk in ((CS1, "cs1"), (CS2, "cs2")):
            for sh in (1, 2, 4, 8):
                K.emit("vector", "tensor_tensor", TMPS[:, :, sh:16], CS[:, :, sh:16],
                       CS[:, :, 0:16 - sh], OP.add, rd=[t(kk)], wr=[t("tmps")])
                K.emit("vector", "tensor_copy", TMPS[:, :, 0:sh], CS[:, :, 0:sh],
                       rd=[t(kk)], wr=[t("tmps")])
                K.emit("vector", "tensor_copy", CS[:, :, :], TMPS[:, :, :],
                       rd=[t("tmps")], wr=[t(kk)])
        K.emit("vector", "tensor_tensor", CS1[:, :, :], CS1[:, :, :], bi(C.invk[:, :]), OP.mult,
               rd=[t("cs1"), ("invk",)], wr=[t("cs1")])
        K.emit("vector", "tensor_tensor", CS2[:, :, :], CS2[:, :, :], bi(C.invk[:, :]), OP.mult,
               rd=[t("cs2"), ("invk",)], wr=[t("cs2")])
        K.emit("vector", "tensor_tensor", TMPS[:, :, :], CS1[:, :, :], CS1[:, :, :], OP.mult,
               rd=[t("cs1")], wr=[t("tmps")])
        K.emit("vector", "tensor_tensor", CS2[:, :, :], CS2[:, :, :], TMPS[:, :, :], OP.subtract,
               rd=[t("cs2"), t("tmps")], wr=[t("cs2")])
        K.emit("vector", "tensor_tensor", CS2[:, :, :], CS2[:, :, :], bi(C.kval[:, :]), OP.mult,
               rd=[t("cs2"), ("kval",)], wr=[t("cs2")])
        K.emit("vector", "tensor_scalar", CS2[:, :, :], CS2[:, :, :], -1.0, 1.0, OP.mult, OP.add,
               rd=[t("cs2")], wr=[t("cs2")])
        K.emit("vector", "tensor_tensor", CS2[:, :, :], CS2[:, :, :], bi(C.invk[:, :]), OP.mult,
               rd=[t("cs2"), ("invk",)], wr=[t("cs2")])
        K.emit("vector", "tensor_scalar_max", CS2[:, :, :], CS2[:, :, :], 0.0,
               rd=[t("cs2")], wr=[t("cs2")])
        K.emit("scalar", "activation", CS2[:, :, :], CS2[:, :, :], AF.Ln, bias=C.consts[:, 3:4],
               rd=[t("cs2"), ("consts",)], wr=[t("cs2")])
        K.emit("scalar", "activation", CS2[:, :, :], CS2[:, :, :], AF.Exp, scale=0.5,
               rd=[t("cs2")], wr=[t("cs2")])
        K.emit("vector", "tensor_tensor", CS1[:, :, :], CS1[:, :, :], CS2[:, :, :], OP.subtract,
               rd=[t("cs1"), t("cs2")], wr=[t("cs1")])
        K.emit("vector", "tensor_tensor", TMPS[:, :, :], CS1[:, :, :], ZS[:, :, :], OP.is_le,
               rd=[t("cs1"), t("zs")], wr=[t("tmps")])
        K.emit("vector", "reduce_sum", SM1[:, :], TMPS[:, :, :], axis=AX.X, rd=[t("tmps")], wr=[t("sm1")])
        K.emit("vector", "tensor_scalar_add", SM1[:, :], SM1[:, :], -1.0, rd=[t("sm1")], wr=[t("sm1")])
        K.emit("vector", "tensor_tensor", TMPS[:, :, :], bi(C.kidx[:, :]), b3(SM1[:, :]), OP.is_equal,
               rd=[("kidx",), t("sm1")], wr=[t("tmps")])
        K.emit("vector", "tensor_tensor", TMPS[:, :, :], TMPS[:, :, :], CS1[:, :, :], OP.mult,
               rd=[t("tmps"), t("cs1")], wr=[t("tmps")])
        K.emit("vector", "reduce_sum", SM2[:, :], TMPS[:, :, :], axis=AX.X, rd=[t("tmps")], wr=[t("sm2")])
        K.emit("vector", "tensor_tensor", ZU[:, :, :], ZU[:, :, :], b3(SM2[:, :]), OP.subtract,
               rd=[t("zu"), t("sm2")], wr=[t("zu")])
        K.emit("vector", "tensor_scalar_max", ZU[:, :, :], ZU[:, :, :], 0.0, rd=[t("zu")], wr=[t("zu")])
        K.emit("scalar", "activation", ZU[:, :, :], ZU[:, :, :], AF.Square, rd=[t("zu")], wr=[t("zu")])
        dbg_tap(K, W, "ffn_p", ZU[:, :, :], t("zu"), col0=6144)
        for j in range(32):
            K.emit("tensor", "transpose", C.PS1[0:16, 0:128], ZU[:, j, :], C.identf[:, :],
                   rd=[t("zu"), ("identf",)], wr=[("PS1",)])
            K.emit("vector", "tensor_copy", AT[:, j * 128:(j + 1) * 128], C.PS1[0:16, 0:128],
                   rd=[("PS1",)], wr=[t("at")])

        for g in range(8):
            gs = slice(g * 512, (g + 1) * 512)
            kxn = t("xn")
            K.emit("sync", "dma_start", XN[:, :, :], xns[:, :, gs],
                   rd=[("XNS", g)], wr=[kxn], dma=True)
            if g == 0:
                dbg_tap(K, W, "ffn_xnc", XN[:, :, :], kxn, col0=10240)
            if g == 0:
                dbg_tap(K, W, "ffn_at", AT[:, :], t("at"))
            for GB, coff, gk in ((G1, 0, "g1"), (G2, 512, "g2")):
                for m in range(4):
                    K.emit("tensor", "matmul", C.pa(m), wc[:, coff + m * 128:coff + (m + 1) * 128],
                           AT[:, gs], start=True, stop=True, rd=[t("wc"), t("at")], wr=[("PA", m)])
                for m in range(4):
                    K.emit("vector", "tensor_copy", GB[:, m, :], C.pa(m),
                           rd=[("PA", m)], wr=[t(gk)])
            if g == 0:
                dbg_tap(K, W, "ffn_g1", G1[:, :, :], t("g1"), col0=8192)
            for m in range(4):
                pk = ("PB", m % 2)
                pap = C.pb(m % 2)
                for k in range(DT):
                    K.emit("tensor", "matmul", pap, wwin1[:, k, m * 128:(m + 1) * 128],
                           XN[:, k, :], start=(k == 0), stop=(k == DT - 1),
                           rd=[t("wwin1"), kxn], wr=[pk])
                if g == 0 and m == 0 and "ffn_z1raw" in DBG:
                    K.emit("vector", "tensor_copy", XG2[:, 0, :], pap, rd=[pk], wr=[t("xg2")])
                    dbg_tap(K, W, "ffn_z1raw", XG2[:, 0, :], t("xg2"), col0=14336)
                K.emit("vector", "scalar_tensor_tensor", Z1[:, m, :], pap, bwin1[:, m:m + 1],
                       G1[:, m, :], OP.add, OP.mult, rd=[pk, t("bwin1"), t("g1")], wr=[t("z1")])
            for half in range(2):
                for mm_ in range(16):
                    m = half * 16 + mm_
                    pk = ("PA", mm_ % 2)
                    pap = C.pa(mm_ % 2)
                    for k in range(4):
                        K.emit("tensor", "matmul", pap, wwout1[:, k, m * 128:(m + 1) * 128],
                               Z1[:, k, :], start=(k == 0), stop=(k == 3),
                               rd=[t("wwout1"), t("z1")], wr=[pk])
                    K.emit("scalar", "activation", HB[:, mm_, :], pap, AF.Gelu,
                           rd=[pk], wr=[t("hb")])
                for mm_ in range(16):
                    k = half * 16 + mm_
                    for m in range(4):
                        pk = ("PB", m) if m < 2 else ("PA", m)
                        pap = C.pb(m) if m < 2 else C.pa(m)
                        K.emit("tensor", "matmul", pap, wwin2[:, k, m * 128:(m + 1) * 128],
                               HB[:, mm_, :], start=(k == 0), stop=(k == 31),
                               rd=[t("wwin2"), t("hb")], wr=[pk])
            for m in range(4):
                pk = ("PB", m) if m < 2 else ("PA", m)
                pap = C.pb(m) if m < 2 else C.pa(m)
                K.emit("vector", "scalar_tensor_tensor", Z1[:, m, :], pap, bwin2[:, m:m + 1],
                       G2[:, m, :], OP.add, OP.mult, rd=[pk, t("bwin2"), t("g2")], wr=[t("z1")])
            if g == 0:
                dbg_tap(K, W, "ffn_z1", Z1[:, :, :], t("z1"), col0=8192)

            K.emit("sync", "dma_start", XG2[:, :, :], src3[:, :, gs],
                   rd=([srckey] if isinstance(srckey, tuple) else [(srckey, g)]),
                   wr=[t("xg2")], dma=True)
            for m in range(8):
                pk = ("PA", m % 2)
                pap = C.pa(m % 2)
                for k in range(4):
                    K.emit("tensor", "matmul", pap, wwout2[:, k, m * 128:(m + 1) * 128],
                           Z1[:, k, :], start=(k == 0), stop=(k == 3),
                           rd=[t("wwout2"), t("z1")], wr=[pk])
                K.emit("vector", "scalar_tensor_tensor", XG2[:, m, :], pap, 0.5,
                       XG2[:, m, :], OP.mult, OP.add, rd=[pk, t("xg2")], wr=[t("xg2")])
            K.emit("sync", "dma_start", dst3[:, :, gs], XG2[:, :, :],
                   rd=[t("xg2")], wr=[(dstkey, g)], dma=True)
    K.barrier()



def build_ln_pass(K, C, src3, dst3, srckey, dstkey, tag, dst_bf=None, dstbfkey=None,
                  row_cache=None):
    nc = K.nc
    with ExitStack() as ctx:
        NB = NormBufs(nc, ctx, tag)
        XO = ctx.enter_context(nc.sbuf_tensor(f"{tag}_xo", [128, 8, 512], F32))
        XOB = ctx.enter_context(nc.sbuf_tensor(f"{tag}_xob", [128, 8, 512], BF16)) \
            if dst_bf is not None else None
        for g in range(8):
            gs = slice(g * 512, (g + 1) * 512)
            srk = [srckey] if isinstance(srckey, tuple) else [(srckey, g)]
            norm_group(K, C, NB, src3, g,
                       XOB[:, :, :] if XOB is not None else None, "ln",
                       srk, (f"{tag}xob",),
                       out_f32=XO[:, :, :] if dst3 is not None else None,
                       f32key=(f"{tag}xo",), row_cache=row_cache)
            if dst3 is not None:
                K.emit("sync", "dma_start", dst3[:, :, gs], XO[:, :, :],
                       rd=[(f"{tag}xo",)], wr=[(dstkey, g)], dma=True)
            if dst_bf is not None:
                K.emit("sync", "dma_start", dst_bf[:, :, gs], XOB[:, :, :],
                       rd=[(f"{tag}xob",)], wr=[(dstbfkey, g)], dma=True)
    K.barrier()


WEIGHT_SHAPES = {
    "L0_attn_in_proj_w": (1024, 4256),
    "L0_attn_attn_w": (1024, 1152),
    "L0_attn_proj_w": (1024, 1024),
    "L0_attn_recept_w": (1024, 1024),
    "L0_ffn_proj": (1024, 16),
    "L0_ffn_C1": (16, 512),
    "L0_ffn_Win1": (1025, 512),
    "L0_ffn_C2": (16, 512),
    "L0_ffn_Win2": (4097, 512),
}

HOST_SHAPES = {
    "c_map64": (32, 128),
    "c_sel32": (32, 16),
    "h_wout1T": (512, 4096),
    "h_wout2T": (512, 1024),
    "h_win1bias": (128, 4),
    "h_win2bias": (128, 4),
    "h_convw": (128, 68),
    "h_convb": (128, 17),
    "h_mnorm": (128, 16),
    "h_dtb": (32, 1),
    "h_alog": (32, 1),
    "h_dvec": (32, 1),
    "h_maak": (128, 8),
    "h_maar": (128, 8),
    "h_outproj_cm": (8, 2048, 128),
    "h_key_cm": (32, 1024, 128),
    "h_value_cm": (8, 4096, 128),
    "h_wqT": (1024, 1024),
    "h_wkT": (1024, 1024),
    "h_wvT": (1024, 1024),
    "h_woT": (1024, 1024),
    "h_gain": (1, 1),
    "h_routerT": (1024, 8),
    "c_tokids": (128, 32),
    "c_iota128": (128, 1),
    "h_ebias": (8, 1),
    "h_w1": (9, 1024, 2048),
    "h_w3": (9, 1024, 2048),
    "h_w2": (9, 2048, 1024),
}

CONST_SHAPES = {
    "c_identf": (128, 128),
    "c_mtri": (128, 1024),
    "c_kidx": (128, 16),
    "c_invk": (128, 16),
    "c_kval": (128, 16),
}


def build_program(phases=("all",)):
    nc = bass.Bass("TRN2", target_bir_lowering=False)
    dram = {}

    def din(name, shape, dt=F32):
        dram[name] = nc.dram_tensor(name, list(shape), dt, kind="ExternalInput").ap()

    din("x_in", (128, 8, 4096))
    for n, s in WEIGHT_SHAPES.items():
        din(n, s)
    for n, s in HOST_SHAPES.items():
        din(n, s, I32 if n == "c_tokids" else F32)
    for n, s in CONST_SHAPES.items():
        din(n, s)
    din("c_ident", (128, 128), BF16)
    out = nc.dram_tensor("out", [128, 8, 4096], F32, kind="ExternalOutput").ap()
    dram["dbg"] = nc.dram_tensor("dbg", [128, 16384], F32, kind="ExternalOutput").ap()
    scr = {}
    for n in ("XB", "XC", "XD", "XE", "XF", "S0", "S1", "S2", "ACC"):
        scr[n] = nc.dram_tensor(f"scr_{n}", [128, 8, 4096], F32).ap()
    for n in ("SNM", "SNA", "XNS"):
        scr[n] = nc.dram_tensor(f"scr_{n}", [128, 8, 4096], BF16).ap()
    scr["CWS"] = nc.dram_tensor("scr_CWS", [8, 4096], F32).ap()
    scr["IPB"] = nc.dram_tensor("scr_IPB", [34, 128, 8, 128], BF16).ap()
    scr["SNMT"] = nc.dram_tensor("scr_SNMT", [4097, 1024], BF16).ap()
    scr["ACCT"] = nc.dram_tensor("scr_ACCT", [4097, 1024], F32).ap()
    scr["IDXD"] = nc.dram_tensor("scr_IDXD", [MOE_NSB * 512 + 128], I32).ap()
    scr["CWL"] = nc.dram_tensor("scr_CWL", [MOE_NSB * 512 + 128], F32).ap()
    scr["W1B"] = nc.dram_tensor("scr_W1B", [9, 128, 8, 2048], BF16).ap()
    scr["W3B"] = nc.dram_tensor("scr_W3B", [9, 128, 8, 2048], BF16).ap()
    scr["W2B"] = nc.dram_tensor("scr_W2B", [9, 128, 16, 1024], BF16).ap()
    scr["RWS"] = nc.dram_tensor("scr_RWS", [2, 4096], F32).ap()
    scr["WNS"] = nc.dram_tensor("scr_WNS", [4, 128, 8, 1024], F32).ap()

    def has(p):
        return "all" in phases or p in phases

    with ExitStack() as ctx:
        C = Ctx()
        alloc_common(nc, ctx, C)
        K = Kit(nc, ctx)
        load_consts(K, C, dram)
        if MOE_SPARSE and (has("moe1") or has("moe2")):
            for e in range(9):
                K.emit("gpsimd", "dma_start", scr["W1B"][e],
                       dram["h_w1"][e].rearrange("(t p) h -> p t h", p=128),
                       wr=[("WPC", "x")], stream="wpc", dma=True)
                K.emit("gpsimd", "dma_start", scr["W3B"][e],
                       dram["h_w3"][e].rearrange("(t p) h -> p t h", p=128),
                       wr=[("WPC", "y")], stream="wpc", dma=True)
                K.emit("gpsimd", "dma_start", scr["W2B"][e],
                       dram["h_w2"][e].rearrange("(t p) d -> p t d", p=128),
                       wr=[("WPC", "z")], stream="wpc", dma=True)
        if has("att0") and has("mamba"):
            ip = dram["L0_attn_in_proj_w"]
            for m in range(34):
                mw = 128 if m < 33 else 32
                K.emit("gpsimd", "dma_start", scr["IPB"][m][:, :, 0:mw],
                       ip.rearrange("(t p) m -> p t m", p=128)[:, :, m * 128:m * 128 + mw],
                       wr=[("IPB",)], stream="ipb", dma=True)

        cur, curkey = dram["x_in"], ("x_in",)

        def nxt(name):
            return scr[name], name

        if has("ffn0a"):
            d, dk = nxt("XB")
            build_ffn0(K, C, dram, scr, cur, d, curkey, dk, 0)
            cur, curkey = d, dk
        if has("att0"):
            build_ln_pass(K, C, cur, scr["S0"], curkey, "S0", "lnA")
            s, skey = scr["S0"], "S0"
            if has("mamba"):
                build_mamba(K, C, dram, scr, s, scr["S1"], skey, "S1")
                s, skey = scr["S1"], "S1"
            if has("mqa"):
                build_mqa(K, C, dram, s, scr["S2"], skey, "S2")
                s, skey = scr["S2"], "S2"
            d, dk = nxt("XC")
            build_cmix(K, C, dram, s, cur, d, skey, curkey, dk, skip=not has("cmix"))
            cur, curkey = d, dk
        if has("ffn0b"):
            d, dk = nxt("XD")
            build_ffn0(K, C, dram, scr, cur, d, curkey, dk, 1)
            cur, curkey = d, dk
        if has("moe1"):
            d, dk = nxt("XE")
            (build_moe_sparse if MOE_SPARSE else build_moe)(K, C, dram, scr, cur, d, curkey, dk, 0)
            cur, curkey = d, dk
        if has("att1"):
            d, dk = nxt("XF")
            build_att1(K, C, dram, scr, cur, d, curkey, dk)
            cur, curkey = d, dk
        if has("moe2"):
            (build_moe_sparse if MOE_SPARSE else build_moe)(K, C, dram, scr, cur, out, curkey, "out", 1)
            cur, curkey = out, "out"

        if curkey != "out":
            cp = ctx.enter_context(nc.sbuf_tensor("cpbuf", [128, 8, 512], F32))
            for g in range(8):
                gs = slice(g * 512, (g + 1) * 512)
                rdk = [curkey if isinstance(curkey, tuple) else (curkey, g)]
                K.emit("sync", "dma_start", cp[:, :, :], cur[:, :, gs], rd=rdk,
                       wr=[("cp",)], dma=True)
                K.emit("sync", "dma_start", out[:, :, gs], cp[:, :, :], rd=[("cp",)],
                       wr=[("out", g)], dma=True)
        K.finish()
        blk = ctx.enter_context(nc.Block())
        K.replay_into(blk)
    return nc




def build_moe(K, C, W, scr, src3, dst3, srckey, dstkey, callid):
    nc = K.nc
    snm, acc = scr["SNM"], scr["ACC"]
    with ExitStack() as ctx:
        sb = lambda name, shape, dt: ctx.enter_context(nc.sbuf_tensor(f"e{callid}_{name}", shape, dt))
        t = lambda n: (f"e{callid}_{n}",)
        NB = NormBufs(nc, ctx, f"e{callid}")
        CWROW = sb("cwrow", [1, 1024], F32)
        ctxA = ExitStack()
        sbA = lambda name, shape, dt: ctxA.enter_context(nc.sbuf_tensor(f"e{callid}_{name}", shape, dt))
        XN = sbA("xn", [128, 8, 512], BF16)
        CWT = sbA("cwt", [8, 4096], F32)
        LGR = sbA("lgr", [8, 512], F32)
        LGT = sbA("lgt", [128, 32, 8], F32)
        SM1 = sbA("sm1", [128, 32], F32)
        SM2 = sbA("sm2", [128, 32], F32)
        MK1 = sbA("mk1", [128, 32, 8], F32)
        MK2 = sbA("mk2", [128, 32, 8], F32)
        wrt = sbA("wrt", [128, 8, 8], BF16)
        ebv = sbA("ebv", [8, 1], F32)

        K.emit("gpsimd", "dma_start", wrt[:, :, :],
               W["h_routerT"].rearrange("(t p) e -> p t e", p=128), wr=[t("wrt")], dma=True)
        K.emit("sync", "dma_start", ebv[:, :], W["h_ebias"], wr=[t("ebv")], dma=True)

        for g in range(8):
            gs = slice(g * 512, (g + 1) * 512)
            srk = [srckey] if isinstance(srckey, tuple) else [(srckey, g)]
            norm_group(K, C, NB, src3, g, XN[:, :, :], "ln", srk, t("xn"))
            K.emit("sync", "dma_start", snm[:, :, gs], XN[:, :, :],
                   rd=[t("xn")], wr=[("SNM", g)], dma=True)
            for k in range(DT):
                K.emit("tensor", "matmul", C.PS1[0:8, :], wrt[:, k, :], XN[:, k, :],
                       start=(k == 0), stop=(k == DT - 1),
                       rd=[t("wrt"), t("xn")], wr=[("PS1",)])
            K.emit("vector", "tensor_scalar_add", LGR[:, :], C.PS1[0:8, :], ebv[:, 0:1],
                   rd=[("PS1",), t("ebv")], wr=[t("lgr")])
            for j in range(4):
                K.emit("tensor", "transpose", C.pb(0)[0:128, 0:8],
                       LGR[:, j * 128:(j + 1) * 128], C.identf[0:8, 0:8],
                       rd=[t("lgr"), ("identf",)], wr=[("PB", 0)])
                K.emit("vector", "tensor_copy", LGT[:, g * 4 + j, :], C.pb(0)[0:128, 0:8],
                       rd=[("PB", 0)], wr=[t("lgt")])
        b3 = lambda ap2: ap2.unsqueeze(2).broadcast_to([128, 32, 8])
        K.emit("vector", "reduce_max", SM1[:, :], LGT[:, :, :], axis=AX.X,
               rd=[t("lgt")], wr=[t("sm1")])
        K.emit("vector", "tensor_tensor", MK1[:, :, :], LGT[:, :, :], b3(SM1[:, :]),
               OP.is_equal, rd=[t("lgt"), t("sm1")], wr=[t("mk1")])
        K.emit("vector", "scalar_tensor_tensor", MK2[:, :, :], MK1[:, :, :], NEG,
               LGT[:, :, :], OP.mult, OP.add, rd=[t("mk1"), t("lgt")], wr=[t("mk2")])
        K.emit("vector", "reduce_max", SM2[:, :], MK2[:, :, :], axis=AX.X,
               rd=[t("mk2")], wr=[t("sm2")])
        K.emit("vector", "tensor_tensor", MK2[:, :, :], MK2[:, :, :], b3(SM2[:, :]),
               OP.is_equal, rd=[t("mk2"), t("sm2")], wr=[t("mk2")])
        K.emit("vector", "tensor_tensor", SM1[:, :], SM1[:, :], SM2[:, :], OP.subtract,
               rd=[t("sm1"), t("sm2")], wr=[t("sm1")])
        K.emit("scalar", "activation", SM1[:, :], SM1[:, :], AF.Sigmoid,
               rd=[t("sm1")], wr=[t("sm1")])
        K.emit("vector", "tensor_scalar", SM2[:, :], SM1[:, :], -1.0, 1.0, OP.mult, OP.add,
               rd=[t("sm1")], wr=[t("sm2")])
        K.emit("vector", "tensor_tensor", MK1[:, :, :], MK1[:, :, :], b3(SM1[:, :]), OP.mult,
               rd=[t("mk1"), t("sm1")], wr=[t("mk1")])
        K.emit("vector", "tensor_tensor", MK2[:, :, :], MK2[:, :, :], b3(SM2[:, :]), OP.mult,
               rd=[t("mk2"), t("sm2")], wr=[t("mk2")])
        K.emit("vector", "tensor_tensor", MK1[:, :, :], MK1[:, :, :], MK2[:, :, :], OP.add,
               rd=[t("mk1"), t("mk2")], wr=[t("mk1")])
        dbg_tap(K, W, "moe_cwtok", MK1[:, :, :], t("mk1"), col0=8192)
        for j in range(32):
            K.emit("tensor", "transpose", C.pb(0)[0:8, 0:128], MK1[:, j, :], C.identf[:, :],
                   rd=[t("mk1"), ("identf",)], wr=[("PB", 0)])
            K.emit("vector", "tensor_copy", CWT[:, j * 128:(j + 1) * 128], C.pb(0)[0:8, 0:128],
                   rd=[("PB", 0)], wr=[t("cwt")])
        K.emit("sync", "dma_start", scr["CWS"], CWT[:, :], rd=[t("cwt")], wr=[("CWS",)], dma=True)

        ctxA.close()
        K.barrier()
        ctxB = ExitStack()
        sbB = lambda name, shape, dt: ctxB.enter_context(nc.sbuf_tensor(f"e{callid}_{name}", shape, dt))
        WX = sbB("wx", [128, 8, 2048], BF16)
        WY = sbB("wy", [128, 8, 2048], BF16)
        WZ = sbB("wz", [128, 16, 1024], BF16)
        XNC = sbB("xnc", [128, 8, 1024], BF16)
        EO = sbB("eo", [128, 16, 512], BF16)
        CWB = sbB("cwb", [128, 512], BF16)
        W2E = sbB("w2e", [128, 4, 512], F32)
        TM1 = sbB("tm1", [128, 2, 512], BF16)
        TM2 = sbB("tm2", [128, 2, 512], BF16)
        for e in range(9):
            K.emit("gpsimd", "dma_start", WZ[:, :, :],
                   W["h_w2"][e].rearrange("(t p) d -> p t d", p=128),
                   wr=[t("wz")], stream="wz", dma=True)
            if e == 0:
                K.emit("gpsimd", "dma_start", WX[:, :, :],
                       W["h_w1"][0].rearrange("(t p) h -> p t h", p=128),
                       wr=[t("wx")], stream="wx", dma=True)
                K.emit("gpsimd", "dma_start", WY[:, :, :],
                       W["h_w3"][0].rearrange("(t p) h -> p t h", p=128),
                       wr=[t("wy")], stream="wy", dma=True)
            for ch in range(4):
                cs = slice(ch * 1024, (ch + 1) * 1024)
                K.emit("sync", "dma_start", XNC[:, :, :], snm[:, :, cs],
                       rd=[("SNM", 2 * ch), ("SNM", 2 * ch + 1)], wr=[t("xnc")],
                       stream="xnc", dma=True)
                if e < 8:
                    K.emit("sync", "dma_start", CWROW[:, :], scr["CWS"][e:e + 1, cs],
                           rd=[("CWS",)], wr=[t("cwrow")], stream="cwrow", dma=True)
                for n2 in range(2):
                    ns = slice(n2 * 512, (n2 + 1) * 512)
                    if e < 8:
                        K.emit("tensor", "matmul", C.pb(0), C.onesf[0:1, 0:128],
                               CWROW[0:1, ns],
                               start=True, stop=True,
                               rd=[("onesf",), t("cwrow")], wr=[("PB", 0)])
                        K.emit("vector", "tensor_copy", CWB[:, :], C.pb(0),
                               rd=[("PB", 0)], wr=[t("cwb")])
                    elif ch == 0 and n2 == 0:
                        K.emit("vector", "memset", CWB[:, :], 1.0, wr=[t("cwb")])
                    for ht in range(16):
                        pg, pu = (0, 1) if ht % 2 == 0 else (2, 3)
                        par = ht % 2
                        for k in range(DT):
                            K.emit("tensor", "matmul", C.pa(pg), WX[:, k, ht * 128:(ht + 1) * 128],
                                   XNC[:, k, ns], start=(k == 0), stop=(k == DT - 1),
                                   rd=[t("wx"), t("xnc")], wr=[("PA", pg)])
                        for k in range(DT):
                            K.emit("tensor", "matmul", C.pa(pu), WY[:, k, ht * 128:(ht + 1) * 128],
                                   XNC[:, k, ns], start=(k == 0), stop=(k == DT - 1),
                                   rd=[t("wy"), t("xnc")], wr=[("PA", pu)])
                        K.emit("scalar", "activation", TM1[:, par, :], C.pa(pg), AF.Silu,
                               rd=[("PA", pg)], wr=[t("tm1")])
                        K.emit("vector", "tensor_tensor", TM2[:, par, :], C.pa(pu), CWB[:, :],
                               OP.mult, rd=[("PA", pu), t("cwb")], wr=[t("tm2")])
                        K.emit("vector", "tensor_tensor", EO[:, ht, :], TM1[:, par, :],
                               TM2[:, par, :], OP.mult, rd=[t("tm1"), t("tm2")], wr=[t("eo")])
                    if ch == 3 and n2 == 1 and e < 8:
                        K.emit("gpsimd", "dma_start", WX[:, :, :],
                               W["h_w1"][e + 1].rearrange("(t p) h -> p t h", p=128),
                               wr=[t("wx")], stream="wx", dma=True)
                        K.emit("gpsimd", "dma_start", WY[:, :, :],
                               W["h_w3"][e + 1].rearrange("(t p) h -> p t h", p=128),
                               wr=[t("wy")], stream="wy", dma=True)
                    for half in range(2):
                        for mm_ in range(4):
                            m = half * 4 + mm_
                            pk = ("PA", 2 + mm_ % 2)
                            pap = C.pa(2 + mm_ % 2)
                            for k in range(16):
                                K.emit("tensor", "matmul", pap, WZ[:, k, m * 128:(m + 1) * 128],
                                       EO[:, k, :], start=(k == 0), stop=(k == 15),
                                       rd=[t("wz"), t("eo")], wr=[pk])
                            K.emit("vector", "tensor_copy", W2E[:, mm_, :], pap,
                                   rd=[pk], wr=[t("w2e")])
                        K.emit("gpsimd", "dma_start", acc[:, half * 4:(half + 1) * 4, cs][:, :, ns],
                               W2E[:, :, :], rd=[t("w2e")], wr=[("ACC", ch, n2, half)],
                               stream="acc", dma=True,
                               **({} if e == 0 else {"accum_op": OP.add}))
        ctxB.close()
        K.barrier()
        XG2 = sb("xg2", [128, 8, 512], F32)
        for g in range(8):
            gs = slice(g * 512, (g + 1) * 512)
            srk = [srckey] if isinstance(srckey, tuple) else [(srckey, g)]
            K.emit("sync", "dma_start", XG2[:, :, :], src3[:, :, gs], rd=srk,
                   wr=[t("xg2")], stream="xg2", dma=True)
            kxg = (f"e{callid}xg",)
            K.emit("sync", "dma_start", NB.xg[:, :, :], acc[:, :, gs],
                   rd=[("ACC", g // 2, g % 2, 0), ("ACC", g // 2, g % 2, 1)], wr=[kxg],
                   stream="accld", dma=True)
            K.emit("vector", "scalar_tensor_tensor", XG2[:, :, :], NB.xg[:, :, :], 0.5,
                   XG2[:, :, :], OP.mult, OP.add,
                   rd=[kxg, t("xg2")], wr=[t("xg2")])
            K.emit("sync", "dma_start", dst3[:, :, gs], XG2[:, :, :],
                   rd=[t("xg2")], wr=[(dstkey, g)], dma=True)
    K.barrier()




def build_moe_sparse(K, C, W, scr, src3, dst3, srckey, dstkey, callid):
    nc = K.nc
    snm, snmt = scr["SNM"], scr["SNMT"]
    acct, idxd, cwl = scr["ACCT"], scr["IDXD"], scr["CWL"]
    w1b, w3b, w2b = scr["W1B"], scr["W3B"], scr["W2B"]
    NSB = MOE_NSB
    DUMP = NSB * 512
    with ExitStack() as ctx:
        sb = lambda name, shape, dt: ctx.enter_context(nc.sbuf_tensor(f"e{callid}_{name}", shape, dt))
        t = lambda n: (f"e{callid}_{n}",)
        OWNF = sb("ownf", [1, NSB], F32)
        ctxA = ExitStack()
        sbA = lambda name, shape, dt: ctxA.enter_context(nc.sbuf_tensor(f"e{callid}_{name}", shape, dt))
        NB = NormBufs(nc, ctxA, f"e{callid}")
        XN = sbA("xn", [128, 8, 512], BF16)
        TMJ = sbA("tmj", [128, 4, 8, 128], BF16)
        CWT = sbA("cwt", [8, 4096], F32)
        MSK = sbA("msk", [8, 4096], F32)
        CUM = sbA("cum", [8, 4096], F32)
        LGR = sbA("lgr", [8, 512], F32)
        LGT = sbA("lgt", [128, 32, 8], F32)
        SM1 = sbA("sm1", [128, 32], F32)
        SM2 = sbA("sm2", [128, 32], F32)
        MK1 = sbA("mk1", [128, 32, 8], F32)
        MK2 = sbA("mk2", [128, 32, 8], F32)
        CWM = sbA("cwm", [128, 32, 8], F32)
        PTOK = sbA("ptok", [128, 32, 8], F32)
        P12 = sbA("p12", [128, 32, 2], F32)
        IDXI2 = sbA("idxi2", [128, 32, 2], I32)
        wrt = sbA("wrt", [128, 8, 8], BF16)
        ebv = sbA("ebv", [8, 1], F32)
        TOKI = sbA("toki", [128, 32], I32)
        ZB = sbA("zbz", [128, 128], I32)
        ZBF = sbA("zbf", [128, 128], F32)
        ZROW = sbA("zrow", [1, 1024], BF16)
        ZROWF = sbA("zrowf", [1, 1024], F32)
        CNTR = sbA("cntr", [1, 8], F32)
        CNTI = sbA("cnti", [1, 8], I32)
        T8 = sbA("t8", [1, 8], F32)
        OFFC = sbA("offc", [8, 1], F32)

        K.emit("gpsimd", "dma_start", wrt[:, :, :],
               W["h_routerT"].rearrange("(t p) e -> p t e", p=128), wr=[t("wrt")], dma=True)
        K.emit("sync", "dma_start", ebv[:, :], W["h_ebias"], wr=[t("ebv")], dma=True)
        K.emit("sync", "dma_start", TOKI[:, :], W["c_tokids"], wr=[t("toki")], dma=True)
        K.emit("vector", "memset", ZB[:, :], 4096, wr=[t("zb")])
        K.emit("vector", "memset", ZBF[:, :], 0.0, wr=[t("zbf")])
        K.emit("vector", "memset", ZROW[:, :], 0.0, wr=[t("zrow")])
        K.emit("vector", "memset", ZROWF[:, :], 0.0, wr=[t("zrowf")])
        K.emit("sync", "dma_start", snmt[4096:4097, :], ZROW[:, :], rd=[t("zrow")],
               wr=[("SNMTZ",)], stream="snmtz", dma=True)
        K.emit("sync", "dma_start", acct[4096:4097, :], ZROWF[:, :], rd=[t("zrowf")],
               wr=[("ACCTZ",)], stream="snmtz", dma=True)
        K.emit("gpsimd", "dma_start", idxd.rearrange("(p f) -> p f", p=128),
               ZB[:, 0:(NSB * 512 + 128) // 128], rd=[t("zb")], wr=[("IDXD",)],
               stream="idxz", dma=True)
        K.emit("gpsimd", "dma_start", cwl.rearrange("(p f) -> p f", p=128),
               ZBF[:, 0:(NSB * 512 + 128) // 128], rd=[t("zbf")], wr=[("CWL",)],
               stream="idxz", dma=True)
        for g in range(8):
            gs = slice(g * 512, (g + 1) * 512)
            srk = [srckey] if isinstance(srckey, tuple) else [(srckey, g)]
            norm_group(K, C, NB, src3, g, XN[:, :, :], "ln", srk, t("xn"))
            K.emit("sync", "dma_start", snm[:, :, gs], XN[:, :, :],
                   rd=[t("xn")], wr=[("SNM", g)], dma=True)
            for j in range(4):
                for kd in range(DT):
                    K.emit("tensor", "transpose", C.PT[0:128, 0:128],
                           XN[:, kd, j * 128:(j + 1) * 128], C.ident[:, :],
                           rd=[t("xn"), ("ident",)], wr=[("PT",)])
                    K.emit("vector", "tensor_copy", TMJ[:, j, kd, :], C.PT[0:128, 0:128],
                           rd=[("PT",)], wr=[t("tmj")])
            K.emit("sync", "dma_start",
                   snmt[g * 512:(g + 1) * 512, :].rearrange("(j p) (kd c) -> p j kd c",
                                                            p=128, c=128),
                   TMJ[:, :, :, :], rd=[t("tmj")], wr=[("SNMT", g)], dma=True)
            for k in range(DT):
                K.emit("tensor", "matmul", C.PS1[0:8, :], wrt[:, k, :], XN[:, k, :],
                       start=(k == 0), stop=(k == DT - 1),
                       rd=[t("wrt"), t("xn")], wr=[("PS1",)])
            K.emit("vector", "tensor_scalar_add", LGR[:, :], C.PS1[0:8, :], ebv[:, 0:1],
                   rd=[("PS1",), t("ebv")], wr=[t("lgr")])
            for j in range(4):
                K.emit("tensor", "transpose", C.pb(0)[0:128, 0:8],
                       LGR[:, j * 128:(j + 1) * 128], C.identf[0:8, 0:8],
                       rd=[t("lgr"), ("identf",)], wr=[("PB", 0)])
                K.emit("vector", "tensor_copy", LGT[:, g * 4 + j, :], C.pb(0)[0:128, 0:8],
                       rd=[("PB", 0)], wr=[t("lgt")])
        b3 = lambda ap2: ap2.unsqueeze(2).broadcast_to([128, 32, 8])
        K.emit("vector", "reduce_max", SM1[:, :], LGT[:, :, :], axis=AX.X,
               rd=[t("lgt")], wr=[t("sm1")])
        K.emit("vector", "tensor_tensor", MK1[:, :, :], LGT[:, :, :], b3(SM1[:, :]),
               OP.is_equal, rd=[t("lgt"), t("sm1")], wr=[t("mk1")])
        K.emit("vector", "scalar_tensor_tensor", MK2[:, :, :], MK1[:, :, :], NEG,
               LGT[:, :, :], OP.mult, OP.add, rd=[t("mk1"), t("lgt")], wr=[t("mk2")])
        K.emit("vector", "reduce_max", SM2[:, :], MK2[:, :, :], axis=AX.X,
               rd=[t("mk2")], wr=[t("sm2")])
        K.emit("vector", "tensor_tensor", MK2[:, :, :], MK2[:, :, :], b3(SM2[:, :]),
               OP.is_equal, rd=[t("mk2"), t("sm2")], wr=[t("mk2")])
        K.emit("vector", "tensor_tensor", SM1[:, :], SM1[:, :], SM2[:, :], OP.subtract,
               rd=[t("sm1"), t("sm2")], wr=[t("sm1")])
        K.emit("scalar", "activation", SM1[:, :], SM1[:, :], AF.Sigmoid,
               rd=[t("sm1")], wr=[t("sm1")])
        K.emit("vector", "tensor_scalar", SM2[:, :], SM1[:, :], -1.0, 1.0, OP.mult, OP.add,
               rd=[t("sm1")], wr=[t("sm2")])
        K.emit("vector", "tensor_tensor", CWM[:, :, :], MK1[:, :, :], MK2[:, :, :], OP.add,
               rd=[t("mk1"), t("mk2")], wr=[t("cwm")])
        for j in range(32):
            K.emit("tensor", "transpose", C.pb(0)[0:8, 0:128], CWM[:, j, :], C.identf[:, :],
                   rd=[t("cwm"), ("identf",)], wr=[("PB", 0)])
            K.emit("vector", "tensor_copy", CWT[:, j * 128:(j + 1) * 128], C.pb(0)[0:8, 0:128],
                   rd=[("PB", 0)], wr=[t("cwt")])
        K.emit("vector", "tensor_single_scalar", MSK[:, :], CWT[:, :], 0.5, OP.is_gt,
               rd=[t("cwt")], wr=[t("msk")])
        K.emit("vector", "tensor_tensor_scan", CUM[:, :],
               C.onesf[0:8, 0:1].broadcast_to([8, 4096]), MSK[:, :], 0.0, OP.mult, OP.add,
               rd=[("onesf",), t("msk")], wr=[t("cum")])
        K.emit("tensor", "transpose", C.PS1[0:1, 0:8], CUM[:, 4095:4096],
               C.identf[0:8, 0:8], rd=[t("cum"), ("identf",)], wr=[("PS1",)])
        K.emit("vector", "tensor_scalar_add", CNTR[:, :], C.PS1[0:1, 0:8], 511.0,
               rd=[("PS1",)], wr=[t("cntr")])
        K.emit("vector", "tensor_copy", CNTI[:, :], CNTR[:, :], rd=[t("cntr")], wr=[t("cnti")])
        K.emit("vector", "tensor_single_scalar", CNTI[:, :], CNTI[:, :], -512,
               OP.bitwise_and, rd=[t("cnti")], wr=[t("cnti")])
        K.emit("vector", "tensor_copy", CNTR[:, :], CNTI[:, :], rd=[t("cnti")], wr=[t("cntr")])
        K.emit("vector", "tensor_tensor_scan", T8[:, :],
               C.onesf[0:1, 0:1].broadcast_to([1, 8]), CNTR[:, :], 0.0, OP.mult, OP.add,
               rd=[("onesf",), t("cntr")], wr=[t("t8")])
        K.emit("vector", "tensor_tensor", T8[:, :], T8[:, :], CNTR[:, :], OP.subtract,
               rd=[t("t8"), t("cntr")], wr=[t("t8")])
        for sbk in range(NSB):
            K.emit("vector", "tensor_single_scalar", CNTR[:, :], T8[:, :],
                   float(sbk * 512), OP.is_le, rd=[t("t8")], wr=[t("cntr")])
            K.emit("vector", "reduce_sum", OWNF[0:1, sbk:sbk + 1], CNTR[:, :], axis=AX.X,
                   rd=[t("cntr")], wr=[t("ownf")])
        K.emit("vector", "tensor_scalar_add", OWNF[:, :], OWNF[:, :], -1.0,
               rd=[t("ownf")], wr=[t("ownf")])
        K.emit("tensor", "transpose", C.PS1[0:8, 0:1], T8[:, :], C.identf[0:1, 0:1],
               rd=[t("t8"), ("identf",)], wr=[("PS1",)])
        K.emit("vector", "tensor_copy", OFFC[:, :], C.PS1[0:8, 0:1], rd=[("PS1",)],
               wr=[t("offc")])
        K.emit("vector", "tensor_scalar_add", CUM[:, :], CUM[:, :], -1.0,
               rd=[t("cum")], wr=[t("cum")])
        K.emit("vector", "tensor_scalar_add", CUM[:, :], CUM[:, :], OFFC[:, 0:1],
               rd=[t("cum"), t("offc")], wr=[t("cum")])
        for j in range(32):
            K.emit("tensor", "transpose", C.pb(0)[0:128, 0:8],
                   CUM[:, j * 128:(j + 1) * 128], C.identf[0:8, 0:8],
                   rd=[t("cum"), ("identf",)], wr=[("PB", 0)])
            K.emit("vector", "tensor_copy", PTOK[:, j, :], C.pb(0)[0:128, 0:8],
                   rd=[("PB", 0)], wr=[t("ptok")])
        K.emit("vector", "tensor_scalar_min", PTOK[:, :, :], PTOK[:, :, :],
               float(DUMP), rd=[t("ptok")], wr=[t("ptok")])
        K.emit("vector", "tensor_tensor", MK1[:, :, :], PTOK[:, :, :], MK1[:, :, :], OP.mult,
               rd=[t("ptok"), t("mk1")], wr=[t("mk1")])
        K.emit("vector", "reduce_sum", P12[:, :, 0], MK1[:, :, :], axis=AX.X,
               rd=[t("mk1")], wr=[t("p12")])
        K.emit("vector", "tensor_tensor", MK2[:, :, :], PTOK[:, :, :], MK2[:, :, :], OP.mult,
               rd=[t("ptok"), t("mk2")], wr=[t("mk2")])
        K.emit("vector", "reduce_sum", P12[:, :, 1], MK2[:, :, :], axis=AX.X,
               rd=[t("mk2")], wr=[t("p12")])
        K.emit("vector", "tensor_copy", IDXI2[:, :, :], P12[:, :, :],
               rd=[t("p12")], wr=[t("idxi")])
        for sel in range(2):
            SMv, smk = (SM1, t("sm1")) if sel == 0 else (SM2, t("sm2"))
            for blk in range(32):
                K.emit("gpsimd", "indirect_dma_start", idxd.unsqueeze(0),
                       IndirectOffsetOnAxis(ap=IDXI2[:, blk, sel:sel + 1], axis=1),
                       TOKI[:, blk:blk + 1], None,
                       rd=[t("idxi"), t("toki")], wr=[("IDXD",)],
                       stream="idxsc", dma=True)
                K.emit("gpsimd", "indirect_dma_start", cwl.unsqueeze(0),
                       IndirectOffsetOnAxis(ap=IDXI2[:, blk, sel:sel + 1], axis=1),
                       SMv[:, blk:blk + 1], None,
                       rd=[t("idxi"), smk], wr=[("CWL",)],
                       stream="idxsc", dma=True)
        ctxA.close()
        K.barrier()
        ctxB = ExitStack()
        sbB = lambda name, shape, dt: ctxB.enter_context(nc.sbuf_tensor(f"e{callid}_{name}", shape, dt))
        WX = sbB("wx", [128, 8, 2048], BF16)
        WY = sbB("wy", [128, 8, 2048], BF16)
        WZ = sbB("wz", [128, 16, 1024], BF16)
        XGE = [sbB(f"xge{i}", [128, 8, 512], BF16) for i in range(2)]
        GB = sbB("gb", [128, 1024], BF16)
        IDXS = [sbB(f"idxs{i}", [128, 4], I32) for i in range(2)]
        IDXW = sbB("idxw", [128, 2], I32)
        IOTAF = sbB("iotaf", [128, 1], F32)
        OWNB = sbB("ownb", [128, 2], F32)
        EO = sbB("eo", [128, 16, 512], BF16)
        CWP = sbB("cwp", [128, 4], F32)
        OUTC = sbB("outc", [128, 8, 512], F32)
        OSCJ = sbB("oscj", [128, 1024], F32)
        TM1 = sbB("tm1", [128, 2, 512], BF16)
        K.emit("sync", "dma_start", IOTAF[:, :], W["c_iota128"], wr=[t("iotaf")], dma=True)

        def w_idx(sbr, par):
            K.emit("tensor", "matmul", C.PS1[0:128, 0:1], C.onesf[0:1, 0:128],
                   OWNF[0:1, sbr:sbr + 1], start=True, stop=True,
                   rd=[("onesf",), t("ownf")], wr=[("PS1",)])
            K.emit("vector", "scalar_tensor_tensor", OWNB[:, par:par + 1],
                   C.PS1[0:128, 0:1], 128.0, IOTAF[:, 0:1], OP.mult, OP.add,
                   rd=[("PS1",), t("iotaf")], wr=[t("ownb")])
            K.emit("vector", "tensor_copy", IDXW[:, par:par + 1], OWNB[:, par:par + 1],
                   rd=[t("ownb")], wr=[t("idxw")])

        def w_load(par, which):
            tgt, kdst, wb = {"x": (WX, t("wx"), w1b), "y": (WY, t("wy"), w3b),
                             "z": (WZ, t("wz"), w2b)}[which]
            K.emit("gpsimd", "indirect_dma_start",
                   tgt.rearrange("p a b -> p (a b)"), None,
                   wb.rearrange("e p a b -> (e p) (a b)"),
                   IndirectOffsetOnAxis(ap=IDXW[:, par:par + 1], axis=0),
                   rd=[t("idxw"), ("WPC", which)], wr=[kdst], stream=f"w{which}", dma=True)

        def gather_tokens(sbr, par):
            kx = t(f"xge{par}")
            ki = t(f"idxs{par}")
            K.emit("sync", "dma_start", IDXS[par][:, :],
                   idxd[sbr * 512:(sbr + 1) * 512].rearrange("(p b) -> p b", b=4),
                   rd=[("IDXD",)], wr=[ki], stream=f"idxs{par}", dma=True)
            for blk in range(4):
                K.emit("gpsimd", "indirect_dma_start", GB[:, :], None, snmt,
                       IndirectOffsetOnAxis(ap=IDXS[par][:, blk:blk + 1], axis=0),
                       rd=[ki, ("SNMTZ",)] + [("SNMT", gg) for gg in range(8)],
                       wr=[t("gb")], stream="gath", dma=True)
                for kd in range(DT):
                    K.emit("tensor", "transpose", C.PT[0:128, 0:128],
                           GB[:, kd * 128:(kd + 1) * 128], C.ident[:, :],
                           rd=[t("gb"), ("ident",)], wr=[("PT",)])
                    K.emit("vector", "tensor_copy", XGE[par][:, kd, blk * 128:(blk + 1) * 128],
                           C.PT[0:128, 0:128], rd=[("PT",)], wr=[kx])

        K.emit("gpsimd", "dma_start", WX[:, :, :],
               W["h_w1"][8].rearrange("(t p) h -> p t h", p=128), wr=[t("wx")],
               stream="wx", dma=True)
        K.emit("gpsimd", "dma_start", WY[:, :, :],
               W["h_w3"][8].rearrange("(t p) h -> p t h", p=128), wr=[t("wy")],
               stream="wy", dma=True)
        K.emit("gpsimd", "dma_start", WZ[:, :, :],
               W["h_w2"][8].rearrange("(t p) d -> p t d", p=128), wr=[t("wz")],
               stream="wz", dma=True)
        for sbk in range(8 + NSB):
            shared = sbk < 8
            sbr = sbk - 8
            par = sbr % 2
            if shared:
                K.emit("sync", "dma_start", XGE[sbk % 2][:, :, :],
                       snm[:, :, sbk * 512:(sbk + 1) * 512], rd=[("SNM", sbk)],
                       wr=[t(f"xge{sbk % 2}")], stream=f"xnc{sbk % 2}", dma=True)
                rhk, rhs_buf = t(f"xge{sbk % 2}"), XGE[sbk % 2]
            else:
                rhk, rhs_buf = t(f"xge{par}"), XGE[par]
                K.emit("sync", "dma_start", CWP[:, :],
                       cwl[sbr * 512:(sbr + 1) * 512].rearrange("(p b) -> p b", b=4),
                       rd=[("CWL",)], wr=[t("cwp")], stream="cwrw", dma=True)
            for ht in range(16):
                pg, pu = (0, 1) if ht % 2 == 0 else (2, 3)
                pr = ht % 2
                for k in range(DT):
                    K.emit("tensor", "matmul", C.pa(pg), WX[:, k, ht * 128:(ht + 1) * 128],
                           rhs_buf[:, k, :], start=(k == 0), stop=(k == DT - 1),
                           rd=[t("wx"), rhk], wr=[("PA", pg)])
                for k in range(DT):
                    K.emit("tensor", "matmul", C.pa(pu), WY[:, k, ht * 128:(ht + 1) * 128],
                           rhs_buf[:, k, :], start=(k == 0), stop=(k == DT - 1),
                           rd=[t("wy"), rhk], wr=[("PA", pu)])
                K.emit("scalar", "activation", TM1[:, pr, :], C.pa(pg), AF.Silu,
                       rd=[("PA", pg)], wr=[t("tm1")])
                K.emit("vector", "tensor_tensor", EO[:, ht, :], TM1[:, pr, :],
                       C.pa(pu), OP.mult, rd=[t("tm1"), ("PA", pu)], wr=[t("eo")])
            nxt = 0 if shared and sbk == 7 else (sbr + 1 if not shared else None)
            if nxt is not None and nxt < NSB and (not shared or sbk == 7):
                w_idx(nxt, nxt % 2)
                w_load(nxt % 2, "x")
                w_load(nxt % 2, "y")
                gather_tokens(nxt, nxt % 2)
            for m in range(DT):
                pk = ("PA", 2 + m % 2)
                pap = C.pa(2 + m % 2)
                for k in range(16):
                    K.emit("tensor", "matmul", pap, WZ[:, k, m * 128:(m + 1) * 128],
                           EO[:, k, :], start=(k == 0), stop=(k == 15),
                           rd=[t("wz"), t("eo")], wr=[pk])
                K.emit("vector", "tensor_copy", OUTC[:, m, :], pap, rd=[pk], wr=[t("outc")])
            if nxt is not None and nxt < NSB and (not shared or sbk == 7):
                w_load(nxt % 2, "z")
            for j in range(4):
                for m in range(DT):
                    K.emit("tensor", "transpose", C.PS1[0:128, 0:128],
                           OUTC[:, m, j * 128:(j + 1) * 128], C.identf[:, :],
                           rd=[t("outc"), ("identf",)], wr=[("PS1",)])
                    if shared:
                        K.emit("vector", "tensor_copy", OSCJ[:, m * 128:(m + 1) * 128],
                               C.PS1[0:128, 0:128], rd=[("PS1",)], wr=[t("oscj")])
                    else:
                        K.emit("vector", "tensor_scalar_mul", OSCJ[:, m * 128:(m + 1) * 128],
                               C.PS1[0:128, 0:128], CWP[:, j:j + 1],
                               rd=[("PS1",), t("cwp")], wr=[t("oscj")])
                if shared:
                    row0 = sbk * 512 + j * 128
                    K.emit("gpsimd", "dma_start", acct[row0:row0 + 128, :],
                           OSCJ[:, :], rd=[t("oscj")], wr=[("ACCTS",)],
                           stream="acct", dma=True)
                else:
                    K.emit("gpsimd", "indirect_dma_start", acct,
                           IndirectOffsetOnAxis(ap=IDXS[par][:, j:j + 1], axis=0),
                           OSCJ[:, :], None, compute_op=OP.add,
                           rd=[t("oscj"), t(f"idxs{par}"), ("ACCTZ",)],
                           wr=[("ACCTS",)], stream="acct", dma=True)
        ctxB.close()
        K.barrier()
        XG2 = sb("xg2", [128, 8, 512], F32)
        AIN = sb("ain", [128, 4, 1024], F32)
        for g in range(8):
            gs = slice(g * 512, (g + 1) * 512)
            srk = [srckey] if isinstance(srckey, tuple) else [(srckey, g)]
            K.emit("sync", "dma_start", XG2[:, :, :], src3[:, :, gs], rd=srk,
                   wr=[t("xg2")], stream="xg2", dma=True)
            K.emit("sync", "dma_start", AIN[:, :, :],
                   acct[g * 512:(g + 1) * 512, :].rearrange("(j p) d -> p j d", p=128),
                   wr=[t("ain")], stream="accld", dma=True)
            for j in range(4):
                for m in range(DT):
                    K.emit("tensor", "transpose", C.PS1[0:128, 0:128],
                           AIN[:, j, m * 128:(m + 1) * 128], C.identf[:, :],
                           rd=[t("ain"), ("identf",)], wr=[("PS1",)])
                    K.emit("vector", "scalar_tensor_tensor",
                           XG2[:, m, j * 128:(j + 1) * 128], C.PS1[0:128, 0:128], 0.5,
                           XG2[:, m, j * 128:(j + 1) * 128], OP.mult, OP.add,
                           rd=[("PS1",), t("xg2")], wr=[t("xg2")])
            K.emit("sync", "dma_start", dst3[:, :, gs], XG2[:, :, :],
                   rd=[t("xg2")], wr=[(dstkey, g)], dma=True)
    K.barrier()



def build_att1(K, C, W, scr, src3, dst3, srckey, dstkey):
    nc = K.nc
    sna, rws, wns = scr["SNA"], scr["RWS"], scr["WNS"]
    build_ln_pass(K, C, src3, None, srckey, None, "lnT", dst_bf=sna, dstbfkey="SNA",
                  row_cache=rws)
    SC = 1.0 / (2048.0 ** 0.5)
    with ExitStack() as ctx:
        sb = lambda name, shape, dt: ctx.enter_context(nc.sbuf_tensor(f"a1_{name}", shape, dt))
        t = lambda n: (f"a1_{n}",)
        WB = [sb(f"wb{i}", [128, 8, 1024], BF16) for i in range(2)]
        SQB = sb("sqb", [128, 1024], BF16)
        WSR = sb("wsr", [1, 1024], F32)
        gn = sb("gn", [1, 1], F32)
        QH = sb("qh", [128, 16, 1024], BF16)
        KH = sb("kh", [128, 16, 1024], BF16)
        VT = sb("vt", [128, 8, 2048], BF16)
        OT = sb("ot", [128, 8, 512], BF16)
        SNSg = sb("snsg", [128, 8, 512], BF16)
        ATT = sb("att", [128, 64], F32)
        ATTB = sb("attb", [128, 64], BF16)
        NMC = sb("nmc", [128, 1], F32)
        DSR = sb("dsr", [128, 1], F32)
        ATN = sb("atn", [128, 8, 64], BF16)
        SSQ = sb("ssq", [128, 16], F32)
        XG2 = sb("xg2", [128, 8, 512], F32)
        TMP = sb("tmp", [128, 512], F32)
        RWL = [sb(f"rwl{i}", [1, 512], F32) for i in range(2)]

        K.emit("sync", "dma_start", gn[:, :], W["h_gain"], wr=[t("gn")], dma=True)
        for wi, wname in enumerate(("h_wqT", "h_wkT", "h_wvT", "h_woT")):
            slot = WB[0]
            K.emit("gpsimd", "dma_start", slot[:, :, :],
                   W[wname].rearrange("(t p) o -> p t o", p=128), wr=[t("wb0")],
                   stream="wb0", dma=True)
            for h2 in range(2):
                pr = C.PS1[0:1, :] if h2 == 0 else C.pb(0)[0:1, :]
                pk = ("PS1",) if h2 == 0 else ("PB", 0)
                for kt in range(DT):
                    K.emit("scalar", "activation", SQB[:, :], slot[:, kt, :], AF.Square,
                           rd=[t("wb0")], wr=[t("sqb")])
                    K.emit("tensor", "matmul", pr, C.onesb[:, 0:1],
                           SQB[:, h2 * 512:(h2 + 1) * 512],
                           start=(kt == 0), stop=(kt == DT - 1),
                           rd=[("onesb",), t("sqb")], wr=[pk])
                K.emit("vector", "tensor_copy", WSR[0:1, h2 * 512:(h2 + 1) * 512], pr,
                       rd=[pk], wr=[t("wsr")])
            K.emit("scalar", "activation", WSR[0:1, :], WSR[0:1, :], AF.Ln,
                   bias=C.consts[0:1, 3:4], rd=[t("wsr"), ("consts",)], wr=[t("wsr")])
            K.emit("scalar", "activation", WSR[0:1, :], WSR[0:1, :], AF.Exp, scale=0.5,
                   rd=[t("wsr")], wr=[t("wsr")])
            K.emit("vector", "tensor_scalar", WSR[0:1, :], WSR[0:1, :], 1.0 / 32.0,
                   C.consts[0:1, 2:3], OP.mult, OP.add,
                   rd=[t("wsr"), ("consts",)], wr=[t("wsr")])
            K.emit("vector", "reciprocal", WSR[0:1, :], WSR[0:1, :], rd=[t("wsr")], wr=[t("wsr")])
            K.emit("vector", "tensor_scalar", WSR[0:1, :], WSR[0:1, :], gn[0:1, 0:1],
                   1.0 / 32.0, OP.mult, OP.mult, rd=[t("wsr"), t("gn")], wr=[t("wsr")])
            for h2 in range(2):
                K.emit("tensor", "matmul", C.pb(h2), C.onesf[0:1, 0:128],
                       WSR[0:1, h2 * 512:(h2 + 1) * 512], start=True, stop=True,
                       rd=[("onesf",), t("wsr")], wr=[("PB", h2)])
            K.emit("vector", "tensor_tensor", slot[:, :, :], slot[:, :, :],
                   C.PB[:, :].unsqueeze(1).broadcast_to([128, 8, 1024]), OP.mult,
                   rd=[t("wb0"), ("PB", 0), ("PB", 1)], wr=[t("wb0")])
            K.emit("gpsimd", "dma_start", wns[wi], slot[:, :, :],
                   rd=[t("wb0")], wr=[("WNS", wi)], stream="wnsw", dma=True)

        for seq in range(2):
            sq0 = seq * 2048
            for wi, dst, isqk in ((0, QH, True), (1, KH, True), (2, VT, False)):
                slot = WB[wi % 2]
                wk = t(f"wb{wi % 2}")
                K.emit("gpsimd", "dma_start", slot[:, :, :], wns[wi], rd=[("WNS", wi)],
                       wr=[wk], stream=f"wb{wi % 2}", dma=True)
                for grp in range(4):
                    gs = slice(sq0 + grp * 512, sq0 + (grp + 1) * 512)
                    K.emit("sync", "dma_start", SNSg[:, :, :], sna[:, :, gs],
                           rd=[("SNA", (sq0 + grp * 512) // 512)], wr=[t("snsg")],
                           stream="snsg", dma=True)
                    if isqk:
                        for tt in range(4):
                            ts_ = slice(tt * 128, (tt + 1) * 128)
                            for n2 in range(2):
                                pk = ("PA", n2)
                                pap = C.pa(n2)
                                for kd in range(DT):
                                    K.emit("tensor", "matmul", pap, SNSg[:, kd, ts_],
                                           slot[:, kd, n2 * 512:(n2 + 1) * 512],
                                           start=(kd == 0), stop=(kd == DT - 1),
                                           rd=[t("snsg"), wk], wr=[pk])
                            K.emit("scalar", "activation", SQB[:, 0:512], C.pa(0), AF.Square,
                                   rd=[("PA", 0)], wr=[t("sqb")])
                            K.emit("scalar", "activation", SQB[:, 512:1024], C.pa(1), AF.Square,
                                   rd=[("PA", 1)], wr=[t("sqb")])
                            K.emit("vector", "reduce_sum", SSQ[:, :],
                                   SQB[:, :].rearrange("p (h d) -> p h d", d=64), axis=AX.X,
                                   rd=[t("sqb")], wr=[t("ssq")])
                            K.emit("scalar", "activation", SSQ[:, :], SSQ[:, :], AF.Ln,
                                   bias=C.consts[:, 3:4], rd=[t("ssq"), ("consts",)], wr=[t("ssq")])
                            K.emit("scalar", "activation", SSQ[:, :], SSQ[:, :], AF.Exp,
                                   scale=0.5, rd=[t("ssq")], wr=[t("ssq")])
                            K.emit("vector", "tensor_scalar", SSQ[:, :], SSQ[:, :], 0.125,
                                   C.consts[:, 2:3], OP.mult, OP.add,
                                   rd=[t("ssq"), ("consts",)], wr=[t("ssq")])
                            K.emit("vector", "reciprocal", SSQ[:, :], SSQ[:, :],
                                   rd=[t("ssq")], wr=[t("ssq")])
                            for n2 in range(2):
                                K.emit("vector", "tensor_tensor",
                                       dst[:, grp * 4 + tt, n2 * 512:(n2 + 1) * 512]
                                       .rearrange("p (h d) -> p h d", d=64),
                                       C.pa(n2).rearrange("p (h d) -> p h d", d=64),
                                       SSQ[:, n2 * 8:(n2 + 1) * 8].unsqueeze(2)
                                       .broadcast_to([128, 8, 64]), OP.mult,
                                       rd=[("PA", n2), t("ssq")],
                                       wr=[t("qh") if dst is QH else t("kh")])
                    else:
                        for m in range(DT):
                            pk = ("PA", m % 2)
                            pap = C.pa(m % 2)
                            for kd in range(DT):
                                K.emit("tensor", "matmul", pap,
                                       slot[:, kd, m * 128:(m + 1) * 128], SNSg[:, kd, :],
                                       start=(kd == 0), stop=(kd == DT - 1),
                                       rd=[wk, t("snsg")], wr=[pk])
                            K.emit("vector", "tensor_copy",
                                   VT[:, m, grp * 512:(grp + 1) * 512], pap,
                                   rd=[pk], wr=[t("vt")])
            for m in range(8):
                n2 = m // 4
                c0 = (m % 4) * 128
                pk = ("PA", m % 2)
                pap = C.pa(m % 2)
                for kt in range(16):
                    K.emit("tensor", "matmul", pap, QH[:, kt, m * 128:(m + 1) * 128],
                           KH[:, kt, n2 * 512:(n2 + 1) * 512],
                           start=(kt == 0), stop=(kt == 15),
                           rd=[t("qh"), t("kh")], wr=[pk])
                for hh in range(2):
                    h = 2 * m + hh
                    rs = slice(hh * 64, (hh + 1) * 64)
                    blk = pap[hh * 64:(hh + 1) * 64, c0 + hh * 64:c0 + (hh + 1) * 64]
                    K.emit("vector", "reduce_max", NMC[rs, :], blk, axis=AX.X, negate=True,
                           rd=[pk], wr=[t("nmc")])
                    K.emit("vector", "tensor_scalar_mul", NMC[rs, :], NMC[rs, :], SC,
                           rd=[t("nmc")], wr=[t("nmc")])
                    K.emit("scalar", "activation", ATT[rs, :], blk, AF.Exp,
                           bias=NMC[rs, 0:1], scale=SC, accum_out=DSR[rs, 0:1],
                           rd=[pk, t("nmc")], wr=[t("att"), t("dsr")])
                    K.emit("vector", "reciprocal", DSR[rs, :], DSR[rs, :],
                           rd=[t("dsr")], wr=[t("dsr")])
                    K.emit("vector", "tensor_scalar_mul", ATTB[rs, :], ATT[rs, :],
                           DSR[rs, 0:1], rd=[t("att"), t("dsr")], wr=[t("attb")])
                    o_ = hh * 64
                    K.emit("tensor", "transpose", C.PT[0:64, 0:64],
                           ATTB[rs, :], C.ident[o_:o_ + 64, o_:o_ + 64],
                           rd=[t("attb"), ("ident",)], wr=[("PT",)])
                    hb_ = (h % 2) * 64
                    K.emit("vector", "tensor_copy", ATN[hb_:hb_ + 64, h // 2, :],
                           C.PT[0:64, 0:64], rd=[("PT",)], wr=[t("atn")])
            K.emit("gpsimd", "dma_start", WB[1][:, :, :], wns[3], rd=[("WNS", 3)],
                   wr=[t("wb1")], stream="wb1", dma=True)
            for nc_ in range(4):
                g = seq * 4 + nc_
                gs = slice(g * 512, (g + 1) * 512)
                ns = slice(nc_ * 512, (nc_ + 1) * 512)
                for hp in range(8):
                    pk = ("PB", hp % 2)
                    pap = C.pb(hp % 2)
                    for hh in range(2):
                        h = 2 * hp + hh
                        hb_ = (h % 2) * 64
                        K.emit("tensor", "matmul", pap[hh * 64:(hh + 1) * 64, :],
                               ATN[hb_:hb_ + 64, h // 2, :],
                               VT[hb_:hb_ + 64, h // 2, ns],
                               start=True, stop=True,
                               tile_position=(hb_, hh * 64),
                               rd=[t("atn"), t("vt")], wr=[pk])
                    K.emit("vector", "tensor_copy", OT[:, hp, :], pap, rd=[pk], wr=[t("ot")])
                K.emit("sync", "dma_start", XG2[:, :, :], src3[:, :, gs],
                       rd=[(srckey, g) if not isinstance(srckey, tuple) else srckey],
                       wr=[t("xg2")], stream="xg2", dma=True)
                K.emit("sync", "dma_start", RWL[0][:, :], rws[0:1, gs], rd=[("RWS", "m", g)],
                       wr=[t("rwl0")], stream="rwl", dma=True)
                K.emit("sync", "dma_start", RWL[1][:, :], rws[1:2, gs], rd=[("RWS", "i", g)],
                       wr=[t("rwl1")], stream="rwl", dma=True)
                for m in range(8):
                    pk = ("PA", 2 + m % 2)
                    pap = C.pa(2 + m % 2)
                    for k in range(DT):
                        K.emit("tensor", "matmul", pap, WB[1][:, k, m * 128:(m + 1) * 128],
                               OT[:, k, :], start=(k == 0), stop=(k == DT - 1),
                               rd=[t("wb1"), t("ot")], wr=[pk])
                    if m == 0:
                        K.emit("tensor", "matmul", C.pb(0), C.onesf[0:1, 0:128],
                               RWL[0][0:1, :], start=True, stop=True,
                               rd=[("onesf",), t("rwl0")], wr=[("PB", 0)])
                        K.emit("tensor", "matmul", C.pb(1), C.onesf[0:1, 0:128],
                               RWL[1][0:1, :], start=True, stop=True,
                               rd=[("onesf",), t("rwl1")], wr=[("PB", 1)])
                    K.emit("vector", "tensor_tensor", TMP[:, :], XG2[:, m, :], C.pb(0),
                           OP.subtract, rd=[t("xg2"), ("PB", 0)], wr=[t("tmp")])
                    K.emit("vector", "tensor_tensor", TMP[:, :], TMP[:, :], C.pb(1),
                           OP.mult, rd=[t("tmp"), ("PB", 1)], wr=[t("tmp")])
                    K.emit("vector", "tensor_tensor", TMP[:, :], TMP[:, :], pap,
                           OP.add, rd=[t("tmp"), pk], wr=[t("tmp")])
                    K.emit("vector", "scalar_tensor_tensor", XG2[:, m, :], TMP[:, :], RS2,
                           XG2[:, m, :], OP.mult, OP.add,
                           rd=[t("tmp"), t("xg2")], wr=[t("xg2")])
                K.emit("sync", "dma_start", dst3[:, :, gs], XG2[:, :, :],
                       rd=[t("xg2")], wr=[(dstkey, g)], dma=True)
    K.barrier()



CMIX_MU = float(np.float32(np.sqrt(0.5)))
CMIX_DEN = float(np.float32(np.sqrt(1.0 / (4.0 * np.pi)) * np.sqrt(2.0)))


def build_cmix(K, C, W, s3, xb3, dst3, skey, xbkey, dstkey, skip=False):
    nc = K.nc
    with ExitStack() as ctx:
        sb = lambda name, shape, dt: ctx.enter_context(nc.sbuf_tensor(f"cm_{name}", shape, dt))
        t = lambda n: (f"cm_{n}",)
        NB = NormBufs(nc, ctx, "cm", rms_only=True)
        XBg = sb("xbg", [128, 8, 512], F32)
        if skip:
            for g in range(8):
                gs = slice(g * 512, (g + 1) * 512)
                K.emit("sync", "dma_start", NB.xg[:, :, :], s3[:, :, gs],
                       rd=[(skey, g)], wr=[("cmxg",)], stream="sld", dma=True)
                K.emit("sync", "dma_start", XBg[:, :, :], xb3[:, :, gs],
                       rd=[(xbkey, g) if not isinstance(xbkey, tuple) else xbkey],
                       wr=[t("xbg")], stream="xbg", dma=True)
                K.emit("vector", "tensor_tensor", XBg[:, :, :], XBg[:, :, :], NB.xg[:, :, :],
                       OP.add, rd=[t("xbg"), ("cmxg",)], wr=[t("xbg")])
                K.emit("sync", "dma_start", dst3[:, :, gs], XBg[:, :, :],
                       rd=[t("xbg")], wr=[(dstkey, g)], dma=True)
            K.barrier()
            return
        RN = sb("rn", [128, 8, 512], BF16)
        T1 = sb("t1", [128, 8, 512], BF16)
        XK = sb("xk", [128, 8, 512], BF16)
        KACT = sb("kact", [128, 32, 512], BF16)
        RB = sb("rb", [128, 8, 512], BF16)
        TK = sb("tk", [128, 8, 1], BF16)
        MAA = sb("maa", [128, 4, 8], F32)
        WRC = sb("wrc", [128, 8, 1024], BF16)
        KS = [sb(f"ks{i}", [128, 8, 128], BF16) for i in range(2)]
        VS = [sb(f"vs{i}", [128, 32, 128], BF16) for i in range(2)]
        TMPF = sb("tmpf", [128, 512], F32)
        TMPG = sb("tmpg", [128, 512], F32)

        K.emit("sync", "dma_start", MAA[:, 0, :], W["h_maak"], wr=[t("maa")], dma=True)
        K.emit("sync", "dma_start", MAA[:, 2, :], W["h_maar"], wr=[t("maa")], dma=True)
        K.emit("vector", "tensor_scalar", MAA[:, 1, :], MAA[:, 0, :], -1.0, 1.0,
               OP.mult, OP.add, rd=[t("maa")], wr=[t("maa")])
        K.emit("vector", "tensor_scalar", MAA[:, 3, :], MAA[:, 2, :], -1.0, 1.0,
               OP.mult, OP.add, rd=[t("maa")], wr=[t("maa")])
        K.emit("gpsimd", "dma_start", WRC[:, :, :],
               W["L0_attn_recept_w"].rearrange("(t p) o -> p t o", p=128),
               wr=[t("wrc")], stream="wrc", dma=True)

        def mix_into(dst_b, c0, c1):
            K.emit("vector", "tensor_tensor", T1[:, :, :], RN[:, :, :],
                   MAA[:, c1, :].unsqueeze(2).broadcast_to([128, 8, 512]), OP.mult,
                   rd=[t("rn"), t("maa")], wr=[t("t1")])
            for td in range(DT):
                K.emit("vector", "scalar_tensor_tensor", dst_b[:, td, 1:512],
                       RN[:, td, 0:511], MAA[:, c0, td:td + 1], T1[:, td, 1:512],
                       OP.mult, OP.add, rd=[t("rn"), t("maa"), t("t1")], wr=[t("xk")])
                K.emit("vector", "scalar_tensor_tensor", dst_b[:, td, 0:1],
                       TK[:, td, 0:1], MAA[:, c0, td:td + 1], T1[:, td, 0:1],
                       OP.mult, OP.add, rd=[t("tk"), t("maa"), t("t1")], wr=[t("xk")])

        for g in range(8):
            gs = slice(g * 512, (g + 1) * 512)
            norm_group(K, C, NB, s3, g, RN[:, :, :], "rms", [(skey, g)], t("rn"))
            if g % 4 == 0:
                K.emit("vector", "memset", TK[:, :, :], 0.0, wr=[t("tk")])
            mix_into(XK, 0, 1)
            for m in range(32):
                sl = m % 2
                K.emit("gpsimd", "dma_start", KS[sl][:, :, :], W["h_key_cm"][m]
                       .rearrange("(t p) o -> p t o", p=128),
                       wr=[t(f"ks{sl}")], stream=f"ks{sl}", dma=True)
                for k in range(DT):
                    K.emit("tensor", "matmul", C.pb(sl), KS[sl][:, k, :], XK[:, k, :],
                           start=(k == 0), stop=(k == DT - 1),
                           rd=[t(f"ks{sl}"), t("xk")], wr=[("PB", sl)])
                c2 = float(np.sqrt(2.0) / CMIX_DEN)
                K.emit("vector", "tensor_scalar", TMPF[:, :], C.pb(sl), c2,
                       -CMIX_MU * c2, OP.mult, OP.add,
                       rd=[("PB", sl)], wr=[t("tmpf")])
                K.emit("vector", "tensor_scalar_add", TMPF[:, :], TMPF[:, :], 1.0e-20,
                       rd=[t("tmpf")], wr=[t("tmpf")])
                K.emit("scalar", "activation", TMPG[:, :], TMPF[:, :], AF.Gelu,
                       rd=[t("tmpf")], wr=[t("tmpg")])
                K.emit("vector", "reciprocal", TMPF[:, :], TMPF[:, :],
                       rd=[t("tmpf")], wr=[t("tmpf")])
                K.emit("vector", "tensor_tensor", KACT[:, m, :], TMPG[:, :], TMPF[:, :],
                       OP.mult, rd=[t("tmpg"), t("tmpf")], wr=[t("kact")])
            mix_into(XK, 2, 3)
            K.emit("vector", "tensor_copy", TK[:, :, :], RN[:, :, 511:512],
                   rd=[t("rn")], wr=[t("tk")])
            for m in range(DT):
                pk = ("PA", m % 2)
                pap = C.pa(m % 2)
                for k in range(DT):
                    K.emit("tensor", "matmul", pap, WRC[:, k, m * 128:(m + 1) * 128],
                           XK[:, k, :], start=(k == 0), stop=(k == DT - 1),
                           rd=[t("wrc"), t("xk")], wr=[pk])
                K.emit("scalar", "activation", RB[:, m, :], pap, AF.Sigmoid,
                       rd=[pk], wr=[t("rb")])
            K.emit("sync", "dma_start", XBg[:, :, :], xb3[:, :, gs],
                   rd=[(xbkey, g) if not isinstance(xbkey, tuple) else xbkey],
                   wr=[t("xbg")], stream="xbg", dma=True)
            for m in range(DT):
                sl = m % 2
                K.emit("gpsimd", "dma_start", VS[sl][:, :, :], W["h_value_cm"][m]
                       .rearrange("(t p) o -> p t o", p=128),
                       wr=[t(f"vs{sl}")], stream=f"vs{sl}", dma=True)
                pk = ("PA", 2 + sl)
                pap = C.pa(2 + sl)
                for k in range(32):
                    K.emit("tensor", "matmul", pap, VS[sl][:, k, :], KACT[:, k, :],
                           start=(k == 0), stop=(k == 31),
                           rd=[t(f"vs{sl}"), t("kact")], wr=[pk])
                K.emit("vector", "tensor_tensor", TMPF[:, :], pap, RB[:, m, :], OP.mult,
                       rd=[pk, t("rb")], wr=[t("tmpf")])
                K.emit("vector", "tensor_tensor", NB.xg[:, m, :], NB.xg[:, m, :],
                       TMPF[:, :], OP.add, rd=[("cmxg",), t("tmpf")], wr=[("cmxg",)])
            K.emit("vector", "tensor_tensor", NB.xg[:, :, :], NB.xg[:, :, :],
                   XBg[:, :, :], OP.add, rd=[("cmxg",), t("xbg")], wr=[("cmxg",)])
            K.emit("sync", "dma_start", dst3[:, :, gs], NB.xg[:, :, :],
                   rd=[("cmxg",)], wr=[(dstkey, g)], dma=True)
    K.barrier()



def build_mqa(K, C, W, src3, dst3, srckey, dstkey):
    nc = K.nc
    with ExitStack() as ctx:
        sb = lambda name, shape, dt: ctx.enter_context(nc.sbuf_tensor(f"q_{name}", shape, dt))
        t = lambda n: (f"q_{n}",)
        NB = NormBufs(nc, ctx, "q", rms_only=True)
        RN = sb("rn", [128, 8, 512], BF16)
        AW = sb("aw", [128, 8, 1152], BF16)
        PW = sb("pw", [128, 8, 1024], BF16)
        QT = sb("qt", [128, 8, 512], BF16)
        KTH = sb("kth", [128, 2048], BF16)
        VTOK = sb("vtok", [128, 16, 65], BF16)
        VB = sb("vbuf", [64, 512], BF16)
        SQT = sb("sqt", [128, 2, 512], BF16)
        QSR = sb("qsr", [1, 512], F32)
        KFR = sb("kfr", [1, 1], F32)
        MAXK = sb("maxk", [1, 1], F32)
        CBC = sb("cbc", [128, 1], F32)
        EB = sb("eb", [128, 2, 512], BF16)
        RD = sb("rd", [1, 512], F32)
        RB64 = sb("rb64", [64, 512], F32)
        OT = sb("ot", [128, 8, 512], BF16)

        K.emit("gpsimd", "dma_start", AW[:, :, :],
               W["L0_attn_attn_w"].rearrange("(t p) o -> p t o", p=128),
               wr=[t("aw")], stream="aw", dma=True)
        K.emit("gpsimd", "dma_start", PW[:, :, :],
               W["L0_attn_proj_w"].rearrange("(t p) o -> p t o", p=128),
               wr=[t("pw")], stream="pw", dma=True)
        for seq in range(2):
            K.emit("vector", "memset", MAXK[:, :], 0.0, wr=[t("maxk")])
            K.emit("vector", "memset", VTOK[:, :, 64:65], 1.0, wr=[t("vtok")])
            for c in range(4):
                g = seq * 4 + c
                gs = slice(g * 512, (g + 1) * 512)
                norm_group(K, C, NB, src3, g, RN[:, :, :], "rms", [(srckey, g)], t("rn"))
                for m in range(DT):
                    pk = ("PA", m % 2)
                    pap = C.pa(m % 2)
                    for k in range(DT):
                        K.emit("tensor", "matmul", pap, AW[:, k, m * 128:(m + 1) * 128],
                               RN[:, k, :], start=(k == 0), stop=(k == DT - 1),
                               rd=[t("aw"), t("rn")], wr=[pk])
                    K.emit("vector", "tensor_copy", QT[:, m, :], pap, rd=[pk], wr=[t("qt")])
                for k in range(DT):
                    K.emit("tensor", "matmul", C.pa(2)[0:64, :], AW[:, k, 1024:1088],
                           RN[:, k, :], start=(k == 0), stop=(k == DT - 1),
                           rd=[t("aw"), t("rn")], wr=[("PA", 2)])
                K.emit("vector", "tensor_copy", KTH[0:64, gs.start - seq * 2048:
                       gs.stop - seq * 2048], C.pa(2)[0:64, :], rd=[("PA", 2)], wr=[t("kth")])
                K.emit("vector", "tensor_copy", KTH[64:128, gs.start - seq * 2048:
                       gs.stop - seq * 2048], C.pa(2)[0:64, :], rd=[("PA", 2)], wr=[t("kth")])
                for k in range(DT):
                    K.emit("tensor", "matmul", C.pa(3)[0:64, :], AW[:, k, 1088:1152],
                           RN[:, k, :], start=(k == 0), stop=(k == DT - 1),
                           rd=[t("aw"), t("rn")], wr=[("PA", 3)])
                K.emit("vector", "tensor_copy", VB[:, :], C.pa(3)[0:64, :],
                       rd=[("PA", 3)], wr=[t("vbuf")])
                for j in range(4):
                    K.emit("tensor", "transpose", C.PT[0:128, 0:64],
                           VB[:, j * 128:(j + 1) * 128], C.ident[0:64, 0:64],
                           rd=[t("vbuf"), ("ident",)], wr=[("PT",)])
                    K.emit("vector", "tensor_copy", VTOK[:, c * 4 + j, 0:64],
                           C.PT[0:128, 0:64], rd=[("PT",)], wr=[t("vtok")])
                for m in range(DT):
                    K.emit("scalar", "activation", SQT[:, m % 2, :], QT[:, m, :], AF.Square,
                           rd=[t("qt")], wr=[("qsqt", m % 2)])
                    K.emit("tensor", "matmul", C.PS1[0:1, :], C.onesb[:, 0:1], SQT[:, m % 2, :],
                           start=(m == 0), stop=(m == DT - 1),
                           rd=[("onesb",), ("qsqt", m % 2)], wr=[("PS1",)])
                K.emit("scalar", "activation", QSR[:, :], C.PS1[0:1, :], AF.Ln,
                       bias=C.consts[0:1, 3:4], rd=[("PS1",), ("consts",)], wr=[t("qsr")])
                K.emit("scalar", "activation", QSR[:, :], QSR[:, :], AF.Exp, scale=0.5,
                       rd=[t("qsr")], wr=[t("qsr")])
                K.emit("scalar", "activation", SQT[0:64, 0, :],
                       KTH[0:64, c * 512:(c + 1) * 512], AF.Square,
                       rd=[t("kth")], wr=[("qsqt", 0)])
                K.emit("tensor", "matmul", C.PS1[0:1, :], C.onesb[0:64, 0:1], SQT[0:64, 0, :],
                       start=True, stop=True, rd=[("onesb",), ("qsqt", 0)], wr=[("PS1",)])
                K.emit("vector", "reduce_max", KFR[:, :], C.PS1[0:1, :], axis=AX.X,
                       rd=[("PS1",)], wr=[t("kfr")])
                K.emit("vector", "tensor_tensor", MAXK[:, :], MAXK[:, :], KFR[:, :], OP.max,
                       rd=[t("maxk"), t("kfr")], wr=[t("maxk")])
                K.emit("scalar", "activation", KFR[:, :], MAXK[:, :], AF.Ln,
                       bias=C.consts[0:1, 3:4], rd=[t("maxk"), ("consts",)], wr=[t("kfr")])
                K.emit("scalar", "activation", KFR[:, :], KFR[:, :], AF.Exp, scale=0.5,
                       rd=[t("kfr")], wr=[t("kfr")])
                K.emit("vector", "reduce_max", QSR[0:1, 0:1], QSR[0:1, :], axis=AX.X,
                       rd=[t("qsr")], wr=[t("qsr")])
                K.emit("vector", "tensor_scalar", QSR[0:1, 0:1], QSR[0:1, 0:1],
                       KFR[0:1, 0:1], -0.125, OP.mult, OP.mult,
                       rd=[t("qsr"), t("kfr")], wr=[t("qsr")])
                K.emit("tensor", "matmul", C.PS1[0:128, 0:1], C.onesf[0:1, 0:128],
                       QSR[0:1, 0:1], start=True, stop=True,
                       rd=[("onesf",), t("qsr")], wr=[("PS1",)])
                K.emit("vector", "tensor_copy", CBC[:, :], C.PS1[0:128, 0:1],
                       rd=[("PS1",)], wr=[t("cbc")])
                for h in range(16):
                    hb_ = (h % 2) * 64
                    pk = ("PA", h % 4)
                    pav = C.pa(h % 4)
                    nst = 4 * (c + 1)
                    for st in range(nst):
                        psc = ("PB", st % 2)
                        pscap = C.pb(st % 2)
                        K.emit("tensor", "matmul", pscap,
                               KTH[hb_:hb_ + 64, st * 128:(st + 1) * 128],
                               QT[hb_:hb_ + 64, h // 2, :],
                               start=True, stop=True, rd=[t("kth"), t("qt")], wr=[psc])
                        delta = st * 128 - c * 512
                        if delta >= 0:
                            K.emit("vector", "tensor_tensor", pscap, pscap,
                                   C.mtri[:, 384 - delta:896 - delta], OP.add,
                                   rd=[psc, ("mtri",)], wr=[psc])
                        K.emit("scalar", "activation", EB[:, st % 2, :], pscap, AF.Exp,
                               scale=0.125, bias=CBC[:, 0:1],
                               rd=[psc, t("cbc")], wr=[(f"q_eb", st % 2)])
                        K.emit("tensor", "matmul", pav[0:65, :], VTOK[:, st, :],
                               EB[:, st % 2, :],
                               start=(st == 0), stop=(st == nst - 1),
                               rd=[(f"q_eb", st % 2), t("vtok")], wr=[pk])
                    K.emit("vector", "reciprocal", RD[:, :], pav[64:65, :], rd=[pk], wr=[t("rd")])
                    K.emit("tensor", "matmul", C.PS1[0:64, :], C.onesf[0:1, 0:64],
                           RD[0:1, :], start=True, stop=True,
                           rd=[("onesf",), t("rd")], wr=[("PS1",)])
                    K.emit("vector", "tensor_copy", RB64[:, :], C.PS1[0:64, :],
                           rd=[("PS1",)], wr=[t("rb64")])
                    K.emit("vector", "tensor_tensor", OT[hb_:hb_ + 64, h // 2, :],
                           pav[0:64, :], RB64[:, :], OP.mult,
                           rd=[pk, t("rb64")], wr=[t("ot")])
                for m in range(DT):
                    pk = ("PB", m % 2)
                    pap = C.pb(m % 2)
                    for k in range(DT):
                        K.emit("tensor", "matmul", pap, PW[:, k, m * 128:(m + 1) * 128],
                               OT[:, k, :], start=(k == 0), stop=(k == DT - 1),
                               rd=[t("pw"), t("ot")], wr=[pk])
                    K.emit("vector", "tensor_tensor", NB.xg[:, m, :], NB.xg[:, m, :], pap,
                           OP.add, rd=[("qxg",), pk], wr=[("qxg",)])
                K.emit("sync", "dma_start", dst3[:, :, gs], NB.xg[:, :, :],
                       rd=[("qxg",)], wr=[(dstkey, g)], dma=True)
    K.barrier()



def build_mamba(K, C, W, scr, src3, dst3, srckey, dstkey):
    nc = K.nc
    ipb = scr["IPB"]
    with ExitStack() as ctx:
        sb = lambda name, shape, dt: ctx.enter_context(nc.sbuf_tensor(f"mb_{name}", shape, dt))
        t = lambda n: (f"mb_{n}",)
        NB = NormBufs(nc, ctx, "mb", rms_only=True)
        RN = sb("rn", [128, 8, 512], BF16)
        IPS = [sb(f"ips{i}", [128, 8, 128], BF16) for i in range(2)]
        Z = sb("z", [128, 16, 512], BF16)
        XBCP = sb("xbcp", [128, 17, 515], BF16)
        XBC = sb("xbc", [128, 17, 512], BF16)
        DTBUF = sb("dtbuf", [32, 512], F32)
        CONVW = sb("convw", [128, 68], F32)
        CONVB = sb("convb", [128, 17], F32)
        MN = sb("mn", [128, 16], F32)
        DBV = sb("dbv", [128, 16], F32)
        DSEL = sb("dsel", [32, 16], F32)
        MAP64 = sb("map64", [32, 128], F32)
        ACOL = sb("acol", [32, 1], F32)
        DTBC = sb("dtbc", [32, 1], F32)
        DVC = sb("dvc", [32, 1], F32)
        DTA = sb("dta", [32, 128], F32)
        S_ = sb("s_", [32, 128], F32)
        PALL = sb("pall", [32, 128], F32)
        SROW = sb("srowf", [1, 1024], F32)
        PROW = sb("prow", [1, 1024], BF16)
        ST = sb("st", [128, 32], F32)
        DTT = sb("dtt", [128, 32], F32)
        DTC = sb("dtc", [128, 32], F32)
        GTB = sb("gtb", [128, 128], BF16)
        CFB = sb("cfb", [64, 128], BF16)
        BT = sb("bt", [128, 64], BF16)
        BSC = sb("bsc", [128, 32, 64], BF16)
        TTMP = sb("ttmp", [128, 8, 128], F32)
        MT = sb("mt", [128, 32, 128], BF16)
        U = sb("u", [128, 16, 128], BF16)
        PBS = sb("pbs", [64, 32, 128], BF16)
        CSC = sb("csc", [64, 16, 128], F32)
        HST = sb("hst", [64, 32, 64], F32)
        DXS = sb("dxs", [128, 16, 128], F32)
        GM = sb("gm", [128, 16, 512], BF16)
        SQB = sb("sqb", [128, 2, 512], BF16)
        CT = sb("ct", [128, 2, 512], F32)
        OPS = [sb(f"ops{i}", [128, 16, 128], BF16) for i in range(2)]

        K.emit("sync", "dma_start", CONVW[:, :], W["h_convw"], wr=[t("convw")], dma=True)
        K.emit("sync", "dma_start", CONVB[:, :], W["h_convb"], wr=[t("convb")], dma=True)
        K.emit("sync", "dma_start", MN[:, :], W["h_mnorm"], wr=[t("mn")], dma=True)
        K.emit("sync", "dma_start", DTBC[:, :], W["h_dtb"], wr=[t("dtbc")], dma=True)
        K.emit("sync", "dma_start", ACOL[:, :], W["h_alog"], wr=[t("acol")], dma=True)
        K.emit("sync", "dma_start", DVC[:, :], W["h_dvec"], wr=[t("dvc")], dma=True)
        K.emit("sync", "dma_start", MAP64[:, :], W["c_map64"], wr=[t("map64")], dma=True)
        K.emit("sync", "dma_start", DSEL[:, :], W["c_sel32"], wr=[t("dsel")], dma=True)
        K.emit("scalar", "activation", ACOL[:, :], ACOL[:, :], AF.Exp,
               rd=[t("acol")], wr=[t("acol")])
        K.emit("vector", "tensor_scalar_mul", ACOL[:, :], ACOL[:, :], -1.0,
               rd=[t("acol")], wr=[t("acol")])
        K.emit("vector", "tensor_tensor", DSEL[:, :], DSEL[:, :],
               DVC[:, 0:1].broadcast_to([32, 16]), OP.mult,
               rd=[t("dsel"), t("dvc")], wr=[t("dsel")])
        K.emit("tensor", "matmul", C.PS1[:, 0:16], MAP64[:, :], DSEL[:, :],
               start=True, stop=True, rd=[t("map64"), t("dsel")], wr=[("PS1",)])
        K.emit("vector", "tensor_copy", DBV[:, :], C.PS1[:, 0:16], rd=[("PS1",)], wr=[t("dbv")])

        for seq in range(2):
            K.emit("vector", "memset", HST[:, :, :], 0.0, wr=[t("hst")])
            K.emit("vector", "memset", XBCP[:, :, 0:3], 0.0, wr=[t("xbcp")])
            for grp in range(4):
                g = seq * 4 + grp
                gs = slice(g * 512, (g + 1) * 512)
                norm_group(K, C, NB, src3, g, RN[:, :, :], "rms", [(srckey, g)], t("rn"))
                def conv_tile(tt):
                    ctp = tt % 2
                    K.emit("vector", "tensor_scalar_mul", CT[:, ctp, :], XBCP[:, tt, 0:512],
                           CONVW[:, 4 * tt:4 * tt + 1], rd=[t("xbcp"), t("convw")],
                           wr=[("mbct", ctp)])
                    for j in range(1, 4):
                        K.emit("vector", "scalar_tensor_tensor", CT[:, ctp, :],
                               XBCP[:, tt, j:512 + j], CONVW[:, 4 * tt + j:4 * tt + j + 1],
                               CT[:, ctp, :], OP.mult, OP.add,
                               rd=[t("xbcp"), t("convw")], wr=[("mbct", ctp)])
                    K.emit("scalar", "activation", XBC[:, tt, :], CT[:, ctp, :], AF.Silu,
                           bias=CONVB[:, tt:tt + 1], rd=[("mbct", ctp), t("convb")],
                           wr=[t("xbc")])

                for m in list(range(16, 34)) + list(range(16)):
                    sl = m % 2
                    mk_ = t(f"ips{sl}")
                    mw = 128 if m < 33 else 32
                    K.emit("sync", "dma_start", IPS[sl][:, :, 0:mw], ipb[m][:, :, 0:mw],
                           rd=[("IPB",)], wr=[mk_], stream=f"ips{sl}", dma=True)
                    pk = ("PB", sl)
                    pap = C.pb(sl)[0:mw, :]
                    for k in range(DT):
                        K.emit("tensor", "matmul", pap, IPS[sl][:, k, 0:mw], RN[:, k, :],
                               start=(k == 0), stop=(k == DT - 1), rd=[mk_, t("rn")], wr=[pk])
                    if m < 16:
                        K.emit("scalar", "activation", Z[:, m, :], pap, AF.Silu,
                               rd=[pk], wr=[t("z")])
                    elif m < 33:
                        K.emit("vector", "tensor_copy", XBCP[:, m - 16, 3:515], pap,
                               rd=[pk], wr=[t("xbcp")])
                        conv_tile(m - 16)
                    else:
                        K.emit("scalar", "activation", CT[0:32, 0, :], pap, AF.Exp,
                               bias=DTBC[:, 0:1], rd=[pk, t("dtbc")], wr=[("mbct", 0)])
                        K.emit("scalar", "activation", DTBUF[:, :], CT[0:32, 0, :], AF.Ln,
                               bias=C.consts[0:32, 1:2], rd=[("mbct", 0), ("consts",)],
                               wr=[t("dtbuf")])
                K.emit("vector", "tensor_copy", XBCP[:, :, 0:3], XBCP[:, :, 512:515],
                       rd=[t("xbcp")], wr=[t("xbcp")])
                for c in range(4):
                    tc = slice(c * 128, (c + 1) * 128)
                    K.emit("vector", "tensor_scalar_mul", DTA[:, :], DTBUF[:, tc],
                           ACOL[:, 0:1], rd=[t("dtbuf"), t("acol")], wr=[t("dta")])
                    K.emit("vector", "tensor_tensor_scan", S_[:, :], C.onesf[0:32, 0:128],
                           DTA[:, :], 0.0, OP.mult, OP.add, rd=[("onesf",), t("dta")],
                           wr=[t("s_")])
                    K.emit("scalar", "activation", PALL[:, :], S_[:, :], AF.Exp,
                           rd=[t("s_")], wr=[t("pall")])
                    K.emit("tensor", "transpose", C.PS1[0:128, 0:32], S_[:, :],
                           C.identf[0:32, 0:32], rd=[t("s_"), ("identf",)], wr=[("PS1",)])
                    K.emit("vector", "tensor_copy", ST[:, :], C.PS1[0:128, 0:32],
                           rd=[("PS1",)], wr=[t("st")])
                    K.emit("tensor", "transpose", C.PS1[0:128, 0:32], DTBUF[:, tc],
                           C.identf[0:32, 0:32], rd=[t("dtbuf"), ("identf",)], wr=[("PS1",)])
                    K.emit("vector", "tensor_copy", DTT[:, :], C.PS1[0:128, 0:32],
                           rd=[("PS1",)], wr=[t("dtt")])

                    K.emit("vector", "tensor_copy", CFB[:, :], XBC[64:128, 16, tc],
                           rd=[t("xbc")], wr=[t("cfb")])
                    K.emit("tensor", "matmul", C.PS1[:, 0:128], XBC[0:64, 16, tc],
                           CFB[:, :], start=True, stop=True, rd=[t("xbc"), t("cfb")],
                           wr=[("PS1",)])
                    K.emit("vector", "tensor_copy", GTB[:, :], C.PS1[:, 0:128],
                           rd=[("PS1",)], wr=[t("gtb")])
                    for r in range(4):
                        K.emit("gpsimd", "dma_start", PROW[:, :], PALL[8 * r:8 * (r + 1), :],
                               rd=[t("pall")], wr=[t("prowb")], stream="prow", dma=True)
                        for j in range(2):
                            K.emit("tensor", "matmul", C.pb(j)[0:64, :],
                                   C.onesbr[0:1, 0:64], PROW[0:1, j * 512:(j + 1) * 512],
                                   start=True, stop=True,
                                   rd=[("onesbr",), t("prowb")], wr=[("PB", j)])
                        K.emit("vector", "tensor_copy", PBS[:, 8 * r:8 * (r + 1), :],
                               C.PB[0:64, :].rearrange("p (a b) -> p a b", b=128),
                               rd=[("PB", 0), ("PB", 1)], wr=[t("pbs")])
                    for r in range(4):
                        hr = slice(8 * r, 8 * (r + 1))
                        K.emit("gpsimd", "dma_start", SROW[:, :], S_[hr, :],
                               rd=[t("s_")], wr=[t("srowb")], stream="srow", dma=True)
                        for j in range(2):
                            K.emit("tensor", "matmul", C.pb(j), C.onesf[0:1, 0:128],
                                   SROW[0:1, j * 512:(j + 1) * 512],
                                   start=True, stop=True, rd=[("onesf",), t("srowb")],
                                   wr=[("PB", j)])
                        K.emit("vector", "tensor_tensor", TTMP[:, :, :],
                               C.PB[:, :].rearrange("p (a b) -> p a b", b=128),
                               ST[:, hr].unsqueeze(2).broadcast_to([128, 8, 128]),
                               OP.subtract, rd=[("PB", 0), ("PB", 1), t("st")], wr=[t("ttmp")])
                        K.emit("vector", "tensor_tensor", TTMP[:, :, :], TTMP[:, :, :],
                               C.mtri[:, 384:512].unsqueeze(1).broadcast_to([128, 8, 128]),
                               OP.add, rd=[t("ttmp"), ("mtri",)], wr=[t("ttmp")])
                        K.emit("scalar", "activation", TTMP[:, :, :], TTMP[:, :, :], AF.Exp,
                               rd=[t("ttmp")], wr=[t("ttmp")])
                        K.emit("vector", "tensor_copy", DTC[:, hr], TTMP[:, :, 127],
                               rd=[t("ttmp")], wr=[t("dtc")])
                        K.emit("vector", "tensor_tensor", MT[:, hr, :], TTMP[:, :, :],
                               GTB[:, :].unsqueeze(1).broadcast_to([128, 8, 128]), OP.mult,
                               rd=[t("ttmp"), t("gtb")], wr=[t("mt")])
                    for tt in range(16):
                        K.emit("tensor", "transpose", C.PT[0:128, 0:128],
                               XBC[:, tt, tc], C.ident[:, :], rd=[t("xbc"), ("ident",)],
                               wr=[("PT",)])
                        K.emit("vector", "tensor_tensor",
                               U[:, tt, :].rearrange("p (a b) -> p a b", b=64),
                               C.PT[0:128, 0:128].rearrange("p (a b) -> p a b", b=64),
                               DTT[:, 2 * tt:2 * tt + 2].unsqueeze(2)
                               .broadcast_to([128, 2, 64]), OP.mult,
                               rd=[("PT",), t("dtt")], wr=[t("u")])
                    for r2 in range(2):
                        K.emit("vector", "tensor_tensor", CSC[:, :, :],
                               CFB[:, :].unsqueeze(1).broadcast_to([64, 16, 128]),
                               PBS[:, 16 * r2:16 * (r2 + 1), :], OP.mult,
                               rd=[t("cfb"), t("pbs")], wr=[t("csc")])
                        for hi in range(16):
                            h = 16 * r2 + hi
                            tt, hh = h // 2, h % 2
                            K.emit("tensor", "matmul",
                                   C.PA[hh * 64:(hh + 1) * 64, tt * 128:(tt + 1) * 128],
                                   U[:, tt, hh * 64:(hh + 1) * 64], MT[:, h, :],
                                   start=True, stop=False, tile_position=(0, hh * 64),
                                   rd=[t("u"), t("mt")], wr=[("PA", tt % 4)])
                            K.emit("tensor", "matmul",
                                   C.PA[hh * 64:(hh + 1) * 64, tt * 128:(tt + 1) * 128],
                                   HST[:, h, :], CSC[:, hi, :],
                                   start=False, stop=True, tile_position=(0, hh * 64),
                                   rd=[t("hst"), t("csc")], wr=[("PA", tt % 4)])
                    K.emit("vector", "tensor_tensor", DXS[:, :, :], XBC[:, 0:16, tc],
                           DBV[:, :].unsqueeze(2).broadcast_to([128, 16, 128]), OP.mult,
                           rd=[t("xbc"), t("dbv")], wr=[t("dxs")])
                    K.emit("vector", "tensor_tensor", DXS[:, :, :],
                           C.PA[:, :].rearrange("p (a b) -> p a b", b=128), DXS[:, :, :],
                           OP.add, rd=[("PA", 0), ("PA", 1), ("PA", 2), ("PA", 3),
                                       t("dxs")], wr=[t("dxs")])
                    K.emit("vector", "tensor_tensor", GM[:, :, tc], DXS[:, :, :],
                           Z[:, :, tc], OP.mult, rd=[t("dxs"), t("z")], wr=[t("gm")])
                    K.emit("tensor", "transpose", C.PT[0:128, 0:64], XBC[0:64, 16, tc],
                           C.ident[0:64, 0:64], rd=[t("xbc"), ("ident",)], wr=[("PT",)])
                    K.emit("vector", "tensor_copy", BT[:, :], C.PT[0:128, 0:64],
                           rd=[("PT",)], wr=[t("bt")])
                    K.emit("vector", "tensor_tensor", BSC[:, :, :],
                           BT[:, :].unsqueeze(1).broadcast_to([128, 32, 64]),
                           DTC[:, :].unsqueeze(2).broadcast_to([128, 32, 64]), OP.mult,
                           rd=[t("bt"), t("dtc")], wr=[t("bsc")])
                    K.emit("vector", "tensor_tensor", HST[:, :, :], HST[:, :, :],
                           PBS[:, :, 127:128].broadcast_to([64, 32, 64]), OP.mult,
                           rd=[t("hst"), t("pbs")], wr=[t("hst")])
                    for h in range(32):
                        K.emit("tensor", "matmul", C.PA[0:64, h * 64:(h + 1) * 64],
                               BSC[:, h, :], U[:, h // 2, (h % 2) * 64:(h % 2) * 64 + 64],
                               start=True, stop=True, rd=[t("bsc"), t("u")],
                               wr=[("PA", h % 4)])
                    K.emit("vector", "tensor_tensor", HST[:, :, :], HST[:, :, :],
                           C.PA[0:64, :].rearrange("p (a b) -> p a b", b=64), OP.add,
                           rd=[t("hst"), ("PA", 0), ("PA", 1), ("PA", 2), ("PA", 3)],
                           wr=[t("hst")])
                for tt in range(16):
                    K.emit("scalar", "activation", SQB[:, tt % 2, :], GM[:, tt, :], AF.Square,
                           rd=[t("gm")], wr=[("mbsqb", tt % 2)])
                    K.emit("tensor", "matmul", C.PS1[0:1, :], C.onesb[:, 0:1], SQB[:, tt % 2, :],
                           start=(tt == 0), stop=(tt == 15), rd=[("onesb",), ("mbsqb", tt % 2)],
                           wr=[("PS1",)])
                K.emit("vector", "tensor_scalar_mul", NB.r1[:, :], C.PS1[0:1, :], 1.0 / 2048,
                       rd=[("PS1",)], wr=[("mbr1",)])
                K.emit("scalar", "activation", NB.r1[:, :], NB.r1[:, :], AF.Ln,
                       bias=C.consts[0:1, 0:1], rd=[("mbr1",), ("consts",)], wr=[("mbr1",)])
                K.emit("scalar", "activation", NB.r1[:, :], NB.r1[:, :], AF.Exp, scale=-0.5,
                       rd=[("mbr1",)], wr=[("mbr1",)])
                K.emit("tensor", "matmul", C.pb(0), C.onesf[0:1, 0:128], NB.r1[0:1, :],
                       start=True, stop=True, rd=[("onesf",), ("mbr1",)], wr=[("PB", 0)])
                K.emit("vector", "tensor_tensor", GM[:, :, :], GM[:, :, :],
                       C.pb(0).unsqueeze(1).broadcast_to([128, 16, 512]), OP.mult,
                       rd=[t("gm"), ("PB", 0)], wr=[t("gm")])
                K.emit("vector", "tensor_tensor", GM[:, :, :], GM[:, :, :],
                       MN[:, :].unsqueeze(2).broadcast_to([128, 16, 512]), OP.mult,
                       rd=[t("gm"), t("mn")], wr=[t("gm")])
                for m in range(DT):
                    sl = m % 2
                    K.emit("gpsimd", "dma_start", OPS[sl][:, :, :], W["h_outproj_cm"][m]
                           .rearrange("(t p) c -> p t c", p=128),
                           wr=[t(f"ops{sl}")], stream=f"ops{sl}", dma=True)
                    pk = ("PA", 2 + sl)
                    pap = C.pa(2 + sl)
                    for k in range(16):
                        K.emit("tensor", "matmul", pap, OPS[sl][:, k, :], GM[:, k, :],
                               start=(k == 0), stop=(k == 15),
                               rd=[t(f"ops{sl}"), t("gm")], wr=[pk])
                    K.emit("vector", "tensor_tensor", NB.xg[:, m, :], NB.xg[:, m, :], pap,
                           OP.add, rd=[("mbxg",), pk], wr=[("mbxg",)])
                K.emit("sync", "dma_start", dst3[:, :, gs], NB.xg[:, :, :],
                       rd=[("mbxg",)], wr=[(dstkey, g)], dma=True)
    K.barrier()


def host_prepare(inputs):
    W = {}
    for n in WEIGHT_SHAPES:
        W[n] = np.ascontiguousarray(np.asarray(inputs[n], dtype=np.float32))
    W["h_wout1T"] = np.ascontiguousarray(np.asarray(inputs["L0_ffn_Wout1"], np.float32).T)
    W["h_wout2T"] = np.ascontiguousarray(np.asarray(inputs["L0_ffn_Wout2"], np.float32).T)
    W["h_win1bias"] = np.ascontiguousarray(
        np.asarray(inputs["L0_ffn_Win1"], np.float32)[1024].reshape(4, 128).T)
    W["h_win2bias"] = np.ascontiguousarray(
        np.asarray(inputs["L0_ffn_Win2"], np.float32)[4096].reshape(4, 128).T)
    cw = np.asarray(inputs["L0_attn_conv_w"], np.float32).reshape(17, 128, 4)
    W["h_convw"] = np.ascontiguousarray(cw.transpose(1, 0, 2).reshape(128, 68))
    W["h_convb"] = np.ascontiguousarray(
        np.asarray(inputs["L0_attn_conv_b"], np.float32).reshape(17, 128).T)
    W["h_mnorm"] = np.ascontiguousarray(
        np.asarray(inputs["L0_attn_mnorm_w"], np.float32).reshape(16, 128).T)
    W["h_dtb"] = np.asarray(inputs["L0_attn_dt_bias"], np.float32).reshape(32, 1)
    W["h_alog"] = np.asarray(inputs["L0_attn_A_log"], np.float32).reshape(32, 1)
    W["h_dvec"] = np.asarray(inputs["L0_attn_D"], np.float32).reshape(32, 1)
    W["h_maak"] = np.ascontiguousarray(
        np.asarray(inputs["L0_attn_time_maa_k"], np.float32).reshape(8, 128).T)
    W["h_maar"] = np.ascontiguousarray(
        np.asarray(inputs["L0_attn_time_maa_r"], np.float32).reshape(8, 128).T)
    op_ = np.asarray(inputs["L0_attn_out_proj_w"], np.float32)
    W["h_outproj_cm"] = np.ascontiguousarray(op_.reshape(2048, 8, 128).transpose(1, 0, 2))
    kw_ = np.asarray(inputs["L0_attn_key_w"], np.float32)
    W["h_key_cm"] = np.ascontiguousarray(kw_.reshape(1024, 32, 128).transpose(1, 0, 2))
    vw_ = np.asarray(inputs["L0_attn_value_w"], np.float32)
    W["h_value_cm"] = np.ascontiguousarray(vw_.reshape(4096, 8, 128).transpose(1, 0, 2))
    for a, b in (("h_wqT", "L1_attn_w_q"), ("h_wkT", "L1_attn_w_k"),
                 ("h_wvT", "L1_attn_w_v"), ("h_woT", "L1_attn_w_o")):
        W[a] = np.ascontiguousarray(np.asarray(inputs[b], np.float32).T)
    W["h_gain"] = np.asarray(inputs["L1_attn_gain"], np.float32).reshape(1, 1)
    W["h_routerT"] = np.ascontiguousarray(np.asarray(inputs["L1_ffn_router_w"], np.float32).T)
    W["h_ebias"] = np.asarray(inputs["L1_ffn_experts_bias"], np.float32).reshape(8, 1)
    W["h_w1"] = np.ascontiguousarray(np.concatenate(
        [np.asarray(inputs["L1_ffn_w1"], np.float32),
         np.asarray(inputs["L1_ffn_sw1"], np.float32)], axis=0))
    W["h_w3"] = np.ascontiguousarray(np.concatenate(
        [np.asarray(inputs["L1_ffn_w3"], np.float32),
         np.asarray(inputs["L1_ffn_sw3"], np.float32)], axis=0))
    W["h_w2"] = np.ascontiguousarray(np.concatenate(
        [np.asarray(inputs["L1_ffn_w2"], np.float32),
         np.asarray(inputs["L1_ffn_sw2"], np.float32)], axis=0))
    p = np.arange(128)[:, None]
    q = np.arange(1024)[None, :]
    W["c_mtri"] = np.where(q >= p + 384, 0.0, NEG).astype(np.float32)
    W["c_ident"] = np.eye(128, dtype=ml_dtypes.bfloat16)
    W["c_identf"] = np.eye(128, dtype=np.float32)
    W["c_kidx"] = np.tile(np.arange(16, dtype=np.float32)[None, :], (128, 1))
    W["c_invk"] = np.tile((1.0 / np.arange(1, 17, dtype=np.float32))[None, :], (128, 1))
    W["c_kval"] = np.tile(np.arange(1, 17, dtype=np.float32)[None, :], (128, 1))
    W["c_tokids"] = np.ascontiguousarray(
        (np.arange(32)[None, :] * 128 + np.arange(128)[:, None]).astype(np.int32))
    W["c_iota128"] = np.arange(128, dtype=np.float32).reshape(128, 1)
    hh = np.arange(32)[:, None]
    W["c_map64"] = (np.arange(128)[None, :] // 64 == hh % 2).astype(np.float32)
    W["c_sel32"] = (np.arange(16)[None, :] == hh // 2).astype(np.float32)
    return W


def shard_x(x, c):
    return np.ascontiguousarray(
        x[2 * c:2 * c + 2].transpose(2, 0, 1).reshape(1024, 4096)
        .reshape(8, 128, 4096).transpose(1, 0, 2))


def unshard_out(oc):
    return np.ascontiguousarray(oc).transpose(1, 0, 2).reshape(1024, 2, 2048).transpose(1, 2, 0)


def kernel(**inputs):
    W = host_prepare(inputs)
    x = np.asarray(inputs["L0_attn_x"], np.float32)
    nc = build_program()
    in_maps = []
    for c in range(NCORES):
        m = {"x_in": shard_x(x, c)}
        m.update(W)
        in_maps.append(m)
    res = run_bass_kernel_spmd(nc, in_maps, core_ids=list(range(NCORES)))
    outs = [unshard_out(np.asarray(res.results[c]["out"])) for c in range(NCORES)]
    return np.concatenate(outs, axis=0).astype(np.float32)


# version 06
# speedup vs baseline: 1.0105x; 1.0105x over previous
import bisect
import numpy as np
import ml_dtypes
from contextlib import ExitStack

import concourse.bass as bass
import concourse.mybir as mybir
from concourse.bass_utils import run_bass_kernel_spmd
from concourse.alu_op_type import AluOpType as OP
from concourse.bass import IndirectOffsetOnAxis

F32 = mybir.dt.float32
I32 = mybir.dt.int32
BF16 = mybir.dt.bfloat16
AF = mybir.ActivationFunctionType
AX = mybir.AxisListType

NCORES = 8
D = 1024
DT = 8
NSEQ = 2
TSEQ = 2048
NTOK = 4096
NEG = -1.0e30
RS2 = 0.70710678118654752
MOE_SPARSE = True
MOE_NSB = 24
MOE_BIG = 1.0e6
DBG = []


class Kit:
    ENGINES = ("tensor", "vector", "scalar", "sync", "gpsimd")
    DMA_Q = ("sync", "gpsimd", "scalar")
    NSLOT = 24

    def __init__(self, nc, ctx):
        self.nc = nc
        self.esem = {e: ctx.enter_context(nc.semaphore(f"es_{e}")) for e in self.ENGINES}
        self.dsem = {(q, i): ctx.enter_context(nc.semaphore(f"ds_{q}{i}"))
                     for q in self.DMA_Q for i in range(self.NSLOT)}
        self.rec = {e: [] for e in self.ENGINES}
        self.instrs = {e: [] for e in self.ENGINES}
        self.incs = {e: [] for e in self.ENGINES}
        self.dcount = {k: 0 for k in self.dsem}
        self.slot_of = {}
        self.slot_rr = {q: 0 for q in self.DMA_Q}
        self.slot_key = {}
        self.slot_mixed = set()
        self.writer = {}
        self.readers = {}
        self.waited = {e: {} for e in self.ENGINES}

    def _slot(self, q, stream):
        key = (q, stream)
        s = self.slot_of.get(key)
        if s is None:
            if self.slot_rr[q] >= self.NSLOT:
                raise RuntimeError(f"dma stream overflow on {q}: {stream}")
            s = self.slot_rr[q]
            self.slot_rr[q] += 1
            self.slot_of[key] = s
        return s

    def _wait(self, e, tok):
        if tok[0] == "e":
            _, pe, pos = tok
            if pe == e == "tensor":
                return
            lst = self.incs[pe]
            i = bisect.bisect_left(lst, (pos, -1))
            if i < len(lst):
                val = lst[i][1]
            else:
                val = (lst[-1][1] + 1) if lst else 1
                self.instrs[pe][pos][4].append((self.esem[pe], 1))
                lst.append((pos, val))
            key, sem = ("es", pe), self.esem[pe]
        else:
            _, qe, slot, val = tok
            if (qe, slot) in self.slot_mixed:
                val = self.dcount[(qe, slot)]
            key, sem = ("ds", qe, slot), self.dsem[(qe, slot)]
        if self.waited[e].get(key, 0) >= val:
            return
        self.rec[e].append(("wait", sem, val))
        self.waited[e][key] = val

    def emit(self, e, op, *args, rd=(), wr=(), dma=False, stream=None, **kw):
        for r in rd:
            t = self.writer.get(r)
            if t is not None:
                self._wait(e, t)
        for r in wr:
            t = self.writer.get(r)
            if t is not None:
                self._wait(e, t)
            for rt in self.readers.get(r, {}).values():
                self._wait(e, rt)
        entry = ["op", op, args, kw, []]
        self.rec[e].append(entry)
        if dma:
            wkey = wr[0] if wr else None
            if stream is None:
                stream = wkey if wkey is not None else (rd[0] if rd else "main")
                if isinstance(stream, tuple):
                    stream = stream[0]
            slot = self._slot(e, stream)
            sk = (e, slot)
            if self.slot_key.get(sk, wkey) != wkey or                     "dram" in str(getattr(args[0], "space", "")).lower():
                self.slot_mixed.add(sk)
            self.slot_key[sk] = wkey
            self.dcount[sk] += 16
            entry[4].append((self.dsem[sk], 16))
            tok = ("d", e, slot, self.dcount[sk])
            src = ("d", e, slot)
        else:
            self.instrs[e].append(entry)
            tok = ("e", e, len(self.instrs[e]) - 1)
            src = ("e", e)
        for r in rd:
            self.readers.setdefault(r, {})[src] = tok
        for r in wr:
            self.writer[r] = tok
            self.readers[r] = {}
        return entry

    def barrier(self):
        toks = []
        for pe in self.ENGINES:
            n = len(self.instrs[pe])
            if n:
                toks.append(("e", pe, n - 1))
        for (qe, slot), cnt in self.dcount.items():
            if cnt:
                toks.append(("d", qe, slot, cnt))
                self.slot_mixed.add((qe, slot))
        for e in self.ENGINES:
            for t in toks:
                self._wait(e, t)
        self.writer.clear()
        self.readers.clear()
        self.slot_of.clear()
        self.slot_key.clear()
        self.slot_mixed.clear()
        self.slot_rr = {q: 0 for q in self.DMA_Q}

    def finish(self):
        for (qe, slot), cnt in self.dcount.items():
            if cnt:
                self.rec[qe].append(("wait", self.dsem[(qe, slot)], cnt))

    def replay_into(self, blk):
        def mk(en):
            def body(h):
                for ent in self.rec[en]:
                    if ent[0] == "wait":
                        h.wait_ge(ent[1], ent[2])
                    else:
                        try:
                            inst = getattr(h, ent[1])(*ent[2], **ent[3])
                        except Exception:
                            import sys as _s
                            print(f"REPLAY FAIL {en} op={ent[1]} args={[str(a)[:150] for a in ent[2]]} kw={ent[3]}", file=_s.stderr)
                            raise
                        for sem, n in ent[4]:
                            inst.then_inc(sem, n)
            return body
        blk.tensor(mk("tensor"))
        blk.vector(mk("vector"))
        blk.scalar(mk("scalar"))
        blk.gpsimd(mk("gpsimd"))
        blk.sync(mk("sync"))


def sort_network(n):
    pairs = []

    def merge(lo, hi, r):
        step = r * 2
        if step < hi - lo:
            merge(lo, hi, step)
            merge(lo + r, hi, step)
            for i in range(lo + r, hi - r, step):
                pairs.append((i, i + r))
        else:
            pairs.append((lo, lo + r))

    def srt(lo, hi):
        if (hi - lo) >= 1:
            mid = lo + ((hi - lo) // 2)
            srt(lo, mid)
            srt(mid + 1, hi)
            merge(lo, hi, 1)

    srt(0, n - 1)
    return pairs


class Ctx:
    pass


def alloc_common(nc, ctx, C):
    sb = lambda name, shape, dt: ctx.enter_context(nc.sbuf_tensor(name, shape, dt))
    C.ident = sb("ident", [128, 128], BF16)
    C.identf = sb("identf", [128, 128], F32)
    C.mtri = sb("mtri", [128, 1024], F32)
    C.kidx = sb("kidx", [128, 16], F32)
    C.invk = sb("invk", [128, 16], F32)
    C.kval = sb("kval", [128, 16], F32)
    C.onesb = sb("onesb", [128, 1], BF16)
    C.onesbr = sb("onesbr", [1, 128], BF16)
    C.onesf = sb("onesf", [128, 512], F32)
    C.consts = sb("consts", [128, 8], F32)
    C.PA = ctx.enter_context(nc.psum_tensor("PA", [128, 2048], F32))
    C.PB = ctx.enter_context(nc.psum_tensor("PB", [128, 1024], F32))
    C.PS1 = ctx.enter_context(nc.psum_tensor("PS1", [128, 512], F32))
    C.PT = ctx.enter_context(nc.psum_tensor("PT", [128, 512], BF16))
    C.pa = lambda m: C.PA[:, m * 512:(m + 1) * 512]
    C.pb = lambda m: C.PB[:, m * 512:(m + 1) * 512]


def load_consts(K, C, dram):
    K.emit("sync", "dma_start", C.ident[:, :], dram["c_ident"], wr=[("ident",)], dma=True)
    K.emit("sync", "dma_start", C.identf[:, :], dram["c_identf"], wr=[("identf",)], dma=True)
    K.emit("sync", "dma_start", C.mtri[:, :], dram["c_mtri"], wr=[("mtri",)], dma=True)
    K.emit("sync", "dma_start", C.kidx[:, :], dram["c_kidx"], wr=[("kidx",)], dma=True)
    K.emit("sync", "dma_start", C.invk[:, :], dram["c_invk"], wr=[("invk",)], dma=True)
    K.emit("sync", "dma_start", C.kval[:, :], dram["c_kval"], wr=[("kval",)], dma=True)
    K.emit("gpsimd", "memset", C.onesb[:, :], 1.0, wr=[("onesb",)])
    K.emit("gpsimd", "memset", C.onesbr[:, :], 1.0, wr=[("onesbr",)])
    K.emit("gpsimd", "memset", C.onesf[:, :], 1.0, wr=[("onesf",)])
    K.emit("gpsimd", "memset", C.consts[:, 0:1], 1.0e-5, wr=[("consts",)])
    K.emit("gpsimd", "memset", C.consts[:, 1:2], 1.0, wr=[("consts",)])
    K.emit("gpsimd", "memset", C.consts[:, 2:3], 1.0e-4, wr=[("consts",)])
    K.emit("gpsimd", "memset", C.consts[:, 3:4], 1.0e-38, wr=[("consts",)])
    K.emit("gpsimd", "memset", C.consts[:, 4:5], -float(np.float32(np.sqrt(0.5))) /
           float(np.float32(np.sqrt(1.0 / (4.0 * np.pi)) * np.sqrt(2.0))), wr=[("consts",)])


def dbg_tap(K, dram, name, ap, key, col0=0):
    if name in DBG:
        n = 1
        for d in ap.shape[1:]:
            n *= d
        q = "scalar" if ap.dtype == F32 else "gpsimd"
        K.emit(q, "dma_start", dram["dbg"][0:ap.shape[0], col0:col0 + n],
               ap, rd=[key], dma=True)


def rsqrt_ip(K, C, ap, key, eps_col, extra_scale=1.0):
    K.emit("scalar", "activation", ap, ap, AF.Ln, bias=eps_col,
           rd=[key, ("consts",)], wr=[key])
    K.emit("scalar", "activation", ap, ap, AF.Exp, scale=-0.5, rd=[key], wr=[key])
    if extra_scale != 1.0:
        K.emit("vector", "tensor_scalar_mul", ap, ap, float(extra_scale), rd=[key], wr=[key])


class NormBufs:
    def __init__(self, nc, ctx, tag, rms_only=False):
        sb = lambda name, shape, dt: ctx.enter_context(nc.sbuf_tensor(name, shape, dt))
        self.xg = sb(f"{tag}_xg", [128, 8, 512], F32)
        self.xgb = sb(f"{tag}_xgb", [128, 8, 512], BF16)
        self.sq = sb(f"{tag}_sq", [128, 2, 512], BF16)
        if not rms_only:
            self.mb = sb(f"{tag}_mb", [128, 512], F32)
            self.vb = sb(f"{tag}_vb", [128, 512], F32)
        self.invb = sb(f"{tag}_invb", [128, 512], F32)
        self.srow = sb(f"{tag}_srow", [1, 1024], F32)
        self.r1 = sb(f"{tag}_r1", [1, 512], F32)
        self.tag = tag


def norm_group(K, C, NB, src3, g, out_xn, kind, srckeys, xnkey, out_f32=None,
               f32key=None, row_cache=None, gslice=None):
    tg = NB.tag
    kxg, kxgb = (f"{tg}xg",), (f"{tg}xgb",)
    gs = gslice if gslice is not None else slice(g * 512, (g + 1) * 512)
    src_slice = src3[:, :, gs]
    K.emit("sync", "dma_start", NB.xg[:, :, :], src_slice, rd=srckeys, wr=[kxg], dma=True)
    K.emit("gpsimd", "dma_start", NB.xgb[:, :, :], src_slice, rd=srckeys, wr=[kxgb], dma=True)
    if kind == "ln":
        for t in range(DT):
            K.emit("tensor", "matmul", C.PS1[0:1, :], C.onesb[:, 0:1], NB.xgb[:, t, :],
                   start=(t == 0), stop=(t == DT - 1), rd=[("onesb",), kxgb], wr=[("PS1",)])
        K.emit("vector", "tensor_copy", NB.srow[:, 0:512], C.PS1[0:1, :],
               rd=[("PS1",)], wr=[(f"{tg}srow",)])
    for t in range(DT):
        K.emit("scalar", "activation", NB.sq[:, t % 2, :], NB.xgb[:, t, :], AF.Square,
               rd=[kxgb], wr=[(f"{tg}sq", t % 2)])
        K.emit("tensor", "matmul", C.PS1[0:1, :], C.onesb[:, 0:1], NB.sq[:, t % 2, :],
               start=(t == 0), stop=(t == DT - 1), rd=[("onesb",), (f"{tg}sq", t % 2)],
               wr=[("PS1",)])
    K.emit("vector", "tensor_copy", NB.srow[:, 512:1024], C.PS1[0:1, :],
           rd=[("PS1",)], wr=[(f"{tg}srow",)])
    if kind == "ln":
        K.emit("tensor", "matmul", C.pb(0), C.onesf[0:1, 0:128], NB.srow[0:1, 0:512],
               start=True, stop=True, rd=[("onesf",), (f"{tg}srow",)], wr=[("PB", 0)])
    K.emit("tensor", "matmul", C.pb(1), C.onesf[0:1, 0:128], NB.srow[0:1, 512:1024],
           start=True, stop=True, rd=[("onesf",), (f"{tg}srow",)], wr=[("PB", 1)])
    if kind == "ln":
        K.emit("vector", "tensor_scalar_mul", NB.mb[:, :], C.pb(0), 1.0 / D,
               rd=[("PB", 0)], wr=[(f"{tg}mb",)])
        K.emit("scalar", "activation", NB.vb[:, :], NB.mb[:, :], AF.Square,
               rd=[(f"{tg}mb",)], wr=[(f"{tg}vb",)])
        K.emit("vector", "scalar_tensor_tensor", NB.invb[:, :], C.pb(1), 1.0 / D,
               NB.vb[:, :], OP.mult, OP.subtract,
               rd=[("PB", 1), (f"{tg}vb",)], wr=[(f"{tg}invb",)])
    else:
        K.emit("vector", "tensor_scalar_mul", NB.invb[:, :], C.pb(1), 1.0 / D,
               rd=[("PB", 1)], wr=[(f"{tg}invb",)])
    rsqrt_ip(K, C, NB.invb[:, :], (f"{tg}invb",), C.consts[:, 0:1])
    if row_cache is not None:
        rws = row_cache
        K.emit("vector", "tensor_scalar_mul", NB.r1[0:1, :],
               NB.srow[0:1, 0:512], 1.0 / D, rd=[(f"{tg}srow",)], wr=[(f"{tg}r1",)])
        K.emit("sync", "dma_start", rws[0:1, g * 512:(g + 1) * 512], NB.r1[0:1, :],
               rd=[(f"{tg}r1",)], wr=[("RWS", "m", g)], stream="rws", dma=True)
        K.emit("sync", "dma_start", rws[1:2, g * 512:(g + 1) * 512], NB.invb[0:1, :],
               rd=[(f"{tg}invb",)], wr=[("RWS", "i", g)], stream="rws", dma=True)
    if kind == "ln":
        K.emit("vector", "tensor_tensor", NB.xg[:, :, :], NB.xg[:, :, :],
               NB.mb[:, :].unsqueeze(1).broadcast_to([128, 8, 512]), OP.subtract,
               rd=[kxg, (f"{tg}mb",)], wr=[kxg])
    if out_xn is not None:
        K.emit("vector", "tensor_tensor", out_xn, NB.xg[:, :, :],
               NB.invb[:, :].unsqueeze(1).broadcast_to([128, 8, 512]), OP.mult,
               rd=[kxg, (f"{tg}invb",)], wr=[xnkey])
    if out_f32 is not None:
        K.emit("vector", "tensor_tensor", out_f32, NB.xg[:, :, :],
               NB.invb[:, :].unsqueeze(1).broadcast_to([128, 8, 512]), OP.mult,
               rd=[kxg, (f"{tg}invb",)], wr=[f32key])
    return kxg



def build_ffn0(K, C, W, scr, src3, dst3, srckey, dstkey, callid):
    nc = K.nc
    dram = W
    xns = scr["XNS"]
    with ExitStack() as ctx:
        sb = lambda name, shape, dt: ctx.enter_context(nc.sbuf_tensor(f"f{callid}_{name}", shape, dt))
        t = lambda n: (f"f{callid}_{n}",)
        NB = NormBufs(nc, ctx, f"f{callid}")
        XN = sb("xn", [128, 8, 512], BF16)
        AMAJ = sb("amaj", [128, 32, 16], F32)
        ZS = sb("zs", [128, 32, 16], F32)
        ZU = sb("zu", [128, 32, 16], F32)
        CS1 = sb("cs1", [128, 32, 16], F32)
        CS2 = sb("cs2", [128, 32, 16], F32)
        TMPS = sb("tmps", [128, 32, 16], F32)
        SM1 = sb("sm1", [128, 32], F32)
        SM2 = sb("sm2", [128, 32], F32)
        TMX = sb("tmx", [128, 32], F32)
        AT = sb("at", [16, 4096], BF16)
        APRE = sb("apre", [16, 512], F32)
        wproj = sb("wproj", [128, 8, 16], BF16)
        wc = sb("wc", [16, 1024], BF16)
        wwin1 = sb("wwin1", [128, 8, 512], BF16)
        bwin1 = sb("bwin1", [128, 4], F32)
        wwout1 = sb("wwout1", [128, 4, 4096], BF16)
        wwin2 = sb("wwin2", [128, 32, 512], BF16)
        bwin2 = sb("bwin2", [128, 4], F32)
        wwout2 = sb("wwout2", [128, 4, 1024], BF16)
        G1 = sb("g1", [128, 4, 512], BF16)
        G2 = sb("g2", [128, 4, 512], BF16)
        Z1 = sb("z1", [128, 4, 512], BF16)
        HB = sb("hb", [128, 16, 512], BF16)
        XG2 = sb("xg2", [128, 8, 512], F32)

        K.emit("gpsimd", "dma_start", wproj[:, :, :],
               W["L0_ffn_proj"].rearrange("(t p) e -> p t e", p=128), wr=[t("wproj")], dma=True)
        K.emit("gpsimd", "dma_start", wc[:, 0:512], W["L0_ffn_C1"], wr=[t("wc")], dma=True)
        K.emit("gpsimd", "dma_start", wc[:, 512:1024], W["L0_ffn_C2"], wr=[t("wc")], dma=True)
        K.emit("gpsimd", "dma_start", wwin1[:, :, :],
               W["L0_ffn_Win1"][0:1024, :].rearrange("(t p) r -> p t r", p=128),
               wr=[t("wwin1")], dma=True)
        K.emit("sync", "dma_start", bwin1[:, :], W["h_win1bias"], wr=[t("bwin1")], dma=True)
        K.emit("gpsimd", "dma_start", wwout1[:, :, :],
               W["h_wout1T"].rearrange("(t p) o -> p t o", p=128), wr=[t("wwout1")], dma=True)
        K.emit("gpsimd", "dma_start", wwin2[:, :, :],
               W["L0_ffn_Win2"][0:4096, :].rearrange("(t p) r -> p t r", p=128),
               wr=[t("wwin2")], dma=True)
        K.emit("sync", "dma_start", bwin2[:, :], W["h_win2bias"], wr=[t("bwin2")], dma=True)
        K.emit("gpsimd", "dma_start", wwout2[:, :, :],
               W["h_wout2T"].rearrange("(t p) o -> p t o", p=128), wr=[t("wwout2")], dma=True)

        for g in range(8):
            gs = slice(g * 512, (g + 1) * 512)
            srk = [srckey] if isinstance(srckey, tuple) else [(srckey, g)]
            norm_group(K, C, NB, src3, g, XN[:, :, :], "ln", srk, t("xn"))
            K.emit("sync", "dma_start", xns[:, :, gs], XN[:, :, :],
                   rd=[t("xn")], wr=[("XNS", g)], dma=True)
            if g == 0:
                dbg_tap(K, W, "ffn_xn", XN[:, :, :], t("xn"))
            for k in range(DT):
                K.emit("tensor", "matmul", C.PS1[0:16, :], wproj[:, k, :], XN[:, k, :],
                       start=(k == 0), stop=(k == DT - 1),
                       rd=[t("wproj"), t("xn")], wr=[("PS1",)])
            K.emit("vector", "tensor_copy", APRE[:, :], C.PS1[0:16, :], rd=[("PS1",)], wr=[t("apre")])
            for j in range(4):
                K.emit("tensor", "transpose", C.pb(0)[0:128, 0:16],
                       APRE[:, j * 128:(j + 1) * 128], C.identf[0:16, 0:16],
                       rd=[t("apre"), ("identf",)], wr=[("PB", 0)])
                K.emit("vector", "tensor_copy", AMAJ[:, g * 4 + j, :], C.pb(0)[0:128, 0:16],
                       rd=[("PB", 0)], wr=[t("amaj")])

        dbg_tap(K, W, "ffn_amaj", AMAJ[:, :, :], t("amaj"), col0=4096)
        am = AMAJ[:, :, :]
        b3 = lambda ap2: ap2.unsqueeze(2).broadcast_to([128, 32, 16])
        bi = lambda ap2: ap2.unsqueeze(1).broadcast_to([128, 32, 16])
        K.emit("vector", "reduce_sum", SM1[:, :], am, axis=AX.X, rd=[t("amaj")], wr=[t("sm1")])
        K.emit("scalar", "activation", TMPS[:, :, :], am, AF.Square, rd=[t("amaj")], wr=[t("tmps")])
        K.emit("vector", "reduce_sum", SM2[:, :], TMPS[:, :, :], axis=AX.X, rd=[t("tmps")], wr=[t("sm2")])
        K.emit("vector", "tensor_scalar_mul", SM1[:, :], SM1[:, :], 1.0 / 16, rd=[t("sm1")], wr=[t("sm1")])
        K.emit("vector", "tensor_tensor", TMX[:, :], SM1[:, :], SM1[:, :], OP.mult,
               rd=[t("sm1")], wr=[t("tmx")])
        K.emit("vector", "scalar_tensor_tensor", SM2[:, :], SM2[:, :], 1.0 / 16, TMX[:, :],
               OP.mult, OP.subtract, rd=[t("sm2"), t("tmx")], wr=[t("sm2")])
        rsqrt_ip(K, C, SM2[:, :], t("sm2"), C.consts[:, 0:1], extra_scale=0.5)
        K.emit("vector", "tensor_tensor", ZU[:, :, :], am, b3(SM1[:, :]), OP.subtract,
               rd=[t("amaj"), t("sm1")], wr=[t("zu")])
        K.emit("vector", "tensor_tensor", ZU[:, :, :], ZU[:, :, :], b3(SM2[:, :]), OP.mult,
               rd=[t("zu"), t("sm2")], wr=[t("zu")])
        K.emit("vector", "reduce_max", SM1[:, :], ZU[:, :, :], axis=AX.X, rd=[t("zu")], wr=[t("sm1")])
        K.emit("vector", "tensor_tensor", ZU[:, :, :], ZU[:, :, :], b3(SM1[:, :]), OP.subtract,
               rd=[t("zu"), t("sm1")], wr=[t("zu")])
        K.emit("vector", "tensor_copy", ZS[:, :, :], ZU[:, :, :], rd=[t("zu")], wr=[t("zs")])
        for (i, j) in sort_network(16):
            K.emit("vector", "tensor_tensor", TMX[:, :], ZS[:, :, i], ZS[:, :, j], OP.max,
                   rd=[t("zs")], wr=[t("tmx")])
            K.emit("vector", "tensor_tensor", ZS[:, :, j], ZS[:, :, i], ZS[:, :, j], OP.min,
                   rd=[t("zs")], wr=[t("zs")])
            K.emit("vector", "tensor_copy", ZS[:, :, i], TMX[:, :], rd=[t("tmx")], wr=[t("zs")])
        K.emit("vector", "tensor_copy", CS1[:, :, :], ZS[:, :, :], rd=[t("zs")], wr=[t("cs1")])
        K.emit("scalar", "activation", CS2[:, :, :], ZS[:, :, :], AF.Square, rd=[t("zs")], wr=[t("cs2")])
        for CS, kk in ((CS1, "cs1"), (CS2, "cs2")):
            for sh in (1, 2, 4, 8):
                K.emit("vector", "tensor_tensor", TMPS[:, :, sh:16], CS[:, :, sh:16],
                       CS[:, :, 0:16 - sh], OP.add, rd=[t(kk)], wr=[t("tmps")])
                K.emit("vector", "tensor_copy", TMPS[:, :, 0:sh], CS[:, :, 0:sh],
                       rd=[t(kk)], wr=[t("tmps")])
                K.emit("vector", "tensor_copy", CS[:, :, :], TMPS[:, :, :],
                       rd=[t("tmps")], wr=[t(kk)])
        K.emit("vector", "tensor_tensor", CS1[:, :, :], CS1[:, :, :], bi(C.invk[:, :]), OP.mult,
               rd=[t("cs1"), ("invk",)], wr=[t("cs1")])
        K.emit("vector", "tensor_tensor", CS2[:, :, :], CS2[:, :, :], bi(C.invk[:, :]), OP.mult,
               rd=[t("cs2"), ("invk",)], wr=[t("cs2")])
        K.emit("vector", "tensor_tensor", TMPS[:, :, :], CS1[:, :, :], CS1[:, :, :], OP.mult,
               rd=[t("cs1")], wr=[t("tmps")])
        K.emit("vector", "tensor_tensor", CS2[:, :, :], CS2[:, :, :], TMPS[:, :, :], OP.subtract,
               rd=[t("cs2"), t("tmps")], wr=[t("cs2")])
        K.emit("vector", "tensor_tensor", CS2[:, :, :], CS2[:, :, :], bi(C.kval[:, :]), OP.mult,
               rd=[t("cs2"), ("kval",)], wr=[t("cs2")])
        K.emit("vector", "tensor_scalar", CS2[:, :, :], CS2[:, :, :], -1.0, 1.0, OP.mult, OP.add,
               rd=[t("cs2")], wr=[t("cs2")])
        K.emit("vector", "tensor_tensor", CS2[:, :, :], CS2[:, :, :], bi(C.invk[:, :]), OP.mult,
               rd=[t("cs2"), ("invk",)], wr=[t("cs2")])
        K.emit("vector", "tensor_scalar_max", CS2[:, :, :], CS2[:, :, :], 0.0,
               rd=[t("cs2")], wr=[t("cs2")])
        K.emit("scalar", "activation", CS2[:, :, :], CS2[:, :, :], AF.Ln, bias=C.consts[:, 3:4],
               rd=[t("cs2"), ("consts",)], wr=[t("cs2")])
        K.emit("scalar", "activation", CS2[:, :, :], CS2[:, :, :], AF.Exp, scale=0.5,
               rd=[t("cs2")], wr=[t("cs2")])
        K.emit("vector", "tensor_tensor", CS1[:, :, :], CS1[:, :, :], CS2[:, :, :], OP.subtract,
               rd=[t("cs1"), t("cs2")], wr=[t("cs1")])
        K.emit("vector", "tensor_tensor", TMPS[:, :, :], CS1[:, :, :], ZS[:, :, :], OP.is_le,
               rd=[t("cs1"), t("zs")], wr=[t("tmps")])
        K.emit("vector", "reduce_sum", SM1[:, :], TMPS[:, :, :], axis=AX.X, rd=[t("tmps")], wr=[t("sm1")])
        K.emit("vector", "tensor_scalar_add", SM1[:, :], SM1[:, :], -1.0, rd=[t("sm1")], wr=[t("sm1")])
        K.emit("vector", "tensor_tensor", TMPS[:, :, :], bi(C.kidx[:, :]), b3(SM1[:, :]), OP.is_equal,
               rd=[("kidx",), t("sm1")], wr=[t("tmps")])
        K.emit("vector", "tensor_tensor", TMPS[:, :, :], TMPS[:, :, :], CS1[:, :, :], OP.mult,
               rd=[t("tmps"), t("cs1")], wr=[t("tmps")])
        K.emit("vector", "reduce_sum", SM2[:, :], TMPS[:, :, :], axis=AX.X, rd=[t("tmps")], wr=[t("sm2")])
        K.emit("vector", "tensor_tensor", ZU[:, :, :], ZU[:, :, :], b3(SM2[:, :]), OP.subtract,
               rd=[t("zu"), t("sm2")], wr=[t("zu")])
        K.emit("vector", "tensor_scalar_max", ZU[:, :, :], ZU[:, :, :], 0.0, rd=[t("zu")], wr=[t("zu")])
        K.emit("scalar", "activation", ZU[:, :, :], ZU[:, :, :], AF.Square, rd=[t("zu")], wr=[t("zu")])
        dbg_tap(K, W, "ffn_p", ZU[:, :, :], t("zu"), col0=6144)
        for j in range(32):
            K.emit("tensor", "transpose", C.PS1[0:16, 0:128], ZU[:, j, :], C.identf[:, :],
                   rd=[t("zu"), ("identf",)], wr=[("PS1",)])
            K.emit("vector", "tensor_copy", AT[:, j * 128:(j + 1) * 128], C.PS1[0:16, 0:128],
                   rd=[("PS1",)], wr=[t("at")])

        for g in range(8):
            gs = slice(g * 512, (g + 1) * 512)
            kxn = t("xn")
            K.emit("sync", "dma_start", XN[:, :, :], xns[:, :, gs],
                   rd=[("XNS", g)], wr=[kxn], dma=True)
            if g == 0:
                dbg_tap(K, W, "ffn_xnc", XN[:, :, :], kxn, col0=10240)
            if g == 0:
                dbg_tap(K, W, "ffn_at", AT[:, :], t("at"))
            for GB, coff, gk in ((G1, 0, "g1"), (G2, 512, "g2")):
                for m in range(4):
                    K.emit("tensor", "matmul", C.pa(m), wc[:, coff + m * 128:coff + (m + 1) * 128],
                           AT[:, gs], start=True, stop=True, rd=[t("wc"), t("at")], wr=[("PA", m)])
                for m in range(4):
                    K.emit("vector", "tensor_copy", GB[:, m, :], C.pa(m),
                           rd=[("PA", m)], wr=[t(gk)])
            if g == 0:
                dbg_tap(K, W, "ffn_g1", G1[:, :, :], t("g1"), col0=8192)
            for m in range(4):
                pk = ("PB", m % 2)
                pap = C.pb(m % 2)
                for k in range(DT):
                    K.emit("tensor", "matmul", pap, wwin1[:, k, m * 128:(m + 1) * 128],
                           XN[:, k, :], start=(k == 0), stop=(k == DT - 1),
                           rd=[t("wwin1"), kxn], wr=[pk])
                if g == 0 and m == 0 and "ffn_z1raw" in DBG:
                    K.emit("vector", "tensor_copy", XG2[:, 0, :], pap, rd=[pk], wr=[t("xg2")])
                    dbg_tap(K, W, "ffn_z1raw", XG2[:, 0, :], t("xg2"), col0=14336)
                K.emit("vector", "scalar_tensor_tensor", Z1[:, m, :], pap, bwin1[:, m:m + 1],
                       G1[:, m, :], OP.add, OP.mult, rd=[pk, t("bwin1"), t("g1")], wr=[t("z1")])
            for half in range(2):
                for mm_ in range(16):
                    m = half * 16 + mm_
                    pk = ("PA", mm_ % 2)
                    pap = C.pa(mm_ % 2)
                    for k in range(4):
                        K.emit("tensor", "matmul", pap, wwout1[:, k, m * 128:(m + 1) * 128],
                               Z1[:, k, :], start=(k == 0), stop=(k == 3),
                               rd=[t("wwout1"), t("z1")], wr=[pk])
                    K.emit("scalar", "activation", HB[:, mm_, :], pap, AF.Gelu,
                           rd=[pk], wr=[t("hb")])
                for mm_ in range(16):
                    k = half * 16 + mm_
                    for m in range(4):
                        pk = ("PB", m) if m < 2 else ("PA", m)
                        pap = C.pb(m) if m < 2 else C.pa(m)
                        K.emit("tensor", "matmul", pap, wwin2[:, k, m * 128:(m + 1) * 128],
                               HB[:, mm_, :], start=(k == 0), stop=(k == 31),
                               rd=[t("wwin2"), t("hb")], wr=[pk])
            for m in range(4):
                pk = ("PB", m) if m < 2 else ("PA", m)
                pap = C.pb(m) if m < 2 else C.pa(m)
                K.emit("vector", "scalar_tensor_tensor", Z1[:, m, :], pap, bwin2[:, m:m + 1],
                       G2[:, m, :], OP.add, OP.mult, rd=[pk, t("bwin2"), t("g2")], wr=[t("z1")])
            if g == 0:
                dbg_tap(K, W, "ffn_z1", Z1[:, :, :], t("z1"), col0=8192)

            K.emit("sync", "dma_start", XG2[:, :, :], src3[:, :, gs],
                   rd=([srckey] if isinstance(srckey, tuple) else [(srckey, g)]),
                   wr=[t("xg2")], dma=True)
            for m in range(8):
                pk = ("PA", m % 2)
                pap = C.pa(m % 2)
                for k in range(4):
                    K.emit("tensor", "matmul", pap, wwout2[:, k, m * 128:(m + 1) * 128],
                           Z1[:, k, :], start=(k == 0), stop=(k == 3),
                           rd=[t("wwout2"), t("z1")], wr=[pk])
                K.emit("vector", "scalar_tensor_tensor", XG2[:, m, :], pap, 0.5,
                       XG2[:, m, :], OP.mult, OP.add, rd=[pk, t("xg2")], wr=[t("xg2")])
            K.emit("sync", "dma_start", dst3[:, :, gs], XG2[:, :, :],
                   rd=[t("xg2")], wr=[(dstkey, g)], dma=True)
    K.barrier()



def build_ln_pass(K, C, src3, dst3, srckey, dstkey, tag, dst_bf=None, dstbfkey=None,
                  row_cache=None):
    nc = K.nc
    with ExitStack() as ctx:
        NB = NormBufs(nc, ctx, tag)
        XO = ctx.enter_context(nc.sbuf_tensor(f"{tag}_xo", [128, 8, 512], F32))
        XOB = ctx.enter_context(nc.sbuf_tensor(f"{tag}_xob", [128, 8, 512], BF16)) \
            if dst_bf is not None else None
        for g in range(8):
            gs = slice(g * 512, (g + 1) * 512)
            srk = [srckey] if isinstance(srckey, tuple) else [(srckey, g)]
            norm_group(K, C, NB, src3, g,
                       XOB[:, :, :] if XOB is not None else None, "ln",
                       srk, (f"{tag}xob",),
                       out_f32=XO[:, :, :] if dst3 is not None else None,
                       f32key=(f"{tag}xo",), row_cache=row_cache)
            if dst3 is not None:
                K.emit("sync", "dma_start", dst3[:, :, gs], XO[:, :, :],
                       rd=[(f"{tag}xo",)], wr=[(dstkey, g)], dma=True)
            if dst_bf is not None:
                K.emit("sync", "dma_start", dst_bf[:, :, gs], XOB[:, :, :],
                       rd=[(f"{tag}xob",)], wr=[(dstbfkey, g)], dma=True)
    K.barrier()


WEIGHT_SHAPES = {
    "L0_attn_in_proj_w": (1024, 4256),
    "L0_attn_attn_w": (1024, 1152),
    "L0_attn_proj_w": (1024, 1024),
    "L0_attn_recept_w": (1024, 1024),
    "L0_ffn_proj": (1024, 16),
    "L0_ffn_C1": (16, 512),
    "L0_ffn_Win1": (1025, 512),
    "L0_ffn_C2": (16, 512),
    "L0_ffn_Win2": (4097, 512),
}

HOST_SHAPES = {
    "c_map64": (32, 128),
    "c_sel32": (32, 16),
    "h_wout1T": (512, 4096),
    "h_wout2T": (512, 1024),
    "h_win1bias": (128, 4),
    "h_win2bias": (128, 4),
    "h_convw": (128, 68),
    "h_convb": (128, 17),
    "h_mnorm": (128, 16),
    "h_dtb": (32, 1),
    "h_alog": (32, 1),
    "h_dvec": (32, 1),
    "h_maak": (128, 8),
    "h_maar": (128, 8),
    "h_outproj_cm": (8, 2048, 128),
    "h_key_cm": (32, 1024, 128),
    "h_value_cm": (8, 4096, 128),
    "h_wqT": (1024, 1024),
    "h_wkT": (1024, 1024),
    "h_wvT": (1024, 1024),
    "h_woT": (1024, 1024),
    "h_gain": (1, 1),
    "h_routerT": (1024, 8),
    "c_tokids": (128, 32),
    "c_iota128": (128, 1),
    "h_ebias": (8, 1),
    "h_w1": (9, 1024, 2048),
    "h_w3": (9, 1024, 2048),
    "h_w2": (9, 2048, 1024),
}

CONST_SHAPES = {
    "c_identf": (128, 128),
    "c_mtri": (128, 1024),
    "c_kidx": (128, 16),
    "c_invk": (128, 16),
    "c_kval": (128, 16),
}


def build_program(phases=("all",)):
    nc = bass.Bass("TRN2", target_bir_lowering=False)
    dram = {}

    def din(name, shape, dt=F32):
        dram[name] = nc.dram_tensor(name, list(shape), dt, kind="ExternalInput").ap()

    din("x_in", (128, 8, 4096))
    for n, s in WEIGHT_SHAPES.items():
        din(n, s)
    for n, s in HOST_SHAPES.items():
        din(n, s, I32 if n == "c_tokids" else F32)
    for n, s in CONST_SHAPES.items():
        din(n, s)
    din("c_ident", (128, 128), BF16)
    out = nc.dram_tensor("out", [128, 8, 4096], F32, kind="ExternalOutput").ap()
    dram["dbg"] = nc.dram_tensor("dbg", [128, 16384], F32, kind="ExternalOutput").ap()
    scr = {}
    for n in ("XB", "XC", "XD", "XE", "XF", "S0", "S1", "S2", "ACC"):
        scr[n] = nc.dram_tensor(f"scr_{n}", [128, 8, 4096], F32).ap()
    for n in ("SNM", "SNA", "XNS"):
        scr[n] = nc.dram_tensor(f"scr_{n}", [128, 8, 4096], BF16).ap()
    scr["CWS"] = nc.dram_tensor("scr_CWS", [8, 4096], F32).ap()
    scr["IPB"] = nc.dram_tensor("scr_IPB", [34, 128, 8, 128], BF16).ap()
    scr["SNMT"] = nc.dram_tensor("scr_SNMT", [4097, 1024], BF16).ap()
    scr["ACCT"] = nc.dram_tensor("scr_ACCT", [4097, 1024], F32).ap()
    scr["IDXD"] = nc.dram_tensor("scr_IDXD", [MOE_NSB * 512 + 128], I32).ap()
    scr["CWL"] = nc.dram_tensor("scr_CWL", [MOE_NSB * 512 + 128], F32).ap()
    scr["W1B"] = nc.dram_tensor("scr_W1B", [9, 128, 8, 2048], BF16).ap()
    scr["W3B"] = nc.dram_tensor("scr_W3B", [9, 128, 8, 2048], BF16).ap()
    scr["W2B"] = nc.dram_tensor("scr_W2B", [9, 128, 16, 1024], BF16).ap()
    scr["RWS"] = nc.dram_tensor("scr_RWS", [2, 4096], F32).ap()
    scr["WNS"] = nc.dram_tensor("scr_WNS", [4, 128, 8, 1024], F32).ap()

    def has(p):
        return "all" in phases or p in phases

    with ExitStack() as ctx:
        C = Ctx()
        alloc_common(nc, ctx, C)
        K = Kit(nc, ctx)
        load_consts(K, C, dram)
        if MOE_SPARSE and (has("moe1") or has("moe2")):
            for e in range(9):
                K.emit("gpsimd", "dma_start", scr["W1B"][e],
                       dram["h_w1"][e].rearrange("(t p) h -> p t h", p=128),
                       wr=[("WPC", "x")], stream="wpc", dma=True)
                K.emit("gpsimd", "dma_start", scr["W3B"][e],
                       dram["h_w3"][e].rearrange("(t p) h -> p t h", p=128),
                       wr=[("WPC", "y")], stream="wpc", dma=True)
                K.emit("gpsimd", "dma_start", scr["W2B"][e],
                       dram["h_w2"][e].rearrange("(t p) d -> p t d", p=128),
                       wr=[("WPC", "z")], stream="wpc", dma=True)
        if has("att0") and has("mamba"):
            ip = dram["L0_attn_in_proj_w"]
            for m in range(34):
                mw = 128 if m < 33 else 32
                K.emit("gpsimd", "dma_start", scr["IPB"][m][:, :, 0:mw],
                       ip.rearrange("(t p) m -> p t m", p=128)[:, :, m * 128:m * 128 + mw],
                       wr=[("IPB",)], stream="ipb", dma=True)

        cur, curkey = dram["x_in"], ("x_in",)

        def nxt(name):
            return scr[name], name

        if has("ffn0a"):
            d, dk = nxt("XB")
            build_ffn0(K, C, dram, scr, cur, d, curkey, dk, 0)
            cur, curkey = d, dk
        if has("att0"):
            build_ln_pass(K, C, cur, scr["S0"], curkey, "S0", "lnA")
            s, skey = scr["S0"], "S0"
            if has("mamba"):
                build_mamba(K, C, dram, scr, s, scr["S1"], skey, "S1")
                s, skey = scr["S1"], "S1"
            if has("mqa"):
                build_mqa(K, C, dram, s, scr["S2"], skey, "S2")
                s, skey = scr["S2"], "S2"
            d, dk = nxt("XC")
            build_cmix(K, C, dram, s, cur, d, skey, curkey, dk, skip=not has("cmix"))
            cur, curkey = d, dk
        if has("ffn0b"):
            d, dk = nxt("XD")
            build_ffn0(K, C, dram, scr, cur, d, curkey, dk, 1)
            cur, curkey = d, dk
        if has("moe1"):
            d, dk = nxt("XE")
            (build_moe_sparse if MOE_SPARSE else build_moe)(K, C, dram, scr, cur, d, curkey, dk, 0)
            cur, curkey = d, dk
        if has("att1"):
            d, dk = nxt("XF")
            build_att1(K, C, dram, scr, cur, d, curkey, dk)
            cur, curkey = d, dk
        if has("moe2"):
            (build_moe_sparse if MOE_SPARSE else build_moe)(K, C, dram, scr, cur, out, curkey, "out", 1)
            cur, curkey = out, "out"

        if curkey != "out":
            cp = ctx.enter_context(nc.sbuf_tensor("cpbuf", [128, 8, 512], F32))
            for g in range(8):
                gs = slice(g * 512, (g + 1) * 512)
                rdk = [curkey if isinstance(curkey, tuple) else (curkey, g)]
                K.emit("sync", "dma_start", cp[:, :, :], cur[:, :, gs], rd=rdk,
                       wr=[("cp",)], dma=True)
                K.emit("sync", "dma_start", out[:, :, gs], cp[:, :, :], rd=[("cp",)],
                       wr=[("out", g)], dma=True)
        K.finish()
        blk = ctx.enter_context(nc.Block())
        K.replay_into(blk)
    return nc




def build_moe(K, C, W, scr, src3, dst3, srckey, dstkey, callid):
    nc = K.nc
    snm, acc = scr["SNM"], scr["ACC"]
    with ExitStack() as ctx:
        sb = lambda name, shape, dt: ctx.enter_context(nc.sbuf_tensor(f"e{callid}_{name}", shape, dt))
        t = lambda n: (f"e{callid}_{n}",)
        NB = NormBufs(nc, ctx, f"e{callid}")
        CWROW = sb("cwrow", [1, 1024], F32)
        ctxA = ExitStack()
        sbA = lambda name, shape, dt: ctxA.enter_context(nc.sbuf_tensor(f"e{callid}_{name}", shape, dt))
        XN = sbA("xn", [128, 8, 512], BF16)
        CWT = sbA("cwt", [8, 4096], F32)
        LGR = sbA("lgr", [8, 512], F32)
        LGT = sbA("lgt", [128, 32, 8], F32)
        SM1 = sbA("sm1", [128, 32], F32)
        SM2 = sbA("sm2", [128, 32], F32)
        MK1 = sbA("mk1", [128, 32, 8], F32)
        MK2 = sbA("mk2", [128, 32, 8], F32)
        wrt = sbA("wrt", [128, 8, 8], BF16)
        ebv = sbA("ebv", [8, 1], F32)

        K.emit("gpsimd", "dma_start", wrt[:, :, :],
               W["h_routerT"].rearrange("(t p) e -> p t e", p=128), wr=[t("wrt")], dma=True)
        K.emit("gpsimd", "dma_start", WX[:, :, :],
               W["h_w1"][8].rearrange("(t p) h -> p t h", p=128), wr=[t("wx")],
               stream="wx", dma=True)
        K.emit("gpsimd", "dma_start", WY[:, :, :],
               W["h_w3"][8].rearrange("(t p) h -> p t h", p=128), wr=[t("wy")],
               stream="wy", dma=True)
        K.emit("gpsimd", "dma_start", WZ[:, :, :],
               W["h_w2"][8].rearrange("(t p) d -> p t d", p=128), wr=[t("wz")],
               stream="wz", dma=True)
        K.emit("sync", "dma_start", ebv[:, :], W["h_ebias"], wr=[t("ebv")], dma=True)

        for g in range(8):
            gs = slice(g * 512, (g + 1) * 512)
            srk = [srckey] if isinstance(srckey, tuple) else [(srckey, g)]
            norm_group(K, C, NB, src3, g, XN[:, :, :], "ln", srk, t("xn"))
            K.emit("sync", "dma_start", snm[:, :, gs], XN[:, :, :],
                   rd=[t("xn")], wr=[("SNM", g)], dma=True)
            for k in range(DT):
                K.emit("tensor", "matmul", C.PS1[0:8, :], wrt[:, k, :], XN[:, k, :],
                       start=(k == 0), stop=(k == DT - 1),
                       rd=[t("wrt"), t("xn")], wr=[("PS1",)])
            K.emit("vector", "tensor_scalar_add", LGR[:, :], C.PS1[0:8, :], ebv[:, 0:1],
                   rd=[("PS1",), t("ebv")], wr=[t("lgr")])
            for j in range(4):
                K.emit("tensor", "transpose", C.pb(0)[0:128, 0:8],
                       LGR[:, j * 128:(j + 1) * 128], C.identf[0:8, 0:8],
                       rd=[t("lgr"), ("identf",)], wr=[("PB", 0)])
                K.emit("vector", "tensor_copy", LGT[:, g * 4 + j, :], C.pb(0)[0:128, 0:8],
                       rd=[("PB", 0)], wr=[t("lgt")])
        b3 = lambda ap2: ap2.unsqueeze(2).broadcast_to([128, 32, 8])
        K.emit("vector", "reduce_max", SM1[:, :], LGT[:, :, :], axis=AX.X,
               rd=[t("lgt")], wr=[t("sm1")])
        K.emit("vector", "tensor_tensor", MK1[:, :, :], LGT[:, :, :], b3(SM1[:, :]),
               OP.is_equal, rd=[t("lgt"), t("sm1")], wr=[t("mk1")])
        K.emit("vector", "scalar_tensor_tensor", MK2[:, :, :], MK1[:, :, :], NEG,
               LGT[:, :, :], OP.mult, OP.add, rd=[t("mk1"), t("lgt")], wr=[t("mk2")])
        K.emit("vector", "reduce_max", SM2[:, :], MK2[:, :, :], axis=AX.X,
               rd=[t("mk2")], wr=[t("sm2")])
        K.emit("vector", "tensor_tensor", MK2[:, :, :], MK2[:, :, :], b3(SM2[:, :]),
               OP.is_equal, rd=[t("mk2"), t("sm2")], wr=[t("mk2")])
        K.emit("vector", "tensor_tensor", SM1[:, :], SM1[:, :], SM2[:, :], OP.subtract,
               rd=[t("sm1"), t("sm2")], wr=[t("sm1")])
        K.emit("scalar", "activation", SM1[:, :], SM1[:, :], AF.Sigmoid,
               rd=[t("sm1")], wr=[t("sm1")])
        K.emit("vector", "tensor_scalar", SM2[:, :], SM1[:, :], -1.0, 1.0, OP.mult, OP.add,
               rd=[t("sm1")], wr=[t("sm2")])
        K.emit("vector", "tensor_tensor", MK1[:, :, :], MK1[:, :, :], b3(SM1[:, :]), OP.mult,
               rd=[t("mk1"), t("sm1")], wr=[t("mk1")])
        K.emit("vector", "tensor_tensor", MK2[:, :, :], MK2[:, :, :], b3(SM2[:, :]), OP.mult,
               rd=[t("mk2"), t("sm2")], wr=[t("mk2")])
        K.emit("vector", "tensor_tensor", MK1[:, :, :], MK1[:, :, :], MK2[:, :, :], OP.add,
               rd=[t("mk1"), t("mk2")], wr=[t("mk1")])
        dbg_tap(K, W, "moe_cwtok", MK1[:, :, :], t("mk1"), col0=8192)
        for j in range(32):
            K.emit("tensor", "transpose", C.pb(0)[0:8, 0:128], MK1[:, j, :], C.identf[:, :],
                   rd=[t("mk1"), ("identf",)], wr=[("PB", 0)])
            K.emit("vector", "tensor_copy", CWT[:, j * 128:(j + 1) * 128], C.pb(0)[0:8, 0:128],
                   rd=[("PB", 0)], wr=[t("cwt")])
        K.emit("sync", "dma_start", scr["CWS"], CWT[:, :], rd=[t("cwt")], wr=[("CWS",)], dma=True)

        ctxA.close()
        K.barrier()
        ctxB = ExitStack()
        sbB = lambda name, shape, dt: ctxB.enter_context(nc.sbuf_tensor(f"e{callid}_{name}", shape, dt))
        WX = sbB("wx", [128, 8, 2048], BF16)
        WY = sbB("wy", [128, 8, 2048], BF16)
        WZ = sbB("wz", [128, 16, 1024], BF16)
        XNC = sbB("xnc", [128, 8, 1024], BF16)
        EO = sbB("eo", [128, 16, 512], BF16)
        CWB = sbB("cwb", [128, 512], BF16)
        W2E = sbB("w2e", [128, 4, 512], F32)
        TM1 = sbB("tm1", [128, 2, 512], BF16)
        TM2 = sbB("tm2", [128, 2, 512], BF16)
        for e in range(9):
            K.emit("gpsimd", "dma_start", WZ[:, :, :],
                   W["h_w2"][e].rearrange("(t p) d -> p t d", p=128),
                   wr=[t("wz")], stream="wz", dma=True)
            if e == 0:
                K.emit("gpsimd", "dma_start", WX[:, :, :],
                       W["h_w1"][0].rearrange("(t p) h -> p t h", p=128),
                       wr=[t("wx")], stream="wx", dma=True)
                K.emit("gpsimd", "dma_start", WY[:, :, :],
                       W["h_w3"][0].rearrange("(t p) h -> p t h", p=128),
                       wr=[t("wy")], stream="wy", dma=True)
            for ch in range(4):
                cs = slice(ch * 1024, (ch + 1) * 1024)
                K.emit("sync", "dma_start", XNC[:, :, :], snm[:, :, cs],
                       rd=[("SNM", 2 * ch), ("SNM", 2 * ch + 1)], wr=[t("xnc")],
                       stream="xnc", dma=True)
                if e < 8:
                    K.emit("sync", "dma_start", CWROW[:, :], scr["CWS"][e:e + 1, cs],
                           rd=[("CWS",)], wr=[t("cwrow")], stream="cwrow", dma=True)
                for n2 in range(2):
                    ns = slice(n2 * 512, (n2 + 1) * 512)
                    if e < 8:
                        K.emit("tensor", "matmul", C.pb(0), C.onesf[0:1, 0:128],
                               CWROW[0:1, ns],
                               start=True, stop=True,
                               rd=[("onesf",), t("cwrow")], wr=[("PB", 0)])
                        K.emit("vector", "tensor_copy", CWB[:, :], C.pb(0),
                               rd=[("PB", 0)], wr=[t("cwb")])
                    elif ch == 0 and n2 == 0:
                        K.emit("vector", "memset", CWB[:, :], 1.0, wr=[t("cwb")])
                    for ht in range(16):
                        pg, pu = (0, 1) if ht % 2 == 0 else (2, 3)
                        par = ht % 2
                        for k in range(DT):
                            K.emit("tensor", "matmul", C.pa(pg), WX[:, k, ht * 128:(ht + 1) * 128],
                                   XNC[:, k, ns], start=(k == 0), stop=(k == DT - 1),
                                   rd=[t("wx"), t("xnc")], wr=[("PA", pg)])
                        for k in range(DT):
                            K.emit("tensor", "matmul", C.pa(pu), WY[:, k, ht * 128:(ht + 1) * 128],
                                   XNC[:, k, ns], start=(k == 0), stop=(k == DT - 1),
                                   rd=[t("wy"), t("xnc")], wr=[("PA", pu)])
                        K.emit("scalar", "activation", TM1[:, par, :], C.pa(pg), AF.Silu,
                               rd=[("PA", pg)], wr=[t("tm1")])
                        K.emit("vector", "tensor_tensor", TM2[:, par, :], C.pa(pu), CWB[:, :],
                               OP.mult, rd=[("PA", pu), t("cwb")], wr=[t("tm2")])
                        K.emit("vector", "tensor_tensor", EO[:, ht, :], TM1[:, par, :],
                               TM2[:, par, :], OP.mult, rd=[t("tm1"), t("tm2")], wr=[t("eo")])
                    if ch == 3 and n2 == 1 and e < 8:
                        K.emit("gpsimd", "dma_start", WX[:, :, :],
                               W["h_w1"][e + 1].rearrange("(t p) h -> p t h", p=128),
                               wr=[t("wx")], stream="wx", dma=True)
                        K.emit("gpsimd", "dma_start", WY[:, :, :],
                               W["h_w3"][e + 1].rearrange("(t p) h -> p t h", p=128),
                               wr=[t("wy")], stream="wy", dma=True)
                    for half in range(2):
                        for mm_ in range(4):
                            m = half * 4 + mm_
                            pk = ("PA", 2 + mm_ % 2)
                            pap = C.pa(2 + mm_ % 2)
                            for k in range(16):
                                K.emit("tensor", "matmul", pap, WZ[:, k, m * 128:(m + 1) * 128],
                                       EO[:, k, :], start=(k == 0), stop=(k == 15),
                                       rd=[t("wz"), t("eo")], wr=[pk])
                            K.emit("vector", "tensor_copy", W2E[:, mm_, :], pap,
                                   rd=[pk], wr=[t("w2e")])
                        K.emit("gpsimd", "dma_start", acc[:, half * 4:(half + 1) * 4, cs][:, :, ns],
                               W2E[:, :, :], rd=[t("w2e")], wr=[("ACC", ch, n2, half)],
                               stream="acc", dma=True,
                               **({} if e == 0 else {"accum_op": OP.add}))
        ctxB.close()
        K.barrier()
        XG2 = sb("xg2", [128, 8, 512], F32)
        for g in range(8):
            gs = slice(g * 512, (g + 1) * 512)
            srk = [srckey] if isinstance(srckey, tuple) else [(srckey, g)]
            K.emit("sync", "dma_start", XG2[:, :, :], src3[:, :, gs], rd=srk,
                   wr=[t("xg2")], stream="xg2", dma=True)
            kxg = (f"e{callid}xg",)
            K.emit("sync", "dma_start", NB.xg[:, :, :], acc[:, :, gs],
                   rd=[("ACC", g // 2, g % 2, 0), ("ACC", g // 2, g % 2, 1)], wr=[kxg],
                   stream="accld", dma=True)
            K.emit("vector", "scalar_tensor_tensor", XG2[:, :, :], NB.xg[:, :, :], 0.5,
                   XG2[:, :, :], OP.mult, OP.add,
                   rd=[kxg, t("xg2")], wr=[t("xg2")])
            K.emit("sync", "dma_start", dst3[:, :, gs], XG2[:, :, :],
                   rd=[t("xg2")], wr=[(dstkey, g)], dma=True)
    K.barrier()




def build_moe_sparse(K, C, W, scr, src3, dst3, srckey, dstkey, callid):
    nc = K.nc
    snm, snmt = scr["SNM"], scr["SNMT"]
    acct, idxd, cwl = scr["ACCT"], scr["IDXD"], scr["CWL"]
    w1b, w3b, w2b = scr["W1B"], scr["W3B"], scr["W2B"]
    NSB = MOE_NSB
    DUMP = NSB * 512
    with ExitStack() as ctx:
        sb = lambda name, shape, dt: ctx.enter_context(nc.sbuf_tensor(f"e{callid}_{name}", shape, dt))
        t = lambda n: (f"e{callid}_{n}",)
        OWNF = sb("ownf", [1, NSB], F32)
        WX = sb("wx", [128, 8, 2048], BF16)
        WY = sb("wy", [128, 8, 2048], BF16)
        WZ = sb("wz", [128, 16, 1024], BF16)
        ctxA = ExitStack()
        sbA = lambda name, shape, dt: ctxA.enter_context(nc.sbuf_tensor(f"e{callid}_{name}", shape, dt))
        NB = NormBufs(nc, ctxA, f"e{callid}")
        XN = sbA("xn", [128, 8, 512], BF16)
        TMJ = sbA("tmj", [128, 4, 8, 128], BF16)
        CWT = sbA("cwt", [8, 4096], F32)
        CUM = sbA("cum", [8, 4096], F32)
        LGR = sbA("lgr", [8, 512], F32)
        LGT = sbA("lgt", [128, 32, 8], F32)
        SM1 = sbA("sm1", [128, 32], F32)
        SM2 = sbA("sm2", [128, 32], F32)
        MK1 = sbA("mk1", [128, 32, 8], F32)
        MK2 = sbA("mk2", [128, 32, 8], F32)
        CWM = sbA("cwm", [128, 32, 8], F32)
        PTOK = sbA("ptok", [128, 32, 8], F32)
        P12 = sbA("p12", [128, 32, 2], F32)
        IDXI2 = sbA("idxi2", [128, 32, 2], I32)
        wrt = sbA("wrt", [128, 8, 8], BF16)
        ebv = sbA("ebv", [8, 1], F32)
        TOKI = sbA("toki", [128, 32], I32)
        ZB = sbA("zbz", [128, 128], I32)
        ZBF = sbA("zbf", [128, 128], F32)
        ZROW = sbA("zrow", [1, 1024], BF16)
        ZROWF = sbA("zrowf", [1, 1024], F32)
        CNTR = sbA("cntr", [1, 8], F32)
        CNTI = sbA("cnti", [1, 8], I32)
        T8 = sbA("t8", [1, 8], F32)
        OFFC = sbA("offc", [8, 1], F32)

        K.emit("gpsimd", "dma_start", wrt[:, :, :],
               W["h_routerT"].rearrange("(t p) e -> p t e", p=128), wr=[t("wrt")], dma=True)
        K.emit("gpsimd", "dma_start", WX[:, :, :],
               W["h_w1"][8].rearrange("(t p) h -> p t h", p=128), wr=[t("wx")],
               stream="wx", dma=True)
        K.emit("gpsimd", "dma_start", WY[:, :, :],
               W["h_w3"][8].rearrange("(t p) h -> p t h", p=128), wr=[t("wy")],
               stream="wy", dma=True)
        K.emit("gpsimd", "dma_start", WZ[:, :, :],
               W["h_w2"][8].rearrange("(t p) d -> p t d", p=128), wr=[t("wz")],
               stream="wz", dma=True)
        K.emit("sync", "dma_start", ebv[:, :], W["h_ebias"], wr=[t("ebv")], dma=True)
        K.emit("sync", "dma_start", TOKI[:, :], W["c_tokids"], wr=[t("toki")], dma=True)
        K.emit("vector", "memset", ZB[:, :], 4096, wr=[t("zb")])
        K.emit("vector", "memset", ZBF[:, :], 0.0, wr=[t("zbf")])
        K.emit("vector", "memset", ZROW[:, :], 0.0, wr=[t("zrow")])
        K.emit("vector", "memset", ZROWF[:, :], 0.0, wr=[t("zrowf")])
        K.emit("sync", "dma_start", snmt[4096:4097, :], ZROW[:, :], rd=[t("zrow")],
               wr=[("SNMTZ",)], stream="snmtz", dma=True)
        K.emit("sync", "dma_start", acct[4096:4097, :], ZROWF[:, :], rd=[t("zrowf")],
               wr=[("ACCTZ",)], stream="snmtz", dma=True)
        K.emit("gpsimd", "dma_start", idxd.rearrange("(p f) -> p f", p=128),
               ZB[:, 0:(NSB * 512 + 128) // 128], rd=[t("zb")], wr=[("IDXD",)],
               stream="idxz", dma=True)
        K.emit("gpsimd", "dma_start", cwl.rearrange("(p f) -> p f", p=128),
               ZBF[:, 0:(NSB * 512 + 128) // 128], rd=[t("zbf")], wr=[("CWL",)],
               stream="idxz", dma=True)
        for g in range(8):
            gs = slice(g * 512, (g + 1) * 512)
            srk = [srckey] if isinstance(srckey, tuple) else [(srckey, g)]
            norm_group(K, C, NB, src3, g, XN[:, :, :], "ln", srk, t("xn"))
            K.emit("sync", "dma_start", snm[:, :, gs], XN[:, :, :],
                   rd=[t("xn")], wr=[("SNM", g)], dma=True)
            for j in range(4):
                for kd in range(DT):
                    K.emit("tensor", "transpose", C.PT[0:128, 0:128],
                           XN[:, kd, j * 128:(j + 1) * 128], C.ident[:, :],
                           rd=[t("xn"), ("ident",)], wr=[("PT",)])
                    K.emit("vector", "tensor_copy", TMJ[:, j, kd, :], C.PT[0:128, 0:128],
                           rd=[("PT",)], wr=[t("tmj")])
            K.emit("sync", "dma_start",
                   snmt[g * 512:(g + 1) * 512, :].rearrange("(j p) (kd c) -> p j kd c",
                                                            p=128, c=128),
                   TMJ[:, :, :, :], rd=[t("tmj")], wr=[("SNMT", g)], dma=True)
            for k in range(DT):
                K.emit("tensor", "matmul", C.PS1[0:8, :], wrt[:, k, :], XN[:, k, :],
                       start=(k == 0), stop=(k == DT - 1),
                       rd=[t("wrt"), t("xn")], wr=[("PS1",)])
            K.emit("vector", "tensor_scalar_add", LGR[:, :], C.PS1[0:8, :], ebv[:, 0:1],
                   rd=[("PS1",), t("ebv")], wr=[t("lgr")])
            for j in range(4):
                K.emit("tensor", "transpose", C.pb(0)[0:128, 0:8],
                       LGR[:, j * 128:(j + 1) * 128], C.identf[0:8, 0:8],
                       rd=[t("lgr"), ("identf",)], wr=[("PB", 0)])
                K.emit("vector", "tensor_copy", LGT[:, g * 4 + j, :], C.pb(0)[0:128, 0:8],
                       rd=[("PB", 0)], wr=[t("lgt")])
        b3 = lambda ap2: ap2.unsqueeze(2).broadcast_to([128, 32, 8])
        K.emit("vector", "reduce_max", SM1[:, :], LGT[:, :, :], axis=AX.X,
               rd=[t("lgt")], wr=[t("sm1")])
        K.emit("vector", "tensor_tensor", MK1[:, :, :], LGT[:, :, :], b3(SM1[:, :]),
               OP.is_equal, rd=[t("lgt"), t("sm1")], wr=[t("mk1")])
        K.emit("vector", "scalar_tensor_tensor", MK2[:, :, :], MK1[:, :, :], NEG,
               LGT[:, :, :], OP.mult, OP.add, rd=[t("mk1"), t("lgt")], wr=[t("mk2")])
        K.emit("vector", "reduce_max", SM2[:, :], MK2[:, :, :], axis=AX.X,
               rd=[t("mk2")], wr=[t("sm2")])
        K.emit("vector", "tensor_tensor", MK2[:, :, :], MK2[:, :, :], b3(SM2[:, :]),
               OP.is_equal, rd=[t("mk2"), t("sm2")], wr=[t("mk2")])
        K.emit("vector", "tensor_tensor", SM1[:, :], SM1[:, :], SM2[:, :], OP.subtract,
               rd=[t("sm1"), t("sm2")], wr=[t("sm1")])
        K.emit("scalar", "activation", SM1[:, :], SM1[:, :], AF.Sigmoid,
               rd=[t("sm1")], wr=[t("sm1")])
        K.emit("vector", "tensor_scalar", SM2[:, :], SM1[:, :], -1.0, 1.0, OP.mult, OP.add,
               rd=[t("sm1")], wr=[t("sm2")])
        K.emit("vector", "tensor_tensor", CWM[:, :, :], MK1[:, :, :], MK2[:, :, :], OP.add,
               rd=[t("mk1"), t("mk2")], wr=[t("cwm")])
        for j in range(32):
            K.emit("tensor", "transpose", C.pb(0)[0:8, 0:128], CWM[:, j, :], C.identf[:, :],
                   rd=[t("cwm"), ("identf",)], wr=[("PB", 0)])
            K.emit("vector", "tensor_copy", CWT[:, j * 128:(j + 1) * 128], C.pb(0)[0:8, 0:128],
                   rd=[("PB", 0)], wr=[t("cwt")])
        K.emit("vector", "tensor_single_scalar", CWT[:, :], CWT[:, :], 0.5, OP.is_gt,
               rd=[t("cwt")], wr=[t("cwt")])
        K.emit("vector", "tensor_tensor_scan", CUM[:, :],
               C.onesf[0:8, 0:1].broadcast_to([8, 4096]), CWT[:, :], 0.0, OP.mult, OP.add,
               rd=[("onesf",), t("cwt")], wr=[t("cum")])
        K.emit("tensor", "transpose", C.PS1[0:1, 0:8], CUM[:, 4095:4096],
               C.identf[0:8, 0:8], rd=[t("cum"), ("identf",)], wr=[("PS1",)])
        K.emit("vector", "tensor_scalar_add", CNTR[:, :], C.PS1[0:1, 0:8], 511.0,
               rd=[("PS1",)], wr=[t("cntr")])
        K.emit("vector", "tensor_copy", CNTI[:, :], CNTR[:, :], rd=[t("cntr")], wr=[t("cnti")])
        K.emit("vector", "tensor_single_scalar", CNTI[:, :], CNTI[:, :], -512,
               OP.bitwise_and, rd=[t("cnti")], wr=[t("cnti")])
        K.emit("vector", "tensor_copy", CNTR[:, :], CNTI[:, :], rd=[t("cnti")], wr=[t("cntr")])
        K.emit("vector", "tensor_tensor_scan", T8[:, :],
               C.onesf[0:1, 0:1].broadcast_to([1, 8]), CNTR[:, :], 0.0, OP.mult, OP.add,
               rd=[("onesf",), t("cntr")], wr=[t("t8")])
        K.emit("vector", "tensor_tensor", T8[:, :], T8[:, :], CNTR[:, :], OP.subtract,
               rd=[t("t8"), t("cntr")], wr=[t("t8")])
        for sbk in range(NSB):
            K.emit("vector", "tensor_single_scalar", CNTR[:, :], T8[:, :],
                   float(sbk * 512), OP.is_le, rd=[t("t8")], wr=[t("cntr")])
            K.emit("vector", "reduce_sum", OWNF[0:1, sbk:sbk + 1], CNTR[:, :], axis=AX.X,
                   rd=[t("cntr")], wr=[t("ownf")])
        K.emit("vector", "tensor_scalar_add", OWNF[:, :], OWNF[:, :], -1.0,
               rd=[t("ownf")], wr=[t("ownf")])
        K.emit("tensor", "transpose", C.PS1[0:8, 0:1], T8[:, :], C.identf[0:1, 0:1],
               rd=[t("t8"), ("identf",)], wr=[("PS1",)])
        K.emit("vector", "tensor_copy", OFFC[:, :], C.PS1[0:8, 0:1], rd=[("PS1",)],
               wr=[t("offc")])
        K.emit("vector", "tensor_scalar_add", CUM[:, :], CUM[:, :], -1.0,
               rd=[t("cum")], wr=[t("cum")])
        K.emit("vector", "tensor_scalar_add", CUM[:, :], CUM[:, :], OFFC[:, 0:1],
               rd=[t("cum"), t("offc")], wr=[t("cum")])
        for j in range(32):
            K.emit("tensor", "transpose", C.pb(0)[0:128, 0:8],
                   CUM[:, j * 128:(j + 1) * 128], C.identf[0:8, 0:8],
                   rd=[t("cum"), ("identf",)], wr=[("PB", 0)])
            K.emit("vector", "tensor_copy", PTOK[:, j, :], C.pb(0)[0:128, 0:8],
                   rd=[("PB", 0)], wr=[t("ptok")])
        K.emit("vector", "tensor_scalar_min", PTOK[:, :, :], PTOK[:, :, :],
               float(DUMP), rd=[t("ptok")], wr=[t("ptok")])
        K.emit("vector", "tensor_tensor", MK1[:, :, :], PTOK[:, :, :], MK1[:, :, :], OP.mult,
               rd=[t("ptok"), t("mk1")], wr=[t("mk1")])
        K.emit("vector", "reduce_sum", P12[:, :, 0], MK1[:, :, :], axis=AX.X,
               rd=[t("mk1")], wr=[t("p12")])
        K.emit("vector", "tensor_tensor", MK2[:, :, :], PTOK[:, :, :], MK2[:, :, :], OP.mult,
               rd=[t("ptok"), t("mk2")], wr=[t("mk2")])
        K.emit("vector", "reduce_sum", P12[:, :, 1], MK2[:, :, :], axis=AX.X,
               rd=[t("mk2")], wr=[t("p12")])
        K.emit("vector", "tensor_copy", IDXI2[:, :, :], P12[:, :, :],
               rd=[t("p12")], wr=[t("idxi")])
        for sel in range(2):
            SMv, smk = (SM1, t("sm1")) if sel == 0 else (SM2, t("sm2"))
            for blk in range(32):
                K.emit("gpsimd", "indirect_dma_start", idxd.unsqueeze(0),
                       IndirectOffsetOnAxis(ap=IDXI2[:, blk, sel:sel + 1], axis=1),
                       TOKI[:, blk:blk + 1], None,
                       rd=[t("idxi"), t("toki")], wr=[("IDXD",)],
                       stream="idxsc", dma=True)
                K.emit("gpsimd", "indirect_dma_start", cwl.unsqueeze(0),
                       IndirectOffsetOnAxis(ap=IDXI2[:, blk, sel:sel + 1], axis=1),
                       SMv[:, blk:blk + 1], None,
                       rd=[t("idxi"), smk], wr=[("CWL",)],
                       stream="idxsc", dma=True)
        ctxA.close()
        K.barrier()
        ctxB = ExitStack()
        sbB = lambda name, shape, dt: ctxB.enter_context(nc.sbuf_tensor(f"e{callid}_{name}", shape, dt))
        XGE = [sbB(f"xge{i}", [128, 8, 512], BF16) for i in range(2)]
        GBs = [sbB(f"gb{i}", [128, 1024], BF16) for i in range(4)]
        IDXS = [sbB(f"idxs{i}", [128, 4], I32) for i in range(2)]
        IDXW = sbB("idxw", [128, NSB], I32)
        IOTAF = sbB("iotaf", [128, 1], F32)
        OWNB = sbB("ownb", [128, 1], F32)
        EO = sbB("eo", [128, 16, 512], BF16)
        CWP = sbB("cwp", [128, 4], F32)
        OUTC = sbB("outc", [128, 8, 512], F32)
        OSCJ2 = [sbB(f"oscj{i}", [128, 1024], F32) for i in range(2)]
        TM1 = sbB("tm1", [128, 2, 512], BF16)
        K.emit("sync", "dma_start", IOTAF[:, :], W["c_iota128"], wr=[t("iotaf")], dma=True)
        for sbr in range(NSB):
            K.emit("tensor", "matmul", C.PS1[0:128, 0:1], C.onesf[0:1, 0:128],
                   OWNF[0:1, sbr:sbr + 1], start=True, stop=True,
                   rd=[("onesf",), t("ownf")], wr=[("PS1",)])
            K.emit("vector", "scalar_tensor_tensor", OWNB[:, 0:1],
                   C.PS1[0:128, 0:1], 128.0, IOTAF[:, 0:1], OP.mult, OP.add,
                   rd=[("PS1",), t("iotaf")], wr=[t("ownb")])
            K.emit("vector", "tensor_copy", IDXW[:, sbr:sbr + 1], OWNB[:, 0:1],
                   rd=[t("ownb")], wr=[t("idxw")])

        def w_load(sbr, which):
            tgt, kdst, wb = {"x": (WX, t("wx"), w1b), "y": (WY, t("wy"), w3b),
                             "z": (WZ, t("wz"), w2b)}[which]
            K.emit("gpsimd", "indirect_dma_start",
                   tgt.rearrange("p a b -> p (a b)"), None,
                   wb.rearrange("e p a b -> (e p) (a b)"),
                   IndirectOffsetOnAxis(ap=IDXW[:, sbr:sbr + 1], axis=0),
                   rd=[t("idxw"), ("WPC", which)], wr=[kdst], stream=f"w{which}", dma=True)

        def gather_tokens(sbr, par):
            kx = t(f"xge{par}")
            ki = t(f"idxs{par}")
            K.emit("sync", "dma_start", IDXS[par][:, :],
                   idxd[sbr * 512:(sbr + 1) * 512].rearrange("(p b) -> p b", b=4),
                   rd=[("IDXD",)], wr=[ki], stream=f"idxs{par}", dma=True)
            for blk in range(4):
                K.emit("gpsimd", "indirect_dma_start", GBs[blk][:, :], None, snmt,
                       IndirectOffsetOnAxis(ap=IDXS[par][:, blk:blk + 1], axis=0),
                       rd=[ki, ("SNMTZ",)] + [("SNMT", gg) for gg in range(8)],
                       wr=[t(f"gb{blk}")], stream="gath", dma=True)
            for blk in range(4):
                for kd in range(DT):
                    K.emit("tensor", "transpose", C.PT[0:128, 0:128],
                           GBs[blk][:, kd * 128:(kd + 1) * 128], C.ident[:, :],
                           rd=[t(f"gb{blk}"), ("ident",)], wr=[("PT",)])
                    K.emit("vector", "tensor_copy", XGE[par][:, kd, blk * 128:(blk + 1) * 128],
                           C.PT[0:128, 0:128], rd=[("PT",)], wr=[kx])

        for sbk in range(8 + NSB):
            shared = sbk < 8
            sbr = sbk - 8
            par = sbr % 2
            if shared:
                K.emit("sync", "dma_start", XGE[sbk % 2][:, :, :],
                       snm[:, :, sbk * 512:(sbk + 1) * 512], rd=[("SNM", sbk)],
                       wr=[t(f"xge{sbk % 2}")], stream=f"xnc{sbk % 2}", dma=True)
                rhk, rhs_buf = t(f"xge{sbk % 2}"), XGE[sbk % 2]
            else:
                rhk, rhs_buf = t(f"xge{par}"), XGE[par]
                K.emit("sync", "dma_start", CWP[:, :],
                       cwl[sbr * 512:(sbr + 1) * 512].rearrange("(p b) -> p b", b=4),
                       rd=[("CWL",)], wr=[t("cwp")], stream="cwrw", dma=True)
            for ht in range(16):
                pg, pu = (0, 1) if ht % 2 == 0 else (2, 3)
                pr = ht % 2
                for k in range(DT):
                    K.emit("tensor", "matmul", C.pa(pg), WX[:, k, ht * 128:(ht + 1) * 128],
                           rhs_buf[:, k, :], start=(k == 0), stop=(k == DT - 1),
                           rd=[t("wx"), rhk], wr=[("PA", pg)])
                for k in range(DT):
                    K.emit("tensor", "matmul", C.pa(pu), WY[:, k, ht * 128:(ht + 1) * 128],
                           rhs_buf[:, k, :], start=(k == 0), stop=(k == DT - 1),
                           rd=[t("wy"), rhk], wr=[("PA", pu)])
                K.emit("scalar", "activation", TM1[:, pr, :], C.pa(pg), AF.Silu,
                       rd=[("PA", pg)], wr=[t("tm1")])
                K.emit("vector", "tensor_tensor", EO[:, ht, :], TM1[:, pr, :],
                       C.pa(pu), OP.mult, rd=[t("tm1"), ("PA", pu)], wr=[t("eo")])
            nxt = 0 if shared and sbk == 7 else (sbr + 1 if not shared else None)
            if nxt is not None and nxt < NSB and (not shared or sbk == 7):
                w_load(nxt, "x")
                w_load(nxt, "y")
                gather_tokens(nxt, nxt % 2)
            for m in range(DT):
                pk = ("PA", 2 + m % 2)
                pap = C.pa(2 + m % 2)
                for k in range(16):
                    K.emit("tensor", "matmul", pap, WZ[:, k, m * 128:(m + 1) * 128],
                           EO[:, k, :], start=(k == 0), stop=(k == 15),
                           rd=[t("wz"), t("eo")], wr=[pk])
                K.emit("vector", "tensor_copy", OUTC[:, m, :], pap, rd=[pk], wr=[t("outc")])
            if nxt is not None and nxt < NSB and (not shared or sbk == 7):
                w_load(nxt, "z")
            for j in range(4):
                OSCJ, koscj = OSCJ2[j % 2], t(f"oscj{j % 2}")
                for m in range(DT):
                    K.emit("tensor", "transpose", C.PS1[0:128, 0:128],
                           OUTC[:, m, j * 128:(j + 1) * 128], C.identf[:, :],
                           rd=[t("outc"), ("identf",)], wr=[("PS1",)])
                    if shared:
                        K.emit("vector", "tensor_copy", OSCJ[:, m * 128:(m + 1) * 128],
                               C.PS1[0:128, 0:128], rd=[("PS1",)], wr=[koscj])
                    else:
                        K.emit("vector", "tensor_scalar_mul", OSCJ[:, m * 128:(m + 1) * 128],
                               C.PS1[0:128, 0:128], CWP[:, j:j + 1],
                               rd=[("PS1",), t("cwp")], wr=[koscj])
                if shared:
                    row0 = sbk * 512 + j * 128
                    K.emit("gpsimd", "dma_start", acct[row0:row0 + 128, :],
                           OSCJ[:, :], rd=[koscj], wr=[("ACCTS",)],
                           stream="acct", dma=True)
                else:
                    K.emit("gpsimd", "indirect_dma_start", acct,
                           IndirectOffsetOnAxis(ap=IDXS[par][:, j:j + 1], axis=0),
                           OSCJ[:, :], None, compute_op=OP.add,
                           rd=[koscj, t(f"idxs{par}"), ("ACCTZ",)],
                           wr=[("ACCTS",)], stream="acct", dma=True)
        ctxB.close()
        K.barrier()
        ctxC = ExitStack()
        sbC = lambda name, shape, dt: ctxC.enter_context(nc.sbuf_tensor(f"e{callid}_{name}", shape, dt))
        XG2 = sbC("xg2", [128, 8, 512], F32)
        AIN = sbC("ain", [128, 4, 1024], F32)
        for g in range(8):
            gs = slice(g * 512, (g + 1) * 512)
            srk = [srckey] if isinstance(srckey, tuple) else [(srckey, g)]
            K.emit("sync", "dma_start", XG2[:, :, :], src3[:, :, gs], rd=srk,
                   wr=[t("xg2")], stream="xg2", dma=True)
            K.emit("sync", "dma_start", AIN[:, :, :],
                   acct[g * 512:(g + 1) * 512, :].rearrange("(j p) d -> p j d", p=128),
                   wr=[t("ain")], stream="accld", dma=True)
            for j in range(4):
                for m in range(DT):
                    K.emit("tensor", "transpose", C.PS1[0:128, 0:128],
                           AIN[:, j, m * 128:(m + 1) * 128], C.identf[:, :],
                           rd=[t("ain"), ("identf",)], wr=[("PS1",)])
                    K.emit("vector", "scalar_tensor_tensor",
                           XG2[:, m, j * 128:(j + 1) * 128], C.PS1[0:128, 0:128], 0.5,
                           XG2[:, m, j * 128:(j + 1) * 128], OP.mult, OP.add,
                           rd=[("PS1",), t("xg2")], wr=[t("xg2")])
            K.emit("sync", "dma_start", dst3[:, :, gs], XG2[:, :, :],
                   rd=[t("xg2")], wr=[(dstkey, g)], dma=True)
        ctxC.close()
    K.barrier()



def build_att1(K, C, W, scr, src3, dst3, srckey, dstkey):
    nc = K.nc
    sna, rws, wns = scr["SNA"], scr["RWS"], scr["WNS"]
    build_ln_pass(K, C, src3, None, srckey, None, "lnT", dst_bf=sna, dstbfkey="SNA",
                  row_cache=rws)
    SC = 1.0 / (2048.0 ** 0.5)
    with ExitStack() as ctx:
        sb = lambda name, shape, dt: ctx.enter_context(nc.sbuf_tensor(f"a1_{name}", shape, dt))
        t = lambda n: (f"a1_{n}",)
        WB = [sb(f"wb{i}", [128, 8, 1024], BF16) for i in range(2)]
        SQB = sb("sqb", [128, 1024], BF16)
        WSR = sb("wsr", [1, 1024], F32)
        gn = sb("gn", [1, 1], F32)
        QH = sb("qh", [128, 16, 1024], BF16)
        KH = sb("kh", [128, 16, 1024], BF16)
        VT = sb("vt", [128, 8, 2048], BF16)
        OT = sb("ot", [128, 8, 512], BF16)
        SNSg = sb("snsg", [128, 8, 512], BF16)
        ATT = sb("att", [128, 64], F32)
        ATTB = sb("attb", [128, 64], BF16)
        NMC = sb("nmc", [128, 1], F32)
        DSR = sb("dsr", [128, 1], F32)
        ATN = sb("atn", [128, 8, 64], BF16)
        SSQ = sb("ssq", [128, 16], F32)
        XG2 = sb("xg2", [128, 8, 512], F32)
        TMP = sb("tmp", [128, 512], F32)
        RWL = [sb(f"rwl{i}", [1, 512], F32) for i in range(2)]

        K.emit("sync", "dma_start", gn[:, :], W["h_gain"], wr=[t("gn")], dma=True)
        for wi, wname in enumerate(("h_wqT", "h_wkT", "h_wvT", "h_woT")):
            slot = WB[0]
            K.emit("gpsimd", "dma_start", slot[:, :, :],
                   W[wname].rearrange("(t p) o -> p t o", p=128), wr=[t("wb0")],
                   stream="wb0", dma=True)
            for h2 in range(2):
                pr = C.PS1[0:1, :] if h2 == 0 else C.pb(0)[0:1, :]
                pk = ("PS1",) if h2 == 0 else ("PB", 0)
                for kt in range(DT):
                    K.emit("scalar", "activation", SQB[:, :], slot[:, kt, :], AF.Square,
                           rd=[t("wb0")], wr=[t("sqb")])
                    K.emit("tensor", "matmul", pr, C.onesb[:, 0:1],
                           SQB[:, h2 * 512:(h2 + 1) * 512],
                           start=(kt == 0), stop=(kt == DT - 1),
                           rd=[("onesb",), t("sqb")], wr=[pk])
                K.emit("vector", "tensor_copy", WSR[0:1, h2 * 512:(h2 + 1) * 512], pr,
                       rd=[pk], wr=[t("wsr")])
            K.emit("scalar", "activation", WSR[0:1, :], WSR[0:1, :], AF.Ln,
                   bias=C.consts[0:1, 3:4], rd=[t("wsr"), ("consts",)], wr=[t("wsr")])
            K.emit("scalar", "activation", WSR[0:1, :], WSR[0:1, :], AF.Exp, scale=0.5,
                   rd=[t("wsr")], wr=[t("wsr")])
            K.emit("vector", "tensor_scalar", WSR[0:1, :], WSR[0:1, :], 1.0 / 32.0,
                   C.consts[0:1, 2:3], OP.mult, OP.add,
                   rd=[t("wsr"), ("consts",)], wr=[t("wsr")])
            K.emit("vector", "reciprocal", WSR[0:1, :], WSR[0:1, :], rd=[t("wsr")], wr=[t("wsr")])
            K.emit("vector", "tensor_scalar", WSR[0:1, :], WSR[0:1, :], gn[0:1, 0:1],
                   1.0 / 32.0, OP.mult, OP.mult, rd=[t("wsr"), t("gn")], wr=[t("wsr")])
            for h2 in range(2):
                K.emit("tensor", "matmul", C.pb(h2), C.onesf[0:1, 0:128],
                       WSR[0:1, h2 * 512:(h2 + 1) * 512], start=True, stop=True,
                       rd=[("onesf",), t("wsr")], wr=[("PB", h2)])
            K.emit("vector", "tensor_tensor", slot[:, :, :], slot[:, :, :],
                   C.PB[:, :].unsqueeze(1).broadcast_to([128, 8, 1024]), OP.mult,
                   rd=[t("wb0"), ("PB", 0), ("PB", 1)], wr=[t("wb0")])
            K.emit("gpsimd", "dma_start", wns[wi], slot[:, :, :],
                   rd=[t("wb0")], wr=[("WNS", wi)], stream="wnsw", dma=True)

        for seq in range(2):
            sq0 = seq * 2048
            for wi, dst, isqk in ((0, QH, True), (1, KH, True), (2, VT, False)):
                slot = WB[wi % 2]
                wk = t(f"wb{wi % 2}")
                K.emit("gpsimd", "dma_start", slot[:, :, :], wns[wi], rd=[("WNS", wi)],
                       wr=[wk], stream=f"wb{wi % 2}", dma=True)
                for grp in range(4):
                    gs = slice(sq0 + grp * 512, sq0 + (grp + 1) * 512)
                    K.emit("sync", "dma_start", SNSg[:, :, :], sna[:, :, gs],
                           rd=[("SNA", (sq0 + grp * 512) // 512)], wr=[t("snsg")],
                           stream="snsg", dma=True)
                    if isqk:
                        for tt in range(4):
                            ts_ = slice(tt * 128, (tt + 1) * 128)
                            for n2 in range(2):
                                pk = ("PA", n2)
                                pap = C.pa(n2)
                                for kd in range(DT):
                                    K.emit("tensor", "matmul", pap, SNSg[:, kd, ts_],
                                           slot[:, kd, n2 * 512:(n2 + 1) * 512],
                                           start=(kd == 0), stop=(kd == DT - 1),
                                           rd=[t("snsg"), wk], wr=[pk])
                            K.emit("scalar", "activation", SQB[:, 0:512], C.pa(0), AF.Square,
                                   rd=[("PA", 0)], wr=[t("sqb")])
                            K.emit("scalar", "activation", SQB[:, 512:1024], C.pa(1), AF.Square,
                                   rd=[("PA", 1)], wr=[t("sqb")])
                            K.emit("vector", "reduce_sum", SSQ[:, :],
                                   SQB[:, :].rearrange("p (h d) -> p h d", d=64), axis=AX.X,
                                   rd=[t("sqb")], wr=[t("ssq")])
                            K.emit("scalar", "activation", SSQ[:, :], SSQ[:, :], AF.Ln,
                                   bias=C.consts[:, 3:4], rd=[t("ssq"), ("consts",)], wr=[t("ssq")])
                            K.emit("scalar", "activation", SSQ[:, :], SSQ[:, :], AF.Exp,
                                   scale=0.5, rd=[t("ssq")], wr=[t("ssq")])
                            K.emit("vector", "tensor_scalar", SSQ[:, :], SSQ[:, :], 0.125,
                                   C.consts[:, 2:3], OP.mult, OP.add,
                                   rd=[t("ssq"), ("consts",)], wr=[t("ssq")])
                            K.emit("vector", "reciprocal", SSQ[:, :], SSQ[:, :],
                                   rd=[t("ssq")], wr=[t("ssq")])
                            for n2 in range(2):
                                K.emit("vector", "tensor_tensor",
                                       dst[:, grp * 4 + tt, n2 * 512:(n2 + 1) * 512]
                                       .rearrange("p (h d) -> p h d", d=64),
                                       C.pa(n2).rearrange("p (h d) -> p h d", d=64),
                                       SSQ[:, n2 * 8:(n2 + 1) * 8].unsqueeze(2)
                                       .broadcast_to([128, 8, 64]), OP.mult,
                                       rd=[("PA", n2), t("ssq")],
                                       wr=[t("qh") if dst is QH else t("kh")])
                    else:
                        for m in range(DT):
                            pk = ("PA", m % 2)
                            pap = C.pa(m % 2)
                            for kd in range(DT):
                                K.emit("tensor", "matmul", pap,
                                       slot[:, kd, m * 128:(m + 1) * 128], SNSg[:, kd, :],
                                       start=(kd == 0), stop=(kd == DT - 1),
                                       rd=[wk, t("snsg")], wr=[pk])
                            K.emit("vector", "tensor_copy",
                                   VT[:, m, grp * 512:(grp + 1) * 512], pap,
                                   rd=[pk], wr=[t("vt")])
            for m in range(8):
                n2 = m // 4
                c0 = (m % 4) * 128
                pk = ("PA", m % 2)
                pap = C.pa(m % 2)
                for kt in range(16):
                    K.emit("tensor", "matmul", pap, QH[:, kt, m * 128:(m + 1) * 128],
                           KH[:, kt, n2 * 512:(n2 + 1) * 512],
                           start=(kt == 0), stop=(kt == 15),
                           rd=[t("qh"), t("kh")], wr=[pk])
                for hh in range(2):
                    h = 2 * m + hh
                    rs = slice(hh * 64, (hh + 1) * 64)
                    blk = pap[hh * 64:(hh + 1) * 64, c0 + hh * 64:c0 + (hh + 1) * 64]
                    K.emit("vector", "reduce_max", NMC[rs, :], blk, axis=AX.X, negate=True,
                           rd=[pk], wr=[t("nmc")])
                    K.emit("vector", "tensor_scalar_mul", NMC[rs, :], NMC[rs, :], SC,
                           rd=[t("nmc")], wr=[t("nmc")])
                    K.emit("scalar", "activation", ATT[rs, :], blk, AF.Exp,
                           bias=NMC[rs, 0:1], scale=SC, accum_out=DSR[rs, 0:1],
                           rd=[pk, t("nmc")], wr=[t("att"), t("dsr")])
                    K.emit("vector", "reciprocal", DSR[rs, :], DSR[rs, :],
                           rd=[t("dsr")], wr=[t("dsr")])
                    K.emit("vector", "tensor_scalar_mul", ATTB[rs, :], ATT[rs, :],
                           DSR[rs, 0:1], rd=[t("att"), t("dsr")], wr=[t("attb")])
                    o_ = hh * 64
                    K.emit("tensor", "transpose", C.PT[0:64, 0:64],
                           ATTB[rs, :], C.ident[o_:o_ + 64, o_:o_ + 64],
                           rd=[t("attb"), ("ident",)], wr=[("PT",)])
                    hb_ = (h % 2) * 64
                    K.emit("vector", "tensor_copy", ATN[hb_:hb_ + 64, h // 2, :],
                           C.PT[0:64, 0:64], rd=[("PT",)], wr=[t("atn")])
            K.emit("gpsimd", "dma_start", WB[1][:, :, :], wns[3], rd=[("WNS", 3)],
                   wr=[t("wb1")], stream="wb1", dma=True)
            for nc_ in range(4):
                g = seq * 4 + nc_
                gs = slice(g * 512, (g + 1) * 512)
                ns = slice(nc_ * 512, (nc_ + 1) * 512)
                for hp in range(8):
                    pk = ("PB", hp % 2)
                    pap = C.pb(hp % 2)
                    for hh in range(2):
                        h = 2 * hp + hh
                        hb_ = (h % 2) * 64
                        K.emit("tensor", "matmul", pap[hh * 64:(hh + 1) * 64, :],
                               ATN[hb_:hb_ + 64, h // 2, :],
                               VT[hb_:hb_ + 64, h // 2, ns],
                               start=True, stop=True,
                               tile_position=(hb_, hh * 64),
                               rd=[t("atn"), t("vt")], wr=[pk])
                    K.emit("vector", "tensor_copy", OT[:, hp, :], pap, rd=[pk], wr=[t("ot")])
                K.emit("sync", "dma_start", XG2[:, :, :], src3[:, :, gs],
                       rd=[(srckey, g) if not isinstance(srckey, tuple) else srckey],
                       wr=[t("xg2")], stream="xg2", dma=True)
                K.emit("sync", "dma_start", RWL[0][:, :], rws[0:1, gs], rd=[("RWS", "m", g)],
                       wr=[t("rwl0")], stream="rwl", dma=True)
                K.emit("sync", "dma_start", RWL[1][:, :], rws[1:2, gs], rd=[("RWS", "i", g)],
                       wr=[t("rwl1")], stream="rwl", dma=True)
                for m in range(8):
                    pk = ("PA", 2 + m % 2)
                    pap = C.pa(2 + m % 2)
                    for k in range(DT):
                        K.emit("tensor", "matmul", pap, WB[1][:, k, m * 128:(m + 1) * 128],
                               OT[:, k, :], start=(k == 0), stop=(k == DT - 1),
                               rd=[t("wb1"), t("ot")], wr=[pk])
                    if m == 0:
                        K.emit("tensor", "matmul", C.pb(0), C.onesf[0:1, 0:128],
                               RWL[0][0:1, :], start=True, stop=True,
                               rd=[("onesf",), t("rwl0")], wr=[("PB", 0)])
                        K.emit("tensor", "matmul", C.pb(1), C.onesf[0:1, 0:128],
                               RWL[1][0:1, :], start=True, stop=True,
                               rd=[("onesf",), t("rwl1")], wr=[("PB", 1)])
                    K.emit("vector", "tensor_tensor", TMP[:, :], XG2[:, m, :], C.pb(0),
                           OP.subtract, rd=[t("xg2"), ("PB", 0)], wr=[t("tmp")])
                    K.emit("vector", "tensor_tensor", TMP[:, :], TMP[:, :], C.pb(1),
                           OP.mult, rd=[t("tmp"), ("PB", 1)], wr=[t("tmp")])
                    K.emit("vector", "tensor_tensor", TMP[:, :], TMP[:, :], pap,
                           OP.add, rd=[t("tmp"), pk], wr=[t("tmp")])
                    K.emit("vector", "scalar_tensor_tensor", XG2[:, m, :], TMP[:, :], RS2,
                           XG2[:, m, :], OP.mult, OP.add,
                           rd=[t("tmp"), t("xg2")], wr=[t("xg2")])
                K.emit("sync", "dma_start", dst3[:, :, gs], XG2[:, :, :],
                       rd=[t("xg2")], wr=[(dstkey, g)], dma=True)
    K.barrier()



CMIX_MU = float(np.float32(np.sqrt(0.5)))
CMIX_DEN = float(np.float32(np.sqrt(1.0 / (4.0 * np.pi)) * np.sqrt(2.0)))


def build_cmix(K, C, W, s3, xb3, dst3, skey, xbkey, dstkey, skip=False):
    nc = K.nc
    with ExitStack() as ctx:
        sb = lambda name, shape, dt: ctx.enter_context(nc.sbuf_tensor(f"cm_{name}", shape, dt))
        t = lambda n: (f"cm_{n}",)
        NB = NormBufs(nc, ctx, "cm", rms_only=True)
        XBg = sb("xbg", [128, 8, 512], F32)
        if skip:
            for g in range(8):
                gs = slice(g * 512, (g + 1) * 512)
                K.emit("sync", "dma_start", NB.xg[:, :, :], s3[:, :, gs],
                       rd=[(skey, g)], wr=[("cmxg",)], stream="sld", dma=True)
                K.emit("sync", "dma_start", XBg[:, :, :], xb3[:, :, gs],
                       rd=[(xbkey, g) if not isinstance(xbkey, tuple) else xbkey],
                       wr=[t("xbg")], stream="xbg", dma=True)
                K.emit("vector", "tensor_tensor", XBg[:, :, :], XBg[:, :, :], NB.xg[:, :, :],
                       OP.add, rd=[t("xbg"), ("cmxg",)], wr=[t("xbg")])
                K.emit("sync", "dma_start", dst3[:, :, gs], XBg[:, :, :],
                       rd=[t("xbg")], wr=[(dstkey, g)], dma=True)
            K.barrier()
            return
        RN = sb("rn", [128, 8, 512], BF16)
        T1 = sb("t1", [128, 8, 512], BF16)
        XK = sb("xk", [128, 8, 512], BF16)
        KACT = sb("kact", [128, 32, 512], BF16)
        RB = sb("rb", [128, 8, 512], BF16)
        TK = sb("tk", [128, 8, 1], BF16)
        MAA = sb("maa", [128, 4, 8], F32)
        WRC = sb("wrc", [128, 8, 1024], BF16)
        KS = [sb(f"ks{i}", [128, 8, 128], BF16) for i in range(2)]
        VS = [sb(f"vs{i}", [128, 32, 128], BF16) for i in range(2)]
        TMPF = sb("tmpf", [128, 512], F32)
        TMPG = sb("tmpg", [128, 512], F32)

        K.emit("sync", "dma_start", MAA[:, 0, :], W["h_maak"], wr=[t("maa")], dma=True)
        K.emit("sync", "dma_start", MAA[:, 2, :], W["h_maar"], wr=[t("maa")], dma=True)
        K.emit("vector", "tensor_scalar", MAA[:, 1, :], MAA[:, 0, :], -1.0, 1.0,
               OP.mult, OP.add, rd=[t("maa")], wr=[t("maa")])
        K.emit("vector", "tensor_scalar", MAA[:, 3, :], MAA[:, 2, :], -1.0, 1.0,
               OP.mult, OP.add, rd=[t("maa")], wr=[t("maa")])
        K.emit("gpsimd", "dma_start", WRC[:, :, :],
               W["L0_attn_recept_w"].rearrange("(t p) o -> p t o", p=128),
               wr=[t("wrc")], stream="wrc", dma=True)

        def mix_into(dst_b, c0, c1):
            K.emit("vector", "tensor_tensor", T1[:, :, :], RN[:, :, :],
                   MAA[:, c1, :].unsqueeze(2).broadcast_to([128, 8, 512]), OP.mult,
                   rd=[t("rn"), t("maa")], wr=[t("t1")])
            for td in range(DT):
                K.emit("vector", "scalar_tensor_tensor", dst_b[:, td, 1:512],
                       RN[:, td, 0:511], MAA[:, c0, td:td + 1], T1[:, td, 1:512],
                       OP.mult, OP.add, rd=[t("rn"), t("maa"), t("t1")], wr=[t("xk")])
                K.emit("vector", "scalar_tensor_tensor", dst_b[:, td, 0:1],
                       TK[:, td, 0:1], MAA[:, c0, td:td + 1], T1[:, td, 0:1],
                       OP.mult, OP.add, rd=[t("tk"), t("maa"), t("t1")], wr=[t("xk")])

        for g in range(8):
            gs = slice(g * 512, (g + 1) * 512)
            norm_group(K, C, NB, s3, g, RN[:, :, :], "rms", [(skey, g)], t("rn"))
            if g % 4 == 0:
                K.emit("vector", "memset", TK[:, :, :], 0.0, wr=[t("tk")])
            mix_into(XK, 0, 1)
            for m in range(32):
                sl = m % 2
                K.emit("gpsimd", "dma_start", KS[sl][:, :, :], W["h_key_cm"][m]
                       .rearrange("(t p) o -> p t o", p=128),
                       wr=[t(f"ks{sl}")], stream=f"ks{sl}", dma=True)
                for k in range(DT):
                    K.emit("tensor", "matmul", C.pb(sl), KS[sl][:, k, :], XK[:, k, :],
                           start=(k == 0), stop=(k == DT - 1),
                           rd=[t(f"ks{sl}"), t("xk")], wr=[("PB", sl)])
                c2 = float(np.sqrt(2.0) / CMIX_DEN)
                K.emit("vector", "tensor_scalar", TMPF[:, :], C.pb(sl), c2,
                       -CMIX_MU * c2, OP.mult, OP.add,
                       rd=[("PB", sl)], wr=[t("tmpf")])
                K.emit("vector", "tensor_scalar_add", TMPF[:, :], TMPF[:, :], 1.0e-20,
                       rd=[t("tmpf")], wr=[t("tmpf")])
                K.emit("scalar", "activation", TMPG[:, :], TMPF[:, :], AF.Gelu,
                       rd=[t("tmpf")], wr=[t("tmpg")])
                K.emit("vector", "reciprocal", TMPF[:, :], TMPF[:, :],
                       rd=[t("tmpf")], wr=[t("tmpf")])
                K.emit("vector", "tensor_tensor", KACT[:, m, :], TMPG[:, :], TMPF[:, :],
                       OP.mult, rd=[t("tmpg"), t("tmpf")], wr=[t("kact")])
            mix_into(XK, 2, 3)
            K.emit("vector", "tensor_copy", TK[:, :, :], RN[:, :, 511:512],
                   rd=[t("rn")], wr=[t("tk")])
            for m in range(DT):
                pk = ("PA", m % 2)
                pap = C.pa(m % 2)
                for k in range(DT):
                    K.emit("tensor", "matmul", pap, WRC[:, k, m * 128:(m + 1) * 128],
                           XK[:, k, :], start=(k == 0), stop=(k == DT - 1),
                           rd=[t("wrc"), t("xk")], wr=[pk])
                K.emit("scalar", "activation", RB[:, m, :], pap, AF.Sigmoid,
                       rd=[pk], wr=[t("rb")])
            K.emit("sync", "dma_start", XBg[:, :, :], xb3[:, :, gs],
                   rd=[(xbkey, g) if not isinstance(xbkey, tuple) else xbkey],
                   wr=[t("xbg")], stream="xbg", dma=True)
            for m in range(DT):
                sl = m % 2
                K.emit("gpsimd", "dma_start", VS[sl][:, :, :], W["h_value_cm"][m]
                       .rearrange("(t p) o -> p t o", p=128),
                       wr=[t(f"vs{sl}")], stream=f"vs{sl}", dma=True)
                pk = ("PA", 2 + sl)
                pap = C.pa(2 + sl)
                for k in range(32):
                    K.emit("tensor", "matmul", pap, VS[sl][:, k, :], KACT[:, k, :],
                           start=(k == 0), stop=(k == 31),
                           rd=[t(f"vs{sl}"), t("kact")], wr=[pk])
                K.emit("vector", "tensor_tensor", TMPF[:, :], pap, RB[:, m, :], OP.mult,
                       rd=[pk, t("rb")], wr=[t("tmpf")])
                K.emit("vector", "tensor_tensor", NB.xg[:, m, :], NB.xg[:, m, :],
                       TMPF[:, :], OP.add, rd=[("cmxg",), t("tmpf")], wr=[("cmxg",)])
            K.emit("vector", "tensor_tensor", NB.xg[:, :, :], NB.xg[:, :, :],
                   XBg[:, :, :], OP.add, rd=[("cmxg",), t("xbg")], wr=[("cmxg",)])
            K.emit("sync", "dma_start", dst3[:, :, gs], NB.xg[:, :, :],
                   rd=[("cmxg",)], wr=[(dstkey, g)], dma=True)
    K.barrier()



def build_mqa(K, C, W, src3, dst3, srckey, dstkey):
    nc = K.nc
    with ExitStack() as ctx:
        sb = lambda name, shape, dt: ctx.enter_context(nc.sbuf_tensor(f"q_{name}", shape, dt))
        t = lambda n: (f"q_{n}",)
        NB = NormBufs(nc, ctx, "q", rms_only=True)
        RN = sb("rn", [128, 8, 512], BF16)
        AW = sb("aw", [128, 8, 1152], BF16)
        PW = sb("pw", [128, 8, 1024], BF16)
        QT = sb("qt", [128, 8, 512], BF16)
        KTH = sb("kth", [128, 2048], BF16)
        VTOK = sb("vtok", [128, 16, 65], BF16)
        VB = sb("vbuf", [64, 512], BF16)
        SQT = sb("sqt", [128, 2, 512], BF16)
        QSR = sb("qsr", [1, 512], F32)
        KFR = sb("kfr", [1, 1], F32)
        MAXK = sb("maxk", [1, 1], F32)
        CBC = sb("cbc", [128, 1], F32)
        EB = sb("eb", [128, 2, 512], BF16)
        RD = sb("rd", [1, 512], F32)
        RB64 = sb("rb64", [64, 512], F32)
        OT = sb("ot", [128, 8, 512], BF16)

        K.emit("gpsimd", "dma_start", AW[:, :, :],
               W["L0_attn_attn_w"].rearrange("(t p) o -> p t o", p=128),
               wr=[t("aw")], stream="aw", dma=True)
        K.emit("gpsimd", "dma_start", PW[:, :, :],
               W["L0_attn_proj_w"].rearrange("(t p) o -> p t o", p=128),
               wr=[t("pw")], stream="pw", dma=True)
        for seq in range(2):
            K.emit("vector", "memset", MAXK[:, :], 0.0, wr=[t("maxk")])
            K.emit("vector", "memset", VTOK[:, :, 64:65], 1.0, wr=[t("vtok")])
            for c in range(4):
                g = seq * 4 + c
                gs = slice(g * 512, (g + 1) * 512)
                norm_group(K, C, NB, src3, g, RN[:, :, :], "rms", [(srckey, g)], t("rn"))
                for m in range(DT):
                    pk = ("PA", m % 2)
                    pap = C.pa(m % 2)
                    for k in range(DT):
                        K.emit("tensor", "matmul", pap, AW[:, k, m * 128:(m + 1) * 128],
                               RN[:, k, :], start=(k == 0), stop=(k == DT - 1),
                               rd=[t("aw"), t("rn")], wr=[pk])
                    K.emit("vector", "tensor_copy", QT[:, m, :], pap, rd=[pk], wr=[t("qt")])
                for k in range(DT):
                    K.emit("tensor", "matmul", C.pa(2)[0:64, :], AW[:, k, 1024:1088],
                           RN[:, k, :], start=(k == 0), stop=(k == DT - 1),
                           rd=[t("aw"), t("rn")], wr=[("PA", 2)])
                K.emit("vector", "tensor_copy", KTH[0:64, gs.start - seq * 2048:
                       gs.stop - seq * 2048], C.pa(2)[0:64, :], rd=[("PA", 2)], wr=[t("kth")])
                K.emit("vector", "tensor_copy", KTH[64:128, gs.start - seq * 2048:
                       gs.stop - seq * 2048], C.pa(2)[0:64, :], rd=[("PA", 2)], wr=[t("kth")])
                for k in range(DT):
                    K.emit("tensor", "matmul", C.pa(3)[0:64, :], AW[:, k, 1088:1152],
                           RN[:, k, :], start=(k == 0), stop=(k == DT - 1),
                           rd=[t("aw"), t("rn")], wr=[("PA", 3)])
                K.emit("vector", "tensor_copy", VB[:, :], C.pa(3)[0:64, :],
                       rd=[("PA", 3)], wr=[t("vbuf")])
                for j in range(4):
                    K.emit("tensor", "transpose", C.PT[0:128, 0:64],
                           VB[:, j * 128:(j + 1) * 128], C.ident[0:64, 0:64],
                           rd=[t("vbuf"), ("ident",)], wr=[("PT",)])
                    K.emit("vector", "tensor_copy", VTOK[:, c * 4 + j, 0:64],
                           C.PT[0:128, 0:64], rd=[("PT",)], wr=[t("vtok")])
                for m in range(DT):
                    K.emit("scalar", "activation", SQT[:, m % 2, :], QT[:, m, :], AF.Square,
                           rd=[t("qt")], wr=[("qsqt", m % 2)])
                    K.emit("tensor", "matmul", C.PS1[0:1, :], C.onesb[:, 0:1], SQT[:, m % 2, :],
                           start=(m == 0), stop=(m == DT - 1),
                           rd=[("onesb",), ("qsqt", m % 2)], wr=[("PS1",)])
                K.emit("scalar", "activation", QSR[:, :], C.PS1[0:1, :], AF.Ln,
                       bias=C.consts[0:1, 3:4], rd=[("PS1",), ("consts",)], wr=[t("qsr")])
                K.emit("scalar", "activation", QSR[:, :], QSR[:, :], AF.Exp, scale=0.5,
                       rd=[t("qsr")], wr=[t("qsr")])
                K.emit("scalar", "activation", SQT[0:64, 0, :],
                       KTH[0:64, c * 512:(c + 1) * 512], AF.Square,
                       rd=[t("kth")], wr=[("qsqt", 0)])
                K.emit("tensor", "matmul", C.PS1[0:1, :], C.onesb[0:64, 0:1], SQT[0:64, 0, :],
                       start=True, stop=True, rd=[("onesb",), ("qsqt", 0)], wr=[("PS1",)])
                K.emit("vector", "reduce_max", KFR[:, :], C.PS1[0:1, :], axis=AX.X,
                       rd=[("PS1",)], wr=[t("kfr")])
                K.emit("vector", "tensor_tensor", MAXK[:, :], MAXK[:, :], KFR[:, :], OP.max,
                       rd=[t("maxk"), t("kfr")], wr=[t("maxk")])
                K.emit("scalar", "activation", KFR[:, :], MAXK[:, :], AF.Ln,
                       bias=C.consts[0:1, 3:4], rd=[t("maxk"), ("consts",)], wr=[t("kfr")])
                K.emit("scalar", "activation", KFR[:, :], KFR[:, :], AF.Exp, scale=0.5,
                       rd=[t("kfr")], wr=[t("kfr")])
                K.emit("vector", "reduce_max", QSR[0:1, 0:1], QSR[0:1, :], axis=AX.X,
                       rd=[t("qsr")], wr=[t("qsr")])
                K.emit("vector", "tensor_scalar", QSR[0:1, 0:1], QSR[0:1, 0:1],
                       KFR[0:1, 0:1], -0.125, OP.mult, OP.mult,
                       rd=[t("qsr"), t("kfr")], wr=[t("qsr")])
                K.emit("tensor", "matmul", C.PS1[0:128, 0:1], C.onesf[0:1, 0:128],
                       QSR[0:1, 0:1], start=True, stop=True,
                       rd=[("onesf",), t("qsr")], wr=[("PS1",)])
                K.emit("vector", "tensor_copy", CBC[:, :], C.PS1[0:128, 0:1],
                       rd=[("PS1",)], wr=[t("cbc")])
                for h in range(16):
                    hb_ = (h % 2) * 64
                    pk = ("PA", h % 4)
                    pav = C.pa(h % 4)
                    nst = 4 * (c + 1)
                    for st in range(nst):
                        psc = ("PB", st % 2)
                        pscap = C.pb(st % 2)
                        K.emit("tensor", "matmul", pscap,
                               KTH[hb_:hb_ + 64, st * 128:(st + 1) * 128],
                               QT[hb_:hb_ + 64, h // 2, :],
                               start=True, stop=True, rd=[t("kth"), t("qt")], wr=[psc])
                        delta = st * 128 - c * 512
                        if delta >= 0:
                            K.emit("vector", "tensor_tensor", pscap, pscap,
                                   C.mtri[:, 384 - delta:896 - delta], OP.add,
                                   rd=[psc, ("mtri",)], wr=[psc])
                        K.emit("scalar", "activation", EB[:, st % 2, :], pscap, AF.Exp,
                               scale=0.125, bias=CBC[:, 0:1],
                               rd=[psc, t("cbc")], wr=[(f"q_eb", st % 2)])
                        K.emit("tensor", "matmul", pav[0:65, :], VTOK[:, st, :],
                               EB[:, st % 2, :],
                               start=(st == 0), stop=(st == nst - 1),
                               rd=[(f"q_eb", st % 2), t("vtok")], wr=[pk])
                    K.emit("vector", "reciprocal", RD[:, :], pav[64:65, :], rd=[pk], wr=[t("rd")])
                    K.emit("tensor", "matmul", C.PS1[0:64, :], C.onesf[0:1, 0:64],
                           RD[0:1, :], start=True, stop=True,
                           rd=[("onesf",), t("rd")], wr=[("PS1",)])
                    K.emit("vector", "tensor_copy", RB64[:, :], C.PS1[0:64, :],
                           rd=[("PS1",)], wr=[t("rb64")])
                    K.emit("vector", "tensor_tensor", OT[hb_:hb_ + 64, h // 2, :],
                           pav[0:64, :], RB64[:, :], OP.mult,
                           rd=[pk, t("rb64")], wr=[t("ot")])
                for m in range(DT):
                    pk = ("PB", m % 2)
                    pap = C.pb(m % 2)
                    for k in range(DT):
                        K.emit("tensor", "matmul", pap, PW[:, k, m * 128:(m + 1) * 128],
                               OT[:, k, :], start=(k == 0), stop=(k == DT - 1),
                               rd=[t("pw"), t("ot")], wr=[pk])
                    K.emit("vector", "tensor_tensor", NB.xg[:, m, :], NB.xg[:, m, :], pap,
                           OP.add, rd=[("qxg",), pk], wr=[("qxg",)])
                K.emit("sync", "dma_start", dst3[:, :, gs], NB.xg[:, :, :],
                       rd=[("qxg",)], wr=[(dstkey, g)], dma=True)
    K.barrier()



def build_mamba(K, C, W, scr, src3, dst3, srckey, dstkey):
    nc = K.nc
    ipb = scr["IPB"]
    with ExitStack() as ctx:
        sb = lambda name, shape, dt: ctx.enter_context(nc.sbuf_tensor(f"mb_{name}", shape, dt))
        t = lambda n: (f"mb_{n}",)
        NB = NormBufs(nc, ctx, "mb", rms_only=True)
        RN = sb("rn", [128, 8, 512], BF16)
        IPS = [sb(f"ips{i}", [128, 8, 128], BF16) for i in range(2)]
        Z = sb("z", [128, 16, 512], BF16)
        XBCP = sb("xbcp", [128, 17, 515], BF16)
        XBC = sb("xbc", [128, 17, 512], BF16)
        DTBUF = sb("dtbuf", [32, 512], F32)
        CONVW = sb("convw", [128, 68], F32)
        CONVB = sb("convb", [128, 17], F32)
        MN = sb("mn", [128, 16], F32)
        DBV = sb("dbv", [128, 16], F32)
        DSEL = sb("dsel", [32, 16], F32)
        MAP64 = sb("map64", [32, 128], F32)
        ACOL = sb("acol", [32, 1], F32)
        DTBC = sb("dtbc", [32, 1], F32)
        DVC = sb("dvc", [32, 1], F32)
        DTA = sb("dta", [32, 128], F32)
        S_ = sb("s_", [32, 128], F32)
        PALL = sb("pall", [32, 128], F32)
        SROW = sb("srowf", [1, 1024], F32)
        PROW = sb("prow", [1, 1024], BF16)
        ST = sb("st", [128, 32], F32)
        DTT = sb("dtt", [128, 32], F32)
        DTC = sb("dtc", [128, 32], F32)
        GTB = sb("gtb", [128, 128], BF16)
        CFB = sb("cfb", [64, 128], BF16)
        BT = sb("bt", [128, 64], BF16)
        BSC = sb("bsc", [128, 32, 64], BF16)
        TTMP = sb("ttmp", [128, 8, 128], F32)
        MT = sb("mt", [128, 32, 128], BF16)
        U = sb("u", [128, 16, 128], BF16)
        PBS = sb("pbs", [64, 32, 128], BF16)
        CSC = sb("csc", [64, 16, 128], F32)
        HST = sb("hst", [64, 32, 64], F32)
        DXS = sb("dxs", [128, 16, 128], F32)
        GM = sb("gm", [128, 16, 512], BF16)
        SQB = sb("sqb", [128, 2, 512], BF16)
        CT = sb("ct", [128, 2, 512], F32)
        OPS = [sb(f"ops{i}", [128, 16, 128], BF16) for i in range(2)]

        K.emit("sync", "dma_start", CONVW[:, :], W["h_convw"], wr=[t("convw")], dma=True)
        K.emit("sync", "dma_start", CONVB[:, :], W["h_convb"], wr=[t("convb")], dma=True)
        K.emit("sync", "dma_start", MN[:, :], W["h_mnorm"], wr=[t("mn")], dma=True)
        K.emit("sync", "dma_start", DTBC[:, :], W["h_dtb"], wr=[t("dtbc")], dma=True)
        K.emit("sync", "dma_start", ACOL[:, :], W["h_alog"], wr=[t("acol")], dma=True)
        K.emit("sync", "dma_start", DVC[:, :], W["h_dvec"], wr=[t("dvc")], dma=True)
        K.emit("sync", "dma_start", MAP64[:, :], W["c_map64"], wr=[t("map64")], dma=True)
        K.emit("sync", "dma_start", DSEL[:, :], W["c_sel32"], wr=[t("dsel")], dma=True)
        K.emit("scalar", "activation", ACOL[:, :], ACOL[:, :], AF.Exp,
               rd=[t("acol")], wr=[t("acol")])
        K.emit("vector", "tensor_scalar_mul", ACOL[:, :], ACOL[:, :], -1.0,
               rd=[t("acol")], wr=[t("acol")])
        K.emit("vector", "tensor_tensor", DSEL[:, :], DSEL[:, :],
               DVC[:, 0:1].broadcast_to([32, 16]), OP.mult,
               rd=[t("dsel"), t("dvc")], wr=[t("dsel")])
        K.emit("tensor", "matmul", C.PS1[:, 0:16], MAP64[:, :], DSEL[:, :],
               start=True, stop=True, rd=[t("map64"), t("dsel")], wr=[("PS1",)])
        K.emit("vector", "tensor_copy", DBV[:, :], C.PS1[:, 0:16], rd=[("PS1",)], wr=[t("dbv")])

        for seq in range(2):
            K.emit("vector", "memset", HST[:, :, :], 0.0, wr=[t("hst")])
            K.emit("vector", "memset", XBCP[:, :, 0:3], 0.0, wr=[t("xbcp")])
            for grp in range(4):
                g = seq * 4 + grp
                gs = slice(g * 512, (g + 1) * 512)
                norm_group(K, C, NB, src3, g, RN[:, :, :], "rms", [(srckey, g)], t("rn"))
                def conv_tile(tt):
                    ctp = tt % 2
                    K.emit("vector", "tensor_scalar_mul", CT[:, ctp, :], XBCP[:, tt, 0:512],
                           CONVW[:, 4 * tt:4 * tt + 1], rd=[t("xbcp"), t("convw")],
                           wr=[("mbct", ctp)])
                    for j in range(1, 4):
                        K.emit("vector", "scalar_tensor_tensor", CT[:, ctp, :],
                               XBCP[:, tt, j:512 + j], CONVW[:, 4 * tt + j:4 * tt + j + 1],
                               CT[:, ctp, :], OP.mult, OP.add,
                               rd=[t("xbcp"), t("convw")], wr=[("mbct", ctp)])
                    K.emit("scalar", "activation", XBC[:, tt, :], CT[:, ctp, :], AF.Silu,
                           bias=CONVB[:, tt:tt + 1], rd=[("mbct", ctp), t("convb")],
                           wr=[t("xbc")])

                for m in list(range(16, 34)) + list(range(16)):
                    sl = m % 2
                    mk_ = t(f"ips{sl}")
                    mw = 128 if m < 33 else 32
                    K.emit("sync", "dma_start", IPS[sl][:, :, 0:mw], ipb[m][:, :, 0:mw],
                           rd=[("IPB",)], wr=[mk_], stream=f"ips{sl}", dma=True)
                    pk = ("PB", sl)
                    pap = C.pb(sl)[0:mw, :]
                    for k in range(DT):
                        K.emit("tensor", "matmul", pap, IPS[sl][:, k, 0:mw], RN[:, k, :],
                               start=(k == 0), stop=(k == DT - 1), rd=[mk_, t("rn")], wr=[pk])
                    if m < 16:
                        K.emit("scalar", "activation", Z[:, m, :], pap, AF.Silu,
                               rd=[pk], wr=[t("z")])
                    elif m < 33:
                        K.emit("vector", "tensor_copy", XBCP[:, m - 16, 3:515], pap,
                               rd=[pk], wr=[t("xbcp")])
                        conv_tile(m - 16)
                    else:
                        K.emit("scalar", "activation", CT[0:32, 0, :], pap, AF.Exp,
                               bias=DTBC[:, 0:1], rd=[pk, t("dtbc")], wr=[("mbct", 0)])
                        K.emit("scalar", "activation", DTBUF[:, :], CT[0:32, 0, :], AF.Ln,
                               bias=C.consts[0:32, 1:2], rd=[("mbct", 0), ("consts",)],
                               wr=[t("dtbuf")])
                K.emit("vector", "tensor_copy", XBCP[:, :, 0:3], XBCP[:, :, 512:515],
                       rd=[t("xbcp")], wr=[t("xbcp")])
                for c in range(4):
                    tc = slice(c * 128, (c + 1) * 128)
                    K.emit("vector", "tensor_scalar_mul", DTA[:, :], DTBUF[:, tc],
                           ACOL[:, 0:1], rd=[t("dtbuf"), t("acol")], wr=[t("dta")])
                    K.emit("vector", "tensor_tensor_scan", S_[:, :], C.onesf[0:32, 0:128],
                           DTA[:, :], 0.0, OP.mult, OP.add, rd=[("onesf",), t("dta")],
                           wr=[t("s_")])
                    K.emit("scalar", "activation", PALL[:, :], S_[:, :], AF.Exp,
                           rd=[t("s_")], wr=[t("pall")])
                    K.emit("tensor", "transpose", C.PS1[0:128, 0:32], S_[:, :],
                           C.identf[0:32, 0:32], rd=[t("s_"), ("identf",)], wr=[("PS1",)])
                    K.emit("vector", "tensor_copy", ST[:, :], C.PS1[0:128, 0:32],
                           rd=[("PS1",)], wr=[t("st")])
                    K.emit("tensor", "transpose", C.PS1[0:128, 0:32], DTBUF[:, tc],
                           C.identf[0:32, 0:32], rd=[t("dtbuf"), ("identf",)], wr=[("PS1",)])
                    K.emit("vector", "tensor_copy", DTT[:, :], C.PS1[0:128, 0:32],
                           rd=[("PS1",)], wr=[t("dtt")])

                    K.emit("vector", "tensor_copy", CFB[:, :], XBC[64:128, 16, tc],
                           rd=[t("xbc")], wr=[t("cfb")])
                    K.emit("tensor", "matmul", C.PS1[:, 0:128], XBC[0:64, 16, tc],
                           CFB[:, :], start=True, stop=True, rd=[t("xbc"), t("cfb")],
                           wr=[("PS1",)])
                    K.emit("vector", "tensor_copy", GTB[:, :], C.PS1[:, 0:128],
                           rd=[("PS1",)], wr=[t("gtb")])
                    for r in range(4):
                        K.emit("gpsimd", "dma_start", PROW[:, :], PALL[8 * r:8 * (r + 1), :],
                               rd=[t("pall")], wr=[t("prowb")], stream="prow", dma=True)
                        for j in range(2):
                            K.emit("tensor", "matmul", C.pb(j)[0:64, :],
                                   C.onesbr[0:1, 0:64], PROW[0:1, j * 512:(j + 1) * 512],
                                   start=True, stop=True,
                                   rd=[("onesbr",), t("prowb")], wr=[("PB", j)])
                        K.emit("vector", "tensor_copy", PBS[:, 8 * r:8 * (r + 1), :],
                               C.PB[0:64, :].rearrange("p (a b) -> p a b", b=128),
                               rd=[("PB", 0), ("PB", 1)], wr=[t("pbs")])
                    for r in range(4):
                        hr = slice(8 * r, 8 * (r + 1))
                        K.emit("gpsimd", "dma_start", SROW[:, :], S_[hr, :],
                               rd=[t("s_")], wr=[t("srowb")], stream="srow", dma=True)
                        for j in range(2):
                            K.emit("tensor", "matmul", C.pb(j), C.onesf[0:1, 0:128],
                                   SROW[0:1, j * 512:(j + 1) * 512],
                                   start=True, stop=True, rd=[("onesf",), t("srowb")],
                                   wr=[("PB", j)])
                        K.emit("vector", "tensor_tensor", TTMP[:, :, :],
                               C.PB[:, :].rearrange("p (a b) -> p a b", b=128),
                               ST[:, hr].unsqueeze(2).broadcast_to([128, 8, 128]),
                               OP.subtract, rd=[("PB", 0), ("PB", 1), t("st")], wr=[t("ttmp")])
                        K.emit("vector", "tensor_tensor", TTMP[:, :, :], TTMP[:, :, :],
                               C.mtri[:, 384:512].unsqueeze(1).broadcast_to([128, 8, 128]),
                               OP.add, rd=[t("ttmp"), ("mtri",)], wr=[t("ttmp")])
                        K.emit("scalar", "activation", TTMP[:, :, :], TTMP[:, :, :], AF.Exp,
                               rd=[t("ttmp")], wr=[t("ttmp")])
                        K.emit("vector", "tensor_copy", DTC[:, hr], TTMP[:, :, 127],
                               rd=[t("ttmp")], wr=[t("dtc")])
                        K.emit("vector", "tensor_tensor", MT[:, hr, :], TTMP[:, :, :],
                               GTB[:, :].unsqueeze(1).broadcast_to([128, 8, 128]), OP.mult,
                               rd=[t("ttmp"), t("gtb")], wr=[t("mt")])
                    for tt in range(16):
                        K.emit("tensor", "transpose", C.PT[0:128, 0:128],
                               XBC[:, tt, tc], C.ident[:, :], rd=[t("xbc"), ("ident",)],
                               wr=[("PT",)])
                        K.emit("vector", "tensor_tensor",
                               U[:, tt, :].rearrange("p (a b) -> p a b", b=64),
                               C.PT[0:128, 0:128].rearrange("p (a b) -> p a b", b=64),
                               DTT[:, 2 * tt:2 * tt + 2].unsqueeze(2)
                               .broadcast_to([128, 2, 64]), OP.mult,
                               rd=[("PT",), t("dtt")], wr=[t("u")])
                    for r2 in range(2):
                        K.emit("vector", "tensor_tensor", CSC[:, :, :],
                               CFB[:, :].unsqueeze(1).broadcast_to([64, 16, 128]),
                               PBS[:, 16 * r2:16 * (r2 + 1), :], OP.mult,
                               rd=[t("cfb"), t("pbs")], wr=[t("csc")])
                        for hi in range(16):
                            h = 16 * r2 + hi
                            tt, hh = h // 2, h % 2
                            K.emit("tensor", "matmul",
                                   C.PA[hh * 64:(hh + 1) * 64, tt * 128:(tt + 1) * 128],
                                   U[:, tt, hh * 64:(hh + 1) * 64], MT[:, h, :],
                                   start=True, stop=False, tile_position=(0, hh * 64),
                                   rd=[t("u"), t("mt")], wr=[("PA", tt % 4)])
                            K.emit("tensor", "matmul",
                                   C.PA[hh * 64:(hh + 1) * 64, tt * 128:(tt + 1) * 128],
                                   HST[:, h, :], CSC[:, hi, :],
                                   start=False, stop=True, tile_position=(0, hh * 64),
                                   rd=[t("hst"), t("csc")], wr=[("PA", tt % 4)])
                    K.emit("vector", "tensor_tensor", DXS[:, :, :], XBC[:, 0:16, tc],
                           DBV[:, :].unsqueeze(2).broadcast_to([128, 16, 128]), OP.mult,
                           rd=[t("xbc"), t("dbv")], wr=[t("dxs")])
                    K.emit("vector", "tensor_tensor", DXS[:, :, :],
                           C.PA[:, :].rearrange("p (a b) -> p a b", b=128), DXS[:, :, :],
                           OP.add, rd=[("PA", 0), ("PA", 1), ("PA", 2), ("PA", 3),
                                       t("dxs")], wr=[t("dxs")])
                    K.emit("vector", "tensor_tensor", GM[:, :, tc], DXS[:, :, :],
                           Z[:, :, tc], OP.mult, rd=[t("dxs"), t("z")], wr=[t("gm")])
                    K.emit("tensor", "transpose", C.PT[0:128, 0:64], XBC[0:64, 16, tc],
                           C.ident[0:64, 0:64], rd=[t("xbc"), ("ident",)], wr=[("PT",)])
                    K.emit("vector", "tensor_copy", BT[:, :], C.PT[0:128, 0:64],
                           rd=[("PT",)], wr=[t("bt")])
                    K.emit("vector", "tensor_tensor", BSC[:, :, :],
                           BT[:, :].unsqueeze(1).broadcast_to([128, 32, 64]),
                           DTC[:, :].unsqueeze(2).broadcast_to([128, 32, 64]), OP.mult,
                           rd=[t("bt"), t("dtc")], wr=[t("bsc")])
                    K.emit("vector", "tensor_tensor", HST[:, :, :], HST[:, :, :],
                           PBS[:, :, 127:128].broadcast_to([64, 32, 64]), OP.mult,
                           rd=[t("hst"), t("pbs")], wr=[t("hst")])
                    for h in range(32):
                        K.emit("tensor", "matmul", C.PA[0:64, h * 64:(h + 1) * 64],
                               BSC[:, h, :], U[:, h // 2, (h % 2) * 64:(h % 2) * 64 + 64],
                               start=True, stop=True, rd=[t("bsc"), t("u")],
                               wr=[("PA", h % 4)])
                    K.emit("vector", "tensor_tensor", HST[:, :, :], HST[:, :, :],
                           C.PA[0:64, :].rearrange("p (a b) -> p a b", b=64), OP.add,
                           rd=[t("hst"), ("PA", 0), ("PA", 1), ("PA", 2), ("PA", 3)],
                           wr=[t("hst")])
                for tt in range(16):
                    K.emit("scalar", "activation", SQB[:, tt % 2, :], GM[:, tt, :], AF.Square,
                           rd=[t("gm")], wr=[("mbsqb", tt % 2)])
                    K.emit("tensor", "matmul", C.PS1[0:1, :], C.onesb[:, 0:1], SQB[:, tt % 2, :],
                           start=(tt == 0), stop=(tt == 15), rd=[("onesb",), ("mbsqb", tt % 2)],
                           wr=[("PS1",)])
                K.emit("vector", "tensor_scalar_mul", NB.r1[:, :], C.PS1[0:1, :], 1.0 / 2048,
                       rd=[("PS1",)], wr=[("mbr1",)])
                K.emit("scalar", "activation", NB.r1[:, :], NB.r1[:, :], AF.Ln,
                       bias=C.consts[0:1, 0:1], rd=[("mbr1",), ("consts",)], wr=[("mbr1",)])
                K.emit("scalar", "activation", NB.r1[:, :], NB.r1[:, :], AF.Exp, scale=-0.5,
                       rd=[("mbr1",)], wr=[("mbr1",)])
                K.emit("tensor", "matmul", C.pb(0), C.onesf[0:1, 0:128], NB.r1[0:1, :],
                       start=True, stop=True, rd=[("onesf",), ("mbr1",)], wr=[("PB", 0)])
                K.emit("vector", "tensor_tensor", GM[:, :, :], GM[:, :, :],
                       C.pb(0).unsqueeze(1).broadcast_to([128, 16, 512]), OP.mult,
                       rd=[t("gm"), ("PB", 0)], wr=[t("gm")])
                K.emit("vector", "tensor_tensor", GM[:, :, :], GM[:, :, :],
                       MN[:, :].unsqueeze(2).broadcast_to([128, 16, 512]), OP.mult,
                       rd=[t("gm"), t("mn")], wr=[t("gm")])
                for m in range(DT):
                    sl = m % 2
                    K.emit("gpsimd", "dma_start", OPS[sl][:, :, :], W["h_outproj_cm"][m]
                           .rearrange("(t p) c -> p t c", p=128),
                           wr=[t(f"ops{sl}")], stream=f"ops{sl}", dma=True)
                    pk = ("PA", 2 + sl)
                    pap = C.pa(2 + sl)
                    for k in range(16):
                        K.emit("tensor", "matmul", pap, OPS[sl][:, k, :], GM[:, k, :],
                               start=(k == 0), stop=(k == 15),
                               rd=[t(f"ops{sl}"), t("gm")], wr=[pk])
                    K.emit("vector", "tensor_tensor", NB.xg[:, m, :], NB.xg[:, m, :], pap,
                           OP.add, rd=[("mbxg",), pk], wr=[("mbxg",)])
                K.emit("sync", "dma_start", dst3[:, :, gs], NB.xg[:, :, :],
                       rd=[("mbxg",)], wr=[(dstkey, g)], dma=True)
    K.barrier()


def host_prepare(inputs):
    W = {}
    for n in WEIGHT_SHAPES:
        W[n] = np.ascontiguousarray(np.asarray(inputs[n], dtype=np.float32))
    W["h_wout1T"] = np.ascontiguousarray(np.asarray(inputs["L0_ffn_Wout1"], np.float32).T)
    W["h_wout2T"] = np.ascontiguousarray(np.asarray(inputs["L0_ffn_Wout2"], np.float32).T)
    W["h_win1bias"] = np.ascontiguousarray(
        np.asarray(inputs["L0_ffn_Win1"], np.float32)[1024].reshape(4, 128).T)
    W["h_win2bias"] = np.ascontiguousarray(
        np.asarray(inputs["L0_ffn_Win2"], np.float32)[4096].reshape(4, 128).T)
    cw = np.asarray(inputs["L0_attn_conv_w"], np.float32).reshape(17, 128, 4)
    W["h_convw"] = np.ascontiguousarray(cw.transpose(1, 0, 2).reshape(128, 68))
    W["h_convb"] = np.ascontiguousarray(
        np.asarray(inputs["L0_attn_conv_b"], np.float32).reshape(17, 128).T)
    W["h_mnorm"] = np.ascontiguousarray(
        np.asarray(inputs["L0_attn_mnorm_w"], np.float32).reshape(16, 128).T)
    W["h_dtb"] = np.asarray(inputs["L0_attn_dt_bias"], np.float32).reshape(32, 1)
    W["h_alog"] = np.asarray(inputs["L0_attn_A_log"], np.float32).reshape(32, 1)
    W["h_dvec"] = np.asarray(inputs["L0_attn_D"], np.float32).reshape(32, 1)
    W["h_maak"] = np.ascontiguousarray(
        np.asarray(inputs["L0_attn_time_maa_k"], np.float32).reshape(8, 128).T)
    W["h_maar"] = np.ascontiguousarray(
        np.asarray(inputs["L0_attn_time_maa_r"], np.float32).reshape(8, 128).T)
    op_ = np.asarray(inputs["L0_attn_out_proj_w"], np.float32)
    W["h_outproj_cm"] = np.ascontiguousarray(op_.reshape(2048, 8, 128).transpose(1, 0, 2))
    kw_ = np.asarray(inputs["L0_attn_key_w"], np.float32)
    W["h_key_cm"] = np.ascontiguousarray(kw_.reshape(1024, 32, 128).transpose(1, 0, 2))
    vw_ = np.asarray(inputs["L0_attn_value_w"], np.float32)
    W["h_value_cm"] = np.ascontiguousarray(vw_.reshape(4096, 8, 128).transpose(1, 0, 2))
    for a, b in (("h_wqT", "L1_attn_w_q"), ("h_wkT", "L1_attn_w_k"),
                 ("h_wvT", "L1_attn_w_v"), ("h_woT", "L1_attn_w_o")):
        W[a] = np.ascontiguousarray(np.asarray(inputs[b], np.float32).T)
    W["h_gain"] = np.asarray(inputs["L1_attn_gain"], np.float32).reshape(1, 1)
    W["h_routerT"] = np.ascontiguousarray(np.asarray(inputs["L1_ffn_router_w"], np.float32).T)
    W["h_ebias"] = np.asarray(inputs["L1_ffn_experts_bias"], np.float32).reshape(8, 1)
    W["h_w1"] = np.ascontiguousarray(np.concatenate(
        [np.asarray(inputs["L1_ffn_w1"], np.float32),
         np.asarray(inputs["L1_ffn_sw1"], np.float32)], axis=0))
    W["h_w3"] = np.ascontiguousarray(np.concatenate(
        [np.asarray(inputs["L1_ffn_w3"], np.float32),
         np.asarray(inputs["L1_ffn_sw3"], np.float32)], axis=0))
    W["h_w2"] = np.ascontiguousarray(np.concatenate(
        [np.asarray(inputs["L1_ffn_w2"], np.float32),
         np.asarray(inputs["L1_ffn_sw2"], np.float32)], axis=0))
    p = np.arange(128)[:, None]
    q = np.arange(1024)[None, :]
    W["c_mtri"] = np.where(q >= p + 384, 0.0, NEG).astype(np.float32)
    W["c_ident"] = np.eye(128, dtype=ml_dtypes.bfloat16)
    W["c_identf"] = np.eye(128, dtype=np.float32)
    W["c_kidx"] = np.tile(np.arange(16, dtype=np.float32)[None, :], (128, 1))
    W["c_invk"] = np.tile((1.0 / np.arange(1, 17, dtype=np.float32))[None, :], (128, 1))
    W["c_kval"] = np.tile(np.arange(1, 17, dtype=np.float32)[None, :], (128, 1))
    W["c_tokids"] = np.ascontiguousarray(
        (np.arange(32)[None, :] * 128 + np.arange(128)[:, None]).astype(np.int32))
    W["c_iota128"] = np.arange(128, dtype=np.float32).reshape(128, 1)
    hh = np.arange(32)[:, None]
    W["c_map64"] = (np.arange(128)[None, :] // 64 == hh % 2).astype(np.float32)
    W["c_sel32"] = (np.arange(16)[None, :] == hh // 2).astype(np.float32)
    return W


def shard_x(x, c):
    return np.ascontiguousarray(
        x[2 * c:2 * c + 2].transpose(2, 0, 1).reshape(1024, 4096)
        .reshape(8, 128, 4096).transpose(1, 0, 2))


def unshard_out(oc):
    return np.ascontiguousarray(oc).transpose(1, 0, 2).reshape(1024, 2, 2048).transpose(1, 2, 0)


def kernel(**inputs):
    W = host_prepare(inputs)
    x = np.asarray(inputs["L0_attn_x"], np.float32)
    nc = build_program()
    in_maps = []
    for c in range(NCORES):
        m = {"x_in": shard_x(x, c)}
        m.update(W)
        in_maps.append(m)
    res = run_bass_kernel_spmd(nc, in_maps, core_ids=list(range(NCORES)))
    outs = [unshard_out(np.asarray(res.results[c]["out"])) for c in range(NCORES)]
    return np.concatenate(outs, axis=0).astype(np.float32)


# version 08
# speedup vs baseline: 1.1420x; 1.1301x over previous
import bisect
import numpy as np
import ml_dtypes
from contextlib import ExitStack

import concourse.bass as bass
import concourse.mybir as mybir
from concourse.bass_utils import run_bass_kernel_spmd
from concourse.alu_op_type import AluOpType as OP
from concourse.bass import IndirectOffsetOnAxis

F32 = mybir.dt.float32
I32 = mybir.dt.int32
BF16 = mybir.dt.bfloat16
AF = mybir.ActivationFunctionType
AX = mybir.AxisListType

NCORES = 8
D = 1024
DT = 8
NSEQ = 2
TSEQ = 2048
NTOK = 4096
NEG = -1.0e30
RS2 = 0.70710678118654752
MOE_SPARSE = True
MOE_NSB = 24
MOE_BIG = 1.0e6
DBG = []


class Kit:
    ENGINES = ("tensor", "vector", "scalar", "sync", "gpsimd")
    DMA_Q = ("sync", "gpsimd", "scalar")
    NSLOT = 24

    def __init__(self, nc, ctx):
        self.nc = nc
        self.esem = {e: ctx.enter_context(nc.semaphore(f"es_{e}")) for e in self.ENGINES}
        self.dsem = {(q, i): ctx.enter_context(nc.semaphore(f"ds_{q}{i}"))
                     for q in self.DMA_Q for i in range(self.NSLOT)}
        self.rec = {e: [] for e in self.ENGINES}
        self.instrs = {e: [] for e in self.ENGINES}
        self.incs = {e: [] for e in self.ENGINES}
        self.dcount = {k: 0 for k in self.dsem}
        self.slot_of = {}
        self.slot_rr = {q: 0 for q in self.DMA_Q}
        self.slot_key = {}
        self.slot_mixed = set()
        self.writer = {}
        self.readers = {}
        self.waited = {e: {} for e in self.ENGINES}

    def _slot(self, q, stream):
        key = (q, stream)
        s = self.slot_of.get(key)
        if s is None:
            if self.slot_rr[q] >= self.NSLOT:
                raise RuntimeError(f"dma stream overflow on {q}: {stream}")
            s = self.slot_rr[q]
            self.slot_rr[q] += 1
            self.slot_of[key] = s
        return s

    def _wait(self, e, tok):
        if tok[0] == "e":
            _, pe, pos = tok
            if pe == e == "tensor":
                return
            lst = self.incs[pe]
            i = bisect.bisect_left(lst, (pos, -1))
            if i < len(lst):
                val = lst[i][1]
            else:
                val = (lst[-1][1] + 1) if lst else 1
                self.instrs[pe][pos][4].append((self.esem[pe], 1))
                lst.append((pos, val))
            key, sem = ("es", pe), self.esem[pe]
        else:
            _, qe, slot, val = tok
            if (qe, slot) in self.slot_mixed:
                val = self.dcount[(qe, slot)]
            key, sem = ("ds", qe, slot), self.dsem[(qe, slot)]
        if self.waited[e].get(key, 0) >= val:
            return
        self.rec[e].append(("wait", sem, val))
        self.waited[e][key] = val

    def emit(self, e, op, *args, rd=(), wr=(), dma=False, stream=None, **kw):
        for r in rd:
            t = self.writer.get(r)
            if t is not None:
                self._wait(e, t)
        for r in wr:
            t = self.writer.get(r)
            if t is not None:
                self._wait(e, t)
            for rt in self.readers.get(r, {}).values():
                self._wait(e, rt)
        entry = ["op", op, args, kw, []]
        self.rec[e].append(entry)
        if dma:
            wkey = wr[0] if wr else None
            if stream is None:
                stream = wkey if wkey is not None else (rd[0] if rd else "main")
                if isinstance(stream, tuple):
                    stream = stream[0]
            slot = self._slot(e, stream)
            sk = (e, slot)
            if self.slot_key.get(sk, wkey) != wkey or                     "dram" in str(getattr(args[0], "space", "")).lower():
                self.slot_mixed.add(sk)
            self.slot_key[sk] = wkey
            self.dcount[sk] += 16
            entry[4].append((self.dsem[sk], 16))
            tok = ("d", e, slot, self.dcount[sk])
            src = ("d", e, slot)
        else:
            self.instrs[e].append(entry)
            tok = ("e", e, len(self.instrs[e]) - 1)
            src = ("e", e)
        for r in rd:
            self.readers.setdefault(r, {})[src] = tok
        for r in wr:
            self.writer[r] = tok
            self.readers[r] = {}
        return entry

    def barrier(self):
        toks = []
        for pe in self.ENGINES:
            n = len(self.instrs[pe])
            if n:
                toks.append(("e", pe, n - 1))
        for (qe, slot), cnt in self.dcount.items():
            if cnt:
                toks.append(("d", qe, slot, cnt))
                self.slot_mixed.add((qe, slot))
        for e in self.ENGINES:
            for t in toks:
                self._wait(e, t)
        self.writer.clear()
        self.readers.clear()
        self.slot_of.clear()
        self.slot_key.clear()
        self.slot_mixed.clear()
        self.slot_rr = {q: 0 for q in self.DMA_Q}

    def finish(self):
        for (qe, slot), cnt in self.dcount.items():
            if cnt:
                self.rec[qe].append(("wait", self.dsem[(qe, slot)], cnt))

    def replay_into(self, blk):
        def mk(en):
            def body(h):
                for ent in self.rec[en]:
                    if ent[0] == "wait":
                        h.wait_ge(ent[1], ent[2])
                    else:
                        try:
                            inst = getattr(h, ent[1])(*ent[2], **ent[3])
                        except Exception:
                            import sys as _s
                            print(f"REPLAY FAIL {en} op={ent[1]} args={[str(a)[:150] for a in ent[2]]} kw={ent[3]}", file=_s.stderr)
                            raise
                        for sem, n in ent[4]:
                            inst.then_inc(sem, n)
            return body
        blk.tensor(mk("tensor"))
        blk.vector(mk("vector"))
        blk.scalar(mk("scalar"))
        blk.gpsimd(mk("gpsimd"))
        blk.sync(mk("sync"))


def sort_network(n):
    pairs = []

    def merge(lo, hi, r):
        step = r * 2
        if step < hi - lo:
            merge(lo, hi, step)
            merge(lo + r, hi, step)
            for i in range(lo + r, hi - r, step):
                pairs.append((i, i + r))
        else:
            pairs.append((lo, lo + r))

    def srt(lo, hi):
        if (hi - lo) >= 1:
            mid = lo + ((hi - lo) // 2)
            srt(lo, mid)
            srt(mid + 1, hi)
            merge(lo, hi, 1)

    srt(0, n - 1)
    return pairs


class Ctx:
    pass


def alloc_common(nc, ctx, C):
    sb = lambda name, shape, dt: ctx.enter_context(nc.sbuf_tensor(name, shape, dt))
    C.ident = sb("ident", [128, 128], BF16)
    C.identf = sb("identf", [128, 128], F32)
    C.mtri = sb("mtri", [128, 1024], F32)
    C.kidx = sb("kidx", [128, 16], F32)
    C.invk = sb("invk", [128, 16], F32)
    C.kval = sb("kval", [128, 16], F32)
    C.onesb = sb("onesb", [128, 1], BF16)
    C.onesbr = sb("onesbr", [1, 128], BF16)
    C.onesf = sb("onesf", [128, 512], F32)
    C.consts = sb("consts", [128, 8], F32)
    C.PA = ctx.enter_context(nc.psum_tensor("PA", [128, 2048], F32))
    C.PB = ctx.enter_context(nc.psum_tensor("PB", [128, 1024], F32))
    C.PS1 = ctx.enter_context(nc.psum_tensor("PS1", [128, 512], F32))
    C.PT = ctx.enter_context(nc.psum_tensor("PT", [128, 512], BF16))
    C.pa = lambda m: C.PA[:, m * 512:(m + 1) * 512]
    C.pb = lambda m: C.PB[:, m * 512:(m + 1) * 512]


def load_consts(K, C, dram):
    K.emit("sync", "dma_start", C.ident[:, :], dram["c_ident"], wr=[("ident",)], dma=True)
    K.emit("sync", "dma_start", C.identf[:, :], dram["c_identf"], wr=[("identf",)], dma=True)
    K.emit("sync", "dma_start", C.mtri[:, :], dram["c_mtri"], wr=[("mtri",)], dma=True)
    K.emit("sync", "dma_start", C.kidx[:, :], dram["c_kidx"], wr=[("kidx",)], dma=True)
    K.emit("sync", "dma_start", C.invk[:, :], dram["c_invk"], wr=[("invk",)], dma=True)
    K.emit("sync", "dma_start", C.kval[:, :], dram["c_kval"], wr=[("kval",)], dma=True)
    K.emit("gpsimd", "memset", C.onesb[:, :], 1.0, wr=[("onesb",)])
    K.emit("gpsimd", "memset", C.onesbr[:, :], 1.0, wr=[("onesbr",)])
    K.emit("gpsimd", "memset", C.onesf[:, :], 1.0, wr=[("onesf",)])
    K.emit("gpsimd", "memset", C.consts[:, 0:1], 1.0e-5, wr=[("consts",)])
    K.emit("gpsimd", "memset", C.consts[:, 1:2], 1.0, wr=[("consts",)])
    K.emit("gpsimd", "memset", C.consts[:, 2:3], 1.0e-4, wr=[("consts",)])
    K.emit("gpsimd", "memset", C.consts[:, 3:4], 1.0e-38, wr=[("consts",)])
    K.emit("gpsimd", "memset", C.consts[:, 4:5], -float(np.float32(np.sqrt(0.5))) /
           float(np.float32(np.sqrt(1.0 / (4.0 * np.pi)) * np.sqrt(2.0))), wr=[("consts",)])


def dbg_tap(K, dram, name, ap, key, col0=0):
    if name in DBG:
        n = 1
        for d in ap.shape[1:]:
            n *= d
        q = "scalar" if ap.dtype == F32 else "gpsimd"
        K.emit(q, "dma_start", dram["dbg"][0:ap.shape[0], col0:col0 + n],
               ap, rd=[key], dma=True)


def rsqrt_ip(K, C, ap, key, eps_col, extra_scale=1.0):
    K.emit("scalar", "activation", ap, ap, AF.Ln, bias=eps_col,
           rd=[key, ("consts",)], wr=[key])
    K.emit("scalar", "activation", ap, ap, AF.Exp, scale=-0.5, rd=[key], wr=[key])
    if extra_scale != 1.0:
        K.emit("vector", "tensor_scalar_mul", ap, ap, float(extra_scale), rd=[key], wr=[key])


class NormBufs:
    def __init__(self, nc, ctx, tag, rms_only=False):
        sb = lambda name, shape, dt: ctx.enter_context(nc.sbuf_tensor(name, shape, dt))
        self.xg = sb(f"{tag}_xg", [128, 8, 512], F32)
        self.xgb = sb(f"{tag}_xgb", [128, 8, 512], BF16)
        self.sq = sb(f"{tag}_sq", [128, 2, 512], BF16)
        if not rms_only:
            self.mb = sb(f"{tag}_mb", [128, 512], F32)
            self.vb = sb(f"{tag}_vb", [128, 512], F32)
        self.invb = sb(f"{tag}_invb", [128, 512], F32)
        self.srow = sb(f"{tag}_srow", [1, 1024], F32)
        self.r1 = sb(f"{tag}_r1", [1, 512], F32)
        self.tag = tag


def norm_group(K, C, NB, src3, g, out_xn, kind, srckeys, xnkey, out_f32=None,
               f32key=None, row_cache=None, gslice=None):
    tg = NB.tag
    kxg, kxgb = (f"{tg}xg",), (f"{tg}xgb",)
    gs = gslice if gslice is not None else slice(g * 512, (g + 1) * 512)
    src_slice = src3[:, :, gs]
    K.emit("sync", "dma_start", NB.xg[:, :, :], src_slice, rd=srckeys, wr=[kxg], dma=True)
    K.emit("gpsimd", "dma_start", NB.xgb[:, :, :], src_slice, rd=srckeys, wr=[kxgb], dma=True)
    if kind == "ln":
        for t in range(DT):
            K.emit("tensor", "matmul", C.PS1[0:1, :], C.onesb[:, 0:1], NB.xgb[:, t, :],
                   start=(t == 0), stop=(t == DT - 1), rd=[("onesb",), kxgb], wr=[("PS1",)])
        K.emit("vector", "tensor_copy", NB.srow[:, 0:512], C.PS1[0:1, :],
               rd=[("PS1",)], wr=[(f"{tg}srow",)])
    for t in range(DT):
        K.emit("scalar", "activation", NB.sq[:, t % 2, :], NB.xgb[:, t, :], AF.Square,
               rd=[kxgb], wr=[(f"{tg}sq", t % 2)])
        K.emit("tensor", "matmul", C.PS1[0:1, :], C.onesb[:, 0:1], NB.sq[:, t % 2, :],
               start=(t == 0), stop=(t == DT - 1), rd=[("onesb",), (f"{tg}sq", t % 2)],
               wr=[("PS1",)])
    K.emit("vector", "tensor_copy", NB.srow[:, 512:1024], C.PS1[0:1, :],
           rd=[("PS1",)], wr=[(f"{tg}srow",)])
    if kind == "ln":
        K.emit("tensor", "matmul", C.pb(0), C.onesf[0:1, 0:128], NB.srow[0:1, 0:512],
               start=True, stop=True, rd=[("onesf",), (f"{tg}srow",)], wr=[("PB", 0)])
    K.emit("tensor", "matmul", C.pb(1), C.onesf[0:1, 0:128], NB.srow[0:1, 512:1024],
           start=True, stop=True, rd=[("onesf",), (f"{tg}srow",)], wr=[("PB", 1)])
    if kind == "ln":
        K.emit("vector", "tensor_scalar_mul", NB.mb[:, :], C.pb(0), 1.0 / D,
               rd=[("PB", 0)], wr=[(f"{tg}mb",)])
        K.emit("scalar", "activation", NB.vb[:, :], NB.mb[:, :], AF.Square,
               rd=[(f"{tg}mb",)], wr=[(f"{tg}vb",)])
        K.emit("vector", "scalar_tensor_tensor", NB.invb[:, :], C.pb(1), 1.0 / D,
               NB.vb[:, :], OP.mult, OP.subtract,
               rd=[("PB", 1), (f"{tg}vb",)], wr=[(f"{tg}invb",)])
    else:
        K.emit("vector", "tensor_scalar_mul", NB.invb[:, :], C.pb(1), 1.0 / D,
               rd=[("PB", 1)], wr=[(f"{tg}invb",)])
    rsqrt_ip(K, C, NB.invb[:, :], (f"{tg}invb",), C.consts[:, 0:1])
    if row_cache is not None:
        rws = row_cache
        K.emit("vector", "tensor_scalar_mul", NB.r1[0:1, :],
               NB.srow[0:1, 0:512], 1.0 / D, rd=[(f"{tg}srow",)], wr=[(f"{tg}r1",)])
        K.emit("sync", "dma_start", rws[0:1, g * 512:(g + 1) * 512], NB.r1[0:1, :],
               rd=[(f"{tg}r1",)], wr=[("RWS", "m", g)], stream="rws", dma=True)
        K.emit("sync", "dma_start", rws[1:2, g * 512:(g + 1) * 512], NB.invb[0:1, :],
               rd=[(f"{tg}invb",)], wr=[("RWS", "i", g)], stream="rws", dma=True)
    if kind == "ln":
        K.emit("vector", "tensor_tensor", NB.xg[:, :, :], NB.xg[:, :, :],
               NB.mb[:, :].unsqueeze(1).broadcast_to([128, 8, 512]), OP.subtract,
               rd=[kxg, (f"{tg}mb",)], wr=[kxg])
    if out_xn is not None:
        K.emit("vector", "tensor_tensor", out_xn, NB.xg[:, :, :],
               NB.invb[:, :].unsqueeze(1).broadcast_to([128, 8, 512]), OP.mult,
               rd=[kxg, (f"{tg}invb",)], wr=[xnkey])
    if out_f32 is not None:
        K.emit("vector", "tensor_tensor", out_f32, NB.xg[:, :, :],
               NB.invb[:, :].unsqueeze(1).broadcast_to([128, 8, 512]), OP.mult,
               rd=[kxg, (f"{tg}invb",)], wr=[f32key])
    return kxg



def build_ffn0(K, C, W, scr, src3, dst3, srckey, dstkey, callid):
    nc = K.nc
    dram = W
    xns = scr["XNS"]
    with ExitStack() as ctx:
        sb = lambda name, shape, dt: ctx.enter_context(nc.sbuf_tensor(f"f{callid}_{name}", shape, dt))
        t = lambda n: (f"f{callid}_{n}",)
        NB = NormBufs(nc, ctx, f"f{callid}")
        XN = sb("xn", [128, 8, 512], BF16)
        AMAJ = sb("amaj", [128, 32, 16], F32)
        ZS = sb("zs", [128, 32, 16], F32)
        ZU = sb("zu", [128, 32, 16], F32)
        CS1 = sb("cs1", [128, 32, 16], F32)
        CS2 = sb("cs2", [128, 32, 16], F32)
        TMPS = sb("tmps", [128, 32, 16], F32)
        SM1 = sb("sm1", [128, 32], F32)
        SM2 = sb("sm2", [128, 32], F32)
        TMX = sb("tmx", [128, 32], F32)
        AT = sb("at", [16, 4096], BF16)
        APRE = sb("apre", [16, 512], F32)
        wproj = sb("wproj", [128, 8, 16], BF16)
        wc = sb("wc", [16, 1024], BF16)
        wwin1 = sb("wwin1", [128, 8, 512], BF16)
        bwin1 = sb("bwin1", [128, 4], F32)
        wwout1 = sb("wwout1", [128, 4, 4096], BF16)
        wwin2 = sb("wwin2", [128, 32, 512], BF16)
        bwin2 = sb("bwin2", [128, 4], F32)
        wwout2 = sb("wwout2", [128, 4, 1024], BF16)
        G1 = sb("g1", [128, 4, 512], BF16)
        G2 = sb("g2", [128, 4, 512], BF16)
        Z1 = sb("z1", [128, 4, 512], BF16)
        HB = sb("hb", [128, 16, 512], BF16)
        XG2 = sb("xg2", [128, 8, 512], F32)

        K.emit("gpsimd", "dma_start", wproj[:, :, :],
               W["L0_ffn_proj"].rearrange("(t p) e -> p t e", p=128), wr=[t("wproj")], dma=True)
        K.emit("gpsimd", "dma_start", wc[:, 0:512], W["L0_ffn_C1"], wr=[t("wc")], dma=True)
        K.emit("gpsimd", "dma_start", wc[:, 512:1024], W["L0_ffn_C2"], wr=[t("wc")], dma=True)
        K.emit("gpsimd", "dma_start", wwin1[:, :, :],
               W["L0_ffn_Win1"][0:1024, :].rearrange("(t p) r -> p t r", p=128),
               wr=[t("wwin1")], dma=True)
        K.emit("sync", "dma_start", bwin1[:, :], W["h_win1bias"], wr=[t("bwin1")], dma=True)
        K.emit("gpsimd", "dma_start", wwout1[:, :, :],
               W["h_wout1T"].rearrange("(t p) o -> p t o", p=128), wr=[t("wwout1")], dma=True)
        K.emit("gpsimd", "dma_start", wwin2[:, :, :],
               W["L0_ffn_Win2"][0:4096, :].rearrange("(t p) r -> p t r", p=128),
               wr=[t("wwin2")], dma=True)
        K.emit("sync", "dma_start", bwin2[:, :], W["h_win2bias"], wr=[t("bwin2")], dma=True)
        K.emit("gpsimd", "dma_start", wwout2[:, :, :],
               W["h_wout2T"].rearrange("(t p) o -> p t o", p=128), wr=[t("wwout2")], dma=True)

        for g in range(8):
            gs = slice(g * 512, (g + 1) * 512)
            srk = [srckey] if isinstance(srckey, tuple) else [(srckey, g)]
            norm_group(K, C, NB, src3, g, XN[:, :, :], "ln", srk, t("xn"))
            K.emit("sync", "dma_start", xns[:, :, gs], XN[:, :, :],
                   rd=[t("xn")], wr=[("XNS", g)], dma=True)
            if g == 0:
                dbg_tap(K, W, "ffn_xn", XN[:, :, :], t("xn"))
            for k in range(DT):
                K.emit("tensor", "matmul", C.PS1[0:16, :], wproj[:, k, :], XN[:, k, :],
                       start=(k == 0), stop=(k == DT - 1),
                       rd=[t("wproj"), t("xn")], wr=[("PS1",)])
            K.emit("vector", "tensor_copy", APRE[:, :], C.PS1[0:16, :], rd=[("PS1",)], wr=[t("apre")])
            for j in range(4):
                K.emit("tensor", "transpose", C.pb(0)[0:128, 0:16],
                       APRE[:, j * 128:(j + 1) * 128], C.identf[0:16, 0:16],
                       rd=[t("apre"), ("identf",)], wr=[("PB", 0)])
                K.emit("vector", "tensor_copy", AMAJ[:, g * 4 + j, :], C.pb(0)[0:128, 0:16],
                       rd=[("PB", 0)], wr=[t("amaj")])

        dbg_tap(K, W, "ffn_amaj", AMAJ[:, :, :], t("amaj"), col0=4096)
        am = AMAJ[:, :, :]
        b3 = lambda ap2: ap2.unsqueeze(2).broadcast_to([128, 32, 16])
        bi = lambda ap2: ap2.unsqueeze(1).broadcast_to([128, 32, 16])
        K.emit("vector", "reduce_sum", SM1[:, :], am, axis=AX.X, rd=[t("amaj")], wr=[t("sm1")])
        K.emit("scalar", "activation", TMPS[:, :, :], am, AF.Square, rd=[t("amaj")], wr=[t("tmps")])
        K.emit("vector", "reduce_sum", SM2[:, :], TMPS[:, :, :], axis=AX.X, rd=[t("tmps")], wr=[t("sm2")])
        K.emit("vector", "tensor_scalar_mul", SM1[:, :], SM1[:, :], 1.0 / 16, rd=[t("sm1")], wr=[t("sm1")])
        K.emit("vector", "tensor_tensor", TMX[:, :], SM1[:, :], SM1[:, :], OP.mult,
               rd=[t("sm1")], wr=[t("tmx")])
        K.emit("vector", "scalar_tensor_tensor", SM2[:, :], SM2[:, :], 1.0 / 16, TMX[:, :],
               OP.mult, OP.subtract, rd=[t("sm2"), t("tmx")], wr=[t("sm2")])
        rsqrt_ip(K, C, SM2[:, :], t("sm2"), C.consts[:, 0:1], extra_scale=0.5)
        K.emit("vector", "tensor_tensor", ZU[:, :, :], am, b3(SM1[:, :]), OP.subtract,
               rd=[t("amaj"), t("sm1")], wr=[t("zu")])
        K.emit("vector", "tensor_tensor", ZU[:, :, :], ZU[:, :, :], b3(SM2[:, :]), OP.mult,
               rd=[t("zu"), t("sm2")], wr=[t("zu")])
        K.emit("vector", "reduce_max", SM1[:, :], ZU[:, :, :], axis=AX.X, rd=[t("zu")], wr=[t("sm1")])
        K.emit("vector", "tensor_tensor", ZU[:, :, :], ZU[:, :, :], b3(SM1[:, :]), OP.subtract,
               rd=[t("zu"), t("sm1")], wr=[t("zu")])
        K.emit("vector", "tensor_copy", ZS[:, :, :], ZU[:, :, :], rd=[t("zu")], wr=[t("zs")])
        for (i, j) in sort_network(16):
            K.emit("vector", "tensor_tensor", TMX[:, :], ZS[:, :, i], ZS[:, :, j], OP.max,
                   rd=[t("zs")], wr=[t("tmx")])
            K.emit("vector", "tensor_tensor", ZS[:, :, j], ZS[:, :, i], ZS[:, :, j], OP.min,
                   rd=[t("zs")], wr=[t("zs")])
            K.emit("vector", "tensor_copy", ZS[:, :, i], TMX[:, :], rd=[t("tmx")], wr=[t("zs")])
        K.emit("vector", "tensor_copy", CS1[:, :, :], ZS[:, :, :], rd=[t("zs")], wr=[t("cs1")])
        K.emit("scalar", "activation", CS2[:, :, :], ZS[:, :, :], AF.Square, rd=[t("zs")], wr=[t("cs2")])
        for CS, kk in ((CS1, "cs1"), (CS2, "cs2")):
            for sh in (1, 2, 4, 8):
                K.emit("vector", "tensor_tensor", TMPS[:, :, sh:16], CS[:, :, sh:16],
                       CS[:, :, 0:16 - sh], OP.add, rd=[t(kk)], wr=[t("tmps")])
                K.emit("vector", "tensor_copy", TMPS[:, :, 0:sh], CS[:, :, 0:sh],
                       rd=[t(kk)], wr=[t("tmps")])
                K.emit("vector", "tensor_copy", CS[:, :, :], TMPS[:, :, :],
                       rd=[t("tmps")], wr=[t(kk)])
        K.emit("vector", "tensor_tensor", CS1[:, :, :], CS1[:, :, :], bi(C.invk[:, :]), OP.mult,
               rd=[t("cs1"), ("invk",)], wr=[t("cs1")])
        K.emit("vector", "tensor_tensor", CS2[:, :, :], CS2[:, :, :], bi(C.invk[:, :]), OP.mult,
               rd=[t("cs2"), ("invk",)], wr=[t("cs2")])
        K.emit("vector", "tensor_tensor", TMPS[:, :, :], CS1[:, :, :], CS1[:, :, :], OP.mult,
               rd=[t("cs1")], wr=[t("tmps")])
        K.emit("vector", "tensor_tensor", CS2[:, :, :], CS2[:, :, :], TMPS[:, :, :], OP.subtract,
               rd=[t("cs2"), t("tmps")], wr=[t("cs2")])
        K.emit("vector", "tensor_tensor", CS2[:, :, :], CS2[:, :, :], bi(C.kval[:, :]), OP.mult,
               rd=[t("cs2"), ("kval",)], wr=[t("cs2")])
        K.emit("vector", "tensor_scalar", CS2[:, :, :], CS2[:, :, :], -1.0, 1.0, OP.mult, OP.add,
               rd=[t("cs2")], wr=[t("cs2")])
        K.emit("vector", "tensor_tensor", CS2[:, :, :], CS2[:, :, :], bi(C.invk[:, :]), OP.mult,
               rd=[t("cs2"), ("invk",)], wr=[t("cs2")])
        K.emit("vector", "tensor_scalar_max", CS2[:, :, :], CS2[:, :, :], 0.0,
               rd=[t("cs2")], wr=[t("cs2")])
        K.emit("scalar", "activation", CS2[:, :, :], CS2[:, :, :], AF.Ln, bias=C.consts[:, 3:4],
               rd=[t("cs2"), ("consts",)], wr=[t("cs2")])
        K.emit("scalar", "activation", CS2[:, :, :], CS2[:, :, :], AF.Exp, scale=0.5,
               rd=[t("cs2")], wr=[t("cs2")])
        K.emit("vector", "tensor_tensor", CS1[:, :, :], CS1[:, :, :], CS2[:, :, :], OP.subtract,
               rd=[t("cs1"), t("cs2")], wr=[t("cs1")])
        K.emit("vector", "tensor_tensor", TMPS[:, :, :], CS1[:, :, :], ZS[:, :, :], OP.is_le,
               rd=[t("cs1"), t("zs")], wr=[t("tmps")])
        K.emit("vector", "reduce_sum", SM1[:, :], TMPS[:, :, :], axis=AX.X, rd=[t("tmps")], wr=[t("sm1")])
        K.emit("vector", "tensor_scalar_add", SM1[:, :], SM1[:, :], -1.0, rd=[t("sm1")], wr=[t("sm1")])
        K.emit("vector", "tensor_tensor", TMPS[:, :, :], bi(C.kidx[:, :]), b3(SM1[:, :]), OP.is_equal,
               rd=[("kidx",), t("sm1")], wr=[t("tmps")])
        K.emit("vector", "tensor_tensor", TMPS[:, :, :], TMPS[:, :, :], CS1[:, :, :], OP.mult,
               rd=[t("tmps"), t("cs1")], wr=[t("tmps")])
        K.emit("vector", "reduce_sum", SM2[:, :], TMPS[:, :, :], axis=AX.X, rd=[t("tmps")], wr=[t("sm2")])
        K.emit("vector", "tensor_tensor", ZU[:, :, :], ZU[:, :, :], b3(SM2[:, :]), OP.subtract,
               rd=[t("zu"), t("sm2")], wr=[t("zu")])
        K.emit("vector", "tensor_scalar_max", ZU[:, :, :], ZU[:, :, :], 0.0, rd=[t("zu")], wr=[t("zu")])
        K.emit("scalar", "activation", ZU[:, :, :], ZU[:, :, :], AF.Square, rd=[t("zu")], wr=[t("zu")])
        dbg_tap(K, W, "ffn_p", ZU[:, :, :], t("zu"), col0=6144)
        for j in range(32):
            K.emit("tensor", "transpose", C.PS1[0:16, 0:128], ZU[:, j, :], C.identf[:, :],
                   rd=[t("zu"), ("identf",)], wr=[("PS1",)])
            K.emit("vector", "tensor_copy", AT[:, j * 128:(j + 1) * 128], C.PS1[0:16, 0:128],
                   rd=[("PS1",)], wr=[t("at")])

        for g in range(8):
            gs = slice(g * 512, (g + 1) * 512)
            kxn = t("xn")
            K.emit("sync", "dma_start", XN[:, :, :], xns[:, :, gs],
                   rd=[("XNS", g)], wr=[kxn], dma=True)
            if g == 0:
                dbg_tap(K, W, "ffn_xnc", XN[:, :, :], kxn, col0=10240)
            if g == 0:
                dbg_tap(K, W, "ffn_at", AT[:, :], t("at"))
            for GB, coff, gk in ((G1, 0, "g1"), (G2, 512, "g2")):
                for m in range(4):
                    K.emit("tensor", "matmul", C.pa(m), wc[:, coff + m * 128:coff + (m + 1) * 128],
                           AT[:, gs], start=True, stop=True, rd=[t("wc"), t("at")], wr=[("PA", m)])
                for m in range(4):
                    K.emit("vector", "tensor_copy", GB[:, m, :], C.pa(m),
                           rd=[("PA", m)], wr=[t(gk)])
            if g == 0:
                dbg_tap(K, W, "ffn_g1", G1[:, :, :], t("g1"), col0=8192)
            for m in range(4):
                pk = ("PB", m % 2)
                pap = C.pb(m % 2)
                for k in range(DT):
                    K.emit("tensor", "matmul", pap, wwin1[:, k, m * 128:(m + 1) * 128],
                           XN[:, k, :], start=(k == 0), stop=(k == DT - 1),
                           rd=[t("wwin1"), kxn], wr=[pk])
                if g == 0 and m == 0 and "ffn_z1raw" in DBG:
                    K.emit("vector", "tensor_copy", XG2[:, 0, :], pap, rd=[pk], wr=[t("xg2")])
                    dbg_tap(K, W, "ffn_z1raw", XG2[:, 0, :], t("xg2"), col0=14336)
                K.emit("vector", "scalar_tensor_tensor", Z1[:, m, :], pap, bwin1[:, m:m + 1],
                       G1[:, m, :], OP.add, OP.mult, rd=[pk, t("bwin1"), t("g1")], wr=[t("z1")])
            for half in range(2):
                for mm_ in range(16):
                    m = half * 16 + mm_
                    pk = ("PA", mm_ % 2)
                    pap = C.pa(mm_ % 2)
                    for k in range(4):
                        K.emit("tensor", "matmul", pap, wwout1[:, k, m * 128:(m + 1) * 128],
                               Z1[:, k, :], start=(k == 0), stop=(k == 3),
                               rd=[t("wwout1"), t("z1")], wr=[pk])
                    K.emit("scalar", "activation", HB[:, mm_, :], pap, AF.Gelu,
                           rd=[pk], wr=[t("hb")])
                for mm_ in range(16):
                    k = half * 16 + mm_
                    for m in range(4):
                        pk = ("PB", m) if m < 2 else ("PA", m)
                        pap = C.pb(m) if m < 2 else C.pa(m)
                        K.emit("tensor", "matmul", pap, wwin2[:, k, m * 128:(m + 1) * 128],
                               HB[:, mm_, :], start=(k == 0), stop=(k == 31),
                               rd=[t("wwin2"), t("hb")], wr=[pk])
            for m in range(4):
                pk = ("PB", m) if m < 2 else ("PA", m)
                pap = C.pb(m) if m < 2 else C.pa(m)
                K.emit("vector", "scalar_tensor_tensor", Z1[:, m, :], pap, bwin2[:, m:m + 1],
                       G2[:, m, :], OP.add, OP.mult, rd=[pk, t("bwin2"), t("g2")], wr=[t("z1")])
            if g == 0:
                dbg_tap(K, W, "ffn_z1", Z1[:, :, :], t("z1"), col0=8192)

            K.emit("sync", "dma_start", XG2[:, :, :], src3[:, :, gs],
                   rd=([srckey] if isinstance(srckey, tuple) else [(srckey, g)]),
                   wr=[t("xg2")], dma=True)
            for m in range(8):
                pk = ("PA", m % 2)
                pap = C.pa(m % 2)
                for k in range(4):
                    K.emit("tensor", "matmul", pap, wwout2[:, k, m * 128:(m + 1) * 128],
                           Z1[:, k, :], start=(k == 0), stop=(k == 3),
                           rd=[t("wwout2"), t("z1")], wr=[pk])
                K.emit("vector", "scalar_tensor_tensor", XG2[:, m, :], pap, 0.5,
                       XG2[:, m, :], OP.mult, OP.add, rd=[pk, t("xg2")], wr=[t("xg2")])
            K.emit("sync", "dma_start", dst3[:, :, gs], XG2[:, :, :],
                   rd=[t("xg2")], wr=[(dstkey, g)], dma=True)
    K.barrier()



def build_ln_pass(K, C, src3, dst3, srckey, dstkey, tag, dst_bf=None, dstbfkey=None,
                  row_cache=None):
    nc = K.nc
    with ExitStack() as ctx:
        NB = NormBufs(nc, ctx, tag)
        XO = ctx.enter_context(nc.sbuf_tensor(f"{tag}_xo", [128, 8, 512], F32))
        XOB = ctx.enter_context(nc.sbuf_tensor(f"{tag}_xob", [128, 8, 512], BF16)) \
            if dst_bf is not None else None
        for g in range(8):
            gs = slice(g * 512, (g + 1) * 512)
            srk = [srckey] if isinstance(srckey, tuple) else [(srckey, g)]
            norm_group(K, C, NB, src3, g,
                       XOB[:, :, :] if XOB is not None else None, "ln",
                       srk, (f"{tag}xob",),
                       out_f32=XO[:, :, :] if dst3 is not None else None,
                       f32key=(f"{tag}xo",), row_cache=row_cache)
            if dst3 is not None:
                K.emit("sync", "dma_start", dst3[:, :, gs], XO[:, :, :],
                       rd=[(f"{tag}xo",)], wr=[(dstkey, g)], dma=True)
            if dst_bf is not None:
                K.emit("sync", "dma_start", dst_bf[:, :, gs], XOB[:, :, :],
                       rd=[(f"{tag}xob",)], wr=[(dstbfkey, g)], dma=True)
    K.barrier()


WEIGHT_SHAPES = {
    "L0_attn_in_proj_w": (1024, 4256),
    "L0_attn_attn_w": (1024, 1152),
    "L0_attn_proj_w": (1024, 1024),
    "L0_attn_recept_w": (1024, 1024),
    "L0_ffn_proj": (1024, 16),
    "L0_ffn_C1": (16, 512),
    "L0_ffn_Win1": (1025, 512),
    "L0_ffn_C2": (16, 512),
    "L0_ffn_Win2": (4097, 512),
}

HOST_SHAPES = {
    "c_map64": (32, 128),
    "c_sel32": (32, 16),
    "h_wout1T": (512, 4096),
    "h_wout2T": (512, 1024),
    "h_win1bias": (128, 4),
    "h_win2bias": (128, 4),
    "h_convw": (128, 68),
    "h_convb": (128, 17),
    "h_mnorm": (128, 16),
    "h_dtb": (32, 1),
    "h_alog": (32, 1),
    "h_dvec": (32, 1),
    "h_maak": (128, 8),
    "h_maar": (128, 8),
    "h_outproj_cm": (8, 2048, 128),
    "h_key_cm": (32, 1024, 128),
    "h_value_cm": (8, 4096, 128),
    "h_wqT": (1024, 1024),
    "h_wkT": (1024, 1024),
    "h_wvT": (1024, 1024),
    "h_woT": (1024, 1024),
    "h_gain": (1, 1),
    "h_routerT": (1024, 8),
    "c_tokids": (128, 32),
    "c_iota128": (128, 1),
    "h_ebias": (8, 1),
    "h_w1": (9, 1024, 2048),
    "h_w3": (9, 1024, 2048),
    "h_w2": (9, 2048, 1024),
}

CONST_SHAPES = {
    "c_identf": (128, 128),
    "c_mtri": (128, 1024),
    "c_kidx": (128, 16),
    "c_invk": (128, 16),
    "c_kval": (128, 16),
}


def build_program(phases=("all",)):
    nc = bass.Bass("TRN2", target_bir_lowering=False)
    dram = {}

    def din(name, shape, dt=F32):
        dram[name] = nc.dram_tensor(name, list(shape), dt, kind="ExternalInput").ap()

    din("x_in", (128, 8, 4096))
    for n, s in WEIGHT_SHAPES.items():
        din(n, s)
    for n, s in HOST_SHAPES.items():
        din(n, s, I32 if n == "c_tokids" else F32)
    for n, s in CONST_SHAPES.items():
        din(n, s)
    din("c_ident", (128, 128), BF16)
    out = nc.dram_tensor("out", [128, 8, 4096], F32, kind="ExternalOutput").ap()
    dram["dbg"] = nc.dram_tensor("dbg", [128, 16384], F32, kind="ExternalOutput").ap()
    scr = {}
    for n in ("XB", "XC", "XD", "XE", "XF", "S0", "S1", "S2", "ACC"):
        scr[n] = nc.dram_tensor(f"scr_{n}", [128, 8, 4096], F32).ap()
    for n in ("SNM", "SNA", "XNS"):
        scr[n] = nc.dram_tensor(f"scr_{n}", [128, 8, 4096], BF16).ap()
    scr["CWS"] = nc.dram_tensor("scr_CWS", [8, 4096], F32).ap()
    scr["IPB"] = nc.dram_tensor("scr_IPB", [34, 128, 8, 128], BF16).ap()
    scr["SNMT"] = nc.dram_tensor("scr_SNMT", [4097, 1024], BF16).ap()
    scr["ACCT"] = nc.dram_tensor("scr_ACCT", [4097, 1024], F32).ap()
    scr["IDXD"] = nc.dram_tensor("scr_IDXD", [MOE_NSB * 512 + 128], I32).ap()
    scr["CWL"] = nc.dram_tensor("scr_CWL", [MOE_NSB * 512 + 128], F32).ap()
    scr["W1B"] = nc.dram_tensor("scr_W1B", [9, 128, 8, 2048], BF16).ap()
    scr["W3B"] = nc.dram_tensor("scr_W3B", [9, 128, 8, 2048], BF16).ap()
    scr["W2B"] = nc.dram_tensor("scr_W2B", [9, 128, 16, 1024], BF16).ap()
    scr["RWS"] = nc.dram_tensor("scr_RWS", [2, 4096], F32).ap()
    scr["WNS"] = nc.dram_tensor("scr_WNS", [4, 128, 8, 1024], F32).ap()

    def has(p):
        return "all" in phases or p in phases

    with ExitStack() as ctx:
        C = Ctx()
        alloc_common(nc, ctx, C)
        K = Kit(nc, ctx)
        load_consts(K, C, dram)
        if MOE_SPARSE and (has("moe1") or has("moe2")):
            for e in range(9):
                K.emit("gpsimd", "dma_start", scr["W1B"][e],
                       dram["h_w1"][e].rearrange("(t p) h -> p t h", p=128),
                       wr=[("WPC", "x")], stream="wpc", dma=True)
                K.emit("gpsimd", "dma_start", scr["W3B"][e],
                       dram["h_w3"][e].rearrange("(t p) h -> p t h", p=128),
                       wr=[("WPC", "y")], stream="wpc", dma=True)
                K.emit("gpsimd", "dma_start", scr["W2B"][e],
                       dram["h_w2"][e].rearrange("(t p) d -> p t d", p=128),
                       wr=[("WPC", "z")], stream="wpc", dma=True)
        if has("att0") and has("mamba"):
            ip = dram["L0_attn_in_proj_w"]
            for m in range(34):
                mw = 128 if m < 33 else 32
                K.emit("gpsimd", "dma_start", scr["IPB"][m][:, :, 0:mw],
                       ip.rearrange("(t p) m -> p t m", p=128)[:, :, m * 128:m * 128 + mw],
                       wr=[("IPB",)], stream="ipb", dma=True)

        cur, curkey = dram["x_in"], ("x_in",)

        def nxt(name):
            return scr[name], name

        if has("ffn0a"):
            d, dk = nxt("XB")
            build_ffn0(K, C, dram, scr, cur, d, curkey, dk, 0)
            cur, curkey = d, dk
        if has("att0"):
            build_ln_pass(K, C, cur, scr["S0"], curkey, "S0", "lnA")
            s, skey = scr["S0"], "S0"
            if has("mamba"):
                build_mamba(K, C, dram, scr, s, scr["S1"], skey, "S1")
                s, skey = scr["S1"], "S1"
            if has("mqa"):
                build_mqa(K, C, dram, s, scr["S2"], skey, "S2")
                s, skey = scr["S2"], "S2"
            d, dk = nxt("XC")
            build_cmix(K, C, dram, s, cur, d, skey, curkey, dk, skip=not has("cmix"))
            cur, curkey = d, dk
        if has("ffn0b"):
            d, dk = nxt("XD")
            build_ffn0(K, C, dram, scr, cur, d, curkey, dk, 1)
            cur, curkey = d, dk
        if has("moe1"):
            d, dk = nxt("XE")
            (build_moe_sparse if MOE_SPARSE else build_moe)(K, C, dram, scr, cur, d, curkey, dk, 0)
            cur, curkey = d, dk
        if has("att1"):
            d, dk = nxt("XF")
            build_att1(K, C, dram, scr, cur, d, curkey, dk)
            cur, curkey = d, dk
        if has("moe2"):
            (build_moe_sparse if MOE_SPARSE else build_moe)(K, C, dram, scr, cur, out, curkey, "out", 1)
            cur, curkey = out, "out"

        if curkey != "out":
            cp = ctx.enter_context(nc.sbuf_tensor("cpbuf", [128, 8, 512], F32))
            for g in range(8):
                gs = slice(g * 512, (g + 1) * 512)
                rdk = [curkey if isinstance(curkey, tuple) else (curkey, g)]
                K.emit("sync", "dma_start", cp[:, :, :], cur[:, :, gs], rd=rdk,
                       wr=[("cp",)], dma=True)
                K.emit("sync", "dma_start", out[:, :, gs], cp[:, :, :], rd=[("cp",)],
                       wr=[("out", g)], dma=True)
        K.finish()
        blk = ctx.enter_context(nc.Block())
        K.replay_into(blk)
    return nc




def build_moe(K, C, W, scr, src3, dst3, srckey, dstkey, callid):
    nc = K.nc
    snm, acc = scr["SNM"], scr["ACC"]
    with ExitStack() as ctx:
        sb = lambda name, shape, dt: ctx.enter_context(nc.sbuf_tensor(f"e{callid}_{name}", shape, dt))
        t = lambda n: (f"e{callid}_{n}",)
        NB = NormBufs(nc, ctx, f"e{callid}")
        CWROW = sb("cwrow", [1, 1024], F32)
        ctxA = ExitStack()
        sbA = lambda name, shape, dt: ctxA.enter_context(nc.sbuf_tensor(f"e{callid}_{name}", shape, dt))
        XN = sbA("xn", [128, 8, 512], BF16)
        CWT = sbA("cwt", [8, 4096], F32)
        LGR = sbA("lgr", [8, 512], F32)
        LGT = sbA("lgt", [128, 32, 8], F32)
        SM1 = sbA("sm1", [128, 32], F32)
        SM2 = sbA("sm2", [128, 32], F32)
        MK1 = sbA("mk1", [128, 32, 8], F32)
        MK2 = sbA("mk2", [128, 32, 8], F32)
        wrt = sbA("wrt", [128, 8, 8], BF16)
        ebv = sbA("ebv", [8, 1], F32)

        K.emit("gpsimd", "dma_start", wrt[:, :, :],
               W["h_routerT"].rearrange("(t p) e -> p t e", p=128), wr=[t("wrt")], dma=True)
        K.emit("gpsimd", "dma_start", WX[:, :, :],
               W["h_w1"][8].rearrange("(t p) h -> p t h", p=128), wr=[t("wx")],
               stream="wx", dma=True)
        K.emit("gpsimd", "dma_start", WY[:, :, :],
               W["h_w3"][8].rearrange("(t p) h -> p t h", p=128), wr=[t("wy")],
               stream="wy", dma=True)
        K.emit("gpsimd", "dma_start", WZ[:, :, :],
               W["h_w2"][8].rearrange("(t p) d -> p t d", p=128), wr=[t("wz")],
               stream="wz", dma=True)
        K.emit("sync", "dma_start", ebv[:, :], W["h_ebias"], wr=[t("ebv")], dma=True)

        for g in range(8):
            gs = slice(g * 512, (g + 1) * 512)
            srk = [srckey] if isinstance(srckey, tuple) else [(srckey, g)]
            norm_group(K, C, NB, src3, g, XN[:, :, :], "ln", srk, t("xn"))
            K.emit("sync", "dma_start", snm[:, :, gs], XN[:, :, :],
                   rd=[t("xn")], wr=[("SNM", g)], dma=True)
            for k in range(DT):
                K.emit("tensor", "matmul", C.PS1[0:8, :], wrt[:, k, :], XN[:, k, :],
                       start=(k == 0), stop=(k == DT - 1),
                       rd=[t("wrt"), t("xn")], wr=[("PS1",)])
            K.emit("vector", "tensor_scalar_add", LGR[:, :], C.PS1[0:8, :], ebv[:, 0:1],
                   rd=[("PS1",), t("ebv")], wr=[t("lgr")])
            for j in range(4):
                K.emit("tensor", "transpose", C.pb(0)[0:128, 0:8],
                       LGR[:, j * 128:(j + 1) * 128], C.identf[0:8, 0:8],
                       rd=[t("lgr"), ("identf",)], wr=[("PB", 0)])
                K.emit("vector", "tensor_copy", LGT[:, g * 4 + j, :], C.pb(0)[0:128, 0:8],
                       rd=[("PB", 0)], wr=[t("lgt")])
        b3 = lambda ap2: ap2.unsqueeze(2).broadcast_to([128, 32, 8])
        K.emit("vector", "reduce_max", SM1[:, :], LGT[:, :, :], axis=AX.X,
               rd=[t("lgt")], wr=[t("sm1")])
        K.emit("vector", "tensor_tensor", MK1[:, :, :], LGT[:, :, :], b3(SM1[:, :]),
               OP.is_equal, rd=[t("lgt"), t("sm1")], wr=[t("mk1")])
        K.emit("vector", "scalar_tensor_tensor", MK2[:, :, :], MK1[:, :, :], NEG,
               LGT[:, :, :], OP.mult, OP.add, rd=[t("mk1"), t("lgt")], wr=[t("mk2")])
        K.emit("vector", "reduce_max", SM2[:, :], MK2[:, :, :], axis=AX.X,
               rd=[t("mk2")], wr=[t("sm2")])
        K.emit("vector", "tensor_tensor", MK2[:, :, :], MK2[:, :, :], b3(SM2[:, :]),
               OP.is_equal, rd=[t("mk2"), t("sm2")], wr=[t("mk2")])
        K.emit("vector", "tensor_tensor", SM1[:, :], SM1[:, :], SM2[:, :], OP.subtract,
               rd=[t("sm1"), t("sm2")], wr=[t("sm1")])
        K.emit("scalar", "activation", SM1[:, :], SM1[:, :], AF.Sigmoid,
               rd=[t("sm1")], wr=[t("sm1")])
        K.emit("vector", "tensor_scalar", SM2[:, :], SM1[:, :], -1.0, 1.0, OP.mult, OP.add,
               rd=[t("sm1")], wr=[t("sm2")])
        K.emit("vector", "tensor_tensor", MK1[:, :, :], MK1[:, :, :], b3(SM1[:, :]), OP.mult,
               rd=[t("mk1"), t("sm1")], wr=[t("mk1")])
        K.emit("vector", "tensor_tensor", MK2[:, :, :], MK2[:, :, :], b3(SM2[:, :]), OP.mult,
               rd=[t("mk2"), t("sm2")], wr=[t("mk2")])
        K.emit("vector", "tensor_tensor", MK1[:, :, :], MK1[:, :, :], MK2[:, :, :], OP.add,
               rd=[t("mk1"), t("mk2")], wr=[t("mk1")])
        dbg_tap(K, W, "moe_cwtok", MK1[:, :, :], t("mk1"), col0=8192)
        for j in range(32):
            K.emit("tensor", "transpose", C.pb(0)[0:8, 0:128], MK1[:, j, :], C.identf[:, :],
                   rd=[t("mk1"), ("identf",)], wr=[("PB", 0)])
            K.emit("vector", "tensor_copy", CWT[:, j * 128:(j + 1) * 128], C.pb(0)[0:8, 0:128],
                   rd=[("PB", 0)], wr=[t("cwt")])
        K.emit("sync", "dma_start", scr["CWS"], CWT[:, :], rd=[t("cwt")], wr=[("CWS",)], dma=True)

        ctxA.close()
        K.barrier()
        ctxB = ExitStack()
        sbB = lambda name, shape, dt: ctxB.enter_context(nc.sbuf_tensor(f"e{callid}_{name}", shape, dt))
        WX = sbB("wx", [128, 8, 2048], BF16)
        WY = sbB("wy", [128, 8, 2048], BF16)
        WZ = sbB("wz", [128, 16, 1024], BF16)
        XNC = sbB("xnc", [128, 8, 1024], BF16)
        EO = sbB("eo", [128, 16, 512], BF16)
        CWB = sbB("cwb", [128, 512], BF16)
        W2E = sbB("w2e", [128, 4, 512], F32)
        TM1 = sbB("tm1", [128, 2, 512], BF16)
        TM2 = sbB("tm2", [128, 2, 512], BF16)
        for e in range(9):
            K.emit("gpsimd", "dma_start", WZ[:, :, :],
                   W["h_w2"][e].rearrange("(t p) d -> p t d", p=128),
                   wr=[t("wz")], stream="wz", dma=True)
            if e == 0:
                K.emit("gpsimd", "dma_start", WX[:, :, :],
                       W["h_w1"][0].rearrange("(t p) h -> p t h", p=128),
                       wr=[t("wx")], stream="wx", dma=True)
                K.emit("gpsimd", "dma_start", WY[:, :, :],
                       W["h_w3"][0].rearrange("(t p) h -> p t h", p=128),
                       wr=[t("wy")], stream="wy", dma=True)
            for ch in range(4):
                cs = slice(ch * 1024, (ch + 1) * 1024)
                K.emit("sync", "dma_start", XNC[:, :, :], snm[:, :, cs],
                       rd=[("SNM", 2 * ch), ("SNM", 2 * ch + 1)], wr=[t("xnc")],
                       stream="xnc", dma=True)
                if e < 8:
                    K.emit("sync", "dma_start", CWROW[:, :], scr["CWS"][e:e + 1, cs],
                           rd=[("CWS",)], wr=[t("cwrow")], stream="cwrow", dma=True)
                for n2 in range(2):
                    ns = slice(n2 * 512, (n2 + 1) * 512)
                    if e < 8:
                        K.emit("tensor", "matmul", C.pb(0), C.onesf[0:1, 0:128],
                               CWROW[0:1, ns],
                               start=True, stop=True,
                               rd=[("onesf",), t("cwrow")], wr=[("PB", 0)])
                        K.emit("vector", "tensor_copy", CWB[:, :], C.pb(0),
                               rd=[("PB", 0)], wr=[t("cwb")])
                    elif ch == 0 and n2 == 0:
                        K.emit("vector", "memset", CWB[:, :], 1.0, wr=[t("cwb")])
                    for ht in range(16):
                        pg, pu = (0, 1) if ht % 2 == 0 else (2, 3)
                        par = ht % 2
                        for k in range(DT):
                            K.emit("tensor", "matmul", C.pa(pg), WX[:, k, ht * 128:(ht + 1) * 128],
                                   XNC[:, k, ns], start=(k == 0), stop=(k == DT - 1),
                                   rd=[t("wx"), t("xnc")], wr=[("PA", pg)])
                        for k in range(DT):
                            K.emit("tensor", "matmul", C.pa(pu), WY[:, k, ht * 128:(ht + 1) * 128],
                                   XNC[:, k, ns], start=(k == 0), stop=(k == DT - 1),
                                   rd=[t("wy"), t("xnc")], wr=[("PA", pu)])
                        K.emit("scalar", "activation", TM1[:, par, :], C.pa(pg), AF.Silu,
                               rd=[("PA", pg)], wr=[t("tm1")])
                        K.emit("vector", "tensor_tensor", TM2[:, par, :], C.pa(pu), CWB[:, :],
                               OP.mult, rd=[("PA", pu), t("cwb")], wr=[t("tm2")])
                        K.emit("vector", "tensor_tensor", EO[:, ht, :], TM1[:, par, :],
                               TM2[:, par, :], OP.mult, rd=[t("tm1"), t("tm2")], wr=[t("eo")])
                    if ch == 3 and n2 == 1 and e < 8:
                        K.emit("gpsimd", "dma_start", WX[:, :, :],
                               W["h_w1"][e + 1].rearrange("(t p) h -> p t h", p=128),
                               wr=[t("wx")], stream="wx", dma=True)
                        K.emit("gpsimd", "dma_start", WY[:, :, :],
                               W["h_w3"][e + 1].rearrange("(t p) h -> p t h", p=128),
                               wr=[t("wy")], stream="wy", dma=True)
                    for half in range(2):
                        for mm_ in range(4):
                            m = half * 4 + mm_
                            pk = ("PA", 2 + mm_ % 2)
                            pap = C.pa(2 + mm_ % 2)
                            for k in range(16):
                                K.emit("tensor", "matmul", pap, WZ[:, k, m * 128:(m + 1) * 128],
                                       EO[:, k, :], start=(k == 0), stop=(k == 15),
                                       rd=[t("wz"), t("eo")], wr=[pk])
                            K.emit("vector", "tensor_copy", W2E[:, mm_, :], pap,
                                   rd=[pk], wr=[t("w2e")])
                        K.emit("gpsimd", "dma_start", acc[:, half * 4:(half + 1) * 4, cs][:, :, ns],
                               W2E[:, :, :], rd=[t("w2e")], wr=[("ACC", ch, n2, half)],
                               stream="acc", dma=True,
                               **({} if e == 0 else {"accum_op": OP.add}))
        ctxB.close()
        K.barrier()
        XG2 = sb("xg2", [128, 8, 512], F32)
        for g in range(8):
            gs = slice(g * 512, (g + 1) * 512)
            srk = [srckey] if isinstance(srckey, tuple) else [(srckey, g)]
            K.emit("sync", "dma_start", XG2[:, :, :], src3[:, :, gs], rd=srk,
                   wr=[t("xg2")], stream="xg2", dma=True)
            kxg = (f"e{callid}xg",)
            K.emit("sync", "dma_start", NB.xg[:, :, :], acc[:, :, gs],
                   rd=[("ACC", g // 2, g % 2, 0), ("ACC", g // 2, g % 2, 1)], wr=[kxg],
                   stream="accld", dma=True)
            K.emit("vector", "scalar_tensor_tensor", XG2[:, :, :], NB.xg[:, :, :], 0.5,
                   XG2[:, :, :], OP.mult, OP.add,
                   rd=[kxg, t("xg2")], wr=[t("xg2")])
            K.emit("sync", "dma_start", dst3[:, :, gs], XG2[:, :, :],
                   rd=[t("xg2")], wr=[(dstkey, g)], dma=True)
    K.barrier()




def build_moe_sparse(K, C, W, scr, src3, dst3, srckey, dstkey, callid):
    nc = K.nc
    snm, snmt = scr["SNM"], scr["SNMT"]
    acct, idxd, cwl = scr["ACCT"], scr["IDXD"], scr["CWL"]
    w1b, w3b, w2b = scr["W1B"], scr["W3B"], scr["W2B"]
    NSB = MOE_NSB
    DUMP = NSB * 512
    with ExitStack() as ctx:
        sb = lambda name, shape, dt: ctx.enter_context(nc.sbuf_tensor(f"e{callid}_{name}", shape, dt))
        t = lambda n: (f"e{callid}_{n}",)
        OWNF = sb("ownf", [1, NSB], F32)
        WX = sb("wx", [128, 8, 2048], BF16)
        WY = sb("wy", [128, 8, 2048], BF16)
        WZ = sb("wz", [128, 16, 1024], BF16)
        ctxA = ExitStack()
        sbA = lambda name, shape, dt: ctxA.enter_context(nc.sbuf_tensor(f"e{callid}_{name}", shape, dt))
        NB = NormBufs(nc, ctxA, f"e{callid}")
        XN = sbA("xn", [128, 8, 512], BF16)
        TMJ = sbA("tmj", [128, 4, 8, 128], BF16)
        CWT = sbA("cwt", [8, 4096], F32)
        CUM = sbA("cum", [8, 4096], F32)
        LGR = sbA("lgr", [8, 512], F32)
        LGT = sbA("lgt", [128, 32, 8], F32)
        SM1 = sbA("sm1", [128, 32], F32)
        SM2 = sbA("sm2", [128, 32], F32)
        MK1 = sbA("mk1", [128, 32, 8], F32)
        MK2 = sbA("mk2", [128, 32, 8], F32)
        CWM = sbA("cwm", [128, 32, 8], F32)
        PTOK = sbA("ptok", [128, 32, 8], F32)
        P12 = sbA("p12", [128, 32, 2], F32)
        IDXI2 = sbA("idxi2", [128, 32, 2], I32)
        wrt = sbA("wrt", [128, 8, 8], BF16)
        ebv = sbA("ebv", [8, 1], F32)
        TOKI = sbA("toki", [128, 32], I32)
        ZB = sbA("zbz", [128, 128], I32)
        ZBF = sbA("zbf", [128, 128], F32)
        ZROW = sbA("zrow", [1, 1024], BF16)
        ZROWF = sbA("zrowf", [1, 1024], F32)
        CNTR = sbA("cntr", [1, 8], F32)
        CNTI = sbA("cnti", [1, 8], I32)
        T8 = sbA("t8", [1, 8], F32)
        OFFC = sbA("offc", [8, 1], F32)

        K.emit("gpsimd", "dma_start", wrt[:, :, :],
               W["h_routerT"].rearrange("(t p) e -> p t e", p=128), wr=[t("wrt")], dma=True)
        K.emit("gpsimd", "dma_start", WX[:, :, :],
               W["h_w1"][8].rearrange("(t p) h -> p t h", p=128), wr=[t("wx")],
               stream="wx", dma=True)
        K.emit("gpsimd", "dma_start", WY[:, :, :],
               W["h_w3"][8].rearrange("(t p) h -> p t h", p=128), wr=[t("wy")],
               stream="wy", dma=True)
        K.emit("gpsimd", "dma_start", WZ[:, :, :],
               W["h_w2"][8].rearrange("(t p) d -> p t d", p=128), wr=[t("wz")],
               stream="wz", dma=True)
        K.emit("sync", "dma_start", ebv[:, :], W["h_ebias"], wr=[t("ebv")], dma=True)
        K.emit("sync", "dma_start", TOKI[:, :], W["c_tokids"], wr=[t("toki")], dma=True)
        K.emit("vector", "memset", ZB[:, :], 4096, wr=[t("zb")])
        K.emit("vector", "memset", ZBF[:, :], 0.0, wr=[t("zbf")])
        K.emit("vector", "memset", ZROW[:, :], 0.0, wr=[t("zrow")])
        K.emit("vector", "memset", ZROWF[:, :], 0.0, wr=[t("zrowf")])
        K.emit("sync", "dma_start", snmt[4096:4097, :], ZROW[:, :], rd=[t("zrow")],
               wr=[("SNMTZ",)], stream="snmtz", dma=True)
        K.emit("sync", "dma_start", acct[4096:4097, :], ZROWF[:, :], rd=[t("zrowf")],
               wr=[("ACCTZ",)], stream="snmtz", dma=True)
        K.emit("gpsimd", "dma_start", idxd.rearrange("(p f) -> p f", p=128),
               ZB[:, 0:(NSB * 512 + 128) // 128], rd=[t("zb")], wr=[("IDXD",)],
               stream="idxz", dma=True)
        K.emit("gpsimd", "dma_start", cwl.rearrange("(p f) -> p f", p=128),
               ZBF[:, 0:(NSB * 512 + 128) // 128], rd=[t("zbf")], wr=[("CWL",)],
               stream="idxz", dma=True)
        for g in range(8):
            gs = slice(g * 512, (g + 1) * 512)
            srk = [srckey] if isinstance(srckey, tuple) else [(srckey, g)]
            norm_group(K, C, NB, src3, g, XN[:, :, :], "ln", srk, t("xn"))
            K.emit("sync", "dma_start", snm[:, :, gs], XN[:, :, :],
                   rd=[t("xn")], wr=[("SNM", g)], dma=True)
            for j in range(4):
                for kd in range(DT):
                    r = (j * DT + kd) % 2
                    pev = C.PT[0:128, 0:128] if r == 0 else C.pb(0).bitcast(BF16)[0:128, 0:128]
                    kev = ("PT",) if r == 0 else ("PB", 0)
                    K.emit("tensor", "transpose", pev,
                           XN[:, kd, j * 128:(j + 1) * 128], C.ident[:, :],
                           rd=[t("xn"), ("ident",)], wr=[kev])
                    K.emit("vector", "tensor_copy", TMJ[:, j, kd, :],
                           pev, rd=[kev], wr=[t("tmj")])
            K.emit("sync", "dma_start",
                   snmt[g * 512:(g + 1) * 512, :].rearrange("(j p) (kd c) -> p j kd c",
                                                            p=128, c=128),
                   TMJ[:, :, :, :], rd=[t("tmj")], wr=[("SNMT", g)], dma=True)
            for k in range(DT):
                K.emit("tensor", "matmul", C.PS1[0:8, :], wrt[:, k, :], XN[:, k, :],
                       start=(k == 0), stop=(k == DT - 1),
                       rd=[t("wrt"), t("xn")], wr=[("PS1",)])
            K.emit("vector", "tensor_scalar_add", LGR[:, :], C.PS1[0:8, :], ebv[:, 0:1],
                   rd=[("PS1",), t("ebv")], wr=[t("lgr")])
            for j in range(4):
                K.emit("tensor", "transpose", C.pb(0)[0:128, 0:8],
                       LGR[:, j * 128:(j + 1) * 128], C.identf[0:8, 0:8],
                       rd=[t("lgr"), ("identf",)], wr=[("PB", 0)])
                K.emit("vector", "tensor_copy", LGT[:, g * 4 + j, :], C.pb(0)[0:128, 0:8],
                       rd=[("PB", 0)], wr=[t("lgt")])
        b3 = lambda ap2: ap2.unsqueeze(2).broadcast_to([128, 32, 8])
        K.emit("vector", "reduce_max", SM1[:, :], LGT[:, :, :], axis=AX.X,
               rd=[t("lgt")], wr=[t("sm1")])
        K.emit("vector", "tensor_tensor", MK1[:, :, :], LGT[:, :, :], b3(SM1[:, :]),
               OP.is_equal, rd=[t("lgt"), t("sm1")], wr=[t("mk1")])
        K.emit("vector", "scalar_tensor_tensor", MK2[:, :, :], MK1[:, :, :], NEG,
               LGT[:, :, :], OP.mult, OP.add, rd=[t("mk1"), t("lgt")], wr=[t("mk2")])
        K.emit("vector", "reduce_max", SM2[:, :], MK2[:, :, :], axis=AX.X,
               rd=[t("mk2")], wr=[t("sm2")])
        K.emit("vector", "tensor_tensor", MK2[:, :, :], MK2[:, :, :], b3(SM2[:, :]),
               OP.is_equal, rd=[t("mk2"), t("sm2")], wr=[t("mk2")])
        K.emit("vector", "tensor_tensor", SM1[:, :], SM1[:, :], SM2[:, :], OP.subtract,
               rd=[t("sm1"), t("sm2")], wr=[t("sm1")])
        K.emit("scalar", "activation", SM1[:, :], SM1[:, :], AF.Sigmoid,
               rd=[t("sm1")], wr=[t("sm1")])
        K.emit("vector", "tensor_scalar", SM2[:, :], SM1[:, :], -1.0, 1.0, OP.mult, OP.add,
               rd=[t("sm1")], wr=[t("sm2")])
        K.emit("vector", "tensor_tensor", CWM[:, :, :], MK1[:, :, :], MK2[:, :, :], OP.add,
               rd=[t("mk1"), t("mk2")], wr=[t("cwm")])
        for j in range(32):
            K.emit("tensor", "transpose", C.pb(0)[0:8, 0:128], CWM[:, j, :], C.identf[:, :],
                   rd=[t("cwm"), ("identf",)], wr=[("PB", 0)])
            K.emit("vector", "tensor_copy", CWT[:, j * 128:(j + 1) * 128], C.pb(0)[0:8, 0:128],
                   rd=[("PB", 0)], wr=[t("cwt")])
        K.emit("vector", "tensor_single_scalar", CWT[:, :], CWT[:, :], 0.5, OP.is_gt,
               rd=[t("cwt")], wr=[t("cwt")])
        K.emit("vector", "tensor_tensor_scan", CUM[:, :],
               C.onesf[0:8, 0:1].broadcast_to([8, 4096]), CWT[:, :], 0.0, OP.mult, OP.add,
               rd=[("onesf",), t("cwt")], wr=[t("cum")])
        K.emit("tensor", "transpose", C.PS1[0:1, 0:8], CUM[:, 4095:4096],
               C.identf[0:8, 0:8], rd=[t("cum"), ("identf",)], wr=[("PS1",)])
        K.emit("vector", "tensor_scalar_add", CNTR[:, :], C.PS1[0:1, 0:8], 511.0,
               rd=[("PS1",)], wr=[t("cntr")])
        K.emit("vector", "tensor_copy", CNTI[:, :], CNTR[:, :], rd=[t("cntr")], wr=[t("cnti")])
        K.emit("vector", "tensor_single_scalar", CNTI[:, :], CNTI[:, :], -512,
               OP.bitwise_and, rd=[t("cnti")], wr=[t("cnti")])
        K.emit("vector", "tensor_copy", CNTR[:, :], CNTI[:, :], rd=[t("cnti")], wr=[t("cntr")])
        K.emit("vector", "tensor_tensor_scan", T8[:, :],
               C.onesf[0:1, 0:1].broadcast_to([1, 8]), CNTR[:, :], 0.0, OP.mult, OP.add,
               rd=[("onesf",), t("cntr")], wr=[t("t8")])
        K.emit("vector", "tensor_tensor", T8[:, :], T8[:, :], CNTR[:, :], OP.subtract,
               rd=[t("t8"), t("cntr")], wr=[t("t8")])
        for sbk in range(NSB):
            K.emit("vector", "tensor_single_scalar", CNTR[:, :], T8[:, :],
                   float(sbk * 512), OP.is_le, rd=[t("t8")], wr=[t("cntr")])
            K.emit("vector", "reduce_sum", OWNF[0:1, sbk:sbk + 1], CNTR[:, :], axis=AX.X,
                   rd=[t("cntr")], wr=[t("ownf")])
        K.emit("vector", "tensor_scalar_add", OWNF[:, :], OWNF[:, :], -1.0,
               rd=[t("ownf")], wr=[t("ownf")])
        K.emit("tensor", "transpose", C.PS1[0:8, 0:1], T8[:, :], C.identf[0:1, 0:1],
               rd=[t("t8"), ("identf",)], wr=[("PS1",)])
        K.emit("vector", "tensor_copy", OFFC[:, :], C.PS1[0:8, 0:1], rd=[("PS1",)],
               wr=[t("offc")])
        K.emit("vector", "tensor_scalar_add", CUM[:, :], CUM[:, :], -1.0,
               rd=[t("cum")], wr=[t("cum")])
        K.emit("vector", "tensor_scalar_add", CUM[:, :], CUM[:, :], OFFC[:, 0:1],
               rd=[t("cum"), t("offc")], wr=[t("cum")])
        for j in range(32):
            K.emit("tensor", "transpose", C.pb(0)[0:128, 0:8],
                   CUM[:, j * 128:(j + 1) * 128], C.identf[0:8, 0:8],
                   rd=[t("cum"), ("identf",)], wr=[("PB", 0)])
            K.emit("vector", "tensor_copy", PTOK[:, j, :], C.pb(0)[0:128, 0:8],
                   rd=[("PB", 0)], wr=[t("ptok")])
        K.emit("vector", "tensor_scalar_min", PTOK[:, :, :], PTOK[:, :, :],
               float(DUMP), rd=[t("ptok")], wr=[t("ptok")])
        K.emit("vector", "tensor_tensor", MK1[:, :, :], PTOK[:, :, :], MK1[:, :, :], OP.mult,
               rd=[t("ptok"), t("mk1")], wr=[t("mk1")])
        K.emit("vector", "reduce_sum", P12[:, :, 0], MK1[:, :, :], axis=AX.X,
               rd=[t("mk1")], wr=[t("p12")])
        K.emit("vector", "tensor_tensor", MK2[:, :, :], PTOK[:, :, :], MK2[:, :, :], OP.mult,
               rd=[t("ptok"), t("mk2")], wr=[t("mk2")])
        K.emit("vector", "reduce_sum", P12[:, :, 1], MK2[:, :, :], axis=AX.X,
               rd=[t("mk2")], wr=[t("p12")])
        K.emit("vector", "tensor_copy", IDXI2[:, :, :], P12[:, :, :],
               rd=[t("p12")], wr=[t("idxi")])
        for sel in range(2):
            SMv, smk = (SM1, t("sm1")) if sel == 0 else (SM2, t("sm2"))
            for blk in range(32):
                K.emit("gpsimd", "indirect_dma_start", idxd.unsqueeze(0),
                       IndirectOffsetOnAxis(ap=IDXI2[:, blk, sel:sel + 1], axis=1),
                       TOKI[:, blk:blk + 1], None,
                       rd=[t("idxi"), t("toki")], wr=[("IDXD",)],
                       stream="idxsc", dma=True)
                K.emit("gpsimd", "indirect_dma_start", cwl.unsqueeze(0),
                       IndirectOffsetOnAxis(ap=IDXI2[:, blk, sel:sel + 1], axis=1),
                       SMv[:, blk:blk + 1], None,
                       rd=[t("idxi"), smk], wr=[("CWL",)],
                       stream="idxsc", dma=True)
        ctxA.close()
        K.barrier()
        ctxB = ExitStack()
        sbB = lambda name, shape, dt: ctxB.enter_context(nc.sbuf_tensor(f"e{callid}_{name}", shape, dt))
        XGE = [sbB(f"xge{i}", [128, 8, 512], BF16) for i in range(2)]
        GBs = [sbB(f"gb{i}", [128, 1024], BF16) for i in range(4)]
        IDXS = [sbB(f"idxs{i}", [128, 4], I32) for i in range(2)]
        IDXW = sbB("idxw", [128, NSB], I32)
        IOTAF = sbB("iotaf", [128, 1], F32)
        OWNB = sbB("ownb", [128, 1], F32)
        EO = sbB("eo", [128, 16, 512], BF16)
        CWP = sbB("cwp", [128, 4], F32)
        OUTC = sbB("outc", [128, 8, 512], F32)
        OSCJ2 = [sbB(f"oscj{i}", [128, 1024], F32) for i in range(2)]
        TM1 = sbB("tm1", [128, 2, 512], BF16)
        K.emit("sync", "dma_start", IOTAF[:, :], W["c_iota128"], wr=[t("iotaf")], dma=True)
        for sbr in range(NSB):
            K.emit("tensor", "matmul", C.PS1[0:128, 0:1], C.onesf[0:1, 0:128],
                   OWNF[0:1, sbr:sbr + 1], start=True, stop=True,
                   rd=[("onesf",), t("ownf")], wr=[("PS1",)])
            K.emit("vector", "scalar_tensor_tensor", OWNB[:, 0:1],
                   C.PS1[0:128, 0:1], 128.0, IOTAF[:, 0:1], OP.mult, OP.add,
                   rd=[("PS1",), t("iotaf")], wr=[t("ownb")])
            K.emit("vector", "tensor_copy", IDXW[:, sbr:sbr + 1], OWNB[:, 0:1],
                   rd=[t("ownb")], wr=[t("idxw")])

        def w_load(sbr, which):
            tgt, kdst, wb = {"x": (WX, t("wx"), w1b), "y": (WY, t("wy"), w3b),
                             "z": (WZ, t("wz"), w2b)}[which]
            K.emit("gpsimd", "indirect_dma_start",
                   tgt.rearrange("p a b -> p (a b)"), None,
                   wb.rearrange("e p a b -> (e p) (a b)"),
                   IndirectOffsetOnAxis(ap=IDXW[:, sbr:sbr + 1], axis=0),
                   rd=[t("idxw"), ("WPC", which)], wr=[kdst], stream=f"w{which}", dma=True)

        def gather_issue(sbr, par):
            ki = t(f"idxs{par}")
            K.emit("sync", "dma_start", IDXS[par][:, :],
                   idxd[sbr * 512:(sbr + 1) * 512].rearrange("(p b) -> p b", b=4),
                   rd=[("IDXD",)], wr=[ki], stream=f"idxs{par}", dma=True)
            for blk in range(4):
                K.emit("gpsimd", "indirect_dma_start", GBs[blk][:, :], None, snmt,
                       IndirectOffsetOnAxis(ap=IDXS[par][:, blk:blk + 1], axis=0),
                       rd=[ki, ("SNMTZ",)] + [("SNMT", gg) for gg in range(8)],
                       wr=[t(f"gb{blk}")], stream="gath", dma=True)

        def gather_drain(par):
            kx = t(f"xge{par}")
            for blk in range(4):
                for kd in range(DT):
                    r = (blk * DT + kd) % 2
                    pev = C.PT[0:128, 0:128] if r == 0 else C.pb(0).bitcast(BF16)[0:128, 0:128]
                    kev = ("PT",) if r == 0 else ("PB", 0)
                    K.emit("tensor", "transpose", pev,
                           GBs[blk][:, kd * 128:(kd + 1) * 128], C.ident[:, :],
                           rd=[t(f"gb{blk}"), ("ident",)], wr=[kev])
                    K.emit("vector", "tensor_copy", XGE[par][:, kd, blk * 128:(blk + 1) * 128],
                           pev, rd=[kev], wr=[kx])

        for sbk in range(8 + NSB):
            shared = sbk < 8
            sbr = sbk - 8
            par = sbr % 2
            if shared:
                K.emit("sync", "dma_start", XGE[sbk % 2][:, :, :],
                       snm[:, :, sbk * 512:(sbk + 1) * 512], rd=[("SNM", sbk)],
                       wr=[t(f"xge{sbk % 2}")], stream=f"xnc{sbk % 2}", dma=True)
                rhk, rhs_buf = t(f"xge{sbk % 2}"), XGE[sbk % 2]
            else:
                rhk, rhs_buf = t(f"xge{par}"), XGE[par]
                K.emit("sync", "dma_start", CWP[:, :],
                       cwl[sbr * 512:(sbr + 1) * 512].rearrange("(p b) -> p b", b=4),
                       rd=[("CWL",)], wr=[t("cwp")], stream="cwrw", dma=True)
            for ht in range(16):
                pg, pu = (0, 1) if ht % 2 == 0 else (2, 3)
                pr = ht % 2
                for k in range(DT):
                    K.emit("tensor", "matmul", C.pa(pg), WX[:, k, ht * 128:(ht + 1) * 128],
                           rhs_buf[:, k, :], start=(k == 0), stop=(k == DT - 1),
                           rd=[t("wx"), rhk], wr=[("PA", pg)])
                for k in range(DT):
                    K.emit("tensor", "matmul", C.pa(pu), WY[:, k, ht * 128:(ht + 1) * 128],
                           rhs_buf[:, k, :], start=(k == 0), stop=(k == DT - 1),
                           rd=[t("wy"), rhk], wr=[("PA", pu)])
                K.emit("scalar", "activation", TM1[:, pr, :], C.pa(pg), AF.Silu,
                       rd=[("PA", pg)], wr=[t("tm1")])
                K.emit("vector", "tensor_tensor", EO[:, ht, :], TM1[:, pr, :],
                       C.pa(pu), OP.mult, rd=[t("tm1"), ("PA", pu)], wr=[t("eo")])
            nxt = 0 if shared and sbk == 7 else (sbr + 1 if not shared else None)
            if nxt is not None and nxt < NSB and (not shared or sbk == 7):
                gather_issue(nxt, nxt % 2)
                w_load(nxt, "x")
                w_load(nxt, "y")
            for m in range(DT):
                pk = ("PA", 2 + m % 2)
                pap = C.pa(2 + m % 2)
                for k in range(16):
                    K.emit("tensor", "matmul", pap, WZ[:, k, m * 128:(m + 1) * 128],
                           EO[:, k, :], start=(k == 0), stop=(k == 15),
                           rd=[t("wz"), t("eo")], wr=[pk])
                K.emit("vector", "tensor_copy", OUTC[:, m, :], pap, rd=[pk], wr=[t("outc")])
            if nxt is not None and nxt < NSB and (not shared or sbk == 7):
                w_load(nxt, "z")
                gather_drain(nxt % 2)
            for j in range(4):
                OSCJ, koscj = OSCJ2[j % 2], t(f"oscj{j % 2}")
                for m in range(DT):
                    pev = C.PS1[0:128, 0:128] if m % 2 == 0 else C.pb(1)[0:128, 0:128]
                    kev = ("PS1",) if m % 2 == 0 else ("PB", 1)
                    K.emit("tensor", "transpose", pev,
                           OUTC[:, m, j * 128:(j + 1) * 128], C.identf[:, :],
                           rd=[t("outc"), ("identf",)], wr=[kev])
                    if shared:
                        K.emit("vector", "tensor_copy", OSCJ[:, m * 128:(m + 1) * 128],
                               pev, rd=[kev], wr=[koscj])
                    else:
                        K.emit("vector", "tensor_scalar_mul", OSCJ[:, m * 128:(m + 1) * 128],
                               pev, CWP[:, j:j + 1],
                               rd=[kev, t("cwp")], wr=[koscj])
                if shared:
                    row0 = sbk * 512 + j * 128
                    K.emit("gpsimd", "dma_start", acct[row0:row0 + 128, :],
                           OSCJ[:, :], rd=[koscj], wr=[("ACCTS",)],
                           stream="acct", dma=True)
                else:
                    K.emit("gpsimd", "indirect_dma_start", acct,
                           IndirectOffsetOnAxis(ap=IDXS[par][:, j:j + 1], axis=0),
                           OSCJ[:, :], None, compute_op=OP.add,
                           rd=[koscj, t(f"idxs{par}"), ("ACCTZ",)],
                           wr=[("ACCTS",)], stream="acct", dma=True)
        ctxB.close()
        K.barrier()
        ctxC = ExitStack()
        sbC = lambda name, shape, dt: ctxC.enter_context(nc.sbuf_tensor(f"e{callid}_{name}", shape, dt))
        XG2 = sbC("xg2", [128, 8, 512], F32)
        AIN = sbC("ain", [128, 4, 1024], F32)
        for g in range(8):
            gs = slice(g * 512, (g + 1) * 512)
            srk = [srckey] if isinstance(srckey, tuple) else [(srckey, g)]
            K.emit("sync", "dma_start", XG2[:, :, :], src3[:, :, gs], rd=srk,
                   wr=[t("xg2")], stream="xg2", dma=True)
            K.emit("sync", "dma_start", AIN[:, :, :],
                   acct[g * 512:(g + 1) * 512, :].rearrange("(j p) d -> p j d", p=128),
                   wr=[t("ain")], stream="accld", dma=True)
            for j in range(4):
                for m in range(DT):
                    pev = C.PS1[0:128, 0:128] if m % 2 == 0 else C.pb(1)[0:128, 0:128]
                    kev = ("PS1",) if m % 2 == 0 else ("PB", 1)
                    K.emit("tensor", "transpose", pev,
                           AIN[:, j, m * 128:(m + 1) * 128], C.identf[:, :],
                           rd=[t("ain"), ("identf",)], wr=[kev])
                    K.emit("vector", "scalar_tensor_tensor",
                           XG2[:, m, j * 128:(j + 1) * 128], pev, 0.5,
                           XG2[:, m, j * 128:(j + 1) * 128], OP.mult, OP.add,
                           rd=[kev, t("xg2")], wr=[t("xg2")])
            K.emit("sync", "dma_start", dst3[:, :, gs], XG2[:, :, :],
                   rd=[t("xg2")], wr=[(dstkey, g)], dma=True)
        ctxC.close()
    K.barrier()



def build_att1(K, C, W, scr, src3, dst3, srckey, dstkey):
    nc = K.nc
    sna, rws, wns = scr["SNA"], scr["RWS"], scr["WNS"]
    build_ln_pass(K, C, src3, None, srckey, None, "lnT", dst_bf=sna, dstbfkey="SNA",
                  row_cache=rws)
    SC = 1.0 / (2048.0 ** 0.5)
    with ExitStack() as ctx:
        sb = lambda name, shape, dt: ctx.enter_context(nc.sbuf_tensor(f"a1_{name}", shape, dt))
        t = lambda n: (f"a1_{n}",)
        WB = [sb(f"wb{i}", [128, 8, 1024], BF16) for i in range(2)]
        SQB = sb("sqb", [128, 1024], BF16)
        WSR = sb("wsr", [1, 1024], F32)
        gn = sb("gn", [1, 1], F32)
        QH = sb("qh", [128, 16, 1024], BF16)
        KH = sb("kh", [128, 16, 1024], BF16)
        VT = sb("vt", [128, 8, 2048], BF16)
        OT = sb("ot", [128, 8, 512], BF16)
        SNSg = sb("snsg", [128, 8, 512], BF16)
        ATT = sb("att", [128, 64], F32)
        ATTB = sb("attb", [128, 64], BF16)
        NMC = sb("nmc", [128, 1], F32)
        DSR = sb("dsr", [128, 1], F32)
        ATN = sb("atn", [128, 8, 64], BF16)
        SSQ = sb("ssq", [128, 16], F32)
        XG2 = sb("xg2", [128, 8, 512], F32)
        TMP = sb("tmp", [128, 512], F32)
        RWL = [sb(f"rwl{i}", [1, 512], F32) for i in range(2)]

        K.emit("sync", "dma_start", gn[:, :], W["h_gain"], wr=[t("gn")], dma=True)
        for wi, wname in enumerate(("h_wqT", "h_wkT", "h_wvT", "h_woT")):
            slot = WB[0]
            K.emit("gpsimd", "dma_start", slot[:, :, :],
                   W[wname].rearrange("(t p) o -> p t o", p=128), wr=[t("wb0")],
                   stream="wb0", dma=True)
            for h2 in range(2):
                pr = C.PS1[0:1, :] if h2 == 0 else C.pb(0)[0:1, :]
                pk = ("PS1",) if h2 == 0 else ("PB", 0)
                for kt in range(DT):
                    K.emit("scalar", "activation", SQB[:, :], slot[:, kt, :], AF.Square,
                           rd=[t("wb0")], wr=[t("sqb")])
                    K.emit("tensor", "matmul", pr, C.onesb[:, 0:1],
                           SQB[:, h2 * 512:(h2 + 1) * 512],
                           start=(kt == 0), stop=(kt == DT - 1),
                           rd=[("onesb",), t("sqb")], wr=[pk])
                K.emit("vector", "tensor_copy", WSR[0:1, h2 * 512:(h2 + 1) * 512], pr,
                       rd=[pk], wr=[t("wsr")])
            K.emit("scalar", "activation", WSR[0:1, :], WSR[0:1, :], AF.Ln,
                   bias=C.consts[0:1, 3:4], rd=[t("wsr"), ("consts",)], wr=[t("wsr")])
            K.emit("scalar", "activation", WSR[0:1, :], WSR[0:1, :], AF.Exp, scale=0.5,
                   rd=[t("wsr")], wr=[t("wsr")])
            K.emit("vector", "tensor_scalar", WSR[0:1, :], WSR[0:1, :], 1.0 / 32.0,
                   C.consts[0:1, 2:3], OP.mult, OP.add,
                   rd=[t("wsr"), ("consts",)], wr=[t("wsr")])
            K.emit("vector", "reciprocal", WSR[0:1, :], WSR[0:1, :], rd=[t("wsr")], wr=[t("wsr")])
            K.emit("vector", "tensor_scalar", WSR[0:1, :], WSR[0:1, :], gn[0:1, 0:1],
                   1.0 / 32.0, OP.mult, OP.mult, rd=[t("wsr"), t("gn")], wr=[t("wsr")])
            for h2 in range(2):
                K.emit("tensor", "matmul", C.pb(h2), C.onesf[0:1, 0:128],
                       WSR[0:1, h2 * 512:(h2 + 1) * 512], start=True, stop=True,
                       rd=[("onesf",), t("wsr")], wr=[("PB", h2)])
            K.emit("vector", "tensor_tensor", slot[:, :, :], slot[:, :, :],
                   C.PB[:, :].unsqueeze(1).broadcast_to([128, 8, 1024]), OP.mult,
                   rd=[t("wb0"), ("PB", 0), ("PB", 1)], wr=[t("wb0")])
            K.emit("gpsimd", "dma_start", wns[wi], slot[:, :, :],
                   rd=[t("wb0")], wr=[("WNS", wi)], stream="wnsw", dma=True)

        for seq in range(2):
            sq0 = seq * 2048
            for wi, dst, isqk in ((0, QH, True), (1, KH, True), (2, VT, False)):
                slot = WB[wi % 2]
                wk = t(f"wb{wi % 2}")
                K.emit("gpsimd", "dma_start", slot[:, :, :], wns[wi], rd=[("WNS", wi)],
                       wr=[wk], stream=f"wb{wi % 2}", dma=True)
                for grp in range(4):
                    gs = slice(sq0 + grp * 512, sq0 + (grp + 1) * 512)
                    K.emit("sync", "dma_start", SNSg[:, :, :], sna[:, :, gs],
                           rd=[("SNA", (sq0 + grp * 512) // 512)], wr=[t("snsg")],
                           stream="snsg", dma=True)
                    if isqk:
                        for tt in range(4):
                            ts_ = slice(tt * 128, (tt + 1) * 128)
                            for n2 in range(2):
                                pk = ("PA", n2)
                                pap = C.pa(n2)
                                for kd in range(DT):
                                    K.emit("tensor", "matmul", pap, SNSg[:, kd, ts_],
                                           slot[:, kd, n2 * 512:(n2 + 1) * 512],
                                           start=(kd == 0), stop=(kd == DT - 1),
                                           rd=[t("snsg"), wk], wr=[pk])
                            K.emit("scalar", "activation", SQB[:, 0:512], C.pa(0), AF.Square,
                                   rd=[("PA", 0)], wr=[t("sqb")])
                            K.emit("scalar", "activation", SQB[:, 512:1024], C.pa(1), AF.Square,
                                   rd=[("PA", 1)], wr=[t("sqb")])
                            K.emit("vector", "reduce_sum", SSQ[:, :],
                                   SQB[:, :].rearrange("p (h d) -> p h d", d=64), axis=AX.X,
                                   rd=[t("sqb")], wr=[t("ssq")])
                            K.emit("scalar", "activation", SSQ[:, :], SSQ[:, :], AF.Ln,
                                   bias=C.consts[:, 3:4], rd=[t("ssq"), ("consts",)], wr=[t("ssq")])
                            K.emit("scalar", "activation", SSQ[:, :], SSQ[:, :], AF.Exp,
                                   scale=0.5, rd=[t("ssq")], wr=[t("ssq")])
                            K.emit("vector", "tensor_scalar", SSQ[:, :], SSQ[:, :], 0.125,
                                   C.consts[:, 2:3], OP.mult, OP.add,
                                   rd=[t("ssq"), ("consts",)], wr=[t("ssq")])
                            K.emit("vector", "reciprocal", SSQ[:, :], SSQ[:, :],
                                   rd=[t("ssq")], wr=[t("ssq")])
                            for n2 in range(2):
                                K.emit("vector", "tensor_tensor",
                                       dst[:, grp * 4 + tt, n2 * 512:(n2 + 1) * 512]
                                       .rearrange("p (h d) -> p h d", d=64),
                                       C.pa(n2).rearrange("p (h d) -> p h d", d=64),
                                       SSQ[:, n2 * 8:(n2 + 1) * 8].unsqueeze(2)
                                       .broadcast_to([128, 8, 64]), OP.mult,
                                       rd=[("PA", n2), t("ssq")],
                                       wr=[t("qh") if dst is QH else t("kh")])
                    else:
                        for m in range(DT):
                            pk = ("PA", m % 2)
                            pap = C.pa(m % 2)
                            for kd in range(DT):
                                K.emit("tensor", "matmul", pap,
                                       slot[:, kd, m * 128:(m + 1) * 128], SNSg[:, kd, :],
                                       start=(kd == 0), stop=(kd == DT - 1),
                                       rd=[wk, t("snsg")], wr=[pk])
                            K.emit("vector", "tensor_copy",
                                   VT[:, m, grp * 512:(grp + 1) * 512], pap,
                                   rd=[pk], wr=[t("vt")])
            for m in range(8):
                n2 = m // 4
                c0 = (m % 4) * 128
                pk = ("PA", m % 2)
                pap = C.pa(m % 2)
                for kt in range(16):
                    K.emit("tensor", "matmul", pap, QH[:, kt, m * 128:(m + 1) * 128],
                           KH[:, kt, n2 * 512:(n2 + 1) * 512],
                           start=(kt == 0), stop=(kt == 15),
                           rd=[t("qh"), t("kh")], wr=[pk])
                for hh in range(2):
                    h = 2 * m + hh
                    rs = slice(hh * 64, (hh + 1) * 64)
                    blk = pap[hh * 64:(hh + 1) * 64, c0 + hh * 64:c0 + (hh + 1) * 64]
                    K.emit("vector", "reduce_max", NMC[rs, :], blk, axis=AX.X, negate=True,
                           rd=[pk], wr=[t("nmc")])
                    K.emit("vector", "tensor_scalar_mul", NMC[rs, :], NMC[rs, :], SC,
                           rd=[t("nmc")], wr=[t("nmc")])
                    K.emit("scalar", "activation", ATT[rs, :], blk, AF.Exp,
                           bias=NMC[rs, 0:1], scale=SC, accum_out=DSR[rs, 0:1],
                           rd=[pk, t("nmc")], wr=[t("att"), t("dsr")])
                    K.emit("vector", "reciprocal", DSR[rs, :], DSR[rs, :],
                           rd=[t("dsr")], wr=[t("dsr")])
                    K.emit("vector", "tensor_scalar_mul", ATTB[rs, :], ATT[rs, :],
                           DSR[rs, 0:1], rd=[t("att"), t("dsr")], wr=[t("attb")])
                    o_ = hh * 64
                    K.emit("tensor", "transpose", C.PT[0:64, 0:64],
                           ATTB[rs, :], C.ident[o_:o_ + 64, o_:o_ + 64],
                           rd=[t("attb"), ("ident",)], wr=[("PT",)])
                    hb_ = (h % 2) * 64
                    K.emit("vector", "tensor_copy", ATN[hb_:hb_ + 64, h // 2, :],
                           C.PT[0:64, 0:64], rd=[("PT",)], wr=[t("atn")])
            K.emit("gpsimd", "dma_start", WB[1][:, :, :], wns[3], rd=[("WNS", 3)],
                   wr=[t("wb1")], stream="wb1", dma=True)
            for nc_ in range(4):
                g = seq * 4 + nc_
                gs = slice(g * 512, (g + 1) * 512)
                ns = slice(nc_ * 512, (nc_ + 1) * 512)
                for hp in range(8):
                    pk = ("PB", hp % 2)
                    pap = C.pb(hp % 2)
                    for hh in range(2):
                        h = 2 * hp + hh
                        hb_ = (h % 2) * 64
                        K.emit("tensor", "matmul", pap[hh * 64:(hh + 1) * 64, :],
                               ATN[hb_:hb_ + 64, h // 2, :],
                               VT[hb_:hb_ + 64, h // 2, ns],
                               start=True, stop=True,
                               tile_position=(hb_, hh * 64),
                               rd=[t("atn"), t("vt")], wr=[pk])
                    K.emit("vector", "tensor_copy", OT[:, hp, :], pap, rd=[pk], wr=[t("ot")])
                K.emit("sync", "dma_start", XG2[:, :, :], src3[:, :, gs],
                       rd=[(srckey, g) if not isinstance(srckey, tuple) else srckey],
                       wr=[t("xg2")], stream="xg2", dma=True)
                K.emit("sync", "dma_start", RWL[0][:, :], rws[0:1, gs], rd=[("RWS", "m", g)],
                       wr=[t("rwl0")], stream="rwl", dma=True)
                K.emit("sync", "dma_start", RWL[1][:, :], rws[1:2, gs], rd=[("RWS", "i", g)],
                       wr=[t("rwl1")], stream="rwl", dma=True)
                for m in range(8):
                    pk = ("PA", 2 + m % 2)
                    pap = C.pa(2 + m % 2)
                    for k in range(DT):
                        K.emit("tensor", "matmul", pap, WB[1][:, k, m * 128:(m + 1) * 128],
                               OT[:, k, :], start=(k == 0), stop=(k == DT - 1),
                               rd=[t("wb1"), t("ot")], wr=[pk])
                    if m == 0:
                        K.emit("tensor", "matmul", C.pb(0), C.onesf[0:1, 0:128],
                               RWL[0][0:1, :], start=True, stop=True,
                               rd=[("onesf",), t("rwl0")], wr=[("PB", 0)])
                        K.emit("tensor", "matmul", C.pb(1), C.onesf[0:1, 0:128],
                               RWL[1][0:1, :], start=True, stop=True,
                               rd=[("onesf",), t("rwl1")], wr=[("PB", 1)])
                    K.emit("vector", "tensor_tensor", TMP[:, :], XG2[:, m, :], C.pb(0),
                           OP.subtract, rd=[t("xg2"), ("PB", 0)], wr=[t("tmp")])
                    K.emit("vector", "tensor_tensor", TMP[:, :], TMP[:, :], C.pb(1),
                           OP.mult, rd=[t("tmp"), ("PB", 1)], wr=[t("tmp")])
                    K.emit("vector", "tensor_tensor", TMP[:, :], TMP[:, :], pap,
                           OP.add, rd=[t("tmp"), pk], wr=[t("tmp")])
                    K.emit("vector", "scalar_tensor_tensor", XG2[:, m, :], TMP[:, :], RS2,
                           XG2[:, m, :], OP.mult, OP.add,
                           rd=[t("tmp"), t("xg2")], wr=[t("xg2")])
                K.emit("sync", "dma_start", dst3[:, :, gs], XG2[:, :, :],
                       rd=[t("xg2")], wr=[(dstkey, g)], dma=True)
    K.barrier()



CMIX_MU = float(np.float32(np.sqrt(0.5)))
CMIX_DEN = float(np.float32(np.sqrt(1.0 / (4.0 * np.pi)) * np.sqrt(2.0)))


def build_cmix(K, C, W, s3, xb3, dst3, skey, xbkey, dstkey, skip=False):
    nc = K.nc
    with ExitStack() as ctx:
        sb = lambda name, shape, dt: ctx.enter_context(nc.sbuf_tensor(f"cm_{name}", shape, dt))
        t = lambda n: (f"cm_{n}",)
        NB = NormBufs(nc, ctx, "cm", rms_only=True)
        XBg = sb("xbg", [128, 8, 512], F32)
        if skip:
            for g in range(8):
                gs = slice(g * 512, (g + 1) * 512)
                K.emit("sync", "dma_start", NB.xg[:, :, :], s3[:, :, gs],
                       rd=[(skey, g)], wr=[("cmxg",)], stream="sld", dma=True)
                K.emit("sync", "dma_start", XBg[:, :, :], xb3[:, :, gs],
                       rd=[(xbkey, g) if not isinstance(xbkey, tuple) else xbkey],
                       wr=[t("xbg")], stream="xbg", dma=True)
                K.emit("vector", "tensor_tensor", XBg[:, :, :], XBg[:, :, :], NB.xg[:, :, :],
                       OP.add, rd=[t("xbg"), ("cmxg",)], wr=[t("xbg")])
                K.emit("sync", "dma_start", dst3[:, :, gs], XBg[:, :, :],
                       rd=[t("xbg")], wr=[(dstkey, g)], dma=True)
            K.barrier()
            return
        RN = sb("rn", [128, 8, 512], BF16)
        T1 = sb("t1", [128, 8, 512], BF16)
        XK = sb("xk", [128, 8, 512], BF16)
        KACT = sb("kact", [128, 32, 512], BF16)
        RB = sb("rb", [128, 8, 512], BF16)
        TK = sb("tk", [128, 8, 1], BF16)
        MAA = sb("maa", [128, 4, 8], F32)
        WRC = sb("wrc", [128, 8, 1024], BF16)
        KS = [sb(f"ks{i}", [128, 8, 128], BF16) for i in range(2)]
        VS = [sb(f"vs{i}", [128, 32, 128], BF16) for i in range(2)]
        TMPF = sb("tmpf", [128, 512], F32)
        TMPG = sb("tmpg", [128, 512], F32)

        K.emit("sync", "dma_start", MAA[:, 0, :], W["h_maak"], wr=[t("maa")], dma=True)
        K.emit("sync", "dma_start", MAA[:, 2, :], W["h_maar"], wr=[t("maa")], dma=True)
        K.emit("vector", "tensor_scalar", MAA[:, 1, :], MAA[:, 0, :], -1.0, 1.0,
               OP.mult, OP.add, rd=[t("maa")], wr=[t("maa")])
        K.emit("vector", "tensor_scalar", MAA[:, 3, :], MAA[:, 2, :], -1.0, 1.0,
               OP.mult, OP.add, rd=[t("maa")], wr=[t("maa")])
        K.emit("gpsimd", "dma_start", WRC[:, :, :],
               W["L0_attn_recept_w"].rearrange("(t p) o -> p t o", p=128),
               wr=[t("wrc")], stream="wrc", dma=True)

        def mix_into(dst_b, c0, c1):
            K.emit("vector", "tensor_tensor", T1[:, :, :], RN[:, :, :],
                   MAA[:, c1, :].unsqueeze(2).broadcast_to([128, 8, 512]), OP.mult,
                   rd=[t("rn"), t("maa")], wr=[t("t1")])
            for td in range(DT):
                K.emit("vector", "scalar_tensor_tensor", dst_b[:, td, 1:512],
                       RN[:, td, 0:511], MAA[:, c0, td:td + 1], T1[:, td, 1:512],
                       OP.mult, OP.add, rd=[t("rn"), t("maa"), t("t1")], wr=[t("xk")])
                K.emit("vector", "scalar_tensor_tensor", dst_b[:, td, 0:1],
                       TK[:, td, 0:1], MAA[:, c0, td:td + 1], T1[:, td, 0:1],
                       OP.mult, OP.add, rd=[t("tk"), t("maa"), t("t1")], wr=[t("xk")])

        for g in range(8):
            gs = slice(g * 512, (g + 1) * 512)
            norm_group(K, C, NB, s3, g, RN[:, :, :], "rms", [(skey, g)], t("rn"))
            if g % 4 == 0:
                K.emit("vector", "memset", TK[:, :, :], 0.0, wr=[t("tk")])
            mix_into(XK, 0, 1)
            for m in range(32):
                sl = m % 2
                K.emit("gpsimd", "dma_start", KS[sl][:, :, :], W["h_key_cm"][m]
                       .rearrange("(t p) o -> p t o", p=128),
                       wr=[t(f"ks{sl}")], stream=f"ks{sl}", dma=True)
                for k in range(DT):
                    K.emit("tensor", "matmul", C.pb(sl), KS[sl][:, k, :], XK[:, k, :],
                           start=(k == 0), stop=(k == DT - 1),
                           rd=[t(f"ks{sl}"), t("xk")], wr=[("PB", sl)])
                c2 = float(np.sqrt(2.0) / CMIX_DEN)
                K.emit("vector", "tensor_scalar", TMPF[:, :], C.pb(sl), c2,
                       -CMIX_MU * c2, OP.mult, OP.add,
                       rd=[("PB", sl)], wr=[t("tmpf")])
                K.emit("vector", "tensor_scalar_add", TMPF[:, :], TMPF[:, :], 1.0e-20,
                       rd=[t("tmpf")], wr=[t("tmpf")])
                K.emit("scalar", "activation", TMPG[:, :], TMPF[:, :], AF.Gelu,
                       rd=[t("tmpf")], wr=[t("tmpg")])
                K.emit("vector", "reciprocal", TMPF[:, :], TMPF[:, :],
                       rd=[t("tmpf")], wr=[t("tmpf")])
                K.emit("vector", "tensor_tensor", KACT[:, m, :], TMPG[:, :], TMPF[:, :],
                       OP.mult, rd=[t("tmpg"), t("tmpf")], wr=[t("kact")])
            mix_into(XK, 2, 3)
            K.emit("vector", "tensor_copy", TK[:, :, :], RN[:, :, 511:512],
                   rd=[t("rn")], wr=[t("tk")])
            for m in range(DT):
                pk = ("PA", m % 2)
                pap = C.pa(m % 2)
                for k in range(DT):
                    K.emit("tensor", "matmul", pap, WRC[:, k, m * 128:(m + 1) * 128],
                           XK[:, k, :], start=(k == 0), stop=(k == DT - 1),
                           rd=[t("wrc"), t("xk")], wr=[pk])
                K.emit("scalar", "activation", RB[:, m, :], pap, AF.Sigmoid,
                       rd=[pk], wr=[t("rb")])
            K.emit("sync", "dma_start", XBg[:, :, :], xb3[:, :, gs],
                   rd=[(xbkey, g) if not isinstance(xbkey, tuple) else xbkey],
                   wr=[t("xbg")], stream="xbg", dma=True)
            for m in range(DT):
                sl = m % 2
                K.emit("gpsimd", "dma_start", VS[sl][:, :, :], W["h_value_cm"][m]
                       .rearrange("(t p) o -> p t o", p=128),
                       wr=[t(f"vs{sl}")], stream=f"vs{sl}", dma=True)
                pk = ("PA", 2 + sl)
                pap = C.pa(2 + sl)
                for k in range(32):
                    K.emit("tensor", "matmul", pap, VS[sl][:, k, :], KACT[:, k, :],
                           start=(k == 0), stop=(k == 31),
                           rd=[t(f"vs{sl}"), t("kact")], wr=[pk])
                K.emit("vector", "tensor_tensor", TMPF[:, :], pap, RB[:, m, :], OP.mult,
                       rd=[pk, t("rb")], wr=[t("tmpf")])
                K.emit("vector", "tensor_tensor", NB.xg[:, m, :], NB.xg[:, m, :],
                       TMPF[:, :], OP.add, rd=[("cmxg",), t("tmpf")], wr=[("cmxg",)])
            K.emit("vector", "tensor_tensor", NB.xg[:, :, :], NB.xg[:, :, :],
                   XBg[:, :, :], OP.add, rd=[("cmxg",), t("xbg")], wr=[("cmxg",)])
            K.emit("sync", "dma_start", dst3[:, :, gs], NB.xg[:, :, :],
                   rd=[("cmxg",)], wr=[(dstkey, g)], dma=True)
    K.barrier()



def build_mqa(K, C, W, src3, dst3, srckey, dstkey):
    nc = K.nc
    with ExitStack() as ctx:
        sb = lambda name, shape, dt: ctx.enter_context(nc.sbuf_tensor(f"q_{name}", shape, dt))
        t = lambda n: (f"q_{n}",)
        NB = NormBufs(nc, ctx, "q", rms_only=True)
        RN = sb("rn", [128, 8, 512], BF16)
        AW = sb("aw", [128, 8, 1152], BF16)
        PW = sb("pw", [128, 8, 1024], BF16)
        QT = sb("qt", [128, 8, 512], BF16)
        KTH = sb("kth", [128, 2048], BF16)
        VTOK = sb("vtok", [128, 16, 65], BF16)
        VB = sb("vbuf", [64, 512], BF16)
        SQT = sb("sqt", [128, 2, 512], BF16)
        QSR = sb("qsr", [1, 512], F32)
        KFR = sb("kfr", [1, 1], F32)
        MAXK = sb("maxk", [1, 1], F32)
        CBC = sb("cbc", [128, 1], F32)
        EB = sb("eb", [128, 2, 512], BF16)
        RD = sb("rd", [1, 512], F32)
        RB64 = sb("rb64", [64, 512], F32)
        OT = sb("ot", [128, 8, 512], BF16)

        K.emit("gpsimd", "dma_start", AW[:, :, :],
               W["L0_attn_attn_w"].rearrange("(t p) o -> p t o", p=128),
               wr=[t("aw")], stream="aw", dma=True)
        K.emit("gpsimd", "dma_start", PW[:, :, :],
               W["L0_attn_proj_w"].rearrange("(t p) o -> p t o", p=128),
               wr=[t("pw")], stream="pw", dma=True)
        for seq in range(2):
            K.emit("vector", "memset", MAXK[:, :], 0.0, wr=[t("maxk")])
            K.emit("vector", "memset", VTOK[:, :, 64:65], 1.0, wr=[t("vtok")])
            for c in range(4):
                g = seq * 4 + c
                gs = slice(g * 512, (g + 1) * 512)
                norm_group(K, C, NB, src3, g, RN[:, :, :], "rms", [(srckey, g)], t("rn"))
                for m in range(DT):
                    pk = ("PA", m % 2)
                    pap = C.pa(m % 2)
                    for k in range(DT):
                        K.emit("tensor", "matmul", pap, AW[:, k, m * 128:(m + 1) * 128],
                               RN[:, k, :], start=(k == 0), stop=(k == DT - 1),
                               rd=[t("aw"), t("rn")], wr=[pk])
                    K.emit("vector", "tensor_copy", QT[:, m, :], pap, rd=[pk], wr=[t("qt")])
                for k in range(DT):
                    K.emit("tensor", "matmul", C.pa(2)[0:64, :], AW[:, k, 1024:1088],
                           RN[:, k, :], start=(k == 0), stop=(k == DT - 1),
                           rd=[t("aw"), t("rn")], wr=[("PA", 2)])
                K.emit("vector", "tensor_copy", KTH[0:64, gs.start - seq * 2048:
                       gs.stop - seq * 2048], C.pa(2)[0:64, :], rd=[("PA", 2)], wr=[t("kth")])
                K.emit("vector", "tensor_copy", KTH[64:128, gs.start - seq * 2048:
                       gs.stop - seq * 2048], C.pa(2)[0:64, :], rd=[("PA", 2)], wr=[t("kth")])
                for k in range(DT):
                    K.emit("tensor", "matmul", C.pa(3)[0:64, :], AW[:, k, 1088:1152],
                           RN[:, k, :], start=(k == 0), stop=(k == DT - 1),
                           rd=[t("aw"), t("rn")], wr=[("PA", 3)])
                K.emit("vector", "tensor_copy", VB[:, :], C.pa(3)[0:64, :],
                       rd=[("PA", 3)], wr=[t("vbuf")])
                for j in range(4):
                    K.emit("tensor", "transpose", C.PT[0:128, 0:64],
                           VB[:, j * 128:(j + 1) * 128], C.ident[0:64, 0:64],
                           rd=[t("vbuf"), ("ident",)], wr=[("PT",)])
                    K.emit("vector", "tensor_copy", VTOK[:, c * 4 + j, 0:64],
                           C.PT[0:128, 0:64], rd=[("PT",)], wr=[t("vtok")])
                for m in range(DT):
                    K.emit("scalar", "activation", SQT[:, m % 2, :], QT[:, m, :], AF.Square,
                           rd=[t("qt")], wr=[("qsqt", m % 2)])
                    K.emit("tensor", "matmul", C.PS1[0:1, :], C.onesb[:, 0:1], SQT[:, m % 2, :],
                           start=(m == 0), stop=(m == DT - 1),
                           rd=[("onesb",), ("qsqt", m % 2)], wr=[("PS1",)])
                K.emit("scalar", "activation", QSR[:, :], C.PS1[0:1, :], AF.Ln,
                       bias=C.consts[0:1, 3:4], rd=[("PS1",), ("consts",)], wr=[t("qsr")])
                K.emit("scalar", "activation", QSR[:, :], QSR[:, :], AF.Exp, scale=0.5,
                       rd=[t("qsr")], wr=[t("qsr")])
                K.emit("scalar", "activation", SQT[0:64, 0, :],
                       KTH[0:64, c * 512:(c + 1) * 512], AF.Square,
                       rd=[t("kth")], wr=[("qsqt", 0)])
                K.emit("tensor", "matmul", C.PS1[0:1, :], C.onesb[0:64, 0:1], SQT[0:64, 0, :],
                       start=True, stop=True, rd=[("onesb",), ("qsqt", 0)], wr=[("PS1",)])
                K.emit("vector", "reduce_max", KFR[:, :], C.PS1[0:1, :], axis=AX.X,
                       rd=[("PS1",)], wr=[t("kfr")])
                K.emit("vector", "tensor_tensor", MAXK[:, :], MAXK[:, :], KFR[:, :], OP.max,
                       rd=[t("maxk"), t("kfr")], wr=[t("maxk")])
                K.emit("scalar", "activation", KFR[:, :], MAXK[:, :], AF.Ln,
                       bias=C.consts[0:1, 3:4], rd=[t("maxk"), ("consts",)], wr=[t("kfr")])
                K.emit("scalar", "activation", KFR[:, :], KFR[:, :], AF.Exp, scale=0.5,
                       rd=[t("kfr")], wr=[t("kfr")])
                K.emit("vector", "reduce_max", QSR[0:1, 0:1], QSR[0:1, :], axis=AX.X,
                       rd=[t("qsr")], wr=[t("qsr")])
                K.emit("vector", "tensor_scalar", QSR[0:1, 0:1], QSR[0:1, 0:1],
                       KFR[0:1, 0:1], -0.125, OP.mult, OP.mult,
                       rd=[t("qsr"), t("kfr")], wr=[t("qsr")])
                K.emit("tensor", "matmul", C.PS1[0:128, 0:1], C.onesf[0:1, 0:128],
                       QSR[0:1, 0:1], start=True, stop=True,
                       rd=[("onesf",), t("qsr")], wr=[("PS1",)])
                K.emit("vector", "tensor_copy", CBC[:, :], C.PS1[0:128, 0:1],
                       rd=[("PS1",)], wr=[t("cbc")])
                for h in range(16):
                    hb_ = (h % 2) * 64
                    pk = ("PA", h % 4)
                    pav = C.pa(h % 4)
                    nst = 4 * (c + 1)
                    for st in range(nst):
                        psc = ("PB", st % 2)
                        pscap = C.pb(st % 2)
                        K.emit("tensor", "matmul", pscap,
                               KTH[hb_:hb_ + 64, st * 128:(st + 1) * 128],
                               QT[hb_:hb_ + 64, h // 2, :],
                               start=True, stop=True, rd=[t("kth"), t("qt")], wr=[psc])
                        delta = st * 128 - c * 512
                        if delta >= 0:
                            K.emit("vector", "tensor_tensor", pscap, pscap,
                                   C.mtri[:, 384 - delta:896 - delta], OP.add,
                                   rd=[psc, ("mtri",)], wr=[psc])
                        K.emit("scalar", "activation", EB[:, st % 2, :], pscap, AF.Exp,
                               scale=0.125, bias=CBC[:, 0:1],
                               rd=[psc, t("cbc")], wr=[(f"q_eb", st % 2)])
                        K.emit("tensor", "matmul", pav[0:65, :], VTOK[:, st, :],
                               EB[:, st % 2, :],
                               start=(st == 0), stop=(st == nst - 1),
                               rd=[(f"q_eb", st % 2), t("vtok")], wr=[pk])
                    K.emit("vector", "reciprocal", RD[:, :], pav[64:65, :], rd=[pk], wr=[t("rd")])
                    K.emit("tensor", "matmul", C.PS1[0:64, :], C.onesf[0:1, 0:64],
                           RD[0:1, :], start=True, stop=True,
                           rd=[("onesf",), t("rd")], wr=[("PS1",)])
                    K.emit("vector", "tensor_copy", RB64[:, :], C.PS1[0:64, :],
                           rd=[("PS1",)], wr=[t("rb64")])
                    K.emit("vector", "tensor_tensor", OT[hb_:hb_ + 64, h // 2, :],
                           pav[0:64, :], RB64[:, :], OP.mult,
                           rd=[pk, t("rb64")], wr=[t("ot")])
                for m in range(DT):
                    pk = ("PB", m % 2)
                    pap = C.pb(m % 2)
                    for k in range(DT):
                        K.emit("tensor", "matmul", pap, PW[:, k, m * 128:(m + 1) * 128],
                               OT[:, k, :], start=(k == 0), stop=(k == DT - 1),
                               rd=[t("pw"), t("ot")], wr=[pk])
                    K.emit("vector", "tensor_tensor", NB.xg[:, m, :], NB.xg[:, m, :], pap,
                           OP.add, rd=[("qxg",), pk], wr=[("qxg",)])
                K.emit("sync", "dma_start", dst3[:, :, gs], NB.xg[:, :, :],
                       rd=[("qxg",)], wr=[(dstkey, g)], dma=True)
    K.barrier()



def build_mamba(K, C, W, scr, src3, dst3, srckey, dstkey):
    nc = K.nc
    ipb = scr["IPB"]
    with ExitStack() as ctx:
        sb = lambda name, shape, dt: ctx.enter_context(nc.sbuf_tensor(f"mb_{name}", shape, dt))
        t = lambda n: (f"mb_{n}",)
        NB = NormBufs(nc, ctx, "mb", rms_only=True)
        RN = sb("rn", [128, 8, 512], BF16)
        IPS = [sb(f"ips{i}", [128, 8, 128], BF16) for i in range(2)]
        Z = sb("z", [128, 16, 512], BF16)
        XBCP = sb("xbcp", [128, 17, 515], BF16)
        XBC = sb("xbc", [128, 17, 512], BF16)
        DTBUF = sb("dtbuf", [32, 512], F32)
        CONVW = sb("convw", [128, 68], F32)
        CONVB = sb("convb", [128, 17], F32)
        MN = sb("mn", [128, 16], F32)
        DBV = sb("dbv", [128, 16], F32)
        DSEL = sb("dsel", [32, 16], F32)
        MAP64 = sb("map64", [32, 128], F32)
        ACOL = sb("acol", [32, 1], F32)
        DTBC = sb("dtbc", [32, 1], F32)
        DVC = sb("dvc", [32, 1], F32)
        DTA = sb("dta", [32, 128], F32)
        S_ = sb("s_", [32, 128], F32)
        PALL = sb("pall", [32, 128], F32)
        SROW = sb("srowf", [1, 1024], F32)
        PROW = sb("prow", [1, 1024], BF16)
        ST = sb("st", [128, 32], F32)
        DTT = sb("dtt", [128, 32], F32)
        DTC = sb("dtc", [128, 32], F32)
        GTB = sb("gtb", [128, 128], BF16)
        CFB = sb("cfb", [64, 128], BF16)
        BT = sb("bt", [128, 64], BF16)
        BSC = sb("bsc", [128, 32, 64], BF16)
        TTMP = sb("ttmp", [128, 8, 128], F32)
        MT = sb("mt", [128, 32, 128], BF16)
        U = sb("u", [128, 16, 128], BF16)
        PBS = sb("pbs", [64, 32, 128], BF16)
        CSC = sb("csc", [64, 16, 128], F32)
        HST = sb("hst", [64, 32, 64], F32)
        DXS = sb("dxs", [128, 16, 128], F32)
        GM = sb("gm", [128, 16, 512], BF16)
        SQB = sb("sqb", [128, 2, 512], BF16)
        CT = sb("ct", [128, 2, 512], F32)
        OPS = [sb(f"ops{i}", [128, 16, 128], BF16) for i in range(2)]

        K.emit("sync", "dma_start", CONVW[:, :], W["h_convw"], wr=[t("convw")], dma=True)
        K.emit("sync", "dma_start", CONVB[:, :], W["h_convb"], wr=[t("convb")], dma=True)
        K.emit("sync", "dma_start", MN[:, :], W["h_mnorm"], wr=[t("mn")], dma=True)
        K.emit("sync", "dma_start", DTBC[:, :], W["h_dtb"], wr=[t("dtbc")], dma=True)
        K.emit("sync", "dma_start", ACOL[:, :], W["h_alog"], wr=[t("acol")], dma=True)
        K.emit("sync", "dma_start", DVC[:, :], W["h_dvec"], wr=[t("dvc")], dma=True)
        K.emit("sync", "dma_start", MAP64[:, :], W["c_map64"], wr=[t("map64")], dma=True)
        K.emit("sync", "dma_start", DSEL[:, :], W["c_sel32"], wr=[t("dsel")], dma=True)
        K.emit("scalar", "activation", ACOL[:, :], ACOL[:, :], AF.Exp,
               rd=[t("acol")], wr=[t("acol")])
        K.emit("vector", "tensor_scalar_mul", ACOL[:, :], ACOL[:, :], -1.0,
               rd=[t("acol")], wr=[t("acol")])
        K.emit("vector", "tensor_tensor", DSEL[:, :], DSEL[:, :],
               DVC[:, 0:1].broadcast_to([32, 16]), OP.mult,
               rd=[t("dsel"), t("dvc")], wr=[t("dsel")])
        K.emit("tensor", "matmul", C.PS1[:, 0:16], MAP64[:, :], DSEL[:, :],
               start=True, stop=True, rd=[t("map64"), t("dsel")], wr=[("PS1",)])
        K.emit("vector", "tensor_copy", DBV[:, :], C.PS1[:, 0:16], rd=[("PS1",)], wr=[t("dbv")])

        for seq in range(2):
            K.emit("vector", "memset", HST[:, :, :], 0.0, wr=[t("hst")])
            K.emit("vector", "memset", XBCP[:, :, 0:3], 0.0, wr=[t("xbcp")])
            for grp in range(4):
                g = seq * 4 + grp
                gs = slice(g * 512, (g + 1) * 512)
                norm_group(K, C, NB, src3, g, RN[:, :, :], "rms", [(srckey, g)], t("rn"))
                def conv_tile(tt):
                    ctp = tt % 2
                    K.emit("vector", "tensor_scalar_mul", CT[:, ctp, :], XBCP[:, tt, 0:512],
                           CONVW[:, 4 * tt:4 * tt + 1], rd=[t("xbcp"), t("convw")],
                           wr=[("mbct", ctp)])
                    for j in range(1, 4):
                        K.emit("vector", "scalar_tensor_tensor", CT[:, ctp, :],
                               XBCP[:, tt, j:512 + j], CONVW[:, 4 * tt + j:4 * tt + j + 1],
                               CT[:, ctp, :], OP.mult, OP.add,
                               rd=[t("xbcp"), t("convw")], wr=[("mbct", ctp)])
                    K.emit("scalar", "activation", XBC[:, tt, :], CT[:, ctp, :], AF.Silu,
                           bias=CONVB[:, tt:tt + 1], rd=[("mbct", ctp), t("convb")],
                           wr=[t("xbc")])

                for m in list(range(16, 34)) + list(range(16)):
                    sl = m % 2
                    mk_ = t(f"ips{sl}")
                    mw = 128 if m < 33 else 32
                    K.emit("sync", "dma_start", IPS[sl][:, :, 0:mw], ipb[m][:, :, 0:mw],
                           rd=[("IPB",)], wr=[mk_], stream=f"ips{sl}", dma=True)
                    pk = ("PB", sl)
                    pap = C.pb(sl)[0:mw, :]
                    for k in range(DT):
                        K.emit("tensor", "matmul", pap, IPS[sl][:, k, 0:mw], RN[:, k, :],
                               start=(k == 0), stop=(k == DT - 1), rd=[mk_, t("rn")], wr=[pk])
                    if m < 16:
                        K.emit("scalar", "activation", Z[:, m, :], pap, AF.Silu,
                               rd=[pk], wr=[t("z")])
                    elif m < 33:
                        K.emit("vector", "tensor_copy", XBCP[:, m - 16, 3:515], pap,
                               rd=[pk], wr=[t("xbcp")])
                        conv_tile(m - 16)
                    else:
                        K.emit("scalar", "activation", CT[0:32, 0, :], pap, AF.Exp,
                               bias=DTBC[:, 0:1], rd=[pk, t("dtbc")], wr=[("mbct", 0)])
                        K.emit("scalar", "activation", DTBUF[:, :], CT[0:32, 0, :], AF.Ln,
                               bias=C.consts[0:32, 1:2], rd=[("mbct", 0), ("consts",)],
                               wr=[t("dtbuf")])
                K.emit("vector", "tensor_copy", XBCP[:, :, 0:3], XBCP[:, :, 512:515],
                       rd=[t("xbcp")], wr=[t("xbcp")])
                for c in range(4):
                    tc = slice(c * 128, (c + 1) * 128)
                    K.emit("vector", "tensor_scalar_mul", DTA[:, :], DTBUF[:, tc],
                           ACOL[:, 0:1], rd=[t("dtbuf"), t("acol")], wr=[t("dta")])
                    K.emit("vector", "tensor_tensor_scan", S_[:, :], C.onesf[0:32, 0:128],
                           DTA[:, :], 0.0, OP.mult, OP.add, rd=[("onesf",), t("dta")],
                           wr=[t("s_")])
                    K.emit("scalar", "activation", PALL[:, :], S_[:, :], AF.Exp,
                           rd=[t("s_")], wr=[t("pall")])
                    K.emit("tensor", "transpose", C.PS1[0:128, 0:32], S_[:, :],
                           C.identf[0:32, 0:32], rd=[t("s_"), ("identf",)], wr=[("PS1",)])
                    K.emit("vector", "tensor_copy", ST[:, :], C.PS1[0:128, 0:32],
                           rd=[("PS1",)], wr=[t("st")])
                    K.emit("tensor", "transpose", C.PS1[0:128, 0:32], DTBUF[:, tc],
                           C.identf[0:32, 0:32], rd=[t("dtbuf"), ("identf",)], wr=[("PS1",)])
                    K.emit("vector", "tensor_copy", DTT[:, :], C.PS1[0:128, 0:32],
                           rd=[("PS1",)], wr=[t("dtt")])

                    K.emit("vector", "tensor_copy", CFB[:, :], XBC[64:128, 16, tc],
                           rd=[t("xbc")], wr=[t("cfb")])
                    K.emit("tensor", "matmul", C.PS1[:, 0:128], XBC[0:64, 16, tc],
                           CFB[:, :], start=True, stop=True, rd=[t("xbc"), t("cfb")],
                           wr=[("PS1",)])
                    K.emit("vector", "tensor_copy", GTB[:, :], C.PS1[:, 0:128],
                           rd=[("PS1",)], wr=[t("gtb")])
                    for r in range(4):
                        K.emit("gpsimd", "dma_start", PROW[:, :], PALL[8 * r:8 * (r + 1), :],
                               rd=[t("pall")], wr=[t("prowb")], stream="prow", dma=True)
                        for j in range(2):
                            K.emit("tensor", "matmul", C.pb(j)[0:64, :],
                                   C.onesbr[0:1, 0:64], PROW[0:1, j * 512:(j + 1) * 512],
                                   start=True, stop=True,
                                   rd=[("onesbr",), t("prowb")], wr=[("PB", j)])
                        K.emit("vector", "tensor_copy", PBS[:, 8 * r:8 * (r + 1), :],
                               C.PB[0:64, :].rearrange("p (a b) -> p a b", b=128),
                               rd=[("PB", 0), ("PB", 1)], wr=[t("pbs")])
                    for r in range(4):
                        hr = slice(8 * r, 8 * (r + 1))
                        K.emit("gpsimd", "dma_start", SROW[:, :], S_[hr, :],
                               rd=[t("s_")], wr=[t("srowb")], stream="srow", dma=True)
                        for j in range(2):
                            K.emit("tensor", "matmul", C.pb(j), C.onesf[0:1, 0:128],
                                   SROW[0:1, j * 512:(j + 1) * 512],
                                   start=True, stop=True, rd=[("onesf",), t("srowb")],
                                   wr=[("PB", j)])
                        K.emit("vector", "tensor_tensor", TTMP[:, :, :],
                               C.PB[:, :].rearrange("p (a b) -> p a b", b=128),
                               ST[:, hr].unsqueeze(2).broadcast_to([128, 8, 128]),
                               OP.subtract, rd=[("PB", 0), ("PB", 1), t("st")], wr=[t("ttmp")])
                        K.emit("vector", "tensor_tensor", TTMP[:, :, :], TTMP[:, :, :],
                               C.mtri[:, 384:512].unsqueeze(1).broadcast_to([128, 8, 128]),
                               OP.add, rd=[t("ttmp"), ("mtri",)], wr=[t("ttmp")])
                        K.emit("scalar", "activation", TTMP[:, :, :], TTMP[:, :, :], AF.Exp,
                               rd=[t("ttmp")], wr=[t("ttmp")])
                        K.emit("vector", "tensor_copy", DTC[:, hr], TTMP[:, :, 127],
                               rd=[t("ttmp")], wr=[t("dtc")])
                        K.emit("vector", "tensor_tensor", MT[:, hr, :], TTMP[:, :, :],
                               GTB[:, :].unsqueeze(1).broadcast_to([128, 8, 128]), OP.mult,
                               rd=[t("ttmp"), t("gtb")], wr=[t("mt")])
                    for tt in range(16):
                        K.emit("tensor", "transpose", C.PT[0:128, 0:128],
                               XBC[:, tt, tc], C.ident[:, :], rd=[t("xbc"), ("ident",)],
                               wr=[("PT",)])
                        K.emit("vector", "tensor_tensor",
                               U[:, tt, :].rearrange("p (a b) -> p a b", b=64),
                               C.PT[0:128, 0:128].rearrange("p (a b) -> p a b", b=64),
                               DTT[:, 2 * tt:2 * tt + 2].unsqueeze(2)
                               .broadcast_to([128, 2, 64]), OP.mult,
                               rd=[("PT",), t("dtt")], wr=[t("u")])
                    for r2 in range(2):
                        K.emit("vector", "tensor_tensor", CSC[:, :, :],
                               CFB[:, :].unsqueeze(1).broadcast_to([64, 16, 128]),
                               PBS[:, 16 * r2:16 * (r2 + 1), :], OP.mult,
                               rd=[t("cfb"), t("pbs")], wr=[t("csc")])
                        for hi in range(16):
                            h = 16 * r2 + hi
                            tt, hh = h // 2, h % 2
                            K.emit("tensor", "matmul",
                                   C.PA[hh * 64:(hh + 1) * 64, tt * 128:(tt + 1) * 128],
                                   U[:, tt, hh * 64:(hh + 1) * 64], MT[:, h, :],
                                   start=True, stop=False, tile_position=(0, hh * 64),
                                   rd=[t("u"), t("mt")], wr=[("PA", tt % 4)])
                            K.emit("tensor", "matmul",
                                   C.PA[hh * 64:(hh + 1) * 64, tt * 128:(tt + 1) * 128],
                                   HST[:, h, :], CSC[:, hi, :],
                                   start=False, stop=True, tile_position=(0, hh * 64),
                                   rd=[t("hst"), t("csc")], wr=[("PA", tt % 4)])
                    K.emit("vector", "tensor_tensor", DXS[:, :, :], XBC[:, 0:16, tc],
                           DBV[:, :].unsqueeze(2).broadcast_to([128, 16, 128]), OP.mult,
                           rd=[t("xbc"), t("dbv")], wr=[t("dxs")])
                    K.emit("vector", "tensor_tensor", DXS[:, :, :],
                           C.PA[:, :].rearrange("p (a b) -> p a b", b=128), DXS[:, :, :],
                           OP.add, rd=[("PA", 0), ("PA", 1), ("PA", 2), ("PA", 3),
                                       t("dxs")], wr=[t("dxs")])
                    K.emit("vector", "tensor_tensor", GM[:, :, tc], DXS[:, :, :],
                           Z[:, :, tc], OP.mult, rd=[t("dxs"), t("z")], wr=[t("gm")])
                    K.emit("tensor", "transpose", C.PT[0:128, 0:64], XBC[0:64, 16, tc],
                           C.ident[0:64, 0:64], rd=[t("xbc"), ("ident",)], wr=[("PT",)])
                    K.emit("vector", "tensor_copy", BT[:, :], C.PT[0:128, 0:64],
                           rd=[("PT",)], wr=[t("bt")])
                    K.emit("vector", "tensor_tensor", BSC[:, :, :],
                           BT[:, :].unsqueeze(1).broadcast_to([128, 32, 64]),
                           DTC[:, :].unsqueeze(2).broadcast_to([128, 32, 64]), OP.mult,
                           rd=[t("bt"), t("dtc")], wr=[t("bsc")])
                    K.emit("vector", "tensor_tensor", HST[:, :, :], HST[:, :, :],
                           PBS[:, :, 127:128].broadcast_to([64, 32, 64]), OP.mult,
                           rd=[t("hst"), t("pbs")], wr=[t("hst")])
                    for h in range(32):
                        K.emit("tensor", "matmul", C.PA[0:64, h * 64:(h + 1) * 64],
                               BSC[:, h, :], U[:, h // 2, (h % 2) * 64:(h % 2) * 64 + 64],
                               start=True, stop=True, rd=[t("bsc"), t("u")],
                               wr=[("PA", h % 4)])
                    K.emit("vector", "tensor_tensor", HST[:, :, :], HST[:, :, :],
                           C.PA[0:64, :].rearrange("p (a b) -> p a b", b=64), OP.add,
                           rd=[t("hst"), ("PA", 0), ("PA", 1), ("PA", 2), ("PA", 3)],
                           wr=[t("hst")])
                for tt in range(16):
                    K.emit("scalar", "activation", SQB[:, tt % 2, :], GM[:, tt, :], AF.Square,
                           rd=[t("gm")], wr=[("mbsqb", tt % 2)])
                    K.emit("tensor", "matmul", C.PS1[0:1, :], C.onesb[:, 0:1], SQB[:, tt % 2, :],
                           start=(tt == 0), stop=(tt == 15), rd=[("onesb",), ("mbsqb", tt % 2)],
                           wr=[("PS1",)])
                K.emit("vector", "tensor_scalar_mul", NB.r1[:, :], C.PS1[0:1, :], 1.0 / 2048,
                       rd=[("PS1",)], wr=[("mbr1",)])
                K.emit("scalar", "activation", NB.r1[:, :], NB.r1[:, :], AF.Ln,
                       bias=C.consts[0:1, 0:1], rd=[("mbr1",), ("consts",)], wr=[("mbr1",)])
                K.emit("scalar", "activation", NB.r1[:, :], NB.r1[:, :], AF.Exp, scale=-0.5,
                       rd=[("mbr1",)], wr=[("mbr1",)])
                K.emit("tensor", "matmul", C.pb(0), C.onesf[0:1, 0:128], NB.r1[0:1, :],
                       start=True, stop=True, rd=[("onesf",), ("mbr1",)], wr=[("PB", 0)])
                K.emit("vector", "tensor_tensor", GM[:, :, :], GM[:, :, :],
                       C.pb(0).unsqueeze(1).broadcast_to([128, 16, 512]), OP.mult,
                       rd=[t("gm"), ("PB", 0)], wr=[t("gm")])
                K.emit("vector", "tensor_tensor", GM[:, :, :], GM[:, :, :],
                       MN[:, :].unsqueeze(2).broadcast_to([128, 16, 512]), OP.mult,
                       rd=[t("gm"), t("mn")], wr=[t("gm")])
                for m in range(DT):
                    sl = m % 2
                    K.emit("gpsimd", "dma_start", OPS[sl][:, :, :], W["h_outproj_cm"][m]
                           .rearrange("(t p) c -> p t c", p=128),
                           wr=[t(f"ops{sl}")], stream=f"ops{sl}", dma=True)
                    pk = ("PA", 2 + sl)
                    pap = C.pa(2 + sl)
                    for k in range(16):
                        K.emit("tensor", "matmul", pap, OPS[sl][:, k, :], GM[:, k, :],
                               start=(k == 0), stop=(k == 15),
                               rd=[t(f"ops{sl}"), t("gm")], wr=[pk])
                    K.emit("vector", "tensor_tensor", NB.xg[:, m, :], NB.xg[:, m, :], pap,
                           OP.add, rd=[("mbxg",), pk], wr=[("mbxg",)])
                K.emit("sync", "dma_start", dst3[:, :, gs], NB.xg[:, :, :],
                       rd=[("mbxg",)], wr=[(dstkey, g)], dma=True)
    K.barrier()


def host_prepare(inputs):
    W = {}
    for n in WEIGHT_SHAPES:
        W[n] = np.ascontiguousarray(np.asarray(inputs[n], dtype=np.float32))
    W["h_wout1T"] = np.ascontiguousarray(np.asarray(inputs["L0_ffn_Wout1"], np.float32).T)
    W["h_wout2T"] = np.ascontiguousarray(np.asarray(inputs["L0_ffn_Wout2"], np.float32).T)
    W["h_win1bias"] = np.ascontiguousarray(
        np.asarray(inputs["L0_ffn_Win1"], np.float32)[1024].reshape(4, 128).T)
    W["h_win2bias"] = np.ascontiguousarray(
        np.asarray(inputs["L0_ffn_Win2"], np.float32)[4096].reshape(4, 128).T)
    cw = np.asarray(inputs["L0_attn_conv_w"], np.float32).reshape(17, 128, 4)
    W["h_convw"] = np.ascontiguousarray(cw.transpose(1, 0, 2).reshape(128, 68))
    W["h_convb"] = np.ascontiguousarray(
        np.asarray(inputs["L0_attn_conv_b"], np.float32).reshape(17, 128).T)
    W["h_mnorm"] = np.ascontiguousarray(
        np.asarray(inputs["L0_attn_mnorm_w"], np.float32).reshape(16, 128).T)
    W["h_dtb"] = np.asarray(inputs["L0_attn_dt_bias"], np.float32).reshape(32, 1)
    W["h_alog"] = np.asarray(inputs["L0_attn_A_log"], np.float32).reshape(32, 1)
    W["h_dvec"] = np.asarray(inputs["L0_attn_D"], np.float32).reshape(32, 1)
    W["h_maak"] = np.ascontiguousarray(
        np.asarray(inputs["L0_attn_time_maa_k"], np.float32).reshape(8, 128).T)
    W["h_maar"] = np.ascontiguousarray(
        np.asarray(inputs["L0_attn_time_maa_r"], np.float32).reshape(8, 128).T)
    op_ = np.asarray(inputs["L0_attn_out_proj_w"], np.float32)
    W["h_outproj_cm"] = np.ascontiguousarray(op_.reshape(2048, 8, 128).transpose(1, 0, 2))
    kw_ = np.asarray(inputs["L0_attn_key_w"], np.float32)
    W["h_key_cm"] = np.ascontiguousarray(kw_.reshape(1024, 32, 128).transpose(1, 0, 2))
    vw_ = np.asarray(inputs["L0_attn_value_w"], np.float32)
    W["h_value_cm"] = np.ascontiguousarray(vw_.reshape(4096, 8, 128).transpose(1, 0, 2))
    for a, b in (("h_wqT", "L1_attn_w_q"), ("h_wkT", "L1_attn_w_k"),
                 ("h_wvT", "L1_attn_w_v"), ("h_woT", "L1_attn_w_o")):
        W[a] = np.ascontiguousarray(np.asarray(inputs[b], np.float32).T)
    W["h_gain"] = np.asarray(inputs["L1_attn_gain"], np.float32).reshape(1, 1)
    W["h_routerT"] = np.ascontiguousarray(np.asarray(inputs["L1_ffn_router_w"], np.float32).T)
    W["h_ebias"] = np.asarray(inputs["L1_ffn_experts_bias"], np.float32).reshape(8, 1)
    W["h_w1"] = np.ascontiguousarray(np.concatenate(
        [np.asarray(inputs["L1_ffn_w1"], np.float32),
         np.asarray(inputs["L1_ffn_sw1"], np.float32)], axis=0))
    W["h_w3"] = np.ascontiguousarray(np.concatenate(
        [np.asarray(inputs["L1_ffn_w3"], np.float32),
         np.asarray(inputs["L1_ffn_sw3"], np.float32)], axis=0))
    W["h_w2"] = np.ascontiguousarray(np.concatenate(
        [np.asarray(inputs["L1_ffn_w2"], np.float32),
         np.asarray(inputs["L1_ffn_sw2"], np.float32)], axis=0))
    p = np.arange(128)[:, None]
    q = np.arange(1024)[None, :]
    W["c_mtri"] = np.where(q >= p + 384, 0.0, NEG).astype(np.float32)
    W["c_ident"] = np.eye(128, dtype=ml_dtypes.bfloat16)
    W["c_identf"] = np.eye(128, dtype=np.float32)
    W["c_kidx"] = np.tile(np.arange(16, dtype=np.float32)[None, :], (128, 1))
    W["c_invk"] = np.tile((1.0 / np.arange(1, 17, dtype=np.float32))[None, :], (128, 1))
    W["c_kval"] = np.tile(np.arange(1, 17, dtype=np.float32)[None, :], (128, 1))
    W["c_tokids"] = np.ascontiguousarray(
        (np.arange(32)[None, :] * 128 + np.arange(128)[:, None]).astype(np.int32))
    W["c_iota128"] = np.arange(128, dtype=np.float32).reshape(128, 1)
    hh = np.arange(32)[:, None]
    W["c_map64"] = (np.arange(128)[None, :] // 64 == hh % 2).astype(np.float32)
    W["c_sel32"] = (np.arange(16)[None, :] == hh // 2).astype(np.float32)
    return W


def shard_x(x, c):
    return np.ascontiguousarray(
        x[2 * c:2 * c + 2].transpose(2, 0, 1).reshape(1024, 4096)
        .reshape(8, 128, 4096).transpose(1, 0, 2))


def unshard_out(oc):
    return np.ascontiguousarray(oc).transpose(1, 0, 2).reshape(1024, 2, 2048).transpose(1, 2, 0)


def kernel(**inputs):
    W = host_prepare(inputs)
    x = np.asarray(inputs["L0_attn_x"], np.float32)
    nc = build_program()
    in_maps = []
    for c in range(NCORES):
        m = {"x_in": shard_x(x, c)}
        m.update(W)
        in_maps.append(m)
    res = run_bass_kernel_spmd(nc, in_maps, core_ids=list(range(NCORES)))
    outs = [unshard_out(np.asarray(res.results[c]["out"])) for c in range(NCORES)]
    return np.concatenate(outs, axis=0).astype(np.float32)
